# version 15
# baseline (speedup 1.0000x reference)
import numpy as np
import ml_dtypes
from contextlib import ExitStack
import concourse.bass as bass
import concourse.mybir as mybir
from concourse.bass_utils import run_bass_kernel_spmd

F32 = mybir.dt.float32
BF16 = mybir.dt.bfloat16
I32 = mybir.dt.int32
U8 = mybir.dt.uint8
AF = mybir.ActivationFunctionType
ALU = mybir.AluOpType
AX = mybir.AxisListType

T = 4096
D = 2048
NCORES = 8

ENGINES = ("sync", "gpsimd", "vector", "scalar", "tensor")


class Sem:
    __slots__ = ("h", "n", "name")

    def __init__(self, h, name):
        self.h = h
        self.n = 0
        self.name = name

    def at(self):
        return (self, self.n)


class P:

    def __init__(self, name):
        self.nc = bass.Bass("TRN2", target_bir_lowering=False, debug=False)
        self.stack = ExitStack()
        self.ops = {e: [] for e in ENGINES}
        self.n_inst = 0
        self.scope = None
        self.prefix = ""

    def push_scope(self, prefix):
        self.scope = ExitStack()
        self.prefix = prefix

    def pop_scope(self):
        self.scope.close()
        self.scope = None
        self.prefix = ""

    def barrier(self, C):
        sems = [C.LD, C.LG, C.ST, C.TS, C.VS, C.SS]
        for eng in ENGINES:
            for s in sems:
                if s.n:
                    self.wait(eng, s, s.n)

    def sem(self, name):
        h = self.stack.enter_context(self.nc.semaphore(name))
        return Sem(h, name)

    def sbuf(self, name, shape, dtype):
        name = self.prefix + name
        if self.scope is not None:
            return self.scope.enter_context(self.nc.sbuf_tensor(name, list(shape), dtype))
        return self.nc.alloc_sbuf_tensor(name, list(shape), dtype)

    def sbuf_scoped(self, st, name, shape, dtype):
        return st.enter_context(self.nc.sbuf_tensor(self.prefix + name, list(shape), dtype))

    def psum(self, name, shape, dtype=F32):
        name = self.prefix + name
        if self.scope is not None:
            return self.scope.enter_context(self.nc.psum_tensor(name, list(shape), dtype))
        return self.nc.alloc_psum_tensor(name, list(shape), dtype)

    def psum_scoped(self, st, name, shape, dtype=F32):
        return st.enter_context(self.nc.psum_tensor(self.prefix + name, list(shape), dtype))

    def dram(self, name, shape, dtype, kind="Internal"):
        if kind == "Internal":
            name = self.prefix + name
        return self.nc.dram_tensor(name, list(shape), dtype, kind=kind)

    def op(self, eng, meth, *args, wait=(), inc=None, lowp=False, **kw):
        self.n_inst += 1
        if inc is not None:
            inc[0].n += inc[1]
        wait = tuple((s, v) for (s, v) in wait)

        if meth in ("transpose", "reciprocal"):
            lowp = True

        def f(e, meth=meth, args=args, kw=kw, wait=wait, inc=inc, lowp=lowp):
            for s, v in wait:
                e.wait_ge(s.h, v)
            try:
                if lowp:
                    with self.nc.allow_low_precision(reason="intentional bf16 path"):
                        ins = getattr(e, meth)(*args, **kw)
                else:
                    ins = getattr(e, meth)(*args, **kw)
            except Exception as ex:
                raise RuntimeError(f"op {meth} failed: {ex}") from ex
            if inc is not None:
                ins.then_inc(inc[0].h, inc[1])

        self.ops[eng].append(f)

    def wait(self, eng, sem, val):
        self.n_inst += 1

        def f(e, sem=sem, val=val):
            e.wait_ge(sem.h, val)

        self.ops[eng].append(f)

    def raw(self, eng, fn):
        self.ops[eng].append(fn)

    def finalize(self):
        with self.stack:
            with self.nc.Block() as block:
                for eng in ENGINES:
                    fns = self.ops[eng]
                    if not fns:
                        continue
                    getattr(block, eng)(lambda e, fns=fns: [f(e) for f in fns])
        return self.nc



def host_consts():
    c = {}
    c["ident"] = np.eye(128, dtype=np.float32)
    hd = 128
    inv = 1.0 / (10000.0 ** (np.arange(0, hd, 2, dtype=np.float64) / hd))
    ang = np.arange(T, dtype=np.float64)[:, None] * inv[None, :]
    c["rope_c"] = np.cos(ang).T.astype(np.float32).copy()
    c["rope_s"] = np.sin(ang).T.astype(np.float32).copy()
    kp = np.arange(128)[:, None]
    qf = np.arange(512)[None, :]
    m = np.concatenate([(qf >= kp + 128 * r).astype(np.float32) for r in range(4)], axis=1)
    c["masks_diag"] = m
    cvals = [-2.0, -6.0, 1e-5, 1e-6, 0.0, 1.0, 0.0, 0.0]
    c["cbank"] = np.tile(np.array(cvals, np.float32)[None, :], (128, 1))
    return c

CB_NEG2, CB_NEG6, CB_EPS_LN, CB_EPS_RMS, CB_ZERO, CB_ONE = 0, 1, 2, 3, 4, 5



class Common:

    def __init__(self, p: P):
        self.p = p
        nc = p.nc
        self.d_ident = p.dram("c_ident", (128, 128), F32, kind="ExternalInput").ap()
        self.d_cbank = p.dram("c_cbank", (128, 8), F32, kind="ExternalInput").ap()
        self.ident = p.sbuf("ident_bf", (128, 128), BF16)
        self.ident_f = p.sbuf("ident_f", (128, 128), F32)
        self.cb = p.sbuf("cbank", (128, 8), F32)
        self.ones_col = p.sbuf("ones_col", (128, 1), BF16)
        self.ones_row = p.sbuf("ones_row", (1, 128), BF16)
        self.LD = p.sem("LD")
        self.LG = p.sem("LG")
        self.ST = p.sem("ST")
        self.TS = p.sem("TSm")
        self.VS = p.sem("VSm")
        self.SS = p.sem("SSm")

    def load_consts(self):
        if getattr(self, "_consts_loaded", False):
            return
        self._consts_loaded = True
        p = self.p
        p.op("gpsimd", "dma_start", out=self.ident[:], in_=self.d_ident, inc=(self.LG, 16))
        p.op("sync", "dma_start", out=self.ident_f[:], in_=self.d_ident, inc=(self.LD, 16))
        p.op("sync", "dma_start", out=self.cb[:], in_=self.d_cbank, inc=(self.LD, 16))
        p.op("vector", "memset", self.ones_col[:], 1.0)
        p.op("vector", "memset", self.ones_row[:], 1.0, inc=(self.VS, 1))

    def cbank_ap(self, idx, nparts=1):
        return self.cb[0:nparts, idx:idx + 1]

    def ln_tile(self, xr, xsq, xln_out, stats, eps_idx=CB_EPS_LN):
        p, cb = self.p, self.cb
        VS, SS = self.VS, self.SS
        for i in range(4):
            p.op("vector", "bn_stats", xsq[:, 8 + i * 6: 8 + (i + 1) * 6],
                 xr[:, i * 512:(i + 1) * 512])
        p.op("vector", "drain")
        p.op("vector", "bn_aggr", stats[:, 2:4], xsq[:, 8:32], inc=(VS, 1))
        p.op("vector", "drain")
        p.op("scalar", "activation", stats[:, 5:6], stats[:, 3:4], AF.Ln,
             bias=self.cbank_ap(eps_idx, 128), scale=1.0, wait=[VS.at()])
        p.op("scalar", "activation", stats[:, 6:7], stats[:, 5:6], AF.Exp,
             scale=-0.5, inc=(SS, 1))
        p.op("vector", "tensor_scalar", xln_out[:], xr[:], stats[:, 2:3], stats[:, 6:7],
             ALU.subtract, ALU.mult, wait=[SS.at()], inc=(VS, 1))



def build_l0attn(debug=False, fused=None):
    if fused is None:
        p = P("l0attn")
        C = Common(p)
        x_in_ap = x_out_ap = None
    else:
        p, C, x_in_ap, x_out_ap = fused
    p.push_scope("a0_")
    x_in = x_in_ap if x_in_ap is not None else p.dram("x_in", (T, D), F32, kind="ExternalInput").ap()
    x_out = x_out_ap if x_out_ap is not None else p.dram("x_out", (T, D), F32, kind="ExternalOutput").ap()
    w_attn = p.dram("w_attn", (D, 3072), F32, kind="ExternalInput").ap()
    w_proj = p.dram("w_proj", (D, D), F32, kind="ExternalInput").ap()
    nw_in = p.dram("norm_w", (128, 16), F32, kind="ExternalInput").ap()
    rope_c_in = p.dram("rope_c", (64, T), F32, kind="ExternalInput").ap()
    rope_s_in = p.dram("rope_s", (64, T), F32, kind="ExternalInput").ap()
    masks_in = p.dram("masks_diag", (128, 2048), F32, kind="ExternalInput").ap()
    dbg_kind = "ExternalOutput" if debug else "Internal"
    if debug:
        xr_dump = p.dram("xr_dump", (4, 128, D), F32, kind="ExternalOutput").ap()
        st_dump = p.dram("st_dump", (4, 128, 8), F32, kind="ExternalOutput").ap()
    qT_hbm = p.dram("qT_scratch", (8, D, 512), BF16, kind=dbg_kind).ap()
    xlnT_hbm = p.dram("xlnT_scratch", (8, D, 512), BF16, kind=dbg_kind).ap()

    stage = p.sbuf("stage", (128, 16 * 512), BF16)
    wbuf = p.sbuf("wbuf", (128, 16 * 1024), BF16)
    kT = p.sbuf("kT", (128, 4 * T), BF16)
    v_tm = p.sbuf("v_tm", (128, 4 * T), BF16)
    masks = p.sbuf("masks_sb", (128, 2048), BF16)
    nw = p.sbuf("nw_sb", (128, 16), F32)

    ps_s = [p.psum(f"ps_s{i}", (128, 512), F32) for i in range(2)]
    ps_av = p.psum("ps_av", (128, 512), F32)
    ps_dn = p.psum("ps_dn", (1, 512), F32)
    ps_r = p.psum("ps_r", (128, 512), F32)
    ps_proj = p.psum("ps_proj", (128, 512), F32)
    ps_tr = [p.psum(f"ps_tr{i}", (128, 128), BF16) for i in range(2)]

    LD, LG, ST, TSm, VS, SS = C.LD, C.LG, C.ST, C.TS, C.VS, C.SS
    C.load_consts()
    p.op("sync", "dma_start", out=nw[:], in_=nw_in, inc=(LD, 16))
    p.op("gpsimd", "dma_start", out=masks[:], in_=masks_in, inc=(LG, 16))
    consts_ld = [LD.at(), LG.at(), VS.at()]

    def qk_cols(base):
        return wbuf[:, base:base + 128]

    ab = ExitStack()
    xr = [p.sbuf_scoped(ab, f"xr{i}", (128, D), F32) for i in range(2)]
    xsq = p.sbuf_scoped(ab, "xsq", (128, D), F32)
    stats = p.sbuf_scoped(ab, "stats", (128, 8), F32)
    xln = [p.sbuf_scoped(ab, f"xln{i}", (128, D), BF16) for i in range(2)]
    qT = p.sbuf_scoped(ab, "qT", (128, 8 * 512), BF16)
    rope_c = p.sbuf_scoped(ab, "rope_c_sb", (64, T), BF16)
    rope_s = p.sbuf_scoped(ab, "rope_s_sb", (64, T), BF16)
    rtmp = [p.sbuf_scoped(ab, f"rtmp{i}", (64, 4 * 512), F32) for i in range(2)]
    vt_stage = p.sbuf_scoped(ab, "vt_stage", (128, 512), BF16)

    p.op("gpsimd", "dma_start", out=rope_c[:], in_=rope_c_in, inc=(LG, 16))
    p.op("gpsimd", "dma_start", out=rope_s[:], in_=rope_s_in, inc=(LG, 16))
    rope_ld = LG.at()

    for h in range(16):
        p.op("gpsimd", "dma_start", out=wbuf[:, h * 1024:(h + 1) * 1024],
             in_=w_attn[h * 128:(h + 1) * 128, 2048:3072], inc=(LG, 16))
    wkv_ld = LG.at()

    xln_stored = []
    pa_vguard = [0]
    for g in range(8):
        for tt in range(4):
            trow = g * 512 + tt * 128
            buf = tt % 2
            p.op("sync", "dma_start", out=xr[buf][:], in_=x_in[trow:trow + 128, :],
                 wait=[VS.at()] if (g > 0 or tt > 0) else consts_ld, inc=(LD, 16))
            p.wait("vector", LD, LD.n)
            if debug and g == 0:
                p.op("sync", "dma_start", out=xr_dump[tt], in_=xr[buf][:],
                     wait=[LD.at()], inc=(ST, 16))
            C.ln_tile(xr[buf], xsq, xln[buf], stats)
            if debug and g == 0:
                p.op("sync", "dma_start", out=st_dump[tt], in_=stats[:],
                     wait=[VS.at()], inc=(ST, 16))
            ln_v = VS.at()
            for c in range(16):
                slot = c % 2
                if c >= 2:
                    p.wait("tensor", SS, SS.n)
                p.op("tensor", "transpose", ps_tr[slot][:],
                     xln[buf][:, c * 128:(c + 1) * 128], C.ident[:],
                     wait=[ln_v] if c == 0 else [], inc=(TSm, 1))
                p.op("scalar", "activation",
                     stage[:, c * 512 + tt * 128: c * 512 + tt * 128 + 128],
                     ps_tr[slot][:], AF.Copy, scale=1.0,
                     wait=[TSm.at()], inc=(SS, 1))
            p.wait("vector", SS, SS.n)
            p.op("vector", "memset", stats[:, 7:8], 0.0, inc=(VS, 1))
        stage_done = SS.at()

        p.op("sync", "dma_start", out=xlnT_hbm[g].rearrange("(h p) t -> p h t", p=128),
             in_=stage[:].rearrange("p (h t) -> p h t", h=16),
             wait=[stage_done], inc=(LD, 16))
        xln_stored.append(LD.at())

        rope_v = {}
        for kvh in range(4):
            sl = kvh % 2
            if kvh >= 2:
                p.wait("tensor", VS, rope_v[kvh - 2])
            else:
                p.wait("tensor", VS, pa_vguard[0])
            for h in range(16):
                p.op("tensor", "matmul", ps_s[sl][:], qk_cols(h * 1024 + kvh * 128),
                     stage[:, h * 512:(h + 1) * 512],
                     start=(h == 0), stop=(h == 15),
                     wait=[stage_done, wkv_ld, rope_ld] if h == 0 else [],
                     inc=(TSm, 1) if h == 15 else None)
            dst = kT[:, kvh * T + g * 512: kvh * T + g * 512 + 512]
            _rope(p, C, ps_s[sl], rope_c[:, g * 512:g * 512 + 512],
                  rope_s[:, g * 512:g * 512 + 512], rtmp[kvh % 2], dst, TSm.at())
            rope_v[kvh] = VS.n
        for kvh in range(4):
            sl = kvh % 2
            p.wait("tensor", SS, SS.n)
            p.wait("tensor", VS, VS.n)
            for h in range(16):
                p.op("tensor", "matmul", ps_s[sl][:],
                     wbuf[:, h * 1024 + 512 + kvh * 128: h * 1024 + 512 + kvh * 128 + 128],
                     stage[:, h * 512:(h + 1) * 512],
                     start=(h == 0), stop=(h == 15), inc=(TSm, 1) if h == 15 else None)
            p.op("vector", "tensor_copy", vt_stage[:], ps_s[sl][:], wait=[TSm.at()], inc=(VS, 1))
            for ts in range(4):
                slot = ts % 2
                if ts >= 2:
                    p.wait("tensor", SS, SS.n)
                p.op("tensor", "transpose", ps_tr[slot][:],
                     vt_stage[:, ts * 128:ts * 128 + 128], C.ident[:],
                     wait=[VS.at()] if ts == 0 else [], inc=(TSm, 1))
                p.op("scalar", "activation",
                     v_tm[:, kvh * T + (g * 4 + ts) * 128: kvh * T + (g * 4 + ts) * 128 + 128],
                     ps_tr[slot][:], AF.Copy, scale=1.0,
                     wait=[TSm.at()], inc=(SS, 1))
            if kvh == 3:
                pa_vguard[0] = VS.n

    pb_vguard = [pa_vguard[0]]
    for half in range(2):
        for h in range(16):
            p.op("gpsimd", "dma_start", out=wbuf[:, h * 1024:(h + 1) * 1024],
                 in_=w_attn[h * 128:(h + 1) * 128, half * 1024:(half + 1) * 1024],
                 wait=[TSm.at()], inc=(LG, 16))
        wq_ld = LG.at()
        for g in range(8):
            p.op("sync", "dma_start", out=stage[:].rearrange("p (h t) -> p h t", h=16),
                 in_=xlnT_hbm[g].rearrange("(h p) t -> p h t", p=128),
                 wait=[xln_stored[g], TSm.at()], inc=(LD, 16))
            st_ld = LD.at()
            ropeq_v = {}
            for ql in range(8):
                qh = half * 8 + ql
                sl = ql % 2
                if ql >= 2:
                    p.wait("tensor", VS, ropeq_v[ql - 2])
                else:
                    p.wait("tensor", VS, pb_vguard[0])
                for h in range(16):
                    p.op("tensor", "matmul", ps_s[sl][:], qk_cols(h * 1024 + ql * 128),
                         stage[:, h * 512:(h + 1) * 512],
                         start=(h == 0), stop=(h == 15),
                         wait=[st_ld, wq_ld] if h == 0 else [],
                         inc=(TSm, 1) if h == 15 else None)
                dst = qT[:, ql * 512:(ql + 1) * 512]
                _rope(p, C, ps_s[sl], rope_c[:, g * 512:g * 512 + 512],
                      rope_s[:, g * 512:g * 512 + 512], rtmp[ql % 2], dst, TSm.at())
                ropeq_v[ql] = VS.n
            pb_vguard[0] = VS.n
            p.op("sync", "dma_start",
                 out=qT_hbm[g][half * 1024:(half + 1) * 1024, :].rearrange("(h p) t -> p h t", p=128),
                 in_=qT[:].rearrange("p (h t) -> p h t", h=8),
                 wait=[VS.at()], inc=(LD, 16))
    qT_stored = LD.at()

    ab.close()

    cs = ExitStack()
    qT_sb = stage
    pexp = [p.sbuf_scoped(cs, f"pexp{i}", (128, 512), BF16) for i in range(2)]
    rD = p.sbuf_scoped(cs, "rD", (1, 512), BF16)
    rB = p.sbuf_scoped(cs, "rB", (128, 512), BF16)
    yT = p.sbuf_scoped(cs, "yT", (128, 16 * 512), BF16)
    y2 = p.sbuf_scoped(cs, "y2", (128, 16 * 512), BF16)
    sqb = [p.sbuf_scoped(cs, f"sqb{i}", (128, 512), BF16) for i in range(2)]
    ssq_f = p.sbuf_scoped(cs, "ssq_f", (1, 512), F32)
    rstd_r = p.sbuf_scoped(cs, "rstd_r", (1, 512), BF16)
    nrm = p.sbuf_scoped(cs, "nrm", (128, 128), BF16)
    xres = p.sbuf_scoped(cs, "xres", (128, 2048), F32)

    for g in range(8):
        p.op("sync", "dma_start", out=qT_sb[:].rearrange("p (h t) -> p h t", h=16),
             in_=qT_hbm[g].rearrange("(h p) t -> p h t", p=128),
             wait=[qT_stored, TSm.at()], inc=(LD, 16))
        qld = LD.at()
        n_kt = 4 * g + 4
        for hh in range(16):
            kv = hh // 4
            qsl = qT_sb[:, hh * 512:(hh + 1) * 512]
            exp_v = {}
            prd_v = {}
            def emit_scores(kt):
                sl = kt % 2
                w0 = []
                if kt == 0 and hh == 0:
                    w0.append(qld)
                if kt >= 2:
                    w0.append((SS, exp_v[kt - 2]))
                else:
                    w0.append((SS, SS.n))
                p.op("tensor", "matmul", ps_s[sl][:],
                     kT[:, kv * T + kt * 128: kv * T + kt * 128 + 128], qsl,
                     start=True, stop=True, wait=w0, inc=(TSm, 1))
            emit_scores(0)
            for kt in range(n_kt):
                sl = kt % 2
                p.op("scalar", "activation", pexp[sl][:], ps_s[sl][:], AF.Exp,
                     scale=float(1.0 / np.sqrt(128.0)), wait=[TSm.at()], inc=(SS, 1))
                exp_v[kt] = SS.n
                if kt >= 4 * g:
                    r = kt - 4 * g
                    p.op("vector", "tensor_mul", pexp[sl][:], pexp[sl][:],
                         masks[:, r * 512:(r + 1) * 512], wait=[SS.at()], inc=(VS, 1))
                    prd_v[kt] = (VS, VS.n)
                else:
                    prd_v[kt] = (SS, SS.n)
                if kt + 1 < n_kt:
                    emit_scores(kt + 1)
                p.wait("tensor", prd_v[kt][0], prd_v[kt][1])
                p.op("tensor", "matmul", ps_dn[:], C.ones_col[:], pexp[sl][:],
                     start=(kt == 0), stop=(kt == n_kt - 1))
                p.op("tensor", "matmul", ps_av[:],
                     v_tm[:, kv * T + kt * 128: kv * T + kt * 128 + 128], pexp[sl][:],
                     start=(kt == 0), stop=(kt == n_kt - 1),
                     inc=(TSm, 1) if kt == n_kt - 1 else None)
            p.op("vector", "reciprocal", rD[:], ps_dn[:], wait=[TSm.at()], inc=(VS, 1))
            p.op("tensor", "matmul", ps_r[:], C.ones_row[:], rD[:], start=True, stop=True,
                 wait=[VS.at()], inc=(TSm, 1))
            p.op("scalar", "activation", rB[:], ps_r[:], AF.Copy, scale=1.0,
                 wait=[TSm.at()], inc=(SS, 1))
            p.op("vector", "tensor_mul", yT[:, hh * 512:(hh + 1) * 512], ps_av[:], rB[:],
                 wait=[SS.at()], inc=(VS, 1))
            p.wait("tensor", VS, VS.n)
        yT_done = VS.at()

        evac_p = {}
        for half in range(2):
            for h in range(16):
                p.op("gpsimd", "dma_start", out=wbuf[:, h * 1024:(h + 1) * 1024],
                     in_=w_proj[h * 128:(h + 1) * 128, half * 1024:(half + 1) * 1024],
                     wait=[yT_done, TSm.at()] if h == 0 else [], inc=(LG, 16))
            wp_ld = LG.at()
            for ol in range(8):
                oc = half * 8 + ol
                pb = ps_proj if oc % 2 == 0 else ps_av
                w0 = [yT_done, wp_ld] if ol == 0 else []
                if oc >= 2:
                    w0.append((SS, evac_p[oc - 2]))
                for h in range(16):
                    p.op("tensor", "matmul", pb[:],
                         wbuf[:, h * 1024 + ol * 128: h * 1024 + ol * 128 + 128],
                         yT[:, h * 512:(h + 1) * 512],
                         start=(h == 0), stop=(h == 15),
                         wait=w0 if h == 0 else [],
                         inc=(TSm, 1) if h == 15 else None)
                p.op("scalar", "activation", y2[:, oc * 512:(oc + 1) * 512], pb[:],
                     AF.Copy, scale=1.0, wait=[TSm.at()], inc=(SS, 1))
                p.op("scalar", "activation", sqb[oc % 2][:], pb[:], AF.Square, inc=(SS, 1))
                evac_p[oc] = SS.n
                p.op("tensor", "matmul", ps_dn[:], C.ones_col[:], sqb[oc % 2][:],
                     start=(oc == 0), stop=(oc == 15), wait=[SS.at()],
                     inc=(TSm, 1) if oc == 15 else None)
        p.op("scalar", "activation", ssq_f[:], ps_dn[:], AF.Ln, scale=1.0 / D,
             bias=C.cbank_ap(CB_EPS_RMS, 1), wait=[TSm.at()])
        p.op("scalar", "activation", rstd_r[:], ssq_f[:], AF.Exp, scale=-0.5, inc=(SS, 1))
        p.op("tensor", "matmul", ps_r[:], C.ones_row[:], rstd_r[:], start=True, stop=True,
             wait=[SS.at()], inc=(TSm, 1))
        p.op("scalar", "activation", rB[:], ps_r[:], AF.Copy, scale=1.0,
             wait=[TSm.at()], inc=(SS, 1))
        rb_done = SS.at()

        for tt in range(4):
            trow = g * 512 + tt * 128
            p.op("sync", "dma_start", out=xres[:], in_=x_in[trow:trow + 128, :],
                 wait=[ST.at()], inc=(LD, 16))
            p.wait("vector", LD, LD.n)
            for oc in range(16):
                slot = oc % 2
                p.op("vector", "scalar_tensor_tensor", nrm[:, 0:128],
                     y2[:, oc * 512 + tt * 128: oc * 512 + tt * 128 + 128],
                     nw[:, oc:oc + 1], rB[:, tt * 128:tt * 128 + 128],
                     ALU.mult, ALU.mult,
                     wait=[rb_done, TSm.at()] if (oc == 0 and tt == 0) else [TSm.at()],
                     inc=(VS, 1))
                if oc >= 2:
                    p.wait("tensor", VS, VS.n)
                p.op("tensor", "transpose", ps_tr[slot][:],
                     nrm[:, 0:128], C.ident[:], wait=[VS.at()], inc=(TSm, 1))
                p.op("vector", "tensor_add", xres[:, oc * 128:(oc + 1) * 128],
                     xres[:, oc * 128:(oc + 1) * 128],
                     ps_tr[slot][:],
                     wait=[TSm.at()], inc=(VS, 1))
            p.op("sync", "dma_start", out=x_out[trow:trow + 128, :], in_=xres[:],
                 wait=[VS.at()], inc=(ST, 16))
    cs.close()
    p.wait("sync", ST, ST.n)
    p.pop_scope()
    return p


def _rope(p, C, ps, ctile, stile, rtmp, dst, mm_done):
    VS = C.VS
    t0, t1 = rtmp[:, 0:512], rtmp[:, 512:1024]
    t2, t3 = rtmp[:, 1024:1536], rtmp[:, 1536:2048]
    p.op("vector", "tensor_mul", t0[0:64, :], ps[0:64, :], ctile[0:64, :], wait=[mm_done])
    p.op("vector", "tensor_mul", t1[0:64, :], ps[64:128, :], stile[0:64, :])
    p.op("vector", "tensor_sub", dst[0:64, :], t0[0:64, :], t1[0:64, :])
    p.op("vector", "tensor_mul", t2[0:64, :], ps[0:64, :], stile[0:64, :])
    p.op("vector", "tensor_mul", t3[0:64, :], ps[64:128, :], ctile[0:64, :])
    p.op("vector", "tensor_add", dst[64:128, :], t2[0:64, :], t3[0:64, :], inc=(VS, 1))



def emit_ln_stage(p, C, x_in, stage, xr, xsq, stats, xln, ps_tr, g, consts_ld=None):
    LD, VS, SS, TSm = C.LD, C.VS, C.SS, C.TS
    for tt in range(4):
        trow = g * 512 + tt * 128
        buf = tt % 2
        first = (g == 0 and tt == 0 and consts_ld is not None)
        p.op("sync", "dma_start", out=xr[buf][:], in_=x_in[trow:trow + 128, :],
             wait=consts_ld if first else [VS.at()], inc=(LD, 16))
        p.wait("vector", LD, LD.n)
        C.ln_tile(xr[buf], xsq, xln[buf], stats)
        ln_v = VS.at()
        for c in range(16):
            slot = c % 2
            if c >= 2:
                p.wait("tensor", SS, SS.n)
            p.op("tensor", "transpose", ps_tr[slot][:],
                 xln[buf][:, c * 128:(c + 1) * 128], C.ident[:],
                 wait=[ln_v] if c == 0 else [], inc=(TSm, 1))
            p.op("scalar", "activation",
                 stage[:, c * 512 + tt * 128: c * 512 + tt * 128 + 128],
                 ps_tr[slot][:], AF.Copy, scale=1.0,
                 wait=[TSm.at(), LD.at()] if (tt == 0 and c == 0) else [TSm.at()],
                 inc=(SS, 1))
        p.wait("vector", SS, SS.n)
        p.op("vector", "memset", stats[:, 7:8], 0.0, inc=(VS, 1))
    return SS.at()



def build_l1ffn(fused=None):
    if fused is None:
        p = P("l1ffn")
        C = Common(p)
        x_in_ap = x_out_ap = None
    else:
        p, C, x_in_ap, x_out_ap = fused
    p.push_scope("f1_")
    x_in = x_in_ap if x_in_ap is not None else p.dram("x_in", (T, D), F32, kind="ExternalInput").ap()
    x_out = x_out_ap if x_out_ap is not None else p.dram("x_out", (T, D), F32, kind="ExternalOutput").ap()
    w_in = p.dram("weight", (D, D), F32, kind="ExternalInput").ap()
    b_in = p.dram("bias2d", (128, 16), F32, kind="ExternalInput").ap()

    stage = p.sbuf("stage", (128, 16 * 512), BF16)
    bwT = p.sbuf("bwT", (128, 16 * 2048), BF16)
    bias = p.sbuf("bias_sb", (128, 16), F32)
    xr = [p.sbuf(f"xr{i}", (128, D), F32) for i in range(2)]
    xsq = p.sbuf("xsq", (128, D), F32)
    stats = p.sbuf("stats", (128, 8), F32)
    xln = [p.sbuf(f"xln{i}", (128, D), BF16) for i in range(2)]
    wrow = p.sbuf("wrow", (128, D), F32)
    sgn = p.sbuf("sgn", (128, D), BF16)
    bwrow = p.sbuf("bwrow", (128, D), BF16)
    alpha = p.sbuf("alpha", (128, 2), F32)
    y_all = p.sbuf("y_all", (128, 16 * 512), F32)
    yrow = p.sbuf("yrow", (128, 128), F32)
    xres = p.sbuf("xres", (128, D), F32)

    ps_mm = p.psum("ps_mm", (128, 512), F32)
    ps_tr = [p.psum(f"ps_tr{i}", (128, 128), BF16) for i in range(2)]
    ps_trf = [p.psum(f"ps_trf{i}", (128, 128), F32) for i in range(2)]

    LD, LG, ST, TSm, VS, SS = C.LD, C.LG, C.ST, C.TS, C.VS, C.SS
    C.load_consts()
    p.op("sync", "dma_start", out=bias[:], in_=b_in, inc=(LD, 16))
    consts_ld = [LD.at(), LG.at(), VS.at()]

    for oc in range(16):
        p.op("gpsimd", "dma_start", out=wrow[:], in_=w_in[oc * 128:(oc + 1) * 128, :],
             wait=[SS.at(), VS.at()] if oc > 0 else [], inc=(LG, 16))
        p.wait("vector", LG, LG.n)
        p.op("vector", "reduce_sum", alpha[:, 0:1], wrow[:], axis=AX.X,
             apply_absolute_value=True)
        p.op("vector", "drain")
        p.op("vector", "tensor_scalar", alpha[:, 1:2], alpha[:, 0:1], 1.0 / D, None,
             ALU.mult, inc=(VS, 1))
        p.op("vector", "drain")
        p.op("scalar", "sign", sgn[:], wrow[:], wait=[VS.at()], inc=(SS, 1))
        p.op("vector", "tensor_scalar", bwrow[:], sgn[:], alpha[:, 1:2], None,
             ALU.mult, wait=[SS.at()], inc=(VS, 1))
        for ic in range(16):
            slot = ic % 2
            if ic >= 2:
                p.wait("tensor", SS, SS.n)
            p.op("tensor", "transpose", ps_tr[slot][:],
                 bwrow[:, ic * 128:(ic + 1) * 128], C.ident[:],
                 wait=[VS.at()] if ic == 0 else [], inc=(TSm, 1))
            p.op("scalar", "activation", bwT[:, ic * 2048 + oc * 128: ic * 2048 + oc * 128 + 128],
                 ps_tr[slot][:], AF.Copy, scale=1.0, wait=[TSm.at()], inc=(SS, 1))
    bw_done = SS.at()

    for g in range(8):
        stage_done = emit_ln_stage(p, C, x_in, stage, xr, xsq, stats, xln, ps_tr, g,
                                   consts_ld if g == 0 else None)
        for oc in range(16):
            for ic in range(16):
                p.op("tensor", "matmul", ps_mm[:],
                     bwT[:, ic * 2048 + oc * 128: ic * 2048 + oc * 128 + 128],
                     stage[:, ic * 512:(ic + 1) * 512],
                     start=(ic == 0), stop=(ic == 15),
                     wait=[stage_done, bw_done] if (oc == 0 and ic == 0) else
                          ([VS.at()] if ic == 0 else []),
                     inc=(TSm, 1) if ic == 15 else None)
            p.op("vector", "tensor_scalar", y_all[:, oc * 512:(oc + 1) * 512],
                 ps_mm[:], bias[:, oc:oc + 1], None, ALU.add,
                 wait=[TSm.at()], inc=(VS, 1))
        y_done = VS.at()
        for tt in range(4):
            trow = g * 512 + tt * 128
            p.op("sync", "dma_start", out=xres[:], in_=x_in[trow:trow + 128, :],
                 wait=[ST.at()], inc=(LD, 16))
            p.wait("vector", LD, LD.n)
            for oc in range(16):
                slot = oc % 2
                if oc >= 2:
                    p.wait("tensor", VS, VS.n)
                p.op("tensor", "transpose", ps_trf[slot][:],
                     y_all[:, oc * 512 + tt * 128: oc * 512 + tt * 128 + 128],
                     C.ident_f[:],
                     wait=[y_done, VS.at()] if (oc == 0 and tt == 0) else
                          ([VS.at()] if oc < 2 else []),
                     inc=(TSm, 1))
                p.op("vector", "tensor_add", xres[:, oc * 128:(oc + 1) * 128],
                     xres[:, oc * 128:(oc + 1) * 128], ps_trf[slot][:],
                     wait=[TSm.at()], inc=(VS, 1))
            p.op("sync", "dma_start", out=x_out[trow:trow + 128, :], in_=xres[:],
                 wait=[VS.at()], inc=(ST, 16))
    p.wait("sync", ST, ST.n)
    p.pop_scope()
    return p


def run_l1ffn(x, weight, bias):
    c = _consts()
    p = build_l1ffn()
    nc = p.finalize()
    b2d = np.ascontiguousarray(bias.reshape(16, 128).T)
    in_maps = []
    for b in range(NCORES):
        in_maps.append({
            "x_in": np.ascontiguousarray(x[b]),
            "weight": weight, "bias2d": b2d,
            "c_ident": c["ident"], "c_cbank": c["cbank"],
        })
    res = run_bass_kernel_spmd(nc, in_maps, core_ids=list(range(NCORES)))
    return np.stack([res.results[b]["x_out"] for b in range(NCORES)])



NEXP = 17
GU_WG = 4
DN_WG = 3

def build_l0ffn(phases=(1, 1, 1), nexp=NEXP, debug=False, fused=None):
    if fused is None:
        p = P("l0ffn")
        C = Common(p)
        x_in_ap = x_out_ap = None
    else:
        p, C, x_in_ap, x_out_ap = fused
    p.push_scope("m0_")
    x_in = x_in_ap if x_in_ap is not None else p.dram("x_in", (T, D), F32, kind="ExternalInput").ap()
    x_out = x_out_ap if x_out_ap is not None else p.dram("x_out", (T, D), F32, kind="ExternalOutput").ap()
    gate_w = p.dram("gate_w", (16, D), F32, kind="ExternalInput").ap()
    gup = p.dram("gup", (NEXP, 256, 3072), I32, kind="ExternalInput").ap()
    gus = p.dram("gus", (NEXP, 16, 3072), F32, kind="ExternalInput").ap()
    dnp = p.dram("dnp", (NEXP, 192, 2048), I32, kind="ExternalInput").ap()
    dns = p.dram("dns", (NEXP, 12, 2048), F32, kind="ExternalInput").ap()
    sel4_in = p.dram("sel4", (4, 128), F32, kind="ExternalInput").ap()
    sel16_in = p.dram("sel16", (16, 16 * 128), F32, kind="ExternalInput").ap()
    xlnT_hbm = p.dram("xlnT_scratch", (8, D, 512), BF16, kind="Internal").ap()
    dbk = "ExternalOutput" if debug else "Internal"
    wgu_d = p.dram("wgu_dense", (NEXP, 12, D, 256), BF16, kind=dbk).ap()
    wdn_d = p.dram("wdn_dense", (NEXP, 1536, D), BF16, kind=dbk).ap()

    combT = p.sbuf("combT", (16, T), BF16)
    gwT = p.sbuf("gwT", (128, 16 * 16), BF16)
    sel4 = p.sbuf("sel4_sb", (4, 128), BF16)
    sel16 = p.sbuf("sel16_sb", (16, 16 * 128), BF16)

    ps_tr = [p.psum(f"ps_tr{i}", (128, 128), BF16) for i in range(2)]

    LD, LG, ST, TSm, VS, SS = C.LD, C.LG, C.ST, C.TS, C.VS, C.SS
    C.load_consts()

    prep = ExitStack()
    gw_f = p.sbuf_scoped(prep, "gw_f", (16, D), F32)
    gw_b = p.sbuf_scoped(prep, "gw_b", (16, D), BF16)
    p.op("sync", "dma_start", out=gw_f[:], in_=gate_w, inc=(LD, 16))
    p.op("gpsimd", "dma_start", out=sel4[:], in_=sel4_in, inc=(LG, 16))
    p.op("gpsimd", "dma_start", out=sel16[:], in_=sel16_in, inc=(LG, 16))
    p.wait("vector", LD, LD.n)
    p.op("vector", "tensor_copy", gw_b[:], gw_f[:], wait=[LG.at()], inc=(VS, 1))
    for h in range(16):
        slot = h % 2
        if h >= 2:
            p.wait("tensor", SS, SS.n)
        p.op("tensor", "transpose", ps_tr[slot][:, 0:16],
             gw_b[0:16, h * 128:(h + 1) * 128], C.ident[0:16, 0:16],
             wait=[VS.at()] if h == 0 else [], inc=(TSm, 1))
        p.op("scalar", "activation", gwT[:, h * 16:(h + 1) * 16],
             ps_tr[slot][:, 0:16], AF.Copy, scale=1.0, wait=[TSm.at()], inc=(SS, 1))
    prep.close()
    consts_ld = [LD.at(), LG.at(), VS.at(), SS.at()]

    do_a, do_c, do_d = phases
    pa = ExitStack()
    ps_p = p.psum_scoped(pa, "ps_p", (128, 16), F32)
    ps_ct = p.psum_scoped(pa, "ps_ct", (16, 128), BF16)
    ps_a = p.psum_scoped(pa, "ps_sea", (128, 512), F32)
    ps_b = p.psum_scoped(pa, "ps_seb", (128, 512), F32)
    xr = [p.sbuf_scoped(pa, f"xr{i}", (128, D), F32) for i in range(2)]
    xsq = p.sbuf_scoped(pa, "xsq", (128, D), F32)
    stats = p.sbuf_scoped(pa, "stats", (128, 8), F32)
    xln = [p.sbuf_scoped(pa, f"xln{i}", (128, D), BF16) for i in range(2)]
    stage = p.sbuf_scoped(pa, "stage", (128, 16 * 512), BF16)
    pr = p.sbuf_scoped(pa, "pr", (128, 16), F32)
    pz = p.sbuf_scoped(pa, "pz", (128, 16), F32)
    msk = p.sbuf_scoped(pa, "msk", (128, 16), U8)
    mkf = p.sbuf_scoped(pa, "mkf", (128, 16), F32)
    num = p.sbuf_scoped(pa, "num", (128, 16), F32)
    rstats = p.sbuf_scoped(pa, "rstats", (128, 8), F32)
    cmb = p.sbuf_scoped(pa, "cmb", (128, 16), BF16)

    pc = pa
    pk = p.sbuf_scoped(pc, "pk", (128, 3072), I32)
    mag = [p.sbuf_scoped(pc, f"mag{i}", (128, 3072), I32) for i in range(2)]
    sgn = [p.sbuf_scoped(pc, f"sgn{i}", (128, 3072), I32) for i in range(2)]
    t_cm = [p.sbuf_scoped(pc, f"t_cm{i}", (128, 3072), BF16) for i in range(2)]
    t_r1 = [p.sbuf_scoped(pc, f"t_r1{i}", (128, 3072), BF16) for i in range(2)]
    t_r2 = [p.sbuf_scoped(pc, f"t_r2{i}", (128, 3072), BF16) for i in range(2)]
    t_sg = [p.sbuf_scoped(pc, f"t_sg{i}", (128, 3072), BF16) for i in range(2)]
    se_f = p.sbuf_scoped(pc, "se_f", (4, 3072), F32)
    se_b = p.sbuf_scoped(pc, "se_b", (4, 3072), BF16)
    se = p.sbuf_scoped(pc, "se", (128, 3072), BF16)
    dq = [p.sbuf_scoped(pc, f"dq{i}", (128, 3072), BF16) for i in range(2)]

    first_dq = [True]
    dqst = {"wr": [0, 0]}

    def dequant_block(packed2d, scales2d, ncols, wg, dense_out_fn, nparts=64):
        p.op("sync", "dma_start", out=pk[0:64, 0:ncols],
             in_=packed2d[wg * 64:(wg + 1) * 64, :], wait=[VS.at()], inc=(LD, 16))
        p.op("sync", "dma_start", out=pk[64:128, 0:ncols],
             in_=packed2d[wg * 64:(wg + 1) * 64, :], inc=(LD, 16))
        p.op("sync", "dma_start", out=se_f[:, 0:ncols],
             in_=scales2d[wg * 4:(wg + 1) * 4, :], inc=(LD, 16))
        p.wait("vector", LD, LD.n)
        p.op("vector", "tensor_copy", se_b[:, 0:ncols], se_f[:, 0:ncols], inc=(VS, 1))
        nse = (ncols + 511) // 512
        for i in range(nse):
            c0, c1 = i * 512, min(ncols, (i + 1) * 512)
            pb = ps_a if i % 2 == 0 else ps_b
            if i >= 2:
                p.wait("tensor", SS, SS.n)
            p.op("tensor", "matmul", pb[:, 0:c1 - c0], sel4[:], se_b[:, c0:c1],
                 start=True, stop=True, wait=[VS.at()] if i == 0 else [],
                 inc=(TSm, 1))
            p.op("scalar", "activation", se[:, c0:c1], pb[:, 0:c1 - c0], AF.Copy,
                 scale=1.0, wait=[TSm.at()], inc=(SS, 1))
        se_done = SS.at()
        cols = slice(0, ncols)

        def extract(j, par):
            for half, jj in ((0, j), (1, j + 1)):
                pkh = pk[0:64, cols] if half == 0 else pk[64:128, cols]
                mg = mag[par][0:64, cols] if half == 0 else mag[par][64:128, cols]
                sg = sgn[par][0:64, cols] if half == 0 else sgn[par][64:128, cols]
                if jj == 0:
                    p.op("vector", "tensor_scalar", mg, pkh, 7, None, ALU.bitwise_and)
                else:
                    p.op("vector", "tensor_scalar", mg, pkh, 4 * jj, 7,
                         ALU.logical_shift_right, ALU.bitwise_and)
                p.op("vector", "tensor_scalar", sg, pkh, 4 * jj + 3, 1,
                     ALU.logical_shift_right, ALU.bitwise_and)
            p.op("vector", "tensor_scalar", t_sg[par][:, cols], sgn[par][:, cols],
                 -2.0, 1.0, ALU.mult, ALU.add, inc=(VS, 1))

        extract(0, 0)
        for jp in range(4):
            par = jp % 2
            buf = dq[par]
            ex_v = VS.at()
            p.op("scalar", "activation", t_cm[par][:, cols], mag[par][:, cols], AF.Copy,
                 scale=0.5, wait=[ex_v])
            p.op("scalar", "activation", t_r1[par][:, cols], mag[par][:, cols], AF.Relu,
                 scale=0.5, bias=C.cbank_ap(CB_NEG2, 128))
            p.op("scalar", "activation", t_r2[par][:, cols], mag[par][:, cols], AF.Relu,
                 scale=1.0, bias=C.cbank_ap(CB_NEG6, 128), inc=(SS, 1))
            acts_v = SS.at()
            if jp < 3:
                extract(2 * (jp + 1), 1 - par)
            p.op("vector", "tensor_add", t_cm[par][:, cols], t_cm[par][:, cols],
                 t_r1[par][:, cols], wait=[acts_v, se_done])
            p.op("vector", "tensor_add", t_cm[par][:, cols], t_cm[par][:, cols],
                 t_r2[par][:, cols])
            p.op("vector", "tensor_mul", t_cm[par][:, cols], t_cm[par][:, cols],
                 t_sg[par][:, cols])
            p.op("vector", "tensor_mul", buf[:, cols], t_cm[par][:, cols], se[:, cols],
                 wait=[(LG, dqst["wr"][par])], inc=(VS, 1))
            dsts = dense_out_fn(jp)
            p.op("gpsimd", "dma_start", out=dsts[0], in_=buf[0:64, cols],
                 wait=[VS.at()], inc=(LG, 16))
            p.op("gpsimd", "dma_start", out=dsts[1], in_=buf[64:128, cols],
                 inc=(LG, 16))
            dqst["wr"][par] = LG.n

    def emit_dequant(e):
        for wg in range(GU_WG):
            def gu_out(jp, e=e, wg=wg):
                hc = wg * 4 + jp
                h0 = wgu_d[e, :, hc * 128:hc * 128 + 64, :].rearrange("cb p i -> p cb i")
                h1 = wgu_d[e, :, hc * 128 + 64:(hc + 1) * 128, :].rearrange("cb p i -> p cb i")
                return (h0, h1)
            dequant_block(gup[e], gus[e], 3072, wg, gu_out)
        for wg in range(DN_WG):
            dequant_block(dnp[e], dns[e], 2048, wg,
                          lambda jp, e=e, wg=wg: _dn_dest(wdn_d, e, wg, jp))

    _dq_pending = list(range(nexp if do_c else 0))

    for g in range(8 if do_a else 0):
        stage_done = emit_ln_stage(p, C, x_in, stage, xr, xsq, stats, xln, ps_tr, g,
                                   consts_ld if g == 0 else None)
        for tt in range(4):
            trow = g * 512 + tt * 128
            p.wait("tensor", VS, VS.n)
            for h in range(16):
                p.op("tensor", "matmul", ps_p[:],
                     stage[:, h * 512 + tt * 128: h * 512 + tt * 128 + 128],
                     gwT[:, h * 16:(h + 1) * 16],
                     start=(h == 0), stop=(h == 15),
                     wait=[stage_done] if h == 0 else [],
                     inc=(TSm, 1) if h == 15 else None)
            p.op("vector", "reduce_max", rstats[:, 0:1], ps_p[:], axis=AX.X,
                 negate=True, wait=[TSm.at()], inc=(VS, 1))
            p.op("vector", "drain")
            p.op("scalar", "activation", pr[:], ps_p[:], AF.Exp,
                 bias=rstats[:, 0:1], scale=1.0, wait=[VS.at()], inc=(SS, 1))
            p.op("vector", "memset", num[:], 0.0, wait=[SS.at()])
            p.op("vector", "memset", rstats[:, 1:2], 0.0)
            p.op("vector", "drain")
            for k in range(4):
                src = pr if k == 0 else pz
                p.op("vector", "reduce_max", rstats[:, 2:3], src[:], axis=AX.X)
                p.op("vector", "drain")
                p.op("vector", "tensor_scalar", msk[:], src[:], rstats[:, 2:3], None,
                     ALU.is_ge)
                p.op("vector", "drain")
                p.op("vector", "tensor_copy", mkf[:], msk[:])
                p.op("vector", "drain")
                p.op("vector", "scalar_tensor_tensor", num[:], mkf[:], rstats[:, 2:3],
                     num[:], ALU.mult, ALU.add)
                p.op("vector", "tensor_add", rstats[:, 1:2], rstats[:, 1:2], rstats[:, 2:3])
                p.op("vector", "drain")
                if k < 3:
                    p.op("vector", "memset", mkf[:], 0.0)
                    p.op("vector", "drain")
                    p.op("vector", "select", pz[:], msk[:], mkf[:], src[:])
                    p.op("vector", "drain")
            p.op("vector", "reciprocal", rstats[:, 3:4], rstats[:, 1:2])
            p.op("vector", "drain")
            p.op("vector", "tensor_scalar", cmb[:], num[:], rstats[:, 3:4], None,
                 ALU.mult, inc=(VS, 1))
            p.op("vector", "drain")
            p.op("tensor", "transpose", ps_ct[:], cmb[:], C.ident[:],
                 wait=[VS.at(), SS.at()], inc=(TSm, 1))
            p.op("scalar", "activation", combT[:, trow:trow + 128], ps_ct[0:16, :],
                 AF.Copy, scale=1.0, wait=[TSm.at()], inc=(SS, 1))
        p.op("sync", "dma_start", out=xlnT_hbm[g].rearrange("(h p) t -> p h t", p=128),
             in_=stage[:].rearrange("p (h t) -> p h t", h=16),
             wait=[SS.at()], inc=(LD, 16))
        for _ in range(2):
            if _dq_pending:
                emit_dequant(_dq_pending.pop(0))
    xln_all = LD.at()
    comb_all = SS.at()

    for e in _dq_pending:
        emit_dequant(e)
    dq_all = LG.at()
    pa.close()

    pd = ExitStack()
    ps_ga = [p.psum_scoped(pd, f"ps_ga{i}", (128, 512), F32) for i in range(2)]
    ps_gb = [p.psum_scoped(pd, f"ps_gb{i}", (128, 512), F32) for i in range(2)]
    ps_dnb = [p.psum_scoped(pd, f"ps_dnn{i}", (128, 512), F32) for i in range(2)]
    xp = p.sbuf_scoped(pd, "xp", (128, 16 * 1024), BF16)
    out_acc = p.sbuf_scoped(pd, "out_acc", (128, 16 * 1024), BF16)
    act = p.sbuf_scoped(pd, "act", (128, 12 * 1024), BF16)
    wdsb = p.sbuf_scoped(pd, "wdsb", (128, 12 * 2048), BF16)
    gblk = [p.sbuf_scoped(pd, f"gblk{i}", (128, 16 * 256), BF16) for i in range(2)]
    ublk = [p.sbuf_scoped(pd, f"ublk{i}", (128, 16 * 256), BF16) for i in range(2)]
    cB = p.sbuf_scoped(pd, "cB", (128, 1024), BF16)
    sg_sb = [p.sbuf_scoped(pd, f"sg_sb{i}", (128, 512), BF16) for i in range(2)]
    m1_sb = [p.sbuf_scoped(pd, f"m1_sb{i}", (128, 512), BF16) for i in range(2)]
    xres = p.sbuf_scoped(pd, "xres", (128, D), F32)

    first_d = [True]
    for tg in range(4 if do_d else 0):
        if first_d[0]:
            p.wait("sync", LG, LG.n)
            p.wait("vector", LG, LG.n)
            first_d[0] = False
        p.wait("sync", TSm, TSm.n)
        for hc in range(16):
            wg, jp = hc // 4, hc % 4
            for gb in range(2):
                g = tg * 2 + gb
                blk = xlnT_hbm[g][512 * wg: 512 * wg + 512, :]
                r = blk.rearrange("(pp j) t -> j pp t", j=8)
                w0 = [xln_all] if (tg == 0 and hc == 0 and gb == 0) else []
                cslice = slice(hc * 1024 + gb * 512, hc * 1024 + (gb + 1) * 512)
                p.op("sync", "dma_start", out=xp[0:64, cslice], in_=r[2 * jp],
                     wait=w0, inc=(LD, 16))
                p.op("sync", "dma_start", out=xp[64:128, cslice], in_=r[2 * jp + 1],
                     inc=(LD, 16))
        xp_ld = LD.at()
        p.op("vector", "memset", out_acc[:], 0.0, wait=[TSm.at()], inc=(VS, 1))
        g_evac = {}
        d_evac = {}
        blk_mm = {}
        dn_mm_v = [0]
        for e in range(nexp):
            if e == 16:
                p.op("vector", "memset", cB[:], 1.0, wait=[TSm.at()], inc=(VS, 1))
                cb_done = VS.at()
            else:
                for i in range(2):
                    p.op("tensor", "matmul", ps_dnb[0][:, 0:512],
                         sel16[:, e * 128:(e + 1) * 128],
                         combT[:, tg * 1024 + i * 512: tg * 1024 + (i + 1) * 512],
                         start=True, stop=True,
                         wait=[comb_all, VS.at()] if i == 0 else [SS.at()], inc=(TSm, 1))
                    p.op("scalar", "activation", cB[:, i * 512:(i + 1) * 512],
                         ps_dnb[0][:, 0:512], AF.Copy, scale=1.0, wait=[TSm.at()], inc=(SS, 1))
                cb_done = SS.at()
            for bp in range(6):
                lb = bp % 2
                gk = (e, bp)
                wblk_wait = [dq_all] if tg == 0 else []
                prev = blk_mm.get((e, bp - 2)) if bp >= 2 else (
                    blk_mm.get((e - 1, 4)) if e > 0 and bp == 0 else (
                    blk_mm.get((e - 1, 5)) if e > 0 and bp == 1 else None))
                if prev is None:
                    prev = TSm.at()
                p.op("gpsimd", "dma_start", out=gblk[lb][:],
                     in_=wgu_d[e, bp].rearrange("(hc p) i -> p hc i", p=128),
                     wait=wblk_wait + [prev], inc=(LG, 16))
                p.op("gpsimd", "dma_start", out=ublk[lb][:],
                     in_=wgu_d[e, 6 + bp].rearrange("(hc p) i -> p hc i", p=128),
                     inc=(LG, 16))
                wld = LG.at()
                for ci_in in range(2):
                    ci = bp * 2 + ci_in
                    for th in range(2):
                        q = (bp * 2 + ci_in) * 2 + th
                        qp = q % 2
                        w0 = [wld, xp_ld, cb_done]
                        pg = g_evac.get((e, q - 2))
                        if pg is not None:
                            w0.append((pg[0], pg[1]))
                        for hc in range(16):
                            p.op("tensor", "matmul", ps_ga[qp][:],
                                 gblk[lb][:, hc * 256 + ci_in * 128: hc * 256 + ci_in * 128 + 128],
                                 xp[:, hc * 1024 + th * 512: hc * 1024 + (th + 1) * 512],
                                 start=(hc == 0), stop=(hc == 15),
                                 wait=w0 if hc == 0 else [],
                                 inc=(TSm, 1) if hc == 15 else None)
                            p.op("tensor", "matmul", ps_gb[qp][:],
                                 ublk[lb][:, hc * 256 + ci_in * 128: hc * 256 + ci_in * 128 + 128],
                                 xp[:, hc * 1024 + th * 512: hc * 1024 + (th + 1) * 512],
                                 start=(hc == 0), stop=(hc == 15),
                                 inc=(TSm, 1) if hc == 15 else None)
                        p.op("scalar", "activation", sg_sb[qp][:], ps_ga[qp][:], AF.Silu,
                             wait=[TSm.at()], inc=(SS, 1))
                        p.op("vector", "tensor_mul", m1_sb[qp][:], sg_sb[qp][:], ps_gb[qp][:],
                             wait=[SS.at()])
                        p.op("vector", "tensor_mul",
                             act[:, ci * 1024 + th * 512: ci * 1024 + (th + 1) * 512],
                             m1_sb[qp][:], cB[:, th * 512:(th + 1) * 512], inc=(VS, 1))
                        g_evac[(e, q)] = (VS, VS.n)
                blk_mm[(e, bp)] = TSm.at()
            act_done = VS.at()
            for ic in range(12):
                wdw = []
                if ic == 0:
                    wdw = [(TSm, dn_mm_v[0])] if dn_mm_v[0] else [TSm.at()]
                p.op("gpsimd", "dma_start", out=wdsb[:, ic * 2048:(ic + 1) * 2048],
                     in_=wdn_d[e, ic * 128:(ic + 1) * 128, :],
                     wait=wdw, inc=(LG, 16))
            wd_ld = LG.at()
            for th in range(2):
                for oc in range(16):
                    dqi = th * 16 + oc
                    dp = dqi % 2
                    w0 = [act_done, wd_ld]
                    pv = d_evac.get((e, dqi - 2))
                    if pv is not None:
                        w0.append((pv[0], pv[1]))
                    elif dqi < 2:
                        w0.append((VS, VS.n))
                    for ic in range(12):
                        p.op("tensor", "matmul", ps_dnb[dp][:],
                             wdsb[:, ic * 2048 + oc * 128: ic * 2048 + oc * 128 + 128],
                             act[:, ic * 1024 + th * 512: ic * 1024 + (th + 1) * 512],
                             start=(ic == 0), stop=(ic == 11),
                             wait=w0 if ic == 0 else [],
                             inc=(TSm, 1) if ic == 11 else None)
                    sl = oc * 1024 + th * 512
                    p.op("vector", "tensor_add", out_acc[:, sl:sl + 512],
                         out_acc[:, sl:sl + 512], ps_dnb[dp][:], wait=[TSm.at()], inc=(VS, 1))
                    d_evac[(e, dqi)] = (VS, VS.n)
                    if dqi == 31:
                        dn_mm_v[0] = TSm.n
        acc_done = VS.at()
        for tt in range(8):
            trow = tg * 1024 + tt * 128
            p.op("sync", "dma_start", out=xres[:], in_=x_in[trow:trow + 128, :],
                 wait=[ST.at()], inc=(LD, 16))
            p.wait("vector", LD, LD.n)
            for oc in range(16):
                slot = oc % 2
                if oc >= 2:
                    p.wait("tensor", VS, VS.n)
                p.op("tensor", "transpose", ps_tr[slot][:],
                     out_acc[:, oc * 1024 + tt * 128: oc * 1024 + tt * 128 + 128],
                     C.ident[:],
                     wait=[acc_done, VS.at()] if oc < 2 else [],
                     inc=(TSm, 1))
                p.op("vector", "tensor_add", xres[:, oc * 128:(oc + 1) * 128],
                     xres[:, oc * 128:(oc + 1) * 128], ps_tr[slot][:],
                     wait=[TSm.at()], inc=(VS, 1))
            p.op("sync", "dma_start", out=x_out[trow:trow + 128, :], in_=xres[:],
                 wait=[VS.at()], inc=(ST, 16))
    pd.close()
    p.wait("sync", ST, ST.n)
    p.pop_scope()
    return p


def _dn_dest(wdn_d, e, wg, jp):
    v = wdn_d[e, 512 * wg: 512 * wg + 512, :]
    r = v.rearrange("(pp j) c -> j pp c", j=8)
    return (r[2 * jp], r[2 * jp + 1])


def run_l0ffn(x, gate_w, gup, gus, dnp, dns, sgup, sgus, sdnp, sdns):
    c = _consts()
    p = build_l0ffn()
    nc = p.finalize()
    gup_a = np.concatenate([gup, sgup[None]], 0)
    gus_a = np.concatenate([gus, sgus[None]], 0)
    dnp_a = np.concatenate([dnp, sdnp[None]], 0)
    dns_a = np.concatenate([dns, sdns[None]], 0)
    pp = np.arange(128)
    sel4 = np.stack([(((pp % 64) >> 4) == k) for k in range(4)]).astype(np.float32)
    sel16 = np.zeros((16, 16 * 128), np.float32)
    for e in range(16):
        sel16[e, e * 128:(e + 1) * 128] = 1.0
    in_maps = []
    for b in range(NCORES):
        in_maps.append({
            "x_in": np.ascontiguousarray(x[b]), "gate_w": gate_w,
            "gup": gup_a, "gus": gus_a, "dnp": dnp_a, "dns": dns_a,
            "sel4": sel4, "sel16": sel16, "c_ident": c["ident"], "c_cbank": c["cbank"],
        })
    res = run_bass_kernel_spmd(nc, in_maps, core_ids=list(range(NCORES)))
    return np.stack([res.results[b]["x_out"] for b in range(NCORES)])



DIN, DST, NHM, HDM, CHK = 4096, 128, 64, 64, 256
CONVD = DIN + 2 * DST

def build_l1attn(debug=False, fused=None):
    if fused is None:
        p = P("l1attn")
        C = Common(p)
        x_in_ap = x_out_ap = None
    else:
        p, C, x_in_ap, x_out_ap = fused
    p.push_scope("a1_")
    x_in = x_in_ap if x_in_ap is not None else p.dram("x_in", (T, D), F32, kind="ExternalInput").ap()
    x_out = x_out_ap if x_out_ap is not None else p.dram("x_out", (T, D), F32, kind="ExternalOutput").ap()
    w_inp = p.dram("in_proj_w", (8512, D), F32, kind="ExternalInput").ap()
    w_outp = p.dram("out_proj_w", (D, DIN), F32, kind="ExternalInput").ap()
    conv_w_in = p.dram("conv_w2d", (128, 34 * 4), F32, kind="ExternalInput").ap()
    conv_b_in = p.dram("conv_b2d", (128, 34), F32, kind="ExternalInput").ap()
    dt_bias_in = p.dram("dt_bias2d", (64, 1), F32, kind="ExternalInput").ap()
    a_log_in = p.dram("a_log2d", (64, 1), F32, kind="ExternalInput").ap()
    d_in = p.dram("d2d", (64, 1), F32, kind="ExternalInput").ap()
    nw_in = p.dram("norm_w2d", (128, 32), F32, kind="ExternalInput").ap()
    sel64_in = p.dram("sel64", (64, 64 * 128), F32, kind="ExternalInput").ap()
    tri_in = p.dram("tri2", (128, 2 * 256), F32, kind="ExternalInput").ap()

    xlnT_hbm = p.dram("xlnT_scratch", (8, D, 512), BF16, kind="Internal").ap()
    w_inT = p.dram("w_inT_t", (67, D, 128), BF16, kind="Internal").ap()
    w_outT = p.dram("w_outT_t", (16, DIN, 128), BF16, kind="Internal").ap()
    dbk = "ExternalOutput" if debug else "Internal"
    if debug:
        dbg2 = p.dram("dbg2", (8, 128, 512), F32, kind="ExternalOutput").ap()
    zxb = p.dram("zxbcdtT", (8512, T), BF16, kind=dbk).ap()
    dtf32_hbm = p.dram("dt_f32", (64, T), F32, kind="Internal").ap()
    convT = p.dram("convT_s", (CONVD, T), BF16, kind=dbk).ap()
    y_hbm = p.dram("y_scr", (16, DIN, 256), BF16, kind=dbk).ap()

    ps_tr = [p.psum(f"ps_tr{i}", (128, 128), BF16) for i in range(2)]
    ps_trf0 = p.psum("ps_trf0", (128, 128), F32)
    ps_trf = [ps_trf0, ps_trf0]
    ps_mm = p.psum("ps_mm", (128, 512), F32)

    LD, LG, ST, TSm, VS, SS = C.LD, C.LG, C.ST, C.TS, C.VS, C.SS
    C.load_consts()
    sel64 = p.sbuf("sel64_sb", (64, 64 * 128), F32)
    tri2 = p.sbuf("tri2_sb", (128, 2 * 256), BF16)
    ones_rf = p.sbuf("ones_rf", (1, 128), F32)
    adt = p.sbuf("adt_fm", (64, T), F32)
    dt_tm = p.sbuf("dt_tm_all", (128, 32 * 64), F32)
    db_t = p.sbuf("db_t", (128, 64), F32)
    state = p.sbuf("state_sb", (128, NHM * 64), F32)
    p.op("gpsimd", "dma_start", out=sel64[:], in_=sel64_in, inc=(LG, 16))
    p.op("gpsimd", "dma_start", out=tri2[:], in_=tri_in, inc=(LG, 16))
    p.op("vector", "memset", ones_rf[:], 1.0, inc=(VS, 1))
    consts_ld = [LD.at(), LG.at(), VS.at()]

    pw = ExitStack()
    wrow = p.sbuf_scoped(pw, "wrow", (128, DIN), BF16)
    wst = p.sbuf_scoped(pw, "wst", (128, DIN), BF16)
    for roc in range(67):
        nr = 128 if roc < 66 else 64
        p.op("gpsimd", "dma_start", out=wrow[0:nr, 0:D],
             in_=w_inp[roc * 128: roc * 128 + nr, :],
             wait=[LD.at()] if roc > 0 else consts_ld, inc=(LG, 16))
        p.wait("tensor", LG, LG.n)
        for hc in range(16):
            slot = hc % 2
            if hc >= 2:
                p.wait("tensor", SS, SS.n)
            p.op("tensor", "transpose", ps_tr[slot][:, 0:nr],
                 wrow[0:nr, hc * 128:(hc + 1) * 128], C.ident[0:nr, 0:nr],
                 inc=(TSm, 1))
            p.op("scalar", "activation", wst[:, hc * 128:hc * 128 + nr],
                 ps_tr[slot][:, 0:nr], AF.Copy, scale=1.0, wait=[TSm.at()], inc=(SS, 1))
        p.op("sync", "dma_start",
             out=w_inT[roc, :, 0:nr].rearrange("(hc p) o -> p hc o", p=128),
             in_=wst[:, 0:2048].rearrange("p (hc o) -> p hc o", hc=16)[:, :, 0:nr],
             wait=[SS.at()], inc=(LD, 16))
    for oc in range(16):
        p.op("gpsimd", "dma_start", out=wrow[:, 0:DIN],
             in_=w_outp[oc * 128:(oc + 1) * 128, :], wait=[LD.at()], inc=(LG, 16))
        p.wait("tensor", LG, LG.n)
        for dc in range(32):
            slot = dc % 2
            if dc >= 2:
                p.wait("tensor", SS, SS.n)
            p.op("tensor", "transpose", ps_tr[slot][:],
                 wrow[:, dc * 128:(dc + 1) * 128], C.ident[:], inc=(TSm, 1))
            p.op("scalar", "activation", wst[:, dc * 128:(dc + 1) * 128],
                 ps_tr[slot][:], AF.Copy, scale=1.0, wait=[TSm.at()], inc=(SS, 1))
        p.op("sync", "dma_start",
             out=w_outT[oc].rearrange("(dc p) o -> p dc o", p=128),
             in_=wst[:, 0:DIN].rearrange("p (dc o) -> p dc o", dc=32),
             wait=[SS.at()], inc=(LD, 16))
    pw.close()
    wT_done_ld = LD.at()

    pa = ExitStack()
    xr = [p.sbuf_scoped(pa, f"xr{i}", (128, D), F32) for i in range(2)]
    xsq = p.sbuf_scoped(pa, "xsq", (128, D), F32)
    stats = p.sbuf_scoped(pa, "stats", (128, 8), F32)
    xln = [p.sbuf_scoped(pa, f"xln{i}", (128, D), BF16) for i in range(2)]
    stage = p.sbuf_scoped(pa, "stage", (128, 16 * 512), BF16)
    wblk = [p.sbuf_scoped(pa, f"wblk{i}", (128, 16 * 128), BF16) for i in range(2)]
    evb = [p.sbuf_scoped(pa, f"evb{i}", (128, 512), BF16) for i in range(2)]
    evbf = p.sbuf_scoped(pa, "evbf", (64, 512), F32)
    ps_mm2 = p.psum_scoped(pa, "ps_mm2", (128, 512), F32)

    zxb_done = {}
    for g in range(8):
        stage_done = emit_ln_stage(p, C, x_in, stage, xr, xsq, stats, xln, ps_tr, g,
                                   consts_ld if g == 0 else None)
        mm_v = {}
        ev_v = {}
        st_v = {}
        for roc in range(67):
            nr = 128 if roc < 66 else 64
            lb = roc % 2
            wl0 = [wT_done_ld] if (g == 0 and roc < 2) else []
            if roc >= 2:
                wl0.append((TSm, mm_v[roc - 2]))
            elif g > 0:
                wl0.append((TSm, TSm.n))
            p.op("gpsimd", "dma_start", out=wblk[lb][:],
                 in_=w_inT[roc].rearrange("(hc p) o -> p hc o", p=128),
                 wait=wl0, inc=(LG, 16))
            wld = LG.at()
            pmb = ps_mm if roc % 2 == 0 else ps_mm2
            w0 = [stage_done, wld]
            if roc >= 2:
                w0.append((SS, ev_v[roc - 2]))
            for hc in range(16):
                p.op("tensor", "matmul", pmb[0:nr, 0:512],
                     wblk[lb][:, hc * 128: hc * 128 + nr],
                     stage[:, hc * 512:(hc + 1) * 512],
                     start=(hc == 0), stop=(hc == 15),
                     wait=w0 if hc == 0 else [],
                     inc=(TSm, 1) if hc == 15 else None)
            mm_v[roc] = TSm.n
            if roc < 66:
                wv = [TSm.at()]
                if roc >= 2:
                    wv.append((LD, st_v[roc - 2]))
                elif g > 0:
                    wv.append((LD, LD.n))
                p.op("scalar", "activation", evb[lb][0:nr, :], pmb[0:nr, 0:512], AF.Copy,
                     scale=1.0, wait=wv, inc=(SS, 1))
                ev_v[roc] = SS.n
                p.op("sync", "dma_start", out=zxb[roc * 128: roc * 128 + nr,
                                                  g * 512:(g + 1) * 512],
                     in_=evb[lb][0:nr, :], wait=[SS.at()], inc=(LD, 16))
                st_v[roc] = LD.n
                if g == 7:
                    zxb_done[roc] = LD.n
            else:
                p.op("scalar", "activation", evbf[0:nr, :], pmb[0:nr, 0:512], AF.Copy,
                     scale=1.0, wait=[TSm.at()], inc=(SS, 1))
                ev_v[roc] = SS.n
                p.op("sync", "dma_start", out=dtf32_hbm[0:nr, g * 512:(g + 1) * 512],
                     in_=evbf[0:nr, :], wait=[SS.at()], inc=(LD, 16))
    pa.close()
    inproj_ld = LD.at()

    pc = ExitStack()
    xc = p.sbuf_scoped(pc, "xc", (128, T + 4), BF16)
    cw = p.sbuf_scoped(pc, "cw_sb", (128, 34 * 4), F32)
    cbv = p.sbuf_scoped(pc, "cbv_sb", (128, 34), F32)
    cacc = p.sbuf_scoped(pc, "cacc", (128, T), F32)
    cout = p.sbuf_scoped(pc, "cout", (128, T), BF16)
    p.op("sync", "dma_start", out=cw[:], in_=conv_w_in, inc=(LD, 16))
    p.op("sync", "dma_start", out=cbv[:], in_=conv_b_in, inc=(LD, 16))
    for cc in range(34):
        p.op("vector", "memset", xc[:, 0:3], 0.0, wait=[SS.at()] if cc > 0 else [],
             inc=(VS, 1))
        p.op("sync", "dma_start", out=xc[:, 3:3 + T],
             in_=zxb[DIN + cc * 128: DIN + (cc + 1) * 128, :],
             wait=[(LD, zxb_done[32 + cc]), VS.at()], inc=(LD, 16))
        p.wait("vector", LD, LD.n)
        p.op("vector", "tensor_scalar", cacc[:], xc[:, 0:T], cw[:, 4 * cc:4 * cc + 1],
             None, ALU.mult)
        for k in range(1, 4):
            p.op("vector", "scalar_tensor_tensor", cacc[:], xc[:, k:k + T],
                 cw[:, 4 * cc + k:4 * cc + k + 1], cacc[:], ALU.mult, ALU.add)
        p.op("vector", "drain")
        p.op("vector", "tensor_copy", xc[:, 0:1], xc[:, 0:1], inc=(VS, 1))
        p.op("scalar", "activation", cout[:], cacc[:], AF.Silu,
             bias=cbv[:, cc:cc + 1], scale=1.0, wait=[VS.at()], inc=(SS, 1))
        p.op("sync", "dma_start", out=convT[cc * 128:(cc + 1) * 128, :], in_=cout[:],
             wait=[SS.at()], inc=(LD, 16))
    pc.close()
    conv_ld = LD.at()

    pef = ExitStack()
    ps_row = p.psum_scoped(pef, "ps_row", (128, 256), F32)
    ps_row2 = ps_mm[0:128, 0:256]
    ps_y = p.psum_scoped(pef, "ps_y", (64, 256), F32)
    ps_y2 = p.psum_scoped(pef, "ps_y2", (64, 256), F32)
    ps_s2 = p.psum_scoped(pef, "ps_s2", (128, 64), F32)
    pe = ExitStack()
    dtw = p.sbuf_scoped(pe, "dtw", (64, T), BF16)
    dtraw = p.sbuf_scoped(pe, "dtraw", (64, T), F32)
    dtf = p.sbuf_scoped(pe, "dtf", (64, T), F32)
    dtt1 = p.sbuf_scoped(pe, "dtt1", (64, T), F32)
    dtt2 = p.sbuf_scoped(pe, "dtt2", (64, T), F32)
    dtb = p.sbuf_scoped(pe, "dtb", (64, 1), F32)
    alog = p.sbuf_scoped(pe, "alog", (64, 4), F32)
    dcol = p.sbuf_scoped(pe, "dcol", (64, 1), F32)
    lr1 = p.sbuf_scoped(pe, "lr1", (1, 64), F32)
    p.op("sync", "dma_start", out=dtraw[:], in_=dtf32_hbm,
         wait=[inproj_ld], inc=(LD, 16))
    p.op("sync", "dma_start", out=dtb[:], in_=dt_bias_in, inc=(LD, 16))
    p.op("sync", "dma_start", out=alog[:, 0:1], in_=a_log_in, inc=(LD, 16))
    p.op("sync", "dma_start", out=dcol[:], in_=d_in, inc=(LD, 16))
    p.wait("vector", LD, LD.n)
    p.op("vector", "tensor_scalar", dtf[:], dtraw[:], dtb[:], None, ALU.add, inc=(VS, 1))
    p.op("scalar", "activation", dtt1[:], dtf[:], AF.Abs, wait=[VS.at()])
    p.op("scalar", "activation", dtt1[:], dtt1[:], AF.Exp, scale=-1.0)
    p.op("scalar", "activation", dtt1[:], dtt1[:], AF.Ln,
         bias=C.cbank_ap(CB_ONE, 64), scale=1.0, inc=(SS, 1))
    p.op("vector", "tensor_scalar", dtt2[:], dtf[:], 0.0, None, ALU.max,
         wait=[SS.at()])
    p.op("vector", "tensor_add", dtf[:], dtt2[:], dtt1[:], inc=(VS, 1))
    p.op("scalar", "activation", alog[:, 1:2], alog[:, 0:1], AF.Exp, wait=[VS.at()],
         inc=(SS, 1))
    p.op("vector", "tensor_scalar", alog[:, 2:3], alog[:, 1:2], -1.0, None, ALU.mult,
         wait=[SS.at()], inc=(VS, 1))
    p.op("vector", "drain")
    p.op("vector", "tensor_scalar", adt[:], dtf[:], alog[:, 2:3], None, ALU.mult,
         inc=(VS, 1))
    p.op("vector", "tensor_copy", dtw[:], dtf[:], inc=(VS, 1))
    for lc in range(32):
        slot = lc % 2
        if lc >= 2:
            p.wait("tensor", SS, SS.n)
        p.op("tensor", "transpose", ps_tr[slot][:, 0:64],
             dtw[:, lc * 128:(lc + 1) * 128], C.ident[0:64, 0:64],
             wait=[VS.at()] if lc == 0 else [], inc=(TSm, 1))
        p.op("scalar", "activation", dt_tm[:, lc * 64:(lc + 1) * 64],
             ps_tr[slot][:, 0:64], AF.Copy, scale=1.0, wait=[TSm.at()], inc=(SS, 1))
    p.op("tensor", "transpose", ps_trf[0][0:1, 0:64], dcol[:], C.ident_f[0:64, 0:64],
         wait=[VS.at()], inc=(TSm, 1))
    p.op("vector", "tensor_copy", lr1[:], ps_trf[0][0:1, 0:64], wait=[TSm.at()],
         inc=(VS, 1))
    p.op("tensor", "matmul", ps_s2[:], ones_rf[:], lr1[:], start=True, stop=True,
         wait=[VS.at()], inc=(TSm, 1))
    p.op("scalar", "activation", db_t[:], ps_s2[:], AF.Copy, scale=1.0,
         wait=[TSm.at()], inc=(SS, 1))
    pe.close()
    dt_done = (SS.at(), VS.at())

    pf = ExitStack()
    xs_sb = p.sbuf_scoped(pf, "xs_sb", (128, 32 * 256), BF16)
    bct = p.sbuf_scoped(pf, "bct", (128, 2 * 256), BF16)
    b_tm = p.sbuf_scoped(pf, "b_tm", (128, 2 * 128), BF16)
    x_tm = p.sbuf_scoped(pf, "x_tm", (128, 2 * 4096), BF16)
    xdt = p.sbuf_scoped(pf, "xdt", (128, 2 * 4096), BF16)
    acs = p.sbuf_scoped(pf, "acs", (64, 256), F32)
    acst = p.sbuf_scoped(pf, "acst", (128, 2 * 64), F32)
    lastb = p.sbuf_scoped(pf, "lastb", (128, 64), F32)
    cdb = p.sbuf_scoped(pf, "cdb", (128, 64), F32)
    dec = p.sbuf_scoped(pf, "dec", (128, 2 * 64), F32)
    cbm = p.sbuf_scoped(pf, "cbm", (128, 2 * 256), BF16)
    dx = [p.sbuf_scoped(pf, f"dx{i}", (128, 2 * 256), F32) for i in range(2)]
    exm = [p.sbuf_scoped(pf, f"exm{i}", (128, 2 * 256), BF16) for i in range(2)]
    p_sb = [p.sbuf_scoped(pf, f"p_sb{i}", (128, 2 * 256), BF16) for i in range(2)]
    exa = [p.sbuf_scoped(pf, f"exa{i}", (128, 256), BF16) for i in range(2)]
    cet = [p.sbuf_scoped(pf, f"cet{i}", (128, 256), BF16) for i in range(2)]
    bd = [p.sbuf_scoped(pf, f"bd{i}", (128, 2 * 128), BF16) for i in range(2)]
    yt_sb = p.sbuf_scoped(pf, "yt_sb", (128, 32 * 256), BF16)
    state_bf = p.sbuf_scoped(pf, "state_bf", (128, NHM * 64), BF16)
    lr2 = p.sbuf_scoped(pf, "lr2", (1, 64), F32)

    p.op("vector", "memset", state[:], 0.0, wait=[dt_done[0]], inc=(VS, 1))

    for c in range(16):
        t0 = c * 256
        p.op("sync", "dma_start", out=xs_sb[:],
             in_=convT[0:DIN, t0:t0 + 256].rearrange("(hc p) t -> p hc t", p=128),
             wait=[conv_ld, TSm.at(), VS.at()], inc=(LD, 16))
        p.op("sync", "dma_start", out=bct[:, 0:256], in_=convT[DIN:DIN + 128, t0:t0 + 256],
             inc=(LD, 16))
        p.op("sync", "dma_start", out=bct[:, 256:512],
             in_=convT[DIN + 128:DIN + 256, t0:t0 + 256], inc=(LD, 16))
        ld_c = LD.at()
        p.op("vector", "memset", dx[0][0:64, 0:256], 1.0, wait=[ld_c])
        p.op("vector", "drain")
        p.op("vector", "tensor_tensor_scan", acs[:], dx[0][0:64, 0:256], adt[:, t0:t0 + 256],
             0.0, ALU.mult, ALU.add, inc=(VS, 1))
        p.op("vector", "drain")
        for lc in range(2):
            p.op("tensor", "transpose", ps_trf[lc][:, 0:64],
                 acs[:, lc * 128:(lc + 1) * 128], C.ident_f[0:64, 0:64],
                 wait=[VS.at(), SS.at()], inc=(TSm, 1))
            p.op("scalar", "activation", acst[:, lc * 64:(lc + 1) * 64],
                 ps_trf[lc][:, 0:64], AF.Copy, scale=1.0, wait=[TSm.at()], inc=(SS, 1))
        p.op("tensor", "transpose", ps_trf[0][0:1, 0:64], acs[:, 255:256],
             C.ident_f[0:64, 0:64], wait=[SS.at()], inc=(TSm, 1))
        p.op("vector", "tensor_copy", lr2[:], ps_trf[0][0:1, 0:64], wait=[TSm.at()],
             inc=(VS, 1))
        p.op("vector", "drain")
        p.op("tensor", "matmul", ps_s2[:], ones_rf[:], lr2[:], start=True, stop=True,
             wait=[VS.at()], inc=(TSm, 1))
        p.op("scalar", "activation", lastb[:], ps_s2[:], AF.Copy, scale=1.0,
             wait=[TSm.at()], inc=(SS, 1))
        p.op("scalar", "activation", cdb[:], lastb[:], AF.Exp, inc=(SS, 1))
        for lc in range(2):
            p.op("vector", "tensor_sub", dx[1][:, 0:64], lastb[:],
                 acst[:, lc * 64:(lc + 1) * 64], wait=[SS.at()])
            p.op("vector", "drain")
            p.op("vector", "tensor_copy", dx[1][:, 64:128], dx[1][:, 0:64], inc=(VS, 1))
            p.op("scalar", "activation", dec[:, lc * 64:(lc + 1) * 64], dx[1][:, 0:64],
                 AF.Exp, wait=[VS.at()], inc=(SS, 1))
        for lc in range(2):
            slot = lc
            p.op("tensor", "transpose", ps_tr[slot][:],
                 bct[:, lc * 128:(lc + 1) * 128], C.ident[:],
                 wait=[SS.at(), VS.at()], inc=(TSm, 1))
            p.op("scalar", "activation", b_tm[:, lc * 128:(lc + 1) * 128],
                 ps_tr[slot][:], AF.Copy, scale=1.0, wait=[TSm.at()], inc=(SS, 1))
        for hc in range(32):
            for lc in range(2):
                slot = (hc * 2 + lc) % 2
                if hc * 2 + lc >= 2:
                    p.wait("tensor", SS, SS.n)
                p.op("tensor", "transpose", ps_tr[slot][:],
                     xs_sb[:, hc * 256 + lc * 128: hc * 256 + (lc + 1) * 128],
                     C.ident[:], inc=(TSm, 1))
                p.op("scalar", "activation",
                     x_tm[:, lc * 4096 + hc * 128: lc * 4096 + (hc + 1) * 128],
                     ps_tr[slot][:], AF.Copy, scale=1.0, wait=[TSm.at()], inc=(SS, 1))
        xtm_done = SS.at()
        for lc in range(2):
            for h in range(NHM):
                p.op("vector", "tensor_scalar",
                     xdt[:, lc * 4096 + h * 64: lc * 4096 + (h + 1) * 64],
                     x_tm[:, lc * 4096 + h * 64: lc * 4096 + (h + 1) * 64],
                     dt_tm[:, (c * 2 + lc) * 64 + h: (c * 2 + lc) * 64 + h + 1],
                     None, ALU.mult,
                     wait=[xtm_done] if (lc == 0 and h == 0) else [])
        p.op("vector", "tensor_copy", dx[0][0:1, 0:1], dx[0][0:1, 0:1], inc=(VS, 1))
        xdt_done = VS.at()
        for sc in range(2):
            p.op("tensor", "matmul", ps_row[:], bct[:, sc * 128:(sc + 1) * 128],
                 bct[:, 256:512], start=True, stop=True,
                 wait=[xdt_done] if sc == 0 else [VS.at()], inc=(TSm, 1))
            p.op("vector", "tensor_mul", cbm[:, sc * 256:(sc + 1) * 256], ps_row[:],
                 tri2[:, sc * 256:(sc + 1) * 256], wait=[TSm.at()], inc=(VS, 1))
        p.op("vector", "tensor_copy", state_bf[:], state[:], wait=[VS.at()], inc=(VS, 1))
        dxv = {}; exav = {}; pv = {}; cetv = {}; ytv = {}; rowv = {}; stv = {}
        yqv = {}; smv = {}
        for h in range(NHM):
            par = h % 2
            rb = ps_row[:, 0:256] if par == 0 else ps_row2
            w0 = []
            if h >= 2:
                w0 += [(VS, dxv[h - 2]), (SS, exav[h - 2])]
            else:
                w0 += [(VS, VS.at()[1]), (SS, SS.at()[1])]
            p.op("tensor", "matmul", rb, sel64[:, h * 128:(h + 1) * 128],
                 acs[:], start=True, stop=True, wait=w0, inc=(TSm, 1))
            rowv[h] = TSm.n
            for sc in range(2):
                p.op("vector", "tensor_scalar", dx[par][:, sc * 256:(sc + 1) * 256],
                     rb,
                     acst[:, sc * 64 + h: sc * 64 + h + 1], 0.0,
                     ALU.subtract, ALU.min,
                     wait=[(TSm, rowv[h])] if sc == 0 else [], inc=(VS, 1))
                p.op("scalar", "activation", exm[par][:, sc * 256:(sc + 1) * 256],
                     dx[par][:, sc * 256:(sc + 1) * 256], AF.Exp,
                     wait=[VS.at()], inc=(SS, 1))
                wp = [SS.at()]
                if sc == 0 and h >= 2:
                    wp.append((TSm, yqv[h - 2]))
                p.op("vector", "tensor_mul", p_sb[par][:, sc * 256:(sc + 1) * 256],
                     exm[par][:, sc * 256:(sc + 1) * 256],
                     cbm[:, sc * 256:(sc + 1) * 256], wait=wp, inc=(VS, 1))
            dxv[h] = VS.n
            pv[h] = VS.n
            p.op("scalar", "activation", exa[par][:], rb, AF.Exp,
                 wait=[VS.at()], inc=(SS, 1))
            exav[h] = SS.n
            wc = [SS.at()]
            if h >= 2:
                wc.append((TSm, yqv[h - 2]))
            p.op("vector", "tensor_mul", cet[par][:], bct[:, 256:512], exa[par][:],
                 wait=wc, inc=(VS, 1))
            cetv[h] = VS.n
            yb = ps_y if par == 0 else ps_y2
            wy = [(VS, pv[h])]
            if h >= 2:
                wy.append((VS, ytv[h - 2]))
            for sc in range(2):
                p.op("tensor", "matmul", yb[:],
                     xdt[:, sc * 4096 + h * 64: sc * 4096 + (h + 1) * 64],
                     p_sb[par][:, sc * 256:(sc + 1) * 256],
                     start=(sc == 0), stop=False, wait=wy if sc == 0 else [])
            p.op("tensor", "matmul", yb[:], state_bf[:, h * 64:(h + 1) * 64], cet[par][:],
                 start=False, stop=True, wait=[(VS, cetv[h])], inc=(TSm, 1))
            yq = TSm.n
            yqv[h] = yq
            hc, off = h // 2, (h % 2) * 64
            p.op("vector", "scalar_tensor_tensor",
                 yt_sb[off:off + 64, hc * 256:(hc + 1) * 256],
                 xs_sb[off:off + 64, hc * 256:(hc + 1) * 256],
                 db_t[off:off + 64, h:h + 1], yb[:], ALU.mult, ALU.add,
                 wait=[(TSm, yq)], inc=(VS, 1))
            ytv[h] = VS.n
            for sc in range(2):
                wb = []
                if sc == 0 and h >= 2:
                    wb.append((TSm, smv[h - 2]))
                p.op("vector", "tensor_scalar", bd[par][:, sc * 128:(sc + 1) * 128],
                     b_tm[:, sc * 128:(sc + 1) * 128],
                     dec[:, sc * 64 + h: sc * 64 + h + 1], None, ALU.mult, wait=wb,
                     inc=(VS, 1))
            bdv = VS.n
            ws = [(VS, bdv)]
            if h >= 1:
                ws.append((VS, stv[h - 1]))
            for sc in range(2):
                p.op("tensor", "matmul", ps_s2[:], bd[par][:, sc * 128:(sc + 1) * 128],
                     xdt[:, sc * 4096 + h * 64: sc * 4096 + (h + 1) * 64],
                     start=(sc == 0), stop=(sc == 1),
                     wait=ws if sc == 0 else [], inc=(TSm, 1) if sc == 1 else None)
                if sc == 1:
                    smv[h] = TSm.n
            p.op("vector", "scalar_tensor_tensor", state[:, h * 64:(h + 1) * 64],
                 state[:, h * 64:(h + 1) * 64], cdb[:, h:h + 1], ps_s2[:],
                 ALU.mult, ALU.add, wait=[TSm.at()], inc=(VS, 1))
            stv[h] = VS.n
        p.op("sync", "dma_start",
             out=y_hbm[c].rearrange("(hc p) t -> p hc t", p=128),
             in_=yt_sb[:].rearrange("p (hc t) -> p hc t", hc=32),
             wait=[VS.at()], inc=(LD, 16))
    pf.close()
    pef.close()
    y_ld = LD.at()

    pg = ExitStack()
    y_sb = p.sbuf_scoped(pg, "y_sb", (128, 32 * 512), BF16)
    z_sb = p.sbuf_scoped(pg, "z_sb", (128, 32 * 512), BF16)
    sqb = [p.sbuf_scoped(pg, f"sqb{i}", (128, 512), BF16) for i in range(2)]
    ssq_f = p.sbuf_scoped(pg, "ssq_f", (1, 512), F32)
    rst = p.sbuf_scoped(pg, "rst", (1, 512), BF16)
    rb = p.sbuf_scoped(pg, "rb", (128, 512), BF16)
    nw = p.sbuf_scoped(pg, "nw32", (128, 32), F32)
    wo = [p.sbuf_scoped(pg, f"wo{i}", (128, 32 * 128), BF16) for i in range(2)]
    y2 = p.sbuf_scoped(pg, "y2", (128, 16 * 512), BF16)
    xres = p.sbuf_scoped(pg, "xres", (128, D), F32)
    ps_dn2 = p.psum_scoped(pg, "ps_dn2", (1, 512), F32)
    ps_r2 = p.psum_scoped(pg, "ps_r2", (128, 512), F32)
    ps_mm2g = p.psum_scoped(pg, "ps_mm2g", (128, 512), F32)

    p.op("sync", "dma_start", out=nw[:], in_=nw_in, inc=(LD, 16))
    for g in range(8):
        for half in range(2):
            cc = 2 * g + half
            p.op("sync", "dma_start",
                 out=y_sb[:, 0:32 * 512].rearrange("p (hc t) -> p hc t", hc=32)[:, :, half * 256:(half + 1) * 256],
                 in_=y_hbm[cc].rearrange("(hc p) t -> p hc t", p=128),
                 wait=[y_ld, TSm.at()] if half == 0 else [], inc=(LD, 16))
        for i in range(4):
            p.op("sync", "dma_start",
                 out=z_sb[:].rearrange("p (hc t) -> p hc t", hc=32)[:, i * 8:(i + 1) * 8, :],
                 in_=zxb[i * 1024:(i + 1) * 1024, g * 512:(g + 1) * 512]
                     .rearrange("(hc p) t -> p hc t", p=128),
                 inc=(LD, 16))
        gld = LD.at()
        p.wait("scalar", LD, LD.n)
        p.op("scalar", "activation", z_sb[:], z_sb[:], AF.Silu, inc=(SS, 1))
        p.op("vector", "tensor_mul", y_sb[:], y_sb[:], z_sb[:], wait=[SS.at()],
             inc=(VS, 1))
        y32_done = VS.at()
        sq_mm = {}
        for hc in range(32):
            wq = [y32_done] if hc == 0 else []
            if hc >= 2:
                wq.append((TSm, sq_mm[hc - 2]))
            p.op("scalar", "activation", sqb[hc % 2][:], y_sb[:, hc * 512:(hc + 1) * 512],
                 AF.Square, wait=wq, inc=(SS, 1))
            p.op("tensor", "matmul", ps_dn2[:], C.ones_col[:], sqb[hc % 2][:],
                 start=(hc == 0), stop=(hc == 31), wait=[SS.at()],
                 inc=(TSm, 1))
            sq_mm[hc] = TSm.n
        p.op("scalar", "activation", ssq_f[:], ps_dn2[:], AF.Ln, scale=1.0 / DIN,
             bias=C.cbank_ap(CB_EPS_LN, 1), wait=[TSm.at()])
        p.op("scalar", "activation", rst[:], ssq_f[:], AF.Exp, scale=-0.5, inc=(SS, 1))
        p.op("tensor", "matmul", ps_r2[:], C.ones_row[:], rst[:], start=True, stop=True,
             wait=[SS.at()], inc=(TSm, 1))
        p.op("scalar", "activation", rb[:], ps_r2[:], AF.Copy, scale=1.0,
             wait=[TSm.at()], inc=(SS, 1))
        rb_done = SS.at()
        for hc in range(32):
            p.op("vector", "scalar_tensor_tensor", y_sb[:, hc * 512:(hc + 1) * 512],
                 y_sb[:, hc * 512:(hc + 1) * 512], nw[:, hc:hc + 1], rb[:],
                 ALU.mult, ALU.mult, wait=[rb_done] if hc == 0 else [], inc=(VS, 1))
        nrm_done = VS.at()
        op_mm = {}
        op_ev = {}
        for oc in range(16):
            lb = oc % 2
            wl = [wT_done_ld]
            if oc >= 2:
                wl.append((TSm, op_mm[oc - 2]))
            else:
                wl += [(TSm, TSm.n)] + ([(VS, VS.n), (SS, SS.n)] if g == 0 else [])
            p.op("gpsimd", "dma_start", out=wo[lb][:],
                 in_=w_outT[oc].rearrange("(dc p) o -> p dc o", p=128),
                 wait=wl, inc=(LG, 16))
            wld = LG.at()
            pb = ps_mm if oc % 2 == 0 else ps_mm2g
            w0 = [nrm_done, wld]
            if oc >= 2:
                w0.append((SS, op_ev[oc - 2]))
            else:
                w0.append((SS, SS.n))
            for dc in range(32):
                p.op("tensor", "matmul", pb[:, 0:512],
                     wo[lb][:, dc * 128:(dc + 1) * 128],
                     y_sb[:, dc * 512:(dc + 1) * 512],
                     start=(dc == 0), stop=(dc == 31),
                     wait=w0 if dc == 0 else [],
                     inc=(TSm, 1) if dc == 31 else None)
            op_mm[oc] = TSm.n
            p.op("scalar", "activation", y2[:, oc * 512:(oc + 1) * 512],
                 pb[:, 0:512], AF.Copy, scale=1.0, wait=[TSm.at()], inc=(SS, 1))
            op_ev[oc] = SS.n
        y2_done = SS.at()
        for tt in range(4):
            trow = g * 512 + tt * 128
            p.op("sync", "dma_start", out=xres[:], in_=x_in[trow:trow + 128, :],
                 wait=[ST.at()], inc=(LD, 16))
            p.wait("vector", LD, LD.n)
            for oc in range(16):
                slot = oc % 2
                if oc >= 2:
                    p.wait("tensor", VS, VS.n)
                p.op("tensor", "transpose", ps_tr[slot][:],
                     y2[:, oc * 512 + tt * 128: oc * 512 + tt * 128 + 128],
                     C.ident[:],
                     wait=[y2_done, VS.at()] if oc < 2 else [], inc=(TSm, 1))
                p.op("vector", "tensor_add", xres[:, oc * 128:(oc + 1) * 128],
                     xres[:, oc * 128:(oc + 1) * 128], ps_tr[slot][:],
                     wait=[TSm.at()], inc=(VS, 1))
            p.op("sync", "dma_start", out=x_out[trow:trow + 128, :], in_=xres[:],
                 wait=[VS.at()], inc=(ST, 16))
    pg.close()
    p.wait("sync", ST, ST.n)
    p.pop_scope()
    return p


def run_l1attn(x, in_proj_w, conv_w, conv_b, dt_bias, A_log, Dp, norm_w, out_proj_w, debug=False):
    c = _consts()
    p = build_l1attn(debug=debug)
    nc = p.finalize()
    cw2d = np.zeros((128, 34 * 4), np.float32)
    cb2d = np.zeros((128, 34), np.float32)
    for cc in range(34):
        cw2d[:, 4 * cc:4 * (cc + 1)] = conv_w[cc * 128:(cc + 1) * 128, :]
        cb2d[:, cc] = conv_b[cc * 128:(cc + 1) * 128]
    sel64 = np.zeros((64, 64 * 128), np.float32)
    for h in range(64):
        sel64[h, h * 128:(h + 1) * 128] = 1.0
    sp = np.arange(128)[:, None]
    lv = np.arange(256)[None, :]
    tri2 = np.concatenate([(sp <= lv).astype(np.float32),
                           (128 + sp <= lv).astype(np.float32)], axis=1)
    in_maps = []
    for b in range(NCORES):
        in_maps.append({
            "x_in": np.ascontiguousarray(x[b]),
            "in_proj_w": in_proj_w, "out_proj_w": out_proj_w,
            "conv_w2d": cw2d, "conv_b2d": cb2d,
            "dt_bias2d": dt_bias.reshape(64, 1), "a_log2d": A_log.reshape(64, 1),
            "d2d": Dp.reshape(64, 1),
            "norm_w2d": np.ascontiguousarray(norm_w.reshape(32, 128).T),
            "sel64": sel64, "tri2": tri2,
            "c_ident": c["ident"], "c_cbank": c["cbank"],
        })
    res = run_bass_kernel_spmd(nc, in_maps, core_ids=list(range(NCORES)))
    out = np.stack([res.results[b]["x_out"] for b in range(NCORES)])
    if debug:
        return out, res.results
    return out



_CONSTS = None

def _consts():
    global _CONSTS
    if _CONSTS is None:
        _CONSTS = host_consts()
    return _CONSTS


def deinterleave_wattn(w_attn):
    w = np.array(w_attn, np.float32, copy=True)
    perm = np.concatenate([np.arange(0, 128, 2), np.arange(1, 128, 2)])
    for base in range(0, 2048 + 512, 128):
        w[:, base:base + 128] = w[:, base + perm]
    return w


def run_l0attn(x, w_attn, w_proj, norm_w, debug=False):
    c = _consts()
    w_attn = deinterleave_wattn(w_attn)
    p = build_l0attn(debug=debug)
    nc = p.finalize()
    in_maps = []
    nw2d = np.ascontiguousarray(norm_w.reshape(16, 128).T)
    for b in range(NCORES):
        in_maps.append({
            "x_in": np.ascontiguousarray(x[b]),
            "w_attn": w_attn, "w_proj": w_proj, "norm_w": nw2d,
            "rope_c": c["rope_c"], "rope_s": c["rope_s"],
            "masks_diag": c["masks_diag"],
            "c_ident": c["ident"], "c_cbank": c["cbank"],
        })
    res = run_bass_kernel_spmd(nc, in_maps, core_ids=list(range(NCORES)))
    out = np.stack([res.results[b]["x_out"] for b in range(NCORES)])
    if debug:
        return out, res.results
    return out


def build_fused():
    p = P("mk")
    C = Common(p)
    x_in = p.dram("x_full_in", (T, D), F32, kind="ExternalInput").ap()
    x_out = p.dram("x_full_out", (T, D), F32, kind="ExternalOutput").ap()
    xA = p.dram("x_chain_a", (T, D), F32, kind="Internal").ap()
    xB = p.dram("x_chain_b", (T, D), F32, kind="Internal").ap()
    build_l0attn(fused=(p, C, x_in, xA))
    p.barrier(C)
    build_l0ffn(fused=(p, C, xA, xB))
    p.barrier(C)
    build_l1attn(fused=(p, C, xB, xA))
    p.barrier(C)
    build_l1ffn(fused=(p, C, xA, x_out))
    return p


_FUSED_NC = None

def _fused_nc():
    global _FUSED_NC
    if _FUSED_NC is None:
        p = build_fused()
        _FUSED_NC = p.finalize()
    return _FUSED_NC


def kernel(**inputs):
    f32 = lambda k: np.asarray(inputs[k], np.float32)
    c = _consts()
    nc = _fused_nc()
    w_attn = deinterleave_wattn(f32("L0_attn_w_attn"))
    gup_a = np.concatenate([np.asarray(inputs["L0_ffn_gate_up_packed"]),
                            np.asarray(inputs["L0_ffn_shared_gate_up_packed"])[None]], 0)
    gus_a = np.concatenate([f32("L0_ffn_gate_up_scales"),
                            f32("L0_ffn_shared_gate_up_scales")[None]], 0)
    dnp_a = np.concatenate([np.asarray(inputs["L0_ffn_down_packed"]),
                            np.asarray(inputs["L0_ffn_shared_down_packed"])[None]], 0)
    dns_a = np.concatenate([f32("L0_ffn_down_scales"),
                            f32("L0_ffn_shared_down_scales")[None]], 0)
    pp = np.arange(128)
    sel4 = np.stack([(((pp % 64) >> 4) == k) for k in range(4)]).astype(np.float32)
    sel16 = np.zeros((16, 16 * 128), np.float32)
    for e in range(16):
        sel16[e, e * 128:(e + 1) * 128] = 1.0
    sel64 = np.zeros((64, 64 * 128), np.float32)
    for h in range(64):
        sel64[h, h * 128:(h + 1) * 128] = 1.0
    sp = np.arange(128)[:, None]
    lv = np.arange(256)[None, :]
    tri2 = np.concatenate([(sp <= lv).astype(np.float32),
                           (128 + sp <= lv).astype(np.float32)], axis=1)
    conv_w = f32("L1_attn_conv_w")
    conv_b = f32("L1_attn_conv_b")
    cw2d = np.zeros((128, 34 * 4), np.float32)
    cb2d = np.zeros((128, 34), np.float32)
    for cc in range(34):
        cw2d[:, 4 * cc:4 * (cc + 1)] = conv_w[cc * 128:(cc + 1) * 128, :]
        cb2d[:, cc] = conv_b[cc * 128:(cc + 1) * 128]
    x = f32("L0_attn_x")
    base = {
        "w_attn": w_attn, "w_proj": f32("L0_attn_w_proj"),
        "norm_w": np.ascontiguousarray(f32("L0_attn_norm_w").reshape(16, 128).T),
        "rope_c": c["rope_c"], "rope_s": c["rope_s"], "masks_diag": c["masks_diag"],
        "c_ident": c["ident"], "c_cbank": c["cbank"],
        "gate_w": f32("L0_ffn_gate_w"),
        "gup": gup_a, "gus": gus_a, "dnp": dnp_a, "dns": dns_a,
        "sel4": sel4, "sel16": sel16,
        "in_proj_w": f32("L1_attn_in_proj_w"), "out_proj_w": f32("L1_attn_out_proj_w"),
        "conv_w2d": cw2d, "conv_b2d": cb2d,
        "dt_bias2d": f32("L1_attn_dt_bias").reshape(64, 1),
        "a_log2d": f32("L1_attn_A_log").reshape(64, 1),
        "d2d": f32("L1_attn_D").reshape(64, 1),
        "norm_w2d": np.ascontiguousarray(f32("L1_attn_norm_w").reshape(32, 128).T),
        "sel64": sel64, "tri2": tri2,
        "weight": f32("L1_ffn_weight"),
        "bias2d": np.ascontiguousarray(f32("L1_ffn_bias").reshape(16, 128).T),
    }
    in_maps = []
    for b in range(NCORES):
        m = dict(base)
        m["x_full_in"] = np.ascontiguousarray(x[b])
        in_maps.append(m)
    res = run_bass_kernel_spmd(nc, in_maps, core_ids=list(range(NCORES)))
    return np.stack([res.results[b]["x_full_out"] for b in range(NCORES)]).astype(np.float32)


# version 16
# speedup vs baseline: 1.1303x; 1.1303x over previous
import numpy as np
import ml_dtypes
from contextlib import ExitStack
import concourse.bass as bass
import concourse.mybir as mybir
from concourse.bass_utils import run_bass_kernel_spmd

F32 = mybir.dt.float32
BF16 = mybir.dt.bfloat16
I32 = mybir.dt.int32
U8 = mybir.dt.uint8
AF = mybir.ActivationFunctionType
ALU = mybir.AluOpType
AX = mybir.AxisListType

T = 4096
D = 2048
NCORES = 8

ENGINES = ("sync", "gpsimd", "vector", "scalar", "tensor")


class Sem:
    __slots__ = ("h", "n", "name")

    def __init__(self, h, name):
        self.h = h
        self.n = 0
        self.name = name

    def at(self):
        return (self, self.n)


class P:

    def __init__(self, name):
        self.nc = bass.Bass("TRN2", target_bir_lowering=False, debug=False)
        self.stack = ExitStack()
        self.ops = {e: [] for e in ENGINES}
        self.n_inst = 0
        self.scope = None
        self.prefix = ""

    def push_scope(self, prefix):
        self.scope = ExitStack()
        self.prefix = prefix

    def pop_scope(self):
        self.scope.close()
        self.scope = None
        self.prefix = ""

    def barrier(self, C):
        sems = [C.LD, C.LG, C.ST, C.TS, C.VS, C.SS]
        for eng in ENGINES:
            for s in sems:
                if s.n:
                    self.wait(eng, s, s.n)

    def sem(self, name):
        h = self.stack.enter_context(self.nc.semaphore(name))
        return Sem(h, name)

    def sbuf(self, name, shape, dtype):
        name = self.prefix + name
        if self.scope is not None:
            return self.scope.enter_context(self.nc.sbuf_tensor(name, list(shape), dtype))
        return self.nc.alloc_sbuf_tensor(name, list(shape), dtype)

    def sbuf_scoped(self, st, name, shape, dtype):
        return st.enter_context(self.nc.sbuf_tensor(self.prefix + name, list(shape), dtype))

    def psum(self, name, shape, dtype=F32):
        name = self.prefix + name
        if self.scope is not None:
            return self.scope.enter_context(self.nc.psum_tensor(name, list(shape), dtype))
        return self.nc.alloc_psum_tensor(name, list(shape), dtype)

    def psum_scoped(self, st, name, shape, dtype=F32):
        return st.enter_context(self.nc.psum_tensor(self.prefix + name, list(shape), dtype))

    def dram(self, name, shape, dtype, kind="Internal"):
        if kind == "Internal":
            name = self.prefix + name
        return self.nc.dram_tensor(name, list(shape), dtype, kind=kind)

    def op(self, eng, meth, *args, wait=(), inc=None, lowp=False, **kw):
        self.n_inst += 1
        if inc is not None:
            inc[0].n += inc[1]
        wait = tuple((s, v) for (s, v) in wait)

        if meth in ("transpose", "reciprocal"):
            lowp = True

        def f(e, meth=meth, args=args, kw=kw, wait=wait, inc=inc, lowp=lowp):
            for s, v in wait:
                e.wait_ge(s.h, v)
            try:
                if lowp:
                    with self.nc.allow_low_precision(reason="intentional bf16 path"):
                        ins = getattr(e, meth)(*args, **kw)
                else:
                    ins = getattr(e, meth)(*args, **kw)
            except Exception as ex:
                raise RuntimeError(f"op {meth} failed: {ex}") from ex
            if inc is not None:
                ins.then_inc(inc[0].h, inc[1])

        self.ops[eng].append(f)

    def wait(self, eng, sem, val):
        self.n_inst += 1

        def f(e, sem=sem, val=val):
            e.wait_ge(sem.h, val)

        self.ops[eng].append(f)

    def raw(self, eng, fn):
        self.ops[eng].append(fn)

    def finalize(self):
        with self.stack:
            with self.nc.Block() as block:
                for eng in ENGINES:
                    fns = self.ops[eng]
                    if not fns:
                        continue
                    getattr(block, eng)(lambda e, fns=fns: [f(e) for f in fns])
        return self.nc



def host_consts():
    c = {}
    c["ident"] = np.eye(128, dtype=np.float32)
    hd = 128
    inv = 1.0 / (10000.0 ** (np.arange(0, hd, 2, dtype=np.float64) / hd))
    ang = np.arange(T, dtype=np.float64)[:, None] * inv[None, :]
    c["rope_c"] = np.cos(ang).T.astype(np.float32).copy()
    c["rope_s"] = np.sin(ang).T.astype(np.float32).copy()
    kp = np.arange(128)[:, None]
    qf = np.arange(512)[None, :]
    m = np.concatenate([(qf >= kp + 128 * r).astype(np.float32) for r in range(4)], axis=1)
    c["masks_diag"] = m
    cvals = [-2.0, -6.0, 1e-5, 1e-6, 0.0, 1.0, 0.0, 0.0]
    c["cbank"] = np.tile(np.array(cvals, np.float32)[None, :], (128, 1))
    return c

CB_NEG2, CB_NEG6, CB_EPS_LN, CB_EPS_RMS, CB_ZERO, CB_ONE = 0, 1, 2, 3, 4, 5



class Common:

    def __init__(self, p: P):
        self.p = p
        nc = p.nc
        self.d_ident = p.dram("c_ident", (128, 128), F32, kind="ExternalInput").ap()
        self.d_cbank = p.dram("c_cbank", (128, 8), F32, kind="ExternalInput").ap()
        self.ident = p.sbuf("ident_bf", (128, 128), BF16)
        self.ident_f = p.sbuf("ident_f", (128, 128), F32)
        self.cb = p.sbuf("cbank", (128, 8), F32)
        self.ones_col = p.sbuf("ones_col", (128, 1), BF16)
        self.ones_row = p.sbuf("ones_row", (1, 128), BF16)
        self.LD = p.sem("LD")
        self.LG = p.sem("LG")
        self.ST = p.sem("ST")
        self.TS = p.sem("TSm")
        self.VS = p.sem("VSm")
        self.SS = p.sem("SSm")

    def load_consts(self):
        if getattr(self, "_consts_loaded", False):
            return
        self._consts_loaded = True
        p = self.p
        p.op("gpsimd", "dma_start", out=self.ident[:], in_=self.d_ident, inc=(self.LG, 16))
        p.op("sync", "dma_start", out=self.ident_f[:], in_=self.d_ident, inc=(self.LD, 16))
        p.op("sync", "dma_start", out=self.cb[:], in_=self.d_cbank, inc=(self.LD, 16))
        p.op("vector", "memset", self.ones_col[:], 1.0)
        p.op("vector", "memset", self.ones_row[:], 1.0, inc=(self.VS, 1))

    def cbank_ap(self, idx, nparts=1):
        return self.cb[0:nparts, idx:idx + 1]

    def ln_tile(self, xr, xsq, xln_out, stats, eps_idx=CB_EPS_LN):
        p, cb = self.p, self.cb
        VS, SS = self.VS, self.SS
        for i in range(4):
            p.op("vector", "bn_stats", xsq[:, 8 + i * 6: 8 + (i + 1) * 6],
                 xr[:, i * 512:(i + 1) * 512])
        p.op("vector", "drain")
        p.op("vector", "bn_aggr", stats[:, 2:4], xsq[:, 8:32], inc=(VS, 1))
        p.op("vector", "drain")
        p.op("scalar", "activation", stats[:, 5:6], stats[:, 3:4], AF.Ln,
             bias=self.cbank_ap(eps_idx, 128), scale=1.0, wait=[VS.at()])
        p.op("scalar", "activation", stats[:, 6:7], stats[:, 5:6], AF.Exp,
             scale=-0.5, inc=(SS, 1))
        p.op("vector", "tensor_scalar", xln_out[:], xr[:], stats[:, 2:3], stats[:, 6:7],
             ALU.subtract, ALU.mult, wait=[SS.at()], inc=(VS, 1))



def build_l0attn(debug=False, fused=None):
    if fused is None:
        p = P("l0attn")
        C = Common(p)
        x_in_ap = x_out_ap = None
    else:
        p, C, x_in_ap, x_out_ap = fused
    p.push_scope("a0_")
    x_in = x_in_ap if x_in_ap is not None else p.dram("x_in", (T, D), F32, kind="ExternalInput").ap()
    x_out = x_out_ap if x_out_ap is not None else p.dram("x_out", (T, D), F32, kind="ExternalOutput").ap()
    w_attn = p.dram("w_attn", (D, 3072), F32, kind="ExternalInput").ap()
    w_proj = p.dram("w_proj", (D, D), F32, kind="ExternalInput").ap()
    nw_in = p.dram("norm_w", (128, 16), F32, kind="ExternalInput").ap()
    rope_c_in = p.dram("rope_c", (64, T), F32, kind="ExternalInput").ap()
    rope_s_in = p.dram("rope_s", (64, T), F32, kind="ExternalInput").ap()
    masks_in = p.dram("masks_diag", (128, 2048), F32, kind="ExternalInput").ap()
    dbg_kind = "ExternalOutput" if debug else "Internal"
    if debug:
        xr_dump = p.dram("xr_dump", (4, 128, D), F32, kind="ExternalOutput").ap()
        st_dump = p.dram("st_dump", (4, 128, 8), F32, kind="ExternalOutput").ap()
    qT_hbm = p.dram("qT_scratch", (8, D, 512), BF16, kind=dbg_kind).ap()
    xlnT_hbm = p.dram("xlnT_scratch", (8, D, 512), BF16, kind=dbg_kind).ap()

    stage = p.sbuf("stage", (128, 16 * 512), BF16)
    wbuf = p.sbuf("wbuf", (128, 16 * 1024), BF16)
    kT = p.sbuf("kT", (128, 4 * T), BF16)
    v_tm = p.sbuf("v_tm", (128, 4 * T), BF16)
    masks = p.sbuf("masks_sb", (128, 2048), BF16)
    nw = p.sbuf("nw_sb", (128, 16), F32)

    ps_s = [p.psum(f"ps_s{i}", (128, 512), F32) for i in range(2)]
    ps_av = p.psum("ps_av", (128, 512), F32)
    ps_dn = p.psum("ps_dn", (1, 512), F32)
    ps_r = p.psum("ps_r", (128, 512), F32)
    ps_proj = p.psum("ps_proj", (128, 512), F32)
    ps_tr = [p.psum(f"ps_tr{i}", (128, 128), BF16) for i in range(2)]

    LD, LG, ST, TSm, VS, SS = C.LD, C.LG, C.ST, C.TS, C.VS, C.SS
    C.load_consts()
    p.op("sync", "dma_start", out=nw[:], in_=nw_in, inc=(LD, 16))
    p.op("gpsimd", "dma_start", out=masks[:], in_=masks_in, inc=(LG, 16))
    consts_ld = [LD.at(), LG.at(), VS.at()]

    def qk_cols(base):
        return wbuf[:, base:base + 128]

    ab = ExitStack()
    xr = [p.sbuf_scoped(ab, f"xr{i}", (128, D), F32) for i in range(2)]
    xsq = p.sbuf_scoped(ab, "xsq", (128, D), F32)
    stats = p.sbuf_scoped(ab, "stats", (128, 8), F32)
    xln = [p.sbuf_scoped(ab, f"xln{i}", (128, D), BF16) for i in range(2)]
    qT = p.sbuf_scoped(ab, "qT", (128, 8 * 512), BF16)
    rope_c = p.sbuf_scoped(ab, "rope_c_sb", (64, T), BF16)
    rope_s = p.sbuf_scoped(ab, "rope_s_sb", (64, T), BF16)
    rtmp = [p.sbuf_scoped(ab, f"rtmp{i}", (64, 4 * 512), F32) for i in range(2)]
    vt_stage = p.sbuf_scoped(ab, "vt_stage", (128, 512), BF16)

    p.op("gpsimd", "dma_start", out=rope_c[:], in_=rope_c_in, inc=(LG, 16))
    p.op("gpsimd", "dma_start", out=rope_s[:], in_=rope_s_in, inc=(LG, 16))
    rope_ld = LG.at()

    for h in range(16):
        p.op("gpsimd", "dma_start", out=wbuf[:, h * 1024:(h + 1) * 1024],
             in_=w_attn[h * 128:(h + 1) * 128, 2048:3072], inc=(LG, 16))
    wkv_ld = LG.at()

    xln_stored = []
    pa_vguard = [0]
    for g in range(8):
        for tt in range(4):
            trow = g * 512 + tt * 128
            buf = tt % 2
            p.op("sync", "dma_start", out=xr[buf][:], in_=x_in[trow:trow + 128, :],
                 wait=[VS.at()] if (g > 0 or tt > 0) else consts_ld, inc=(LD, 16))
            p.wait("vector", LD, LD.n)
            if debug and g == 0:
                p.op("sync", "dma_start", out=xr_dump[tt], in_=xr[buf][:],
                     wait=[LD.at()], inc=(ST, 16))
            C.ln_tile(xr[buf], xsq, xln[buf], stats)
            if debug and g == 0:
                p.op("sync", "dma_start", out=st_dump[tt], in_=stats[:],
                     wait=[VS.at()], inc=(ST, 16))
            ln_v = VS.at()
            for c in range(16):
                slot = c % 2
                if c >= 2:
                    p.wait("tensor", SS, SS.n)
                p.op("tensor", "transpose", ps_tr[slot][:],
                     xln[buf][:, c * 128:(c + 1) * 128], C.ident[:],
                     wait=[ln_v] if c == 0 else [], inc=(TSm, 1))
                p.op("scalar", "activation",
                     stage[:, c * 512 + tt * 128: c * 512 + tt * 128 + 128],
                     ps_tr[slot][:], AF.Copy, scale=1.0,
                     wait=[TSm.at()], inc=(SS, 1))
            p.wait("vector", SS, SS.n)
            p.op("vector", "memset", stats[:, 7:8], 0.0, inc=(VS, 1))
        stage_done = SS.at()

        p.op("sync", "dma_start", out=xlnT_hbm[g].rearrange("(h p) t -> p h t", p=128),
             in_=stage[:].rearrange("p (h t) -> p h t", h=16),
             wait=[stage_done], inc=(LD, 16))
        xln_stored.append(LD.at())

        rope_v = {}
        for kvh in range(4):
            sl = kvh % 2
            if kvh >= 2:
                p.wait("tensor", VS, rope_v[kvh - 2])
            else:
                p.wait("tensor", VS, pa_vguard[0])
            for h in range(16):
                p.op("tensor", "matmul", ps_s[sl][:], qk_cols(h * 1024 + kvh * 128),
                     stage[:, h * 512:(h + 1) * 512],
                     start=(h == 0), stop=(h == 15),
                     wait=[stage_done, wkv_ld, rope_ld] if h == 0 else [],
                     inc=(TSm, 1) if h == 15 else None)
            dst = kT[:, kvh * T + g * 512: kvh * T + g * 512 + 512]
            _rope(p, C, ps_s[sl], rope_c[:, g * 512:g * 512 + 512],
                  rope_s[:, g * 512:g * 512 + 512], rtmp[kvh % 2], dst, TSm.at())
            rope_v[kvh] = VS.n
        for kvh in range(4):
            sl = kvh % 2
            p.wait("tensor", SS, SS.n)
            p.wait("tensor", VS, VS.n)
            for h in range(16):
                p.op("tensor", "matmul", ps_s[sl][:],
                     wbuf[:, h * 1024 + 512 + kvh * 128: h * 1024 + 512 + kvh * 128 + 128],
                     stage[:, h * 512:(h + 1) * 512],
                     start=(h == 0), stop=(h == 15), inc=(TSm, 1) if h == 15 else None)
            p.op("vector", "tensor_copy", vt_stage[:], ps_s[sl][:], wait=[TSm.at()], inc=(VS, 1))
            for ts in range(4):
                slot = ts % 2
                if ts >= 2:
                    p.wait("tensor", SS, SS.n)
                p.op("tensor", "transpose", ps_tr[slot][:],
                     vt_stage[:, ts * 128:ts * 128 + 128], C.ident[:],
                     wait=[VS.at()] if ts == 0 else [], inc=(TSm, 1))
                p.op("scalar", "activation",
                     v_tm[:, kvh * T + (g * 4 + ts) * 128: kvh * T + (g * 4 + ts) * 128 + 128],
                     ps_tr[slot][:], AF.Copy, scale=1.0,
                     wait=[TSm.at()], inc=(SS, 1))
            if kvh == 3:
                pa_vguard[0] = VS.n

    pb_vguard = [pa_vguard[0]]
    for half in range(2):
        for h in range(16):
            p.op("gpsimd", "dma_start", out=wbuf[:, h * 1024:(h + 1) * 1024],
                 in_=w_attn[h * 128:(h + 1) * 128, half * 1024:(half + 1) * 1024],
                 wait=[TSm.at()], inc=(LG, 16))
        wq_ld = LG.at()
        for g in range(8):
            p.op("sync", "dma_start", out=stage[:].rearrange("p (h t) -> p h t", h=16),
                 in_=xlnT_hbm[g].rearrange("(h p) t -> p h t", p=128),
                 wait=[xln_stored[g], TSm.at()], inc=(LD, 16))
            st_ld = LD.at()
            ropeq_v = {}
            for ql in range(8):
                qh = half * 8 + ql
                sl = ql % 2
                if ql >= 2:
                    p.wait("tensor", VS, ropeq_v[ql - 2])
                else:
                    p.wait("tensor", VS, pb_vguard[0])
                for h in range(16):
                    p.op("tensor", "matmul", ps_s[sl][:], qk_cols(h * 1024 + ql * 128),
                         stage[:, h * 512:(h + 1) * 512],
                         start=(h == 0), stop=(h == 15),
                         wait=[st_ld, wq_ld] if h == 0 else [],
                         inc=(TSm, 1) if h == 15 else None)
                dst = qT[:, ql * 512:(ql + 1) * 512]
                _rope(p, C, ps_s[sl], rope_c[:, g * 512:g * 512 + 512],
                      rope_s[:, g * 512:g * 512 + 512], rtmp[ql % 2], dst, TSm.at())
                ropeq_v[ql] = VS.n
            pb_vguard[0] = VS.n
            p.op("sync", "dma_start",
                 out=qT_hbm[g][half * 1024:(half + 1) * 1024, :].rearrange("(h p) t -> p h t", p=128),
                 in_=qT[:].rearrange("p (h t) -> p h t", h=8),
                 wait=[VS.at()], inc=(LD, 16))
    qT_stored = LD.at()

    ab.close()

    cs = ExitStack()
    qT_sb = stage
    pexp = [p.sbuf_scoped(cs, f"pexp{i}", (128, 512), BF16) for i in range(2)]
    rD = p.sbuf_scoped(cs, "rD", (1, 512), BF16)
    rB = p.sbuf_scoped(cs, "rB", (128, 512), BF16)
    yT = p.sbuf_scoped(cs, "yT", (128, 16 * 512), BF16)
    y2 = p.sbuf_scoped(cs, "y2", (128, 16 * 512), BF16)
    sqb = [p.sbuf_scoped(cs, f"sqb{i}", (128, 512), BF16) for i in range(2)]
    ssq_f = p.sbuf_scoped(cs, "ssq_f", (1, 512), F32)
    rstd_r = p.sbuf_scoped(cs, "rstd_r", (1, 512), BF16)
    nrm = p.sbuf_scoped(cs, "nrm", (128, 128), BF16)
    xres = p.sbuf_scoped(cs, "xres", (128, 2048), F32)

    for g in range(8):
        p.op("sync", "dma_start", out=qT_sb[:].rearrange("p (h t) -> p h t", h=16),
             in_=qT_hbm[g].rearrange("(h p) t -> p h t", p=128),
             wait=[qT_stored, TSm.at()], inc=(LD, 16))
        qld = LD.at()
        n_kt = 4 * g + 4
        for hh in range(16):
            kv = hh // 4
            qsl = qT_sb[:, hh * 512:(hh + 1) * 512]
            exp_v = {}
            prd_v = {}
            def emit_scores(kt):
                sl = kt % 2
                w0 = []
                if kt == 0 and hh == 0:
                    w0.append(qld)
                if kt >= 2:
                    w0.append((SS, exp_v[kt - 2]))
                else:
                    w0.append((SS, SS.n))
                p.op("tensor", "matmul", ps_s[sl][:],
                     kT[:, kv * T + kt * 128: kv * T + kt * 128 + 128], qsl,
                     start=True, stop=True, wait=w0, inc=(TSm, 1))
            emit_scores(0)
            for kt in range(n_kt):
                sl = kt % 2
                p.op("scalar", "activation", pexp[sl][:], ps_s[sl][:], AF.Exp,
                     scale=float(1.0 / np.sqrt(128.0)), wait=[TSm.at()], inc=(SS, 1))
                exp_v[kt] = SS.n
                if kt >= 4 * g:
                    r = kt - 4 * g
                    p.op("vector", "tensor_mul", pexp[sl][:], pexp[sl][:],
                         masks[:, r * 512:(r + 1) * 512], wait=[SS.at()], inc=(VS, 1))
                    prd_v[kt] = (VS, VS.n)
                else:
                    prd_v[kt] = (SS, SS.n)
                if kt + 1 < n_kt:
                    emit_scores(kt + 1)
                p.wait("tensor", prd_v[kt][0], prd_v[kt][1])
                p.op("tensor", "matmul", ps_dn[:], C.ones_col[:], pexp[sl][:],
                     start=(kt == 0), stop=(kt == n_kt - 1))
                p.op("tensor", "matmul", ps_av[:],
                     v_tm[:, kv * T + kt * 128: kv * T + kt * 128 + 128], pexp[sl][:],
                     start=(kt == 0), stop=(kt == n_kt - 1),
                     inc=(TSm, 1) if kt == n_kt - 1 else None)
            p.op("vector", "reciprocal", rD[:], ps_dn[:], wait=[TSm.at()], inc=(VS, 1))
            p.op("tensor", "matmul", ps_r[:], C.ones_row[:], rD[:], start=True, stop=True,
                 wait=[VS.at()], inc=(TSm, 1))
            p.op("scalar", "activation", rB[:], ps_r[:], AF.Copy, scale=1.0,
                 wait=[TSm.at()], inc=(SS, 1))
            p.op("vector", "tensor_mul", yT[:, hh * 512:(hh + 1) * 512], ps_av[:], rB[:],
                 wait=[SS.at()], inc=(VS, 1))
            p.wait("tensor", VS, VS.n)
        yT_done = VS.at()

        evac_p = {}
        for half in range(2):
            for h in range(16):
                p.op("gpsimd", "dma_start", out=wbuf[:, h * 1024:(h + 1) * 1024],
                     in_=w_proj[h * 128:(h + 1) * 128, half * 1024:(half + 1) * 1024],
                     wait=[yT_done, TSm.at()] if h == 0 else [], inc=(LG, 16))
            wp_ld = LG.at()
            for ol in range(8):
                oc = half * 8 + ol
                pb = ps_proj if oc % 2 == 0 else ps_av
                w0 = [yT_done, wp_ld] if ol == 0 else []
                if oc >= 2:
                    w0.append((SS, evac_p[oc - 2]))
                for h in range(16):
                    p.op("tensor", "matmul", pb[:],
                         wbuf[:, h * 1024 + ol * 128: h * 1024 + ol * 128 + 128],
                         yT[:, h * 512:(h + 1) * 512],
                         start=(h == 0), stop=(h == 15),
                         wait=w0 if h == 0 else [],
                         inc=(TSm, 1) if h == 15 else None)
                p.op("scalar", "activation", y2[:, oc * 512:(oc + 1) * 512], pb[:],
                     AF.Copy, scale=1.0, wait=[TSm.at()], inc=(SS, 1))
                p.op("scalar", "activation", sqb[oc % 2][:], pb[:], AF.Square, inc=(SS, 1))
                evac_p[oc] = SS.n
                p.op("tensor", "matmul", ps_dn[:], C.ones_col[:], sqb[oc % 2][:],
                     start=(oc == 0), stop=(oc == 15), wait=[SS.at()],
                     inc=(TSm, 1) if oc == 15 else None)
        p.op("scalar", "activation", ssq_f[:], ps_dn[:], AF.Ln, scale=1.0 / D,
             bias=C.cbank_ap(CB_EPS_RMS, 1), wait=[TSm.at()])
        p.op("scalar", "activation", rstd_r[:], ssq_f[:], AF.Exp, scale=-0.5, inc=(SS, 1))
        p.op("tensor", "matmul", ps_r[:], C.ones_row[:], rstd_r[:], start=True, stop=True,
             wait=[SS.at()], inc=(TSm, 1))
        p.op("scalar", "activation", rB[:], ps_r[:], AF.Copy, scale=1.0,
             wait=[TSm.at()], inc=(SS, 1))
        rb_done = SS.at()

        for tt in range(4):
            trow = g * 512 + tt * 128
            p.op("sync", "dma_start", out=xres[:], in_=x_in[trow:trow + 128, :],
                 wait=[ST.at()], inc=(LD, 16))
            p.wait("vector", LD, LD.n)
            for oc in range(16):
                slot = oc % 2
                p.op("vector", "scalar_tensor_tensor", nrm[:, 0:128],
                     y2[:, oc * 512 + tt * 128: oc * 512 + tt * 128 + 128],
                     nw[:, oc:oc + 1], rB[:, tt * 128:tt * 128 + 128],
                     ALU.mult, ALU.mult,
                     wait=[rb_done, TSm.at()] if (oc == 0 and tt == 0) else [TSm.at()],
                     inc=(VS, 1))
                if oc >= 2:
                    p.wait("tensor", VS, VS.n)
                p.op("tensor", "transpose", ps_tr[slot][:],
                     nrm[:, 0:128], C.ident[:], wait=[VS.at()], inc=(TSm, 1))
                p.op("vector", "tensor_add", xres[:, oc * 128:(oc + 1) * 128],
                     xres[:, oc * 128:(oc + 1) * 128],
                     ps_tr[slot][:],
                     wait=[TSm.at()], inc=(VS, 1))
            p.op("sync", "dma_start", out=x_out[trow:trow + 128, :], in_=xres[:],
                 wait=[VS.at()], inc=(ST, 16))
    cs.close()
    p.wait("sync", ST, ST.n)
    p.pop_scope()
    return p


def _rope(p, C, ps, ctile, stile, rtmp, dst, mm_done):
    VS = C.VS
    t0, t1 = rtmp[:, 0:512], rtmp[:, 512:1024]
    t2, t3 = rtmp[:, 1024:1536], rtmp[:, 1536:2048]
    p.op("vector", "tensor_mul", t0[0:64, :], ps[0:64, :], ctile[0:64, :], wait=[mm_done])
    p.op("vector", "tensor_mul", t1[0:64, :], ps[64:128, :], stile[0:64, :])
    p.op("vector", "tensor_sub", dst[0:64, :], t0[0:64, :], t1[0:64, :])
    p.op("vector", "tensor_mul", t2[0:64, :], ps[0:64, :], stile[0:64, :])
    p.op("vector", "tensor_mul", t3[0:64, :], ps[64:128, :], ctile[0:64, :])
    p.op("vector", "tensor_add", dst[64:128, :], t2[0:64, :], t3[0:64, :], inc=(VS, 1))



def emit_ln_stage(p, C, x_in, stage, xr, xsq, stats, xln, ps_tr, g, consts_ld=None):
    LD, VS, SS, TSm = C.LD, C.VS, C.SS, C.TS
    for tt in range(4):
        trow = g * 512 + tt * 128
        buf = tt % 2
        first = (g == 0 and tt == 0 and consts_ld is not None)
        p.op("sync", "dma_start", out=xr[buf][:], in_=x_in[trow:trow + 128, :],
             wait=consts_ld if first else [VS.at()], inc=(LD, 16))
        p.wait("vector", LD, LD.n)
        C.ln_tile(xr[buf], xsq, xln[buf], stats)
        ln_v = VS.at()
        for c in range(16):
            slot = c % 2
            if c >= 2:
                p.wait("tensor", SS, SS.n)
            p.op("tensor", "transpose", ps_tr[slot][:],
                 xln[buf][:, c * 128:(c + 1) * 128], C.ident[:],
                 wait=[ln_v] if c == 0 else [], inc=(TSm, 1))
            p.op("scalar", "activation",
                 stage[:, c * 512 + tt * 128: c * 512 + tt * 128 + 128],
                 ps_tr[slot][:], AF.Copy, scale=1.0,
                 wait=[TSm.at(), LD.at()] if (tt == 0 and c == 0) else [TSm.at()],
                 inc=(SS, 1))
        p.wait("vector", SS, SS.n)
        p.op("vector", "memset", stats[:, 7:8], 0.0, inc=(VS, 1))
    return SS.at()



def build_l1ffn(fused=None):
    if fused is None:
        p = P("l1ffn")
        C = Common(p)
        x_in_ap = x_out_ap = None
    else:
        p, C, x_in_ap, x_out_ap = fused
    p.push_scope("f1_")
    x_in = x_in_ap if x_in_ap is not None else p.dram("x_in", (T, D), F32, kind="ExternalInput").ap()
    x_out = x_out_ap if x_out_ap is not None else p.dram("x_out", (T, D), F32, kind="ExternalOutput").ap()
    w_in = p.dram("weight", (D, D), F32, kind="ExternalInput").ap()
    b_in = p.dram("bias2d", (128, 16), F32, kind="ExternalInput").ap()

    stage = p.sbuf("stage", (128, 16 * 512), BF16)
    bwT = p.sbuf("bwT", (128, 16 * 2048), BF16)
    bias = p.sbuf("bias_sb", (128, 16), F32)
    xr = [p.sbuf(f"xr{i}", (128, D), F32) for i in range(2)]
    xsq = p.sbuf("xsq", (128, D), F32)
    stats = p.sbuf("stats", (128, 8), F32)
    xln = [p.sbuf(f"xln{i}", (128, D), BF16) for i in range(2)]
    wrow = p.sbuf("wrow", (128, D), F32)
    sgn = p.sbuf("sgn", (128, D), BF16)
    bwrow = p.sbuf("bwrow", (128, D), BF16)
    alpha = p.sbuf("alpha", (128, 2), F32)
    y_all = p.sbuf("y_all", (128, 16 * 512), F32)
    yrow = p.sbuf("yrow", (128, 128), F32)
    xres = p.sbuf("xres", (128, D), F32)

    ps_mm = p.psum("ps_mm", (128, 512), F32)
    ps_tr = [p.psum(f"ps_tr{i}", (128, 128), BF16) for i in range(2)]
    ps_trf = [p.psum(f"ps_trf{i}", (128, 128), F32) for i in range(2)]

    LD, LG, ST, TSm, VS, SS = C.LD, C.LG, C.ST, C.TS, C.VS, C.SS
    C.load_consts()
    p.op("sync", "dma_start", out=bias[:], in_=b_in, inc=(LD, 16))
    consts_ld = [LD.at(), LG.at(), VS.at()]

    for oc in range(16):
        p.op("gpsimd", "dma_start", out=wrow[:], in_=w_in[oc * 128:(oc + 1) * 128, :],
             wait=[SS.at(), VS.at()] if oc > 0 else [], inc=(LG, 16))
        p.wait("vector", LG, LG.n)
        p.op("vector", "reduce_sum", alpha[:, 0:1], wrow[:], axis=AX.X,
             apply_absolute_value=True)
        p.op("vector", "drain")
        p.op("vector", "tensor_scalar", alpha[:, 1:2], alpha[:, 0:1], 1.0 / D, None,
             ALU.mult, inc=(VS, 1))
        p.op("vector", "drain")
        p.op("scalar", "sign", sgn[:], wrow[:], wait=[VS.at()], inc=(SS, 1))
        p.op("vector", "tensor_scalar", bwrow[:], sgn[:], alpha[:, 1:2], None,
             ALU.mult, wait=[SS.at()], inc=(VS, 1))
        for ic in range(16):
            slot = ic % 2
            if ic >= 2:
                p.wait("tensor", SS, SS.n)
            p.op("tensor", "transpose", ps_tr[slot][:],
                 bwrow[:, ic * 128:(ic + 1) * 128], C.ident[:],
                 wait=[VS.at()] if ic == 0 else [], inc=(TSm, 1))
            p.op("scalar", "activation", bwT[:, ic * 2048 + oc * 128: ic * 2048 + oc * 128 + 128],
                 ps_tr[slot][:], AF.Copy, scale=1.0, wait=[TSm.at()], inc=(SS, 1))
    bw_done = SS.at()

    for g in range(8):
        stage_done = emit_ln_stage(p, C, x_in, stage, xr, xsq, stats, xln, ps_tr, g,
                                   consts_ld if g == 0 else None)
        for oc in range(16):
            for ic in range(16):
                p.op("tensor", "matmul", ps_mm[:],
                     bwT[:, ic * 2048 + oc * 128: ic * 2048 + oc * 128 + 128],
                     stage[:, ic * 512:(ic + 1) * 512],
                     start=(ic == 0), stop=(ic == 15),
                     wait=[stage_done, bw_done] if (oc == 0 and ic == 0) else
                          ([VS.at()] if ic == 0 else []),
                     inc=(TSm, 1) if ic == 15 else None)
            p.op("vector", "tensor_scalar", y_all[:, oc * 512:(oc + 1) * 512],
                 ps_mm[:], bias[:, oc:oc + 1], None, ALU.add,
                 wait=[TSm.at()], inc=(VS, 1))
        y_done = VS.at()
        for tt in range(4):
            trow = g * 512 + tt * 128
            p.op("sync", "dma_start", out=xres[:], in_=x_in[trow:trow + 128, :],
                 wait=[ST.at()], inc=(LD, 16))
            p.wait("vector", LD, LD.n)
            for oc in range(16):
                slot = oc % 2
                if oc >= 2:
                    p.wait("tensor", VS, VS.n)
                p.op("tensor", "transpose", ps_trf[slot][:],
                     y_all[:, oc * 512 + tt * 128: oc * 512 + tt * 128 + 128],
                     C.ident_f[:],
                     wait=[y_done, VS.at()] if (oc == 0 and tt == 0) else
                          ([VS.at()] if oc < 2 else []),
                     inc=(TSm, 1))
                p.op("vector", "tensor_add", xres[:, oc * 128:(oc + 1) * 128],
                     xres[:, oc * 128:(oc + 1) * 128], ps_trf[slot][:],
                     wait=[TSm.at()], inc=(VS, 1))
            p.op("sync", "dma_start", out=x_out[trow:trow + 128, :], in_=xres[:],
                 wait=[VS.at()], inc=(ST, 16))
    p.wait("sync", ST, ST.n)
    p.pop_scope()
    return p


def run_l1ffn(x, weight, bias):
    c = _consts()
    p = build_l1ffn()
    nc = p.finalize()
    b2d = np.ascontiguousarray(bias.reshape(16, 128).T)
    in_maps = []
    for b in range(NCORES):
        in_maps.append({
            "x_in": np.ascontiguousarray(x[b]),
            "weight": weight, "bias2d": b2d,
            "c_ident": c["ident"], "c_cbank": c["cbank"],
        })
    res = run_bass_kernel_spmd(nc, in_maps, core_ids=list(range(NCORES)))
    return np.stack([res.results[b]["x_out"] for b in range(NCORES)])



NEXP = 17
GU_WG = 4
DN_WG = 3

def build_l0ffn(phases=(1, 1, 1), nexp=NEXP, debug=False, fused=None):
    if fused is None:
        p = P("l0ffn")
        C = Common(p)
        x_in_ap = x_out_ap = None
    else:
        p, C, x_in_ap, x_out_ap = fused
    p.push_scope("m0_")
    x_in = x_in_ap if x_in_ap is not None else p.dram("x_in", (T, D), F32, kind="ExternalInput").ap()
    x_out = x_out_ap if x_out_ap is not None else p.dram("x_out", (T, D), F32, kind="ExternalOutput").ap()
    gate_w = p.dram("gate_w", (16, D), F32, kind="ExternalInput").ap()
    gup = p.dram("gup", (NEXP, 256, 3072), I32, kind="ExternalInput").ap()
    gus = p.dram("gus", (NEXP, 16, 3072), F32, kind="ExternalInput").ap()
    dnp = p.dram("dnp", (NEXP, 192, 2048), I32, kind="ExternalInput").ap()
    dns = p.dram("dns", (NEXP, 12, 2048), F32, kind="ExternalInput").ap()
    sel4_in = p.dram("sel4", (4, 128), F32, kind="ExternalInput").ap()
    sel16_in = p.dram("sel16", (16, 16 * 128), F32, kind="ExternalInput").ap()
    shb_in = p.dram("shbank", (128, 8), I32, kind="ExternalInput").ap()
    xlnT_hbm = p.dram("xlnT_scratch", (8, D, 512), BF16, kind="Internal").ap()
    dbk = "ExternalOutput" if debug else "Internal"
    wgu_d = p.dram("wgu_dense", (NEXP, 12, D, 256), BF16, kind=dbk).ap()
    wdn_d = p.dram("wdn_dense", (NEXP, 1536, D), BF16, kind=dbk).ap()

    combT = p.sbuf("combT", (16, T), BF16)
    gwT = p.sbuf("gwT", (128, 16 * 16), BF16)
    sel4 = p.sbuf("sel4_sb", (4, 128), BF16)
    sel16 = p.sbuf("sel16_sb", (16, 16 * 128), BF16)
    shb = p.sbuf("shb_sb", (128, 8), I32)

    ps_tr = [p.psum(f"ps_tr{i}", (128, 128), BF16) for i in range(2)]

    LD, LG, ST, TSm, VS, SS = C.LD, C.LG, C.ST, C.TS, C.VS, C.SS
    C.load_consts()

    prep = ExitStack()
    gw_f = p.sbuf_scoped(prep, "gw_f", (16, D), F32)
    gw_b = p.sbuf_scoped(prep, "gw_b", (16, D), BF16)
    p.op("sync", "dma_start", out=gw_f[:], in_=gate_w, inc=(LD, 16))
    p.op("gpsimd", "dma_start", out=sel4[:], in_=sel4_in, inc=(LG, 16))
    p.op("gpsimd", "dma_start", out=sel16[:], in_=sel16_in, inc=(LG, 16))
    p.op("gpsimd", "dma_start", out=shb[:], in_=shb_in, inc=(LG, 16))
    p.wait("vector", LD, LD.n)
    p.op("vector", "tensor_copy", gw_b[:], gw_f[:], wait=[LG.at()], inc=(VS, 1))
    for h in range(16):
        slot = h % 2
        if h >= 2:
            p.wait("tensor", SS, SS.n)
        p.op("tensor", "transpose", ps_tr[slot][:, 0:16],
             gw_b[0:16, h * 128:(h + 1) * 128], C.ident[0:16, 0:16],
             wait=[VS.at()] if h == 0 else [], inc=(TSm, 1))
        p.op("scalar", "activation", gwT[:, h * 16:(h + 1) * 16],
             ps_tr[slot][:, 0:16], AF.Copy, scale=1.0, wait=[TSm.at()], inc=(SS, 1))
    prep.close()
    consts_ld = [LD.at(), LG.at(), VS.at(), SS.at()]

    do_a, do_c, do_d = phases
    pa = ExitStack()
    ps_p = p.psum_scoped(pa, "ps_p", (128, 16), F32)
    ps_ct = p.psum_scoped(pa, "ps_ct", (16, 128), BF16)
    ps_a = p.psum_scoped(pa, "ps_sea", (128, 512), F32)
    ps_b = p.psum_scoped(pa, "ps_seb", (128, 512), F32)
    xr = [p.sbuf_scoped(pa, f"xr{i}", (128, D), F32) for i in range(2)]
    xsq = p.sbuf_scoped(pa, "xsq", (128, D), F32)
    stats = p.sbuf_scoped(pa, "stats", (128, 8), F32)
    xln = [p.sbuf_scoped(pa, f"xln{i}", (128, D), BF16) for i in range(2)]
    stage = p.sbuf_scoped(pa, "stage", (128, 16 * 512), BF16)
    pr = p.sbuf_scoped(pa, "pr", (128, 16), F32)
    pz = p.sbuf_scoped(pa, "pz", (128, 16), F32)
    msk = p.sbuf_scoped(pa, "msk", (128, 16), U8)
    mkf = p.sbuf_scoped(pa, "mkf", (128, 16), F32)
    num = p.sbuf_scoped(pa, "num", (128, 16), F32)
    rstats = p.sbuf_scoped(pa, "rstats", (128, 8), F32)
    cmb = p.sbuf_scoped(pa, "cmb", (128, 16), BF16)

    pc = pa
    pk = p.sbuf_scoped(pc, "pk", (128, 3072), I32)
    mag = [p.sbuf_scoped(pc, f"mag{i}", (128, 3072), I32) for i in range(2)]
    sgn = [p.sbuf_scoped(pc, f"sgn{i}", (128, 3072), I32) for i in range(2)]
    t_cm = [p.sbuf_scoped(pc, f"t_cm{i}", (128, 3072), BF16) for i in range(2)]
    t_r1 = [p.sbuf_scoped(pc, f"t_r1{i}", (128, 3072), BF16) for i in range(2)]
    t_r2 = [p.sbuf_scoped(pc, f"t_r2{i}", (128, 3072), BF16) for i in range(2)]
    t_sg = [p.sbuf_scoped(pc, f"t_sg{i}", (128, 3072), BF16) for i in range(2)]
    se_f = p.sbuf_scoped(pc, "se_f", (4, 3072), F32)
    se_b = p.sbuf_scoped(pc, "se_b", (4, 3072), BF16)
    se = p.sbuf_scoped(pc, "se", (128, 3072), BF16)
    dq = [p.sbuf_scoped(pc, f"dq{i}", (128, 3072), BF16) for i in range(2)]

    first_dq = [True]
    dqst = {"wr": [0, 0], "pk_free": 0}

    def dequant_block(packed2d, scales2d, ncols, wg, dense_out_fn, nparts=64):
        p.op("sync", "dma_start", out=pk[0:64, 0:ncols],
             in_=packed2d[wg * 64:(wg + 1) * 64, :], wait=[(VS, dqst["pk_free"])],
             inc=(LD, 16))
        p.op("sync", "dma_start", out=pk[64:128, 0:ncols],
             in_=packed2d[wg * 64:(wg + 1) * 64, :], inc=(LD, 16))
        p.op("sync", "dma_start", out=se_f[:, 0:ncols],
             in_=scales2d[wg * 4:(wg + 1) * 4, :], inc=(LD, 16))
        p.wait("vector", LD, LD.n)
        p.op("vector", "tensor_copy", se_b[:, 0:ncols], se_f[:, 0:ncols], inc=(VS, 1))
        nse = (ncols + 511) // 512
        for i in range(nse):
            c0, c1 = i * 512, min(ncols, (i + 1) * 512)
            pb = ps_a if i % 2 == 0 else ps_b
            if i >= 2:
                p.wait("tensor", SS, SS.n)
            p.op("tensor", "matmul", pb[:, 0:c1 - c0], sel4[:], se_b[:, c0:c1],
                 start=True, stop=True, wait=[VS.at()] if i == 0 else [],
                 inc=(TSm, 1))
            p.op("scalar", "activation", se[:, c0:c1], pb[:, 0:c1 - c0], AF.Copy,
                 scale=1.0, wait=[TSm.at()], inc=(SS, 1))
        se_done = SS.at()
        cols = slice(0, ncols)

        def extract(j, par):
            pi = j // 2
            p.op("vector", "tensor_scalar", mag[par][:, cols], pk[:, cols],
                 shb[:, pi:pi + 1], 7, ALU.logical_shift_right, ALU.bitwise_and)
            p.op("vector", "tensor_scalar", sgn[par][:, cols], pk[:, cols],
                 shb[:, 4 + pi:5 + pi], 1, ALU.logical_shift_right, ALU.bitwise_and)
            p.op("vector", "tensor_scalar", t_sg[par][:, cols], sgn[par][:, cols],
                 -2.0, 1.0, ALU.mult, ALU.add, inc=(VS, 1))
            if j == 6:
                dqst["pk_free"] = VS.n

        extract(0, 0)
        for jp in range(4):
            par = jp % 2
            buf = dq[par]
            ex_v = VS.at()
            p.op("scalar", "activation", t_cm[par][:, cols], mag[par][:, cols], AF.Copy,
                 scale=0.5, wait=[ex_v])
            p.op("scalar", "activation", t_r1[par][:, cols], mag[par][:, cols], AF.Relu,
                 scale=0.5, bias=C.cbank_ap(CB_NEG2, 128))
            p.op("scalar", "activation", t_r2[par][:, cols], mag[par][:, cols], AF.Relu,
                 scale=1.0, bias=C.cbank_ap(CB_NEG6, 128), inc=(SS, 1))
            acts_v = SS.at()
            if jp < 3:
                extract(2 * (jp + 1), 1 - par)
            p.op("vector", "tensor_add", t_cm[par][:, cols], t_cm[par][:, cols],
                 t_r1[par][:, cols], wait=[acts_v, se_done])
            p.op("vector", "tensor_add", t_cm[par][:, cols], t_cm[par][:, cols],
                 t_r2[par][:, cols])
            p.op("vector", "tensor_mul", t_cm[par][:, cols], t_cm[par][:, cols],
                 t_sg[par][:, cols])
            p.op("vector", "tensor_mul", buf[:, cols], t_cm[par][:, cols], se[:, cols],
                 wait=[(LG, dqst["wr"][par])], inc=(VS, 1))
            dsts = dense_out_fn(jp)
            p.op("gpsimd", "dma_start", out=dsts[0], in_=buf[0:64, cols],
                 wait=[VS.at()], inc=(LG, 16))
            p.op("gpsimd", "dma_start", out=dsts[1], in_=buf[64:128, cols],
                 inc=(LG, 16))
            dqst["wr"][par] = LG.n

    def emit_dequant(e):
        for wg in range(GU_WG):
            def gu_out(jp, e=e, wg=wg):
                hc = wg * 4 + jp
                h0 = wgu_d[e, :, hc * 128:hc * 128 + 64, :].rearrange("cb p i -> p cb i")
                h1 = wgu_d[e, :, hc * 128 + 64:(hc + 1) * 128, :].rearrange("cb p i -> p cb i")
                return (h0, h1)
            dequant_block(gup[e], gus[e], 3072, wg, gu_out)
        for wg in range(DN_WG):
            dequant_block(dnp[e], dns[e], 2048, wg,
                          lambda jp, e=e, wg=wg: _dn_dest(wdn_d, e, wg, jp))

    _dq_pending = list(range(nexp if do_c else 0))

    for g in range(8 if do_a else 0):
        stage_done = emit_ln_stage(p, C, x_in, stage, xr, xsq, stats, xln, ps_tr, g,
                                   consts_ld if g == 0 else None)
        for tt in range(4):
            trow = g * 512 + tt * 128
            p.wait("tensor", VS, VS.n)
            for h in range(16):
                p.op("tensor", "matmul", ps_p[:],
                     stage[:, h * 512 + tt * 128: h * 512 + tt * 128 + 128],
                     gwT[:, h * 16:(h + 1) * 16],
                     start=(h == 0), stop=(h == 15),
                     wait=[stage_done] if h == 0 else [],
                     inc=(TSm, 1) if h == 15 else None)
            p.op("vector", "reduce_max", rstats[:, 0:1], ps_p[:], axis=AX.X,
                 negate=True, wait=[TSm.at()], inc=(VS, 1))
            p.op("vector", "drain")
            p.op("scalar", "activation", pr[:], ps_p[:], AF.Exp,
                 bias=rstats[:, 0:1], scale=1.0, wait=[VS.at()], inc=(SS, 1))
            p.op("vector", "memset", num[:], 0.0, wait=[SS.at()])
            p.op("vector", "memset", rstats[:, 1:2], 0.0)
            p.op("vector", "drain")
            for k in range(4):
                src = pr if k == 0 else pz
                p.op("vector", "reduce_max", rstats[:, 2:3], src[:], axis=AX.X)
                p.op("vector", "drain")
                p.op("vector", "tensor_scalar", msk[:], src[:], rstats[:, 2:3], None,
                     ALU.is_ge)
                p.op("vector", "drain")
                p.op("vector", "tensor_copy", mkf[:], msk[:])
                p.op("vector", "drain")
                p.op("vector", "scalar_tensor_tensor", num[:], mkf[:], rstats[:, 2:3],
                     num[:], ALU.mult, ALU.add)
                p.op("vector", "tensor_add", rstats[:, 1:2], rstats[:, 1:2], rstats[:, 2:3])
                p.op("vector", "drain")
                if k < 3:
                    p.op("vector", "memset", mkf[:], 0.0)
                    p.op("vector", "drain")
                    p.op("vector", "select", pz[:], msk[:], mkf[:], src[:])
                    p.op("vector", "drain")
            p.op("vector", "reciprocal", rstats[:, 3:4], rstats[:, 1:2])
            p.op("vector", "drain")
            p.op("vector", "tensor_scalar", cmb[:], num[:], rstats[:, 3:4], None,
                 ALU.mult, inc=(VS, 1))
            p.op("vector", "drain")
            p.op("tensor", "transpose", ps_ct[:], cmb[:], C.ident[:],
                 wait=[VS.at(), SS.at()], inc=(TSm, 1))
            p.op("scalar", "activation", combT[:, trow:trow + 128], ps_ct[0:16, :],
                 AF.Copy, scale=1.0, wait=[TSm.at()], inc=(SS, 1))
        p.op("sync", "dma_start", out=xlnT_hbm[g].rearrange("(h p) t -> p h t", p=128),
             in_=stage[:].rearrange("p (h t) -> p h t", h=16),
             wait=[SS.at()], inc=(LD, 16))
        for _ in range(2):
            if _dq_pending:
                emit_dequant(_dq_pending.pop(0))
    xln_all = LD.at()
    comb_all = SS.at()

    for e in _dq_pending:
        emit_dequant(e)
    dq_all = LG.at()
    pa.close()

    pd = ExitStack()
    ps_ga = [p.psum_scoped(pd, f"ps_ga{i}", (128, 512), F32) for i in range(2)]
    ps_gb = [p.psum_scoped(pd, f"ps_gb{i}", (128, 512), F32) for i in range(2)]
    ps_dnb = [p.psum_scoped(pd, f"ps_dnn{i}", (128, 512), F32) for i in range(2)]
    xp = p.sbuf_scoped(pd, "xp", (128, 16 * 1024), BF16)
    out_acc = p.sbuf_scoped(pd, "out_acc", (128, 16 * 1024), BF16)
    act = p.sbuf_scoped(pd, "act", (128, 12 * 1024), BF16)
    wdsb = p.sbuf_scoped(pd, "wdsb", (128, 12 * 2048), BF16)
    gblk = [p.sbuf_scoped(pd, f"gblk{i}", (128, 16 * 256), BF16) for i in range(2)]
    ublk = [p.sbuf_scoped(pd, f"ublk{i}", (128, 16 * 256), BF16) for i in range(2)]
    cB = p.sbuf_scoped(pd, "cB", (128, 1024), BF16)
    sg_sb = [p.sbuf_scoped(pd, f"sg_sb{i}", (128, 512), BF16) for i in range(2)]
    m1_sb = [p.sbuf_scoped(pd, f"m1_sb{i}", (128, 512), BF16) for i in range(2)]
    xres = p.sbuf_scoped(pd, "xres", (128, D), F32)

    first_d = [True]
    for tg in range(4 if do_d else 0):
        if first_d[0]:
            p.wait("sync", LG, LG.n)
            p.wait("vector", LG, LG.n)
            first_d[0] = False
        p.wait("sync", TSm, TSm.n)
        for hc in range(16):
            wg, jp = hc // 4, hc % 4
            for gb in range(2):
                g = tg * 2 + gb
                blk = xlnT_hbm[g][512 * wg: 512 * wg + 512, :]
                r = blk.rearrange("(pp j) t -> j pp t", j=8)
                w0 = [xln_all] if (tg == 0 and hc == 0 and gb == 0) else []
                cslice = slice(hc * 1024 + gb * 512, hc * 1024 + (gb + 1) * 512)
                p.op("sync", "dma_start", out=xp[0:64, cslice], in_=r[2 * jp],
                     wait=w0, inc=(LD, 16))
                p.op("sync", "dma_start", out=xp[64:128, cslice], in_=r[2 * jp + 1],
                     inc=(LD, 16))
        xp_ld = LD.at()
        p.op("vector", "memset", out_acc[:], 0.0, wait=[TSm.at()], inc=(VS, 1))
        g_evac = {}
        d_evac = {}
        blk_mm = {}
        for e in range(nexp):
            if e == 16:
                p.op("vector", "memset", cB[:], 1.0, wait=[TSm.at()], inc=(VS, 1))
                cb_done = VS.at()
            else:
                for i in range(2):
                    p.op("tensor", "matmul", ps_dnb[0][:, 0:512],
                         sel16[:, e * 128:(e + 1) * 128],
                         combT[:, tg * 1024 + i * 512: tg * 1024 + (i + 1) * 512],
                         start=True, stop=True,
                         wait=[comb_all, VS.at()] if i == 0 else [SS.at()], inc=(TSm, 1))
                    p.op("scalar", "activation", cB[:, i * 512:(i + 1) * 512],
                         ps_dnb[0][:, 0:512], AF.Copy, scale=1.0, wait=[TSm.at()], inc=(SS, 1))
                cb_done = SS.at()
            for bp in range(6):
                lb = bp % 2
                gk = (e, bp)
                wblk_wait = [dq_all] if tg == 0 else []
                prev = blk_mm.get((e, bp - 2)) if bp >= 2 else (
                    blk_mm.get((e - 1, 4)) if e > 0 and bp == 0 else (
                    blk_mm.get((e - 1, 5)) if e > 0 and bp == 1 else None))
                if prev is None:
                    prev = TSm.at()
                p.op("gpsimd", "dma_start", out=gblk[lb][:],
                     in_=wgu_d[e, bp].rearrange("(hc p) i -> p hc i", p=128),
                     wait=wblk_wait + [prev], inc=(LG, 16))
                p.op("gpsimd", "dma_start", out=ublk[lb][:],
                     in_=wgu_d[e, 6 + bp].rearrange("(hc p) i -> p hc i", p=128),
                     inc=(LG, 16))
                wld = LG.at()
                for ci_in in range(2):
                    ci = bp * 2 + ci_in
                    for th in range(2):
                        q = (bp * 2 + ci_in) * 2 + th
                        qp = q % 2
                        w0 = [wld, xp_ld, cb_done]
                        pg = g_evac.get((e, q - 2))
                        if pg is not None:
                            w0.append((pg[0], pg[1]))
                        for hc in range(16):
                            p.op("tensor", "matmul", ps_ga[qp][:],
                                 gblk[lb][:, hc * 256 + ci_in * 128: hc * 256 + ci_in * 128 + 128],
                                 xp[:, hc * 1024 + th * 512: hc * 1024 + (th + 1) * 512],
                                 start=(hc == 0), stop=(hc == 15),
                                 wait=w0 if hc == 0 else [],
                                 inc=(TSm, 1) if hc == 15 else None)
                            p.op("tensor", "matmul", ps_gb[qp][:],
                                 ublk[lb][:, hc * 256 + ci_in * 128: hc * 256 + ci_in * 128 + 128],
                                 xp[:, hc * 1024 + th * 512: hc * 1024 + (th + 1) * 512],
                                 start=(hc == 0), stop=(hc == 15),
                                 inc=(TSm, 1) if hc == 15 else None)
                        p.op("scalar", "activation", sg_sb[qp][:], ps_ga[qp][:], AF.Silu,
                             wait=[TSm.at()], inc=(SS, 1))
                        p.op("vector", "tensor_mul", m1_sb[qp][:], sg_sb[qp][:], ps_gb[qp][:],
                             wait=[SS.at()])
                        p.op("vector", "tensor_mul",
                             act[:, ci * 1024 + th * 512: ci * 1024 + (th + 1) * 512],
                             m1_sb[qp][:], cB[:, th * 512:(th + 1) * 512], inc=(VS, 1))
                        g_evac[(e, q)] = (VS, VS.n)
                blk_mm[(e, bp)] = TSm.at()
            act_done = VS.at()
            for ic in range(12):
                p.op("gpsimd", "dma_start", out=wdsb[:, ic * 2048:(ic + 1) * 2048],
                     in_=wdn_d[e, ic * 128:(ic + 1) * 128, :],
                     wait=[TSm.at()] if ic == 0 else [], inc=(LG, 16))
            wd_ld = LG.at()
            for th in range(2):
                for oc in range(16):
                    dqi = th * 16 + oc
                    dp = dqi % 2
                    w0 = [act_done, wd_ld]
                    pv = d_evac.get((e, dqi - 2))
                    if pv is not None:
                        w0.append((pv[0], pv[1]))
                    elif dqi < 2:
                        w0.append((VS, VS.n))
                    for ic in range(12):
                        p.op("tensor", "matmul", ps_dnb[dp][:],
                             wdsb[:, ic * 2048 + oc * 128: ic * 2048 + oc * 128 + 128],
                             act[:, ic * 1024 + th * 512: ic * 1024 + (th + 1) * 512],
                             start=(ic == 0), stop=(ic == 11),
                             wait=w0 if ic == 0 else [],
                             inc=(TSm, 1) if ic == 11 else None)
                    sl = oc * 1024 + th * 512
                    p.op("vector", "tensor_add", out_acc[:, sl:sl + 512],
                         out_acc[:, sl:sl + 512], ps_dnb[dp][:], wait=[TSm.at()], inc=(VS, 1))
                    d_evac[(e, dqi)] = (VS, VS.n)
        acc_done = VS.at()
        for tt in range(8):
            trow = tg * 1024 + tt * 128
            p.op("sync", "dma_start", out=xres[:], in_=x_in[trow:trow + 128, :],
                 wait=[ST.at()], inc=(LD, 16))
            p.wait("vector", LD, LD.n)
            for oc in range(16):
                slot = oc % 2
                if oc >= 2:
                    p.wait("tensor", VS, VS.n)
                p.op("tensor", "transpose", ps_tr[slot][:],
                     out_acc[:, oc * 1024 + tt * 128: oc * 1024 + tt * 128 + 128],
                     C.ident[:],
                     wait=[acc_done, VS.at()] if oc < 2 else [],
                     inc=(TSm, 1))
                p.op("vector", "tensor_add", xres[:, oc * 128:(oc + 1) * 128],
                     xres[:, oc * 128:(oc + 1) * 128], ps_tr[slot][:],
                     wait=[TSm.at()], inc=(VS, 1))
            p.op("sync", "dma_start", out=x_out[trow:trow + 128, :], in_=xres[:],
                 wait=[VS.at()], inc=(ST, 16))
    pd.close()
    p.wait("sync", ST, ST.n)
    p.pop_scope()
    return p


def _dn_dest(wdn_d, e, wg, jp):
    v = wdn_d[e, 512 * wg: 512 * wg + 512, :]
    r = v.rearrange("(pp j) c -> j pp c", j=8)
    return (r[2 * jp], r[2 * jp + 1])


def _shbank():
    shbank = np.zeros((128, 8), np.int32)
    for pi in range(4):
        shbank[0:64, pi] = 8 * pi
        shbank[64:128, pi] = 8 * pi + 4
    shbank[:, 4:8] = shbank[:, 0:4] + 3
    return shbank


def run_l0ffn(x, gate_w, gup, gus, dnp, dns, sgup, sgus, sdnp, sdns):
    c = _consts()
    p = build_l0ffn()
    nc = p.finalize()
    gup_a = np.concatenate([gup, sgup[None]], 0)
    gus_a = np.concatenate([gus, sgus[None]], 0)
    dnp_a = np.concatenate([dnp, sdnp[None]], 0)
    dns_a = np.concatenate([dns, sdns[None]], 0)
    pp = np.arange(128)
    sel4 = np.stack([(((pp % 64) >> 4) == k) for k in range(4)]).astype(np.float32)
    sel16 = np.zeros((16, 16 * 128), np.float32)
    for e in range(16):
        sel16[e, e * 128:(e + 1) * 128] = 1.0
    in_maps = []
    for b in range(NCORES):
        in_maps.append({
            "x_in": np.ascontiguousarray(x[b]), "gate_w": gate_w,
            "gup": gup_a, "gus": gus_a, "dnp": dnp_a, "dns": dns_a,
            "sel4": sel4, "sel16": sel16, "shbank": _shbank(),
            "c_ident": c["ident"], "c_cbank": c["cbank"],
        })
    res = run_bass_kernel_spmd(nc, in_maps, core_ids=list(range(NCORES)))
    return np.stack([res.results[b]["x_out"] for b in range(NCORES)])



DIN, DST, NHM, HDM, CHK = 4096, 128, 64, 64, 256
CONVD = DIN + 2 * DST

def build_l1attn(debug=False, fused=None):
    if fused is None:
        p = P("l1attn")
        C = Common(p)
        x_in_ap = x_out_ap = None
    else:
        p, C, x_in_ap, x_out_ap = fused
    p.push_scope("a1_")
    x_in = x_in_ap if x_in_ap is not None else p.dram("x_in", (T, D), F32, kind="ExternalInput").ap()
    x_out = x_out_ap if x_out_ap is not None else p.dram("x_out", (T, D), F32, kind="ExternalOutput").ap()
    w_inp = p.dram("in_proj_w", (8512, D), F32, kind="ExternalInput").ap()
    w_outp = p.dram("out_proj_w", (D, DIN), F32, kind="ExternalInput").ap()
    conv_w_in = p.dram("conv_w2d", (128, 34 * 4), F32, kind="ExternalInput").ap()
    conv_b_in = p.dram("conv_b2d", (128, 34), F32, kind="ExternalInput").ap()
    dt_bias_in = p.dram("dt_bias2d", (64, 1), F32, kind="ExternalInput").ap()
    a_log_in = p.dram("a_log2d", (64, 1), F32, kind="ExternalInput").ap()
    d_in = p.dram("d2d", (64, 1), F32, kind="ExternalInput").ap()
    nw_in = p.dram("norm_w2d", (128, 32), F32, kind="ExternalInput").ap()
    sel64_in = p.dram("sel64", (64, 64 * 128), F32, kind="ExternalInput").ap()
    tri_in = p.dram("tri2", (128, 2 * 256), F32, kind="ExternalInput").ap()

    xlnT_hbm = p.dram("xlnT_scratch", (8, D, 512), BF16, kind="Internal").ap()
    w_inT = p.dram("w_inT_t", (67, D, 128), BF16, kind="Internal").ap()
    w_outT = p.dram("w_outT_t", (16, DIN, 128), BF16, kind="Internal").ap()
    dbk = "ExternalOutput" if debug else "Internal"
    if debug:
        dbg2 = p.dram("dbg2", (8, 128, 512), F32, kind="ExternalOutput").ap()
    zxb = p.dram("zxbcdtT", (8512, T), BF16, kind=dbk).ap()
    dtf32_hbm = p.dram("dt_f32", (64, T), F32, kind="Internal").ap()
    convT = p.dram("convT_s", (CONVD, T), BF16, kind=dbk).ap()
    y_hbm = p.dram("y_scr", (16, DIN, 256), BF16, kind=dbk).ap()

    ps_tr = [p.psum(f"ps_tr{i}", (128, 128), BF16) for i in range(2)]
    ps_trf0 = p.psum("ps_trf0", (128, 128), F32)
    ps_trf = [ps_trf0, ps_trf0]
    ps_mm = p.psum("ps_mm", (128, 512), F32)

    LD, LG, ST, TSm, VS, SS = C.LD, C.LG, C.ST, C.TS, C.VS, C.SS
    C.load_consts()
    sel64 = p.sbuf("sel64_sb", (64, 64 * 128), F32)
    tri2 = p.sbuf("tri2_sb", (128, 2 * 256), BF16)
    ones_rf = p.sbuf("ones_rf", (1, 128), F32)
    adt = p.sbuf("adt_fm", (64, T), F32)
    dt_tm = p.sbuf("dt_tm_all", (128, 32 * 64), F32)
    db_t = p.sbuf("db_t", (128, 64), F32)
    state = p.sbuf("state_sb", (128, NHM * 64), F32)
    p.op("gpsimd", "dma_start", out=sel64[:], in_=sel64_in, inc=(LG, 16))
    p.op("gpsimd", "dma_start", out=tri2[:], in_=tri_in, inc=(LG, 16))
    p.op("vector", "memset", ones_rf[:], 1.0, inc=(VS, 1))
    consts_ld = [LD.at(), LG.at(), VS.at()]

    pw = ExitStack()
    wrow = p.sbuf_scoped(pw, "wrow", (128, DIN), BF16)
    wst = p.sbuf_scoped(pw, "wst", (128, DIN), BF16)
    for roc in range(67):
        nr = 128 if roc < 66 else 64
        p.op("gpsimd", "dma_start", out=wrow[0:nr, 0:D],
             in_=w_inp[roc * 128: roc * 128 + nr, :],
             wait=[LD.at()] if roc > 0 else consts_ld, inc=(LG, 16))
        p.wait("tensor", LG, LG.n)
        for hc in range(16):
            slot = hc % 2
            if hc >= 2:
                p.wait("tensor", SS, SS.n)
            p.op("tensor", "transpose", ps_tr[slot][:, 0:nr],
                 wrow[0:nr, hc * 128:(hc + 1) * 128], C.ident[0:nr, 0:nr],
                 inc=(TSm, 1))
            p.op("scalar", "activation", wst[:, hc * 128:hc * 128 + nr],
                 ps_tr[slot][:, 0:nr], AF.Copy, scale=1.0, wait=[TSm.at()], inc=(SS, 1))
        p.op("sync", "dma_start",
             out=w_inT[roc, :, 0:nr].rearrange("(hc p) o -> p hc o", p=128),
             in_=wst[:, 0:2048].rearrange("p (hc o) -> p hc o", hc=16)[:, :, 0:nr],
             wait=[SS.at()], inc=(LD, 16))
    for oc in range(16):
        p.op("gpsimd", "dma_start", out=wrow[:, 0:DIN],
             in_=w_outp[oc * 128:(oc + 1) * 128, :], wait=[LD.at()], inc=(LG, 16))
        p.wait("tensor", LG, LG.n)
        for dc in range(32):
            slot = dc % 2
            if dc >= 2:
                p.wait("tensor", SS, SS.n)
            p.op("tensor", "transpose", ps_tr[slot][:],
                 wrow[:, dc * 128:(dc + 1) * 128], C.ident[:], inc=(TSm, 1))
            p.op("scalar", "activation", wst[:, dc * 128:(dc + 1) * 128],
                 ps_tr[slot][:], AF.Copy, scale=1.0, wait=[TSm.at()], inc=(SS, 1))
        p.op("sync", "dma_start",
             out=w_outT[oc].rearrange("(dc p) o -> p dc o", p=128),
             in_=wst[:, 0:DIN].rearrange("p (dc o) -> p dc o", dc=32),
             wait=[SS.at()], inc=(LD, 16))
    pw.close()
    wT_done_ld = LD.at()

    pa = ExitStack()
    xr = [p.sbuf_scoped(pa, f"xr{i}", (128, D), F32) for i in range(2)]
    xsq = p.sbuf_scoped(pa, "xsq", (128, D), F32)
    stats = p.sbuf_scoped(pa, "stats", (128, 8), F32)
    xln = [p.sbuf_scoped(pa, f"xln{i}", (128, D), BF16) for i in range(2)]
    stage = p.sbuf_scoped(pa, "stage", (128, 16 * 512), BF16)
    wblk = [p.sbuf_scoped(pa, f"wblk{i}", (128, 16 * 128), BF16) for i in range(2)]
    evb = [p.sbuf_scoped(pa, f"evb{i}", (128, 512), BF16) for i in range(2)]
    evbf = p.sbuf_scoped(pa, "evbf", (64, 512), F32)
    ps_mm2 = p.psum_scoped(pa, "ps_mm2", (128, 512), F32)

    zxb_done = {}
    for g in range(8):
        stage_done = emit_ln_stage(p, C, x_in, stage, xr, xsq, stats, xln, ps_tr, g,
                                   consts_ld if g == 0 else None)
        mm_v = {}
        ev_v = {}
        st_v = {}
        for roc in range(67):
            nr = 128 if roc < 66 else 64
            lb = roc % 2
            wl0 = [wT_done_ld] if (g == 0 and roc < 2) else []
            if roc >= 2:
                wl0.append((TSm, mm_v[roc - 2]))
            elif g > 0:
                wl0.append((TSm, TSm.n))
            p.op("gpsimd", "dma_start", out=wblk[lb][:],
                 in_=w_inT[roc].rearrange("(hc p) o -> p hc o", p=128),
                 wait=wl0, inc=(LG, 16))
            wld = LG.at()
            pmb = ps_mm if roc % 2 == 0 else ps_mm2
            w0 = [stage_done, wld]
            if roc >= 2:
                w0.append((SS, ev_v[roc - 2]))
            for hc in range(16):
                p.op("tensor", "matmul", pmb[0:nr, 0:512],
                     wblk[lb][:, hc * 128: hc * 128 + nr],
                     stage[:, hc * 512:(hc + 1) * 512],
                     start=(hc == 0), stop=(hc == 15),
                     wait=w0 if hc == 0 else [],
                     inc=(TSm, 1) if hc == 15 else None)
            mm_v[roc] = TSm.n
            if roc < 66:
                wv = [TSm.at()]
                if roc >= 2:
                    wv.append((LD, st_v[roc - 2]))
                elif g > 0:
                    wv.append((LD, LD.n))
                p.op("scalar", "activation", evb[lb][0:nr, :], pmb[0:nr, 0:512], AF.Copy,
                     scale=1.0, wait=wv, inc=(SS, 1))
                ev_v[roc] = SS.n
                p.op("sync", "dma_start", out=zxb[roc * 128: roc * 128 + nr,
                                                  g * 512:(g + 1) * 512],
                     in_=evb[lb][0:nr, :], wait=[SS.at()], inc=(LD, 16))
                st_v[roc] = LD.n
                if g == 7:
                    zxb_done[roc] = LD.n
            else:
                p.op("scalar", "activation", evbf[0:nr, :], pmb[0:nr, 0:512], AF.Copy,
                     scale=1.0, wait=[TSm.at()], inc=(SS, 1))
                ev_v[roc] = SS.n
                p.op("sync", "dma_start", out=dtf32_hbm[0:nr, g * 512:(g + 1) * 512],
                     in_=evbf[0:nr, :], wait=[SS.at()], inc=(LD, 16))
    pa.close()
    inproj_ld = LD.at()

    pc = ExitStack()
    xc = p.sbuf_scoped(pc, "xc", (128, T + 4), BF16)
    cw = p.sbuf_scoped(pc, "cw_sb", (128, 34 * 4), F32)
    cbv = p.sbuf_scoped(pc, "cbv_sb", (128, 34), F32)
    cacc = p.sbuf_scoped(pc, "cacc", (128, T), F32)
    cout = p.sbuf_scoped(pc, "cout", (128, T), BF16)
    p.op("sync", "dma_start", out=cw[:], in_=conv_w_in, inc=(LD, 16))
    p.op("sync", "dma_start", out=cbv[:], in_=conv_b_in, inc=(LD, 16))
    for cc in range(34):
        p.op("vector", "memset", xc[:, 0:3], 0.0, wait=[SS.at()] if cc > 0 else [],
             inc=(VS, 1))
        p.op("sync", "dma_start", out=xc[:, 3:3 + T],
             in_=zxb[DIN + cc * 128: DIN + (cc + 1) * 128, :],
             wait=[(LD, zxb_done[32 + cc]), VS.at()], inc=(LD, 16))
        p.wait("vector", LD, LD.n)
        p.op("vector", "tensor_scalar", cacc[:], xc[:, 0:T], cw[:, 4 * cc:4 * cc + 1],
             None, ALU.mult)
        for k in range(1, 4):
            p.op("vector", "scalar_tensor_tensor", cacc[:], xc[:, k:k + T],
                 cw[:, 4 * cc + k:4 * cc + k + 1], cacc[:], ALU.mult, ALU.add)
        p.op("vector", "drain")
        p.op("vector", "tensor_copy", xc[:, 0:1], xc[:, 0:1], inc=(VS, 1))
        p.op("scalar", "activation", cout[:], cacc[:], AF.Silu,
             bias=cbv[:, cc:cc + 1], scale=1.0, wait=[VS.at()], inc=(SS, 1))
        p.op("sync", "dma_start", out=convT[cc * 128:(cc + 1) * 128, :], in_=cout[:],
             wait=[SS.at()], inc=(LD, 16))
    pc.close()
    conv_ld = LD.at()

    pef = ExitStack()
    ps_row = p.psum_scoped(pef, "ps_row", (128, 256), F32)
    ps_row2 = ps_mm[0:128, 0:256]
    ps_y = p.psum_scoped(pef, "ps_y", (64, 256), F32)
    ps_y2 = p.psum_scoped(pef, "ps_y2", (64, 256), F32)
    ps_s2 = p.psum_scoped(pef, "ps_s2", (128, 64), F32)
    pe = ExitStack()
    dtw = p.sbuf_scoped(pe, "dtw", (64, T), BF16)
    dtraw = p.sbuf_scoped(pe, "dtraw", (64, T), F32)
    dtf = p.sbuf_scoped(pe, "dtf", (64, T), F32)
    dtt1 = p.sbuf_scoped(pe, "dtt1", (64, T), F32)
    dtt2 = p.sbuf_scoped(pe, "dtt2", (64, T), F32)
    dtb = p.sbuf_scoped(pe, "dtb", (64, 1), F32)
    alog = p.sbuf_scoped(pe, "alog", (64, 4), F32)
    dcol = p.sbuf_scoped(pe, "dcol", (64, 1), F32)
    lr1 = p.sbuf_scoped(pe, "lr1", (1, 64), F32)
    p.op("sync", "dma_start", out=dtraw[:], in_=dtf32_hbm,
         wait=[inproj_ld], inc=(LD, 16))
    p.op("sync", "dma_start", out=dtb[:], in_=dt_bias_in, inc=(LD, 16))
    p.op("sync", "dma_start", out=alog[:, 0:1], in_=a_log_in, inc=(LD, 16))
    p.op("sync", "dma_start", out=dcol[:], in_=d_in, inc=(LD, 16))
    p.wait("vector", LD, LD.n)
    p.op("vector", "tensor_scalar", dtf[:], dtraw[:], dtb[:], None, ALU.add, inc=(VS, 1))
    p.op("scalar", "activation", dtt1[:], dtf[:], AF.Abs, wait=[VS.at()])
    p.op("scalar", "activation", dtt1[:], dtt1[:], AF.Exp, scale=-1.0)
    p.op("scalar", "activation", dtt1[:], dtt1[:], AF.Ln,
         bias=C.cbank_ap(CB_ONE, 64), scale=1.0, inc=(SS, 1))
    p.op("vector", "tensor_scalar", dtt2[:], dtf[:], 0.0, None, ALU.max,
         wait=[SS.at()])
    p.op("vector", "tensor_add", dtf[:], dtt2[:], dtt1[:], inc=(VS, 1))
    p.op("scalar", "activation", alog[:, 1:2], alog[:, 0:1], AF.Exp, wait=[VS.at()],
         inc=(SS, 1))
    p.op("vector", "tensor_scalar", alog[:, 2:3], alog[:, 1:2], -1.0, None, ALU.mult,
         wait=[SS.at()], inc=(VS, 1))
    p.op("vector", "drain")
    p.op("vector", "tensor_scalar", adt[:], dtf[:], alog[:, 2:3], None, ALU.mult,
         inc=(VS, 1))
    p.op("vector", "tensor_copy", dtw[:], dtf[:], inc=(VS, 1))
    for lc in range(32):
        slot = lc % 2
        if lc >= 2:
            p.wait("tensor", SS, SS.n)
        p.op("tensor", "transpose", ps_tr[slot][:, 0:64],
             dtw[:, lc * 128:(lc + 1) * 128], C.ident[0:64, 0:64],
             wait=[VS.at()] if lc == 0 else [], inc=(TSm, 1))
        p.op("scalar", "activation", dt_tm[:, lc * 64:(lc + 1) * 64],
             ps_tr[slot][:, 0:64], AF.Copy, scale=1.0, wait=[TSm.at()], inc=(SS, 1))
    p.op("tensor", "transpose", ps_trf[0][0:1, 0:64], dcol[:], C.ident_f[0:64, 0:64],
         wait=[VS.at()], inc=(TSm, 1))
    p.op("vector", "tensor_copy", lr1[:], ps_trf[0][0:1, 0:64], wait=[TSm.at()],
         inc=(VS, 1))
    p.op("tensor", "matmul", ps_s2[:], ones_rf[:], lr1[:], start=True, stop=True,
         wait=[VS.at()], inc=(TSm, 1))
    p.op("scalar", "activation", db_t[:], ps_s2[:], AF.Copy, scale=1.0,
         wait=[TSm.at()], inc=(SS, 1))
    pe.close()
    dt_done = (SS.at(), VS.at())

    pf = ExitStack()
    xs_sb = p.sbuf_scoped(pf, "xs_sb", (128, 32 * 256), BF16)
    bct = p.sbuf_scoped(pf, "bct", (128, 2 * 256), BF16)
    b_tm = p.sbuf_scoped(pf, "b_tm", (128, 2 * 128), BF16)
    x_tm = p.sbuf_scoped(pf, "x_tm", (128, 2 * 4096), BF16)
    xdt = p.sbuf_scoped(pf, "xdt", (128, 2 * 4096), BF16)
    acs = p.sbuf_scoped(pf, "acs", (64, 256), F32)
    acst = p.sbuf_scoped(pf, "acst", (128, 2 * 64), F32)
    lastb = p.sbuf_scoped(pf, "lastb", (128, 64), F32)
    cdb = p.sbuf_scoped(pf, "cdb", (128, 64), F32)
    dec = p.sbuf_scoped(pf, "dec", (128, 2 * 64), F32)
    cbm = p.sbuf_scoped(pf, "cbm", (128, 2 * 256), BF16)
    dx = [p.sbuf_scoped(pf, f"dx{i}", (128, 2 * 256), F32) for i in range(2)]
    exm = [p.sbuf_scoped(pf, f"exm{i}", (128, 2 * 256), BF16) for i in range(2)]
    p_sb = [p.sbuf_scoped(pf, f"p_sb{i}", (128, 2 * 256), BF16) for i in range(2)]
    exa = [p.sbuf_scoped(pf, f"exa{i}", (128, 256), BF16) for i in range(2)]
    cet = [p.sbuf_scoped(pf, f"cet{i}", (128, 256), BF16) for i in range(2)]
    bd = [p.sbuf_scoped(pf, f"bd{i}", (128, 2 * 128), BF16) for i in range(2)]
    yt_sb = p.sbuf_scoped(pf, "yt_sb", (128, 32 * 256), BF16)
    state_bf = p.sbuf_scoped(pf, "state_bf", (128, NHM * 64), BF16)
    lr2 = p.sbuf_scoped(pf, "lr2", (1, 64), F32)

    p.op("vector", "memset", state[:], 0.0, wait=[dt_done[0]], inc=(VS, 1))

    for c in range(16):
        t0 = c * 256
        p.op("sync", "dma_start", out=xs_sb[:],
             in_=convT[0:DIN, t0:t0 + 256].rearrange("(hc p) t -> p hc t", p=128),
             wait=[conv_ld, TSm.at(), VS.at()], inc=(LD, 16))
        p.op("sync", "dma_start", out=bct[:, 0:256], in_=convT[DIN:DIN + 128, t0:t0 + 256],
             inc=(LD, 16))
        p.op("sync", "dma_start", out=bct[:, 256:512],
             in_=convT[DIN + 128:DIN + 256, t0:t0 + 256], inc=(LD, 16))
        ld_c = LD.at()
        p.op("vector", "memset", dx[0][0:64, 0:256], 1.0, wait=[ld_c])
        p.op("vector", "drain")
        p.op("vector", "tensor_tensor_scan", acs[:], dx[0][0:64, 0:256], adt[:, t0:t0 + 256],
             0.0, ALU.mult, ALU.add, inc=(VS, 1))
        p.op("vector", "drain")
        for lc in range(2):
            p.op("tensor", "transpose", ps_trf[lc][:, 0:64],
                 acs[:, lc * 128:(lc + 1) * 128], C.ident_f[0:64, 0:64],
                 wait=[VS.at(), SS.at()], inc=(TSm, 1))
            p.op("scalar", "activation", acst[:, lc * 64:(lc + 1) * 64],
                 ps_trf[lc][:, 0:64], AF.Copy, scale=1.0, wait=[TSm.at()], inc=(SS, 1))
        p.op("tensor", "transpose", ps_trf[0][0:1, 0:64], acs[:, 255:256],
             C.ident_f[0:64, 0:64], wait=[SS.at()], inc=(TSm, 1))
        p.op("vector", "tensor_copy", lr2[:], ps_trf[0][0:1, 0:64], wait=[TSm.at()],
             inc=(VS, 1))
        p.op("vector", "drain")
        p.op("tensor", "matmul", ps_s2[:], ones_rf[:], lr2[:], start=True, stop=True,
             wait=[VS.at()], inc=(TSm, 1))
        p.op("scalar", "activation", lastb[:], ps_s2[:], AF.Copy, scale=1.0,
             wait=[TSm.at()], inc=(SS, 1))
        p.op("scalar", "activation", cdb[:], lastb[:], AF.Exp, inc=(SS, 1))
        for lc in range(2):
            p.op("vector", "tensor_sub", dx[1][:, 0:64], lastb[:],
                 acst[:, lc * 64:(lc + 1) * 64], wait=[SS.at()])
            p.op("vector", "drain")
            p.op("vector", "tensor_copy", dx[1][:, 64:128], dx[1][:, 0:64], inc=(VS, 1))
            p.op("scalar", "activation", dec[:, lc * 64:(lc + 1) * 64], dx[1][:, 0:64],
                 AF.Exp, wait=[VS.at()], inc=(SS, 1))
        for lc in range(2):
            slot = lc
            p.op("tensor", "transpose", ps_tr[slot][:],
                 bct[:, lc * 128:(lc + 1) * 128], C.ident[:],
                 wait=[SS.at(), VS.at()], inc=(TSm, 1))
            p.op("scalar", "activation", b_tm[:, lc * 128:(lc + 1) * 128],
                 ps_tr[slot][:], AF.Copy, scale=1.0, wait=[TSm.at()], inc=(SS, 1))
        for hc in range(32):
            for lc in range(2):
                slot = (hc * 2 + lc) % 2
                if hc * 2 + lc >= 2:
                    p.wait("tensor", SS, SS.n)
                p.op("tensor", "transpose", ps_tr[slot][:],
                     xs_sb[:, hc * 256 + lc * 128: hc * 256 + (lc + 1) * 128],
                     C.ident[:], inc=(TSm, 1))
                p.op("scalar", "activation",
                     x_tm[:, lc * 4096 + hc * 128: lc * 4096 + (hc + 1) * 128],
                     ps_tr[slot][:], AF.Copy, scale=1.0, wait=[TSm.at()], inc=(SS, 1))
        xtm_done = SS.at()
        for lc in range(2):
            for h in range(NHM):
                p.op("vector", "tensor_scalar",
                     xdt[:, lc * 4096 + h * 64: lc * 4096 + (h + 1) * 64],
                     x_tm[:, lc * 4096 + h * 64: lc * 4096 + (h + 1) * 64],
                     dt_tm[:, (c * 2 + lc) * 64 + h: (c * 2 + lc) * 64 + h + 1],
                     None, ALU.mult,
                     wait=[xtm_done] if (lc == 0 and h == 0) else [])
        p.op("vector", "tensor_copy", dx[0][0:1, 0:1], dx[0][0:1, 0:1], inc=(VS, 1))
        xdt_done = VS.at()
        for sc in range(2):
            p.op("tensor", "matmul", ps_row[:], bct[:, sc * 128:(sc + 1) * 128],
                 bct[:, 256:512], start=True, stop=True,
                 wait=[xdt_done] if sc == 0 else [VS.at()], inc=(TSm, 1))
            p.op("vector", "tensor_mul", cbm[:, sc * 256:(sc + 1) * 256], ps_row[:],
                 tri2[:, sc * 256:(sc + 1) * 256], wait=[TSm.at()], inc=(VS, 1))
        p.op("vector", "tensor_copy", state_bf[:], state[:], wait=[VS.at()], inc=(VS, 1))
        dxv = {}; exav = {}; pv = {}; cetv = {}; ytv = {}; rowv = {}; stv = {}
        yqv = {}; smv = {}
        for h in range(NHM):
            par = h % 2
            rb = ps_row[:, 0:256] if par == 0 else ps_row2
            w0 = []
            if h >= 2:
                w0 += [(VS, dxv[h - 2]), (SS, exav[h - 2])]
            else:
                w0 += [(VS, VS.at()[1]), (SS, SS.at()[1])]
            p.op("tensor", "matmul", rb, sel64[:, h * 128:(h + 1) * 128],
                 acs[:], start=True, stop=True, wait=w0, inc=(TSm, 1))
            rowv[h] = TSm.n
            for sc in range(2):
                p.op("vector", "tensor_scalar", dx[par][:, sc * 256:(sc + 1) * 256],
                     rb,
                     acst[:, sc * 64 + h: sc * 64 + h + 1], 0.0,
                     ALU.subtract, ALU.min,
                     wait=[(TSm, rowv[h])] if sc == 0 else [], inc=(VS, 1))
                p.op("scalar", "activation", exm[par][:, sc * 256:(sc + 1) * 256],
                     dx[par][:, sc * 256:(sc + 1) * 256], AF.Exp,
                     wait=[VS.at()], inc=(SS, 1))
                wp = [SS.at()]
                if sc == 0 and h >= 2:
                    wp.append((TSm, yqv[h - 2]))
                p.op("vector", "tensor_mul", p_sb[par][:, sc * 256:(sc + 1) * 256],
                     exm[par][:, sc * 256:(sc + 1) * 256],
                     cbm[:, sc * 256:(sc + 1) * 256], wait=wp, inc=(VS, 1))
            dxv[h] = VS.n
            pv[h] = VS.n
            p.op("scalar", "activation", exa[par][:], rb, AF.Exp,
                 wait=[VS.at()], inc=(SS, 1))
            exav[h] = SS.n
            wc = [SS.at()]
            if h >= 2:
                wc.append((TSm, yqv[h - 2]))
            p.op("vector", "tensor_mul", cet[par][:], bct[:, 256:512], exa[par][:],
                 wait=wc, inc=(VS, 1))
            cetv[h] = VS.n
            yb = ps_y if par == 0 else ps_y2
            wy = [(VS, pv[h])]
            if h >= 2:
                wy.append((VS, ytv[h - 2]))
            for sc in range(2):
                p.op("tensor", "matmul", yb[:],
                     xdt[:, sc * 4096 + h * 64: sc * 4096 + (h + 1) * 64],
                     p_sb[par][:, sc * 256:(sc + 1) * 256],
                     start=(sc == 0), stop=False, wait=wy if sc == 0 else [])
            p.op("tensor", "matmul", yb[:], state_bf[:, h * 64:(h + 1) * 64], cet[par][:],
                 start=False, stop=True, wait=[(VS, cetv[h])], inc=(TSm, 1))
            yq = TSm.n
            yqv[h] = yq
            hc, off = h // 2, (h % 2) * 64
            p.op("vector", "scalar_tensor_tensor",
                 yt_sb[off:off + 64, hc * 256:(hc + 1) * 256],
                 xs_sb[off:off + 64, hc * 256:(hc + 1) * 256],
                 db_t[off:off + 64, h:h + 1], yb[:], ALU.mult, ALU.add,
                 wait=[(TSm, yq)], inc=(VS, 1))
            ytv[h] = VS.n
            for sc in range(2):
                wb = []
                if sc == 0 and h >= 2:
                    wb.append((TSm, smv[h - 2]))
                p.op("vector", "tensor_scalar", bd[par][:, sc * 128:(sc + 1) * 128],
                     b_tm[:, sc * 128:(sc + 1) * 128],
                     dec[:, sc * 64 + h: sc * 64 + h + 1], None, ALU.mult, wait=wb,
                     inc=(VS, 1))
            bdv = VS.n
            ws = [(VS, bdv)]
            if h >= 1:
                ws.append((VS, stv[h - 1]))
            for sc in range(2):
                p.op("tensor", "matmul", ps_s2[:], bd[par][:, sc * 128:(sc + 1) * 128],
                     xdt[:, sc * 4096 + h * 64: sc * 4096 + (h + 1) * 64],
                     start=(sc == 0), stop=(sc == 1),
                     wait=ws if sc == 0 else [], inc=(TSm, 1) if sc == 1 else None)
                if sc == 1:
                    smv[h] = TSm.n
            p.op("vector", "scalar_tensor_tensor", state[:, h * 64:(h + 1) * 64],
                 state[:, h * 64:(h + 1) * 64], cdb[:, h:h + 1], ps_s2[:],
                 ALU.mult, ALU.add, wait=[TSm.at()], inc=(VS, 1))
            stv[h] = VS.n
        p.op("sync", "dma_start",
             out=y_hbm[c].rearrange("(hc p) t -> p hc t", p=128),
             in_=yt_sb[:].rearrange("p (hc t) -> p hc t", hc=32),
             wait=[VS.at()], inc=(LD, 16))
    pf.close()
    pef.close()
    y_ld = LD.at()

    pg = ExitStack()
    y_sb = p.sbuf_scoped(pg, "y_sb", (128, 32 * 512), BF16)
    z_sb = p.sbuf_scoped(pg, "z_sb", (128, 32 * 512), BF16)
    sqb = [p.sbuf_scoped(pg, f"sqb{i}", (128, 512), BF16) for i in range(2)]
    ssq_f = p.sbuf_scoped(pg, "ssq_f", (1, 512), F32)
    rst = p.sbuf_scoped(pg, "rst", (1, 512), BF16)
    rb = p.sbuf_scoped(pg, "rb", (128, 512), BF16)
    nw = p.sbuf_scoped(pg, "nw32", (128, 32), F32)
    wo = [p.sbuf_scoped(pg, f"wo{i}", (128, 32 * 128), BF16) for i in range(2)]
    y2 = p.sbuf_scoped(pg, "y2", (128, 16 * 512), BF16)
    xres = p.sbuf_scoped(pg, "xres", (128, D), F32)
    ps_dn2 = p.psum_scoped(pg, "ps_dn2", (1, 512), F32)
    ps_r2 = p.psum_scoped(pg, "ps_r2", (128, 512), F32)
    ps_mm2g = p.psum_scoped(pg, "ps_mm2g", (128, 512), F32)

    p.op("sync", "dma_start", out=nw[:], in_=nw_in, inc=(LD, 16))
    for g in range(8):
        for half in range(2):
            cc = 2 * g + half
            p.op("sync", "dma_start",
                 out=y_sb[:, 0:32 * 512].rearrange("p (hc t) -> p hc t", hc=32)[:, :, half * 256:(half + 1) * 256],
                 in_=y_hbm[cc].rearrange("(hc p) t -> p hc t", p=128),
                 wait=[y_ld, TSm.at()] if half == 0 else [], inc=(LD, 16))
        for i in range(4):
            p.op("sync", "dma_start",
                 out=z_sb[:].rearrange("p (hc t) -> p hc t", hc=32)[:, i * 8:(i + 1) * 8, :],
                 in_=zxb[i * 1024:(i + 1) * 1024, g * 512:(g + 1) * 512]
                     .rearrange("(hc p) t -> p hc t", p=128),
                 inc=(LD, 16))
        gld = LD.at()
        p.wait("scalar", LD, LD.n)
        p.op("scalar", "activation", z_sb[:], z_sb[:], AF.Silu, inc=(SS, 1))
        p.op("vector", "tensor_mul", y_sb[:], y_sb[:], z_sb[:], wait=[SS.at()],
             inc=(VS, 1))
        y32_done = VS.at()
        sq_mm = {}
        for hc in range(32):
            wq = [y32_done] if hc == 0 else []
            if hc >= 2:
                wq.append((TSm, sq_mm[hc - 2]))
            p.op("scalar", "activation", sqb[hc % 2][:], y_sb[:, hc * 512:(hc + 1) * 512],
                 AF.Square, wait=wq, inc=(SS, 1))
            p.op("tensor", "matmul", ps_dn2[:], C.ones_col[:], sqb[hc % 2][:],
                 start=(hc == 0), stop=(hc == 31), wait=[SS.at()],
                 inc=(TSm, 1))
            sq_mm[hc] = TSm.n
        p.op("scalar", "activation", ssq_f[:], ps_dn2[:], AF.Ln, scale=1.0 / DIN,
             bias=C.cbank_ap(CB_EPS_LN, 1), wait=[TSm.at()])
        p.op("scalar", "activation", rst[:], ssq_f[:], AF.Exp, scale=-0.5, inc=(SS, 1))
        p.op("tensor", "matmul", ps_r2[:], C.ones_row[:], rst[:], start=True, stop=True,
             wait=[SS.at()], inc=(TSm, 1))
        p.op("scalar", "activation", rb[:], ps_r2[:], AF.Copy, scale=1.0,
             wait=[TSm.at()], inc=(SS, 1))
        rb_done = SS.at()
        for hc in range(32):
            p.op("vector", "scalar_tensor_tensor", y_sb[:, hc * 512:(hc + 1) * 512],
                 y_sb[:, hc * 512:(hc + 1) * 512], nw[:, hc:hc + 1], rb[:],
                 ALU.mult, ALU.mult, wait=[rb_done] if hc == 0 else [], inc=(VS, 1))
        nrm_done = VS.at()
        op_mm = {}
        op_ev = {}
        for oc in range(16):
            lb = oc % 2
            wl = [wT_done_ld]
            if oc >= 2:
                wl.append((TSm, op_mm[oc - 2]))
            else:
                wl += [(TSm, TSm.n)] + ([(VS, VS.n), (SS, SS.n)] if g == 0 else [])
            p.op("gpsimd", "dma_start", out=wo[lb][:],
                 in_=w_outT[oc].rearrange("(dc p) o -> p dc o", p=128),
                 wait=wl, inc=(LG, 16))
            wld = LG.at()
            pb = ps_mm if oc % 2 == 0 else ps_mm2g
            w0 = [nrm_done, wld]
            if oc >= 2:
                w0.append((SS, op_ev[oc - 2]))
            else:
                w0.append((SS, SS.n))
            for dc in range(32):
                p.op("tensor", "matmul", pb[:, 0:512],
                     wo[lb][:, dc * 128:(dc + 1) * 128],
                     y_sb[:, dc * 512:(dc + 1) * 512],
                     start=(dc == 0), stop=(dc == 31),
                     wait=w0 if dc == 0 else [],
                     inc=(TSm, 1) if dc == 31 else None)
            op_mm[oc] = TSm.n
            p.op("scalar", "activation", y2[:, oc * 512:(oc + 1) * 512],
                 pb[:, 0:512], AF.Copy, scale=1.0, wait=[TSm.at()], inc=(SS, 1))
            op_ev[oc] = SS.n
        y2_done = SS.at()
        for tt in range(4):
            trow = g * 512 + tt * 128
            p.op("sync", "dma_start", out=xres[:], in_=x_in[trow:trow + 128, :],
                 wait=[ST.at()], inc=(LD, 16))
            p.wait("vector", LD, LD.n)
            for oc in range(16):
                slot = oc % 2
                if oc >= 2:
                    p.wait("tensor", VS, VS.n)
                p.op("tensor", "transpose", ps_tr[slot][:],
                     y2[:, oc * 512 + tt * 128: oc * 512 + tt * 128 + 128],
                     C.ident[:],
                     wait=[y2_done, VS.at()] if oc < 2 else [], inc=(TSm, 1))
                p.op("vector", "tensor_add", xres[:, oc * 128:(oc + 1) * 128],
                     xres[:, oc * 128:(oc + 1) * 128], ps_tr[slot][:],
                     wait=[TSm.at()], inc=(VS, 1))
            p.op("sync", "dma_start", out=x_out[trow:trow + 128, :], in_=xres[:],
                 wait=[VS.at()], inc=(ST, 16))
    pg.close()
    p.wait("sync", ST, ST.n)
    p.pop_scope()
    return p


def run_l1attn(x, in_proj_w, conv_w, conv_b, dt_bias, A_log, Dp, norm_w, out_proj_w, debug=False):
    c = _consts()
    p = build_l1attn(debug=debug)
    nc = p.finalize()
    cw2d = np.zeros((128, 34 * 4), np.float32)
    cb2d = np.zeros((128, 34), np.float32)
    for cc in range(34):
        cw2d[:, 4 * cc:4 * (cc + 1)] = conv_w[cc * 128:(cc + 1) * 128, :]
        cb2d[:, cc] = conv_b[cc * 128:(cc + 1) * 128]
    sel64 = np.zeros((64, 64 * 128), np.float32)
    for h in range(64):
        sel64[h, h * 128:(h + 1) * 128] = 1.0
    sp = np.arange(128)[:, None]
    lv = np.arange(256)[None, :]
    tri2 = np.concatenate([(sp <= lv).astype(np.float32),
                           (128 + sp <= lv).astype(np.float32)], axis=1)
    in_maps = []
    for b in range(NCORES):
        in_maps.append({
            "x_in": np.ascontiguousarray(x[b]),
            "in_proj_w": in_proj_w, "out_proj_w": out_proj_w,
            "conv_w2d": cw2d, "conv_b2d": cb2d,
            "dt_bias2d": dt_bias.reshape(64, 1), "a_log2d": A_log.reshape(64, 1),
            "d2d": Dp.reshape(64, 1),
            "norm_w2d": np.ascontiguousarray(norm_w.reshape(32, 128).T),
            "sel64": sel64, "tri2": tri2,
            "c_ident": c["ident"], "c_cbank": c["cbank"],
        })
    res = run_bass_kernel_spmd(nc, in_maps, core_ids=list(range(NCORES)))
    out = np.stack([res.results[b]["x_out"] for b in range(NCORES)])
    if debug:
        return out, res.results
    return out



_CONSTS = None

def _consts():
    global _CONSTS
    if _CONSTS is None:
        _CONSTS = host_consts()
    return _CONSTS


def deinterleave_wattn(w_attn):
    w = np.array(w_attn, np.float32, copy=True)
    perm = np.concatenate([np.arange(0, 128, 2), np.arange(1, 128, 2)])
    for base in range(0, 2048 + 512, 128):
        w[:, base:base + 128] = w[:, base + perm]
    return w


def run_l0attn(x, w_attn, w_proj, norm_w, debug=False):
    c = _consts()
    w_attn = deinterleave_wattn(w_attn)
    p = build_l0attn(debug=debug)
    nc = p.finalize()
    in_maps = []
    nw2d = np.ascontiguousarray(norm_w.reshape(16, 128).T)
    for b in range(NCORES):
        in_maps.append({
            "x_in": np.ascontiguousarray(x[b]),
            "w_attn": w_attn, "w_proj": w_proj, "norm_w": nw2d,
            "rope_c": c["rope_c"], "rope_s": c["rope_s"],
            "masks_diag": c["masks_diag"],
            "c_ident": c["ident"], "c_cbank": c["cbank"],
        })
    res = run_bass_kernel_spmd(nc, in_maps, core_ids=list(range(NCORES)))
    out = np.stack([res.results[b]["x_out"] for b in range(NCORES)])
    if debug:
        return out, res.results
    return out


def build_fused():
    p = P("mk")
    C = Common(p)
    x_in = p.dram("x_full_in", (T, D), F32, kind="ExternalInput").ap()
    x_out = p.dram("x_full_out", (T, D), F32, kind="ExternalOutput").ap()
    xA = p.dram("x_chain_a", (T, D), F32, kind="Internal").ap()
    xB = p.dram("x_chain_b", (T, D), F32, kind="Internal").ap()
    build_l0attn(fused=(p, C, x_in, xA))
    p.barrier(C)
    build_l0ffn(fused=(p, C, xA, xB))
    p.barrier(C)
    build_l1attn(fused=(p, C, xB, xA))
    p.barrier(C)
    build_l1ffn(fused=(p, C, xA, x_out))
    return p


_FUSED_NC = None

def _fused_nc():
    global _FUSED_NC
    if _FUSED_NC is None:
        p = build_fused()
        _FUSED_NC = p.finalize()
    return _FUSED_NC


def kernel(**inputs):
    f32 = lambda k: np.asarray(inputs[k], np.float32)
    c = _consts()
    nc = _fused_nc()
    w_attn = deinterleave_wattn(f32("L0_attn_w_attn"))
    gup_a = np.concatenate([np.asarray(inputs["L0_ffn_gate_up_packed"]),
                            np.asarray(inputs["L0_ffn_shared_gate_up_packed"])[None]], 0)
    gus_a = np.concatenate([f32("L0_ffn_gate_up_scales"),
                            f32("L0_ffn_shared_gate_up_scales")[None]], 0)
    dnp_a = np.concatenate([np.asarray(inputs["L0_ffn_down_packed"]),
                            np.asarray(inputs["L0_ffn_shared_down_packed"])[None]], 0)
    dns_a = np.concatenate([f32("L0_ffn_down_scales"),
                            f32("L0_ffn_shared_down_scales")[None]], 0)
    pp = np.arange(128)
    sel4 = np.stack([(((pp % 64) >> 4) == k) for k in range(4)]).astype(np.float32)
    sel16 = np.zeros((16, 16 * 128), np.float32)
    for e in range(16):
        sel16[e, e * 128:(e + 1) * 128] = 1.0
    sel64 = np.zeros((64, 64 * 128), np.float32)
    for h in range(64):
        sel64[h, h * 128:(h + 1) * 128] = 1.0
    sp = np.arange(128)[:, None]
    lv = np.arange(256)[None, :]
    tri2 = np.concatenate([(sp <= lv).astype(np.float32),
                           (128 + sp <= lv).astype(np.float32)], axis=1)
    conv_w = f32("L1_attn_conv_w")
    conv_b = f32("L1_attn_conv_b")
    cw2d = np.zeros((128, 34 * 4), np.float32)
    cb2d = np.zeros((128, 34), np.float32)
    for cc in range(34):
        cw2d[:, 4 * cc:4 * (cc + 1)] = conv_w[cc * 128:(cc + 1) * 128, :]
        cb2d[:, cc] = conv_b[cc * 128:(cc + 1) * 128]
    x = f32("L0_attn_x")
    base = {
        "w_attn": w_attn, "w_proj": f32("L0_attn_w_proj"),
        "norm_w": np.ascontiguousarray(f32("L0_attn_norm_w").reshape(16, 128).T),
        "rope_c": c["rope_c"], "rope_s": c["rope_s"], "masks_diag": c["masks_diag"],
        "c_ident": c["ident"], "c_cbank": c["cbank"],
        "gate_w": f32("L0_ffn_gate_w"),
        "gup": gup_a, "gus": gus_a, "dnp": dnp_a, "dns": dns_a,
        "sel4": sel4, "sel16": sel16, "shbank": _shbank(),
        "in_proj_w": f32("L1_attn_in_proj_w"), "out_proj_w": f32("L1_attn_out_proj_w"),
        "conv_w2d": cw2d, "conv_b2d": cb2d,
        "dt_bias2d": f32("L1_attn_dt_bias").reshape(64, 1),
        "a_log2d": f32("L1_attn_A_log").reshape(64, 1),
        "d2d": f32("L1_attn_D").reshape(64, 1),
        "norm_w2d": np.ascontiguousarray(f32("L1_attn_norm_w").reshape(32, 128).T),
        "sel64": sel64, "tri2": tri2,
        "weight": f32("L1_ffn_weight"),
        "bias2d": np.ascontiguousarray(f32("L1_ffn_bias").reshape(16, 128).T),
    }
    in_maps = []
    for b in range(NCORES):
        m = dict(base)
        m["x_full_in"] = np.ascontiguousarray(x[b])
        in_maps.append(m)
    res = run_bass_kernel_spmd(nc, in_maps, core_ids=list(range(NCORES)))
    return np.stack([res.results[b]["x_full_out"] for b in range(NCORES)]).astype(np.float32)


# version 17
# speedup vs baseline: 1.1513x; 1.0186x over previous
import numpy as np
import ml_dtypes
from contextlib import ExitStack
import concourse.bass as bass
import concourse.mybir as mybir
from concourse.bass_utils import run_bass_kernel_spmd

F32 = mybir.dt.float32
BF16 = mybir.dt.bfloat16
I32 = mybir.dt.int32
U8 = mybir.dt.uint8
AF = mybir.ActivationFunctionType
ALU = mybir.AluOpType
AX = mybir.AxisListType

T = 4096
D = 2048
NCORES = 8

ENGINES = ("sync", "gpsimd", "vector", "scalar", "tensor")


class Sem:
    __slots__ = ("h", "n", "name")

    def __init__(self, h, name):
        self.h = h
        self.n = 0
        self.name = name

    def at(self):
        return (self, self.n)


class P:

    def __init__(self, name):
        self.nc = bass.Bass("TRN2", target_bir_lowering=False, debug=False)
        self.stack = ExitStack()
        self.ops = {e: [] for e in ENGINES}
        self.n_inst = 0
        self.scope = None
        self.prefix = ""

    def push_scope(self, prefix):
        self.scope = ExitStack()
        self.prefix = prefix

    def pop_scope(self):
        self.scope.close()
        self.scope = None
        self.prefix = ""

    def barrier(self, C):
        sems = [C.LD, C.LG, C.ST, C.TS, C.VS, C.SS]
        for eng in ENGINES:
            for s in sems:
                if s.n:
                    self.wait(eng, s, s.n)

    def sem(self, name):
        h = self.stack.enter_context(self.nc.semaphore(name))
        return Sem(h, name)

    def sbuf(self, name, shape, dtype):
        name = self.prefix + name
        if self.scope is not None:
            return self.scope.enter_context(self.nc.sbuf_tensor(name, list(shape), dtype))
        return self.nc.alloc_sbuf_tensor(name, list(shape), dtype)

    def sbuf_scoped(self, st, name, shape, dtype):
        return st.enter_context(self.nc.sbuf_tensor(self.prefix + name, list(shape), dtype))

    def psum(self, name, shape, dtype=F32):
        name = self.prefix + name
        if self.scope is not None:
            return self.scope.enter_context(self.nc.psum_tensor(name, list(shape), dtype))
        return self.nc.alloc_psum_tensor(name, list(shape), dtype)

    def psum_scoped(self, st, name, shape, dtype=F32):
        return st.enter_context(self.nc.psum_tensor(self.prefix + name, list(shape), dtype))

    def dram(self, name, shape, dtype, kind="Internal"):
        if kind == "Internal":
            name = self.prefix + name
        return self.nc.dram_tensor(name, list(shape), dtype, kind=kind)

    def op(self, eng, meth, *args, wait=(), inc=None, lowp=False, **kw):
        self.n_inst += 1
        if inc is not None:
            inc[0].n += inc[1]
        wait = tuple((s, v) for (s, v) in wait)

        if meth in ("transpose", "reciprocal"):
            lowp = True

        def f(e, meth=meth, args=args, kw=kw, wait=wait, inc=inc, lowp=lowp):
            for s, v in wait:
                e.wait_ge(s.h, v)
            try:
                if lowp:
                    with self.nc.allow_low_precision(reason="intentional bf16 path"):
                        ins = getattr(e, meth)(*args, **kw)
                else:
                    ins = getattr(e, meth)(*args, **kw)
            except Exception as ex:
                raise RuntimeError(f"op {meth} failed: {ex}") from ex
            if inc is not None:
                ins.then_inc(inc[0].h, inc[1])

        self.ops[eng].append(f)

    def wait(self, eng, sem, val):
        self.n_inst += 1

        def f(e, sem=sem, val=val):
            e.wait_ge(sem.h, val)

        self.ops[eng].append(f)

    def raw(self, eng, fn):
        self.ops[eng].append(fn)

    def finalize(self):
        with self.stack:
            with self.nc.Block() as block:
                for eng in ENGINES:
                    fns = self.ops[eng]
                    if not fns:
                        continue
                    getattr(block, eng)(lambda e, fns=fns: [f(e) for f in fns])
        return self.nc



def host_consts():
    c = {}
    c["ident"] = np.eye(128, dtype=np.float32)
    hd = 128
    inv = 1.0 / (10000.0 ** (np.arange(0, hd, 2, dtype=np.float64) / hd))
    ang = np.arange(T, dtype=np.float64)[:, None] * inv[None, :]
    c["rope_c"] = np.cos(ang).T.astype(np.float32).copy()
    c["rope_s"] = np.sin(ang).T.astype(np.float32).copy()
    kp = np.arange(128)[:, None]
    qf = np.arange(512)[None, :]
    m = np.concatenate([(qf >= kp + 128 * r).astype(np.float32) for r in range(4)], axis=1)
    c["masks_diag"] = m
    cvals = [-2.0, -6.0, 1e-5, 1e-6, 0.0, 1.0, 0.0, 0.0]
    c["cbank"] = np.tile(np.array(cvals, np.float32)[None, :], (128, 1))
    return c

CB_NEG2, CB_NEG6, CB_EPS_LN, CB_EPS_RMS, CB_ZERO, CB_ONE = 0, 1, 2, 3, 4, 5



class Common:

    def __init__(self, p: P):
        self.p = p
        nc = p.nc
        self.d_ident = p.dram("c_ident", (128, 128), F32, kind="ExternalInput").ap()
        self.d_cbank = p.dram("c_cbank", (128, 8), F32, kind="ExternalInput").ap()
        self.ident = p.sbuf("ident_bf", (128, 128), BF16)
        self.ident_f = p.sbuf("ident_f", (128, 128), F32)
        self.cb = p.sbuf("cbank", (128, 8), F32)
        self.ones_col = p.sbuf("ones_col", (128, 1), BF16)
        self.ones_row = p.sbuf("ones_row", (1, 128), BF16)
        self.LD = p.sem("LD")
        self.LG = p.sem("LG")
        self.ST = p.sem("ST")
        self.TS = p.sem("TSm")
        self.VS = p.sem("VSm")
        self.SS = p.sem("SSm")

    def load_consts(self):
        if getattr(self, "_consts_loaded", False):
            return
        self._consts_loaded = True
        p = self.p
        p.op("gpsimd", "dma_start", out=self.ident[:], in_=self.d_ident, inc=(self.LG, 16))
        p.op("sync", "dma_start", out=self.ident_f[:], in_=self.d_ident, inc=(self.LD, 16))
        p.op("sync", "dma_start", out=self.cb[:], in_=self.d_cbank, inc=(self.LD, 16))
        p.op("vector", "memset", self.ones_col[:], 1.0)
        p.op("vector", "memset", self.ones_row[:], 1.0, inc=(self.VS, 1))

    def cbank_ap(self, idx, nparts=1):
        return self.cb[0:nparts, idx:idx + 1]

    def ln_tile(self, xr, xsq, xln_out, stats, eps_idx=CB_EPS_LN):
        p, cb = self.p, self.cb
        VS, SS = self.VS, self.SS
        for i in range(4):
            p.op("vector", "bn_stats", xsq[:, 8 + i * 6: 8 + (i + 1) * 6],
                 xr[:, i * 512:(i + 1) * 512])
        p.op("vector", "drain")
        p.op("vector", "bn_aggr", stats[:, 2:4], xsq[:, 8:32], inc=(VS, 1))
        p.op("vector", "drain")
        p.op("scalar", "activation", stats[:, 5:6], stats[:, 3:4], AF.Ln,
             bias=self.cbank_ap(eps_idx, 128), scale=1.0, wait=[VS.at()])
        p.op("scalar", "activation", stats[:, 6:7], stats[:, 5:6], AF.Exp,
             scale=-0.5, inc=(SS, 1))
        p.op("vector", "tensor_scalar", xln_out[:], xr[:], stats[:, 2:3], stats[:, 6:7],
             ALU.subtract, ALU.mult, wait=[SS.at()], inc=(VS, 1))



def build_l0attn(debug=False, fused=None):
    if fused is None:
        p = P("l0attn")
        C = Common(p)
        x_in_ap = x_out_ap = None
    else:
        p, C, x_in_ap, x_out_ap = fused
    p.push_scope("a0_")
    x_in = x_in_ap if x_in_ap is not None else p.dram("x_in", (T, D), F32, kind="ExternalInput").ap()
    x_out = x_out_ap if x_out_ap is not None else p.dram("x_out", (T, D), F32, kind="ExternalOutput").ap()
    w_attn = p.dram("w_attn", (D, 3072), F32, kind="ExternalInput").ap()
    w_proj = p.dram("w_proj", (D, D), F32, kind="ExternalInput").ap()
    nw_in = p.dram("norm_w", (128, 16), F32, kind="ExternalInput").ap()
    rope_c_in = p.dram("rope_c", (64, T), F32, kind="ExternalInput").ap()
    rope_s_in = p.dram("rope_s", (64, T), F32, kind="ExternalInput").ap()
    masks_in = p.dram("masks_diag", (128, 2048), F32, kind="ExternalInput").ap()
    dbg_kind = "ExternalOutput" if debug else "Internal"
    if debug:
        xr_dump = p.dram("xr_dump", (4, 128, D), F32, kind="ExternalOutput").ap()
        st_dump = p.dram("st_dump", (4, 128, 8), F32, kind="ExternalOutput").ap()
    qT_hbm = p.dram("qT_scratch", (8, D, 512), BF16, kind=dbg_kind).ap()
    xlnT_hbm = p.dram("xlnT_scratch", (8, D, 512), BF16, kind=dbg_kind).ap()

    stage = p.sbuf("stage", (128, 16 * 512), BF16)
    wbuf = p.sbuf("wbuf", (128, 16 * 1024), BF16)
    kT = p.sbuf("kT", (128, 4 * T), BF16)
    v_tm = p.sbuf("v_tm", (128, 4 * T), BF16)
    masks = p.sbuf("masks_sb", (128, 2048), BF16)
    nw = p.sbuf("nw_sb", (128, 16), F32)

    ps_s = [p.psum(f"ps_s{i}", (128, 512), F32) for i in range(2)]
    ps_av = p.psum("ps_av", (128, 512), F32)
    ps_dn = p.psum("ps_dn", (1, 512), F32)
    ps_r = p.psum("ps_r", (128, 512), F32)
    ps_proj = p.psum("ps_proj", (128, 512), F32)
    ps_tr = [p.psum(f"ps_tr{i}", (128, 128), BF16) for i in range(2)]

    LD, LG, ST, TSm, VS, SS = C.LD, C.LG, C.ST, C.TS, C.VS, C.SS
    C.load_consts()
    p.op("sync", "dma_start", out=nw[:], in_=nw_in, inc=(LD, 16))
    p.op("gpsimd", "dma_start", out=masks[:], in_=masks_in, inc=(LG, 16))
    consts_ld = [LD.at(), LG.at(), VS.at()]

    def qk_cols(base):
        return wbuf[:, base:base + 128]

    ab = ExitStack()
    xr = [p.sbuf_scoped(ab, f"xr{i}", (128, D), F32) for i in range(2)]
    xsq = p.sbuf_scoped(ab, "xsq", (128, D), F32)
    stats = p.sbuf_scoped(ab, "stats", (128, 8), F32)
    xln = [p.sbuf_scoped(ab, f"xln{i}", (128, D), BF16) for i in range(2)]
    qT = p.sbuf_scoped(ab, "qT", (128, 8 * 512), BF16)
    rope_c = p.sbuf_scoped(ab, "rope_c_sb", (64, T), BF16)
    rope_s = p.sbuf_scoped(ab, "rope_s_sb", (64, T), BF16)
    rtmp = [p.sbuf_scoped(ab, f"rtmp{i}", (64, 4 * 512), F32) for i in range(2)]
    vt_stage = p.sbuf_scoped(ab, "vt_stage", (128, 512), BF16)

    p.op("gpsimd", "dma_start", out=rope_c[:], in_=rope_c_in, inc=(LG, 16))
    p.op("gpsimd", "dma_start", out=rope_s[:], in_=rope_s_in, inc=(LG, 16))
    rope_ld = LG.at()

    for h in range(16):
        p.op("gpsimd", "dma_start", out=wbuf[:, h * 1024:(h + 1) * 1024],
             in_=w_attn[h * 128:(h + 1) * 128, 2048:3072], inc=(LG, 16))
    wkv_ld = LG.at()

    xln_stored = []
    pa_vguard = [0]
    for g in range(8):
        for tt in range(4):
            trow = g * 512 + tt * 128
            buf = tt % 2
            p.op("sync", "dma_start", out=xr[buf][:], in_=x_in[trow:trow + 128, :],
                 wait=[VS.at()] if (g > 0 or tt > 0) else consts_ld, inc=(LD, 16))
            p.wait("vector", LD, LD.n)
            if debug and g == 0:
                p.op("sync", "dma_start", out=xr_dump[tt], in_=xr[buf][:],
                     wait=[LD.at()], inc=(ST, 16))
            C.ln_tile(xr[buf], xsq, xln[buf], stats)
            if debug and g == 0:
                p.op("sync", "dma_start", out=st_dump[tt], in_=stats[:],
                     wait=[VS.at()], inc=(ST, 16))
            ln_v = VS.at()
            for c in range(16):
                slot = c % 2
                if c >= 2:
                    p.wait("tensor", SS, SS.n)
                p.op("tensor", "transpose", ps_tr[slot][:],
                     xln[buf][:, c * 128:(c + 1) * 128], C.ident[:],
                     wait=[ln_v] if c == 0 else [], inc=(TSm, 1))
                p.op("scalar", "activation",
                     stage[:, c * 512 + tt * 128: c * 512 + tt * 128 + 128],
                     ps_tr[slot][:], AF.Copy, scale=1.0,
                     wait=[TSm.at()], inc=(SS, 1))
            p.wait("vector", SS, SS.n)
            p.op("vector", "memset", stats[:, 7:8], 0.0, inc=(VS, 1))
        stage_done = SS.at()

        p.op("sync", "dma_start", out=xlnT_hbm[g].rearrange("(h p) t -> p h t", p=128),
             in_=stage[:].rearrange("p (h t) -> p h t", h=16),
             wait=[stage_done], inc=(LD, 16))
        xln_stored.append(LD.at())

        rope_v = {}
        for kvh in range(4):
            sl = kvh % 2
            if kvh >= 2:
                p.wait("tensor", VS, rope_v[kvh - 2])
            else:
                p.wait("tensor", VS, pa_vguard[0])
            for h in range(16):
                p.op("tensor", "matmul", ps_s[sl][:], qk_cols(h * 1024 + kvh * 128),
                     stage[:, h * 512:(h + 1) * 512],
                     start=(h == 0), stop=(h == 15),
                     wait=[stage_done, wkv_ld, rope_ld] if h == 0 else [],
                     inc=(TSm, 1) if h == 15 else None)
            dst = kT[:, kvh * T + g * 512: kvh * T + g * 512 + 512]
            _rope(p, C, ps_s[sl], rope_c[:, g * 512:g * 512 + 512],
                  rope_s[:, g * 512:g * 512 + 512], rtmp[kvh % 2], dst, TSm.at())
            rope_v[kvh] = VS.n
        for kvh in range(4):
            sl = kvh % 2
            p.wait("tensor", SS, SS.n)
            p.wait("tensor", VS, VS.n)
            for h in range(16):
                p.op("tensor", "matmul", ps_s[sl][:],
                     wbuf[:, h * 1024 + 512 + kvh * 128: h * 1024 + 512 + kvh * 128 + 128],
                     stage[:, h * 512:(h + 1) * 512],
                     start=(h == 0), stop=(h == 15), inc=(TSm, 1) if h == 15 else None)
            p.op("vector", "tensor_copy", vt_stage[:], ps_s[sl][:], wait=[TSm.at()], inc=(VS, 1))
            for ts in range(4):
                slot = ts % 2
                if ts >= 2:
                    p.wait("tensor", SS, SS.n)
                p.op("tensor", "transpose", ps_tr[slot][:],
                     vt_stage[:, ts * 128:ts * 128 + 128], C.ident[:],
                     wait=[VS.at()] if ts == 0 else [], inc=(TSm, 1))
                p.op("scalar", "activation",
                     v_tm[:, kvh * T + (g * 4 + ts) * 128: kvh * T + (g * 4 + ts) * 128 + 128],
                     ps_tr[slot][:], AF.Copy, scale=1.0,
                     wait=[TSm.at()], inc=(SS, 1))
            if kvh == 3:
                pa_vguard[0] = VS.n

    pb_vguard = [pa_vguard[0]]
    for half in range(2):
        for h in range(16):
            p.op("gpsimd", "dma_start", out=wbuf[:, h * 1024:(h + 1) * 1024],
                 in_=w_attn[h * 128:(h + 1) * 128, half * 1024:(half + 1) * 1024],
                 wait=[TSm.at()], inc=(LG, 16))
        wq_ld = LG.at()
        for g in range(8):
            p.op("sync", "dma_start", out=stage[:].rearrange("p (h t) -> p h t", h=16),
                 in_=xlnT_hbm[g].rearrange("(h p) t -> p h t", p=128),
                 wait=[xln_stored[g], TSm.at()], inc=(LD, 16))
            st_ld = LD.at()
            ropeq_v = {}
            for ql in range(8):
                qh = half * 8 + ql
                sl = ql % 2
                if ql >= 2:
                    p.wait("tensor", VS, ropeq_v[ql - 2])
                else:
                    p.wait("tensor", VS, pb_vguard[0])
                for h in range(16):
                    p.op("tensor", "matmul", ps_s[sl][:], qk_cols(h * 1024 + ql * 128),
                         stage[:, h * 512:(h + 1) * 512],
                         start=(h == 0), stop=(h == 15),
                         wait=[st_ld, wq_ld] if h == 0 else [],
                         inc=(TSm, 1) if h == 15 else None)
                dst = qT[:, ql * 512:(ql + 1) * 512]
                _rope(p, C, ps_s[sl], rope_c[:, g * 512:g * 512 + 512],
                      rope_s[:, g * 512:g * 512 + 512], rtmp[ql % 2], dst, TSm.at())
                ropeq_v[ql] = VS.n
            pb_vguard[0] = VS.n
            p.op("sync", "dma_start",
                 out=qT_hbm[g][half * 1024:(half + 1) * 1024, :].rearrange("(h p) t -> p h t", p=128),
                 in_=qT[:].rearrange("p (h t) -> p h t", h=8),
                 wait=[VS.at()], inc=(LD, 16))
    qT_stored = LD.at()

    ab.close()

    cs = ExitStack()
    qT_sb = stage
    pexp = [p.sbuf_scoped(cs, f"pexp{i}", (128, 512), BF16) for i in range(2)]
    rD = p.sbuf_scoped(cs, "rD", (1, 512), BF16)
    rB = p.sbuf_scoped(cs, "rB", (128, 512), BF16)
    yT = p.sbuf_scoped(cs, "yT", (128, 16 * 512), BF16)
    y2 = p.sbuf_scoped(cs, "y2", (128, 16 * 512), BF16)
    sqb = [p.sbuf_scoped(cs, f"sqb{i}", (128, 512), BF16) for i in range(2)]
    ssq_f = p.sbuf_scoped(cs, "ssq_f", (1, 512), F32)
    rstd_r = p.sbuf_scoped(cs, "rstd_r", (1, 512), BF16)
    nrm = p.sbuf_scoped(cs, "nrm", (128, 128), BF16)
    xres = p.sbuf_scoped(cs, "xres", (128, 2048), F32)

    for g in range(8):
        p.op("sync", "dma_start", out=qT_sb[:].rearrange("p (h t) -> p h t", h=16),
             in_=qT_hbm[g].rearrange("(h p) t -> p h t", p=128),
             wait=[qT_stored, TSm.at()], inc=(LD, 16))
        qld = LD.at()
        n_kt = 4 * g + 4
        for hh in range(16):
            kv = hh // 4
            qsl = qT_sb[:, hh * 512:(hh + 1) * 512]
            exp_v = {}
            prd_v = {}
            def emit_scores(kt):
                sl = kt % 2
                w0 = []
                if kt == 0 and hh == 0:
                    w0.append(qld)
                if kt >= 2:
                    w0.append((SS, exp_v[kt - 2]))
                else:
                    w0.append((SS, SS.n))
                p.op("tensor", "matmul", ps_s[sl][:],
                     kT[:, kv * T + kt * 128: kv * T + kt * 128 + 128], qsl,
                     start=True, stop=True, wait=w0, inc=(TSm, 1))
            emit_scores(0)
            for kt in range(n_kt):
                sl = kt % 2
                p.op("scalar", "activation", pexp[sl][:], ps_s[sl][:], AF.Exp,
                     scale=float(1.0 / np.sqrt(128.0)), wait=[TSm.at()], inc=(SS, 1))
                exp_v[kt] = SS.n
                if kt >= 4 * g:
                    r = kt - 4 * g
                    p.op("vector", "tensor_mul", pexp[sl][:], pexp[sl][:],
                         masks[:, r * 512:(r + 1) * 512], wait=[SS.at()], inc=(VS, 1))
                    prd_v[kt] = (VS, VS.n)
                else:
                    prd_v[kt] = (SS, SS.n)
                if kt + 1 < n_kt:
                    emit_scores(kt + 1)
                p.wait("tensor", prd_v[kt][0], prd_v[kt][1])
                p.op("tensor", "matmul", ps_dn[:], C.ones_col[:], pexp[sl][:],
                     start=(kt == 0), stop=(kt == n_kt - 1))
                p.op("tensor", "matmul", ps_av[:],
                     v_tm[:, kv * T + kt * 128: kv * T + kt * 128 + 128], pexp[sl][:],
                     start=(kt == 0), stop=(kt == n_kt - 1),
                     inc=(TSm, 1) if kt == n_kt - 1 else None)
            p.op("vector", "reciprocal", rD[:], ps_dn[:], wait=[TSm.at()], inc=(VS, 1))
            p.op("tensor", "matmul", ps_r[:], C.ones_row[:], rD[:], start=True, stop=True,
                 wait=[VS.at()], inc=(TSm, 1))
            p.op("scalar", "activation", rB[:], ps_r[:], AF.Copy, scale=1.0,
                 wait=[TSm.at()], inc=(SS, 1))
            p.op("vector", "tensor_mul", yT[:, hh * 512:(hh + 1) * 512], ps_av[:], rB[:],
                 wait=[SS.at()], inc=(VS, 1))
            p.wait("tensor", VS, VS.n)
        yT_done = VS.at()

        evac_p = {}
        for half in range(2):
            for h in range(16):
                p.op("gpsimd", "dma_start", out=wbuf[:, h * 1024:(h + 1) * 1024],
                     in_=w_proj[h * 128:(h + 1) * 128, half * 1024:(half + 1) * 1024],
                     wait=[yT_done, TSm.at()] if h == 0 else [], inc=(LG, 16))
            wp_ld = LG.at()
            for ol in range(8):
                oc = half * 8 + ol
                pb = ps_proj if oc % 2 == 0 else ps_av
                w0 = [yT_done, wp_ld] if ol == 0 else []
                if oc >= 2:
                    w0.append((SS, evac_p[oc - 2]))
                for h in range(16):
                    p.op("tensor", "matmul", pb[:],
                         wbuf[:, h * 1024 + ol * 128: h * 1024 + ol * 128 + 128],
                         yT[:, h * 512:(h + 1) * 512],
                         start=(h == 0), stop=(h == 15),
                         wait=w0 if h == 0 else [],
                         inc=(TSm, 1) if h == 15 else None)
                p.op("scalar", "activation", y2[:, oc * 512:(oc + 1) * 512], pb[:],
                     AF.Copy, scale=1.0, wait=[TSm.at()], inc=(SS, 1))
                p.op("scalar", "activation", sqb[oc % 2][:], pb[:], AF.Square, inc=(SS, 1))
                evac_p[oc] = SS.n
                p.op("tensor", "matmul", ps_dn[:], C.ones_col[:], sqb[oc % 2][:],
                     start=(oc == 0), stop=(oc == 15), wait=[SS.at()],
                     inc=(TSm, 1) if oc == 15 else None)
        p.op("scalar", "activation", ssq_f[:], ps_dn[:], AF.Ln, scale=1.0 / D,
             bias=C.cbank_ap(CB_EPS_RMS, 1), wait=[TSm.at()])
        p.op("scalar", "activation", rstd_r[:], ssq_f[:], AF.Exp, scale=-0.5, inc=(SS, 1))
        p.op("tensor", "matmul", ps_r[:], C.ones_row[:], rstd_r[:], start=True, stop=True,
             wait=[SS.at()], inc=(TSm, 1))
        p.op("scalar", "activation", rB[:], ps_r[:], AF.Copy, scale=1.0,
             wait=[TSm.at()], inc=(SS, 1))
        rb_done = SS.at()

        for tt in range(4):
            trow = g * 512 + tt * 128
            p.op("sync", "dma_start", out=xres[:], in_=x_in[trow:trow + 128, :],
                 wait=[ST.at()], inc=(LD, 16))
            p.wait("vector", LD, LD.n)
            for oc in range(16):
                slot = oc % 2
                p.op("vector", "scalar_tensor_tensor", nrm[:, 0:128],
                     y2[:, oc * 512 + tt * 128: oc * 512 + tt * 128 + 128],
                     nw[:, oc:oc + 1], rB[:, tt * 128:tt * 128 + 128],
                     ALU.mult, ALU.mult,
                     wait=[rb_done, TSm.at()] if (oc == 0 and tt == 0) else [TSm.at()],
                     inc=(VS, 1))
                if oc >= 2:
                    p.wait("tensor", VS, VS.n)
                p.op("tensor", "transpose", ps_tr[slot][:],
                     nrm[:, 0:128], C.ident[:], wait=[VS.at()], inc=(TSm, 1))
                p.op("vector", "tensor_add", xres[:, oc * 128:(oc + 1) * 128],
                     xres[:, oc * 128:(oc + 1) * 128],
                     ps_tr[slot][:],
                     wait=[TSm.at()], inc=(VS, 1))
            p.op("sync", "dma_start", out=x_out[trow:trow + 128, :], in_=xres[:],
                 wait=[VS.at()], inc=(ST, 16))
    cs.close()
    p.wait("sync", ST, ST.n)
    p.pop_scope()
    return p


def _rope(p, C, ps, ctile, stile, rtmp, dst, mm_done):
    VS = C.VS
    t0, t1 = rtmp[:, 0:512], rtmp[:, 512:1024]
    t2, t3 = rtmp[:, 1024:1536], rtmp[:, 1536:2048]
    p.op("vector", "tensor_mul", t0[0:64, :], ps[0:64, :], ctile[0:64, :], wait=[mm_done])
    p.op("vector", "tensor_mul", t1[0:64, :], ps[64:128, :], stile[0:64, :])
    p.op("vector", "tensor_sub", dst[0:64, :], t0[0:64, :], t1[0:64, :])
    p.op("vector", "tensor_mul", t2[0:64, :], ps[0:64, :], stile[0:64, :])
    p.op("vector", "tensor_mul", t3[0:64, :], ps[64:128, :], ctile[0:64, :])
    p.op("vector", "tensor_add", dst[64:128, :], t2[0:64, :], t3[0:64, :], inc=(VS, 1))



def emit_ln_stage(p, C, x_in, stage, xr, xsq, stats, xln, ps_tr, g, consts_ld=None):
    LD, VS, SS, TSm = C.LD, C.VS, C.SS, C.TS
    for tt in range(4):
        trow = g * 512 + tt * 128
        buf = tt % 2
        first = (g == 0 and tt == 0 and consts_ld is not None)
        p.op("sync", "dma_start", out=xr[buf][:], in_=x_in[trow:trow + 128, :],
             wait=consts_ld if first else [VS.at()], inc=(LD, 16))
        p.wait("vector", LD, LD.n)
        C.ln_tile(xr[buf], xsq, xln[buf], stats)
        ln_v = VS.at()
        for c in range(16):
            slot = c % 2
            if c >= 2:
                p.wait("tensor", SS, SS.n)
            p.op("tensor", "transpose", ps_tr[slot][:],
                 xln[buf][:, c * 128:(c + 1) * 128], C.ident[:],
                 wait=[ln_v] if c == 0 else [], inc=(TSm, 1))
            p.op("scalar", "activation",
                 stage[:, c * 512 + tt * 128: c * 512 + tt * 128 + 128],
                 ps_tr[slot][:], AF.Copy, scale=1.0,
                 wait=[TSm.at(), LD.at()] if (tt == 0 and c == 0) else [TSm.at()],
                 inc=(SS, 1))
        p.wait("vector", SS, SS.n)
        p.op("vector", "memset", stats[:, 7:8], 0.0, inc=(VS, 1))
    return SS.at()



def build_l1ffn(fused=None):
    if fused is None:
        p = P("l1ffn")
        C = Common(p)
        x_in_ap = x_out_ap = None
    else:
        p, C, x_in_ap, x_out_ap = fused
    p.push_scope("f1_")
    x_in = x_in_ap if x_in_ap is not None else p.dram("x_in", (T, D), F32, kind="ExternalInput").ap()
    x_out = x_out_ap if x_out_ap is not None else p.dram("x_out", (T, D), F32, kind="ExternalOutput").ap()
    w_in = p.dram("weight", (D, D), F32, kind="ExternalInput").ap()
    b_in = p.dram("bias2d", (128, 16), F32, kind="ExternalInput").ap()

    stage = p.sbuf("stage", (128, 16 * 512), BF16)
    bwT = p.sbuf("bwT", (128, 16 * 2048), BF16)
    bias = p.sbuf("bias_sb", (128, 16), F32)
    xr = [p.sbuf(f"xr{i}", (128, D), F32) for i in range(2)]
    xsq = p.sbuf("xsq", (128, D), F32)
    stats = p.sbuf("stats", (128, 8), F32)
    xln = [p.sbuf(f"xln{i}", (128, D), BF16) for i in range(2)]
    wrow = p.sbuf("wrow", (128, D), F32)
    sgn = p.sbuf("sgn", (128, D), BF16)
    bwrow = p.sbuf("bwrow", (128, D), BF16)
    alpha = p.sbuf("alpha", (128, 2), F32)
    y_all = p.sbuf("y_all", (128, 16 * 512), F32)
    yrow = p.sbuf("yrow", (128, 128), F32)
    xres = p.sbuf("xres", (128, D), F32)

    ps_mm = p.psum("ps_mm", (128, 512), F32)
    ps_tr = [p.psum(f"ps_tr{i}", (128, 128), BF16) for i in range(2)]
    ps_trf = [p.psum(f"ps_trf{i}", (128, 128), F32) for i in range(2)]

    LD, LG, ST, TSm, VS, SS = C.LD, C.LG, C.ST, C.TS, C.VS, C.SS
    C.load_consts()
    p.op("sync", "dma_start", out=bias[:], in_=b_in, inc=(LD, 16))
    consts_ld = [LD.at(), LG.at(), VS.at()]

    for oc in range(16):
        p.op("gpsimd", "dma_start", out=wrow[:], in_=w_in[oc * 128:(oc + 1) * 128, :],
             wait=[SS.at(), VS.at()] if oc > 0 else [], inc=(LG, 16))
        p.wait("vector", LG, LG.n)
        p.op("vector", "reduce_sum", alpha[:, 0:1], wrow[:], axis=AX.X,
             apply_absolute_value=True)
        p.op("vector", "drain")
        p.op("vector", "tensor_scalar", alpha[:, 1:2], alpha[:, 0:1], 1.0 / D, None,
             ALU.mult, inc=(VS, 1))
        p.op("vector", "drain")
        p.op("scalar", "sign", sgn[:], wrow[:], wait=[VS.at()], inc=(SS, 1))
        p.op("vector", "tensor_scalar", bwrow[:], sgn[:], alpha[:, 1:2], None,
             ALU.mult, wait=[SS.at()], inc=(VS, 1))
        for ic in range(16):
            slot = ic % 2
            if ic >= 2:
                p.wait("tensor", SS, SS.n)
            p.op("tensor", "transpose", ps_tr[slot][:],
                 bwrow[:, ic * 128:(ic + 1) * 128], C.ident[:],
                 wait=[VS.at()] if ic == 0 else [], inc=(TSm, 1))
            p.op("scalar", "activation", bwT[:, ic * 2048 + oc * 128: ic * 2048 + oc * 128 + 128],
                 ps_tr[slot][:], AF.Copy, scale=1.0, wait=[TSm.at()], inc=(SS, 1))
    bw_done = SS.at()

    for g in range(8):
        stage_done = emit_ln_stage(p, C, x_in, stage, xr, xsq, stats, xln, ps_tr, g,
                                   consts_ld if g == 0 else None)
        for oc in range(16):
            for ic in range(16):
                p.op("tensor", "matmul", ps_mm[:],
                     bwT[:, ic * 2048 + oc * 128: ic * 2048 + oc * 128 + 128],
                     stage[:, ic * 512:(ic + 1) * 512],
                     start=(ic == 0), stop=(ic == 15),
                     wait=[stage_done, bw_done] if (oc == 0 and ic == 0) else
                          ([VS.at()] if ic == 0 else []),
                     inc=(TSm, 1) if ic == 15 else None)
            p.op("vector", "tensor_scalar", y_all[:, oc * 512:(oc + 1) * 512],
                 ps_mm[:], bias[:, oc:oc + 1], None, ALU.add,
                 wait=[TSm.at()], inc=(VS, 1))
        y_done = VS.at()
        for tt in range(4):
            trow = g * 512 + tt * 128
            p.op("sync", "dma_start", out=xres[:], in_=x_in[trow:trow + 128, :],
                 wait=[ST.at()], inc=(LD, 16))
            p.wait("vector", LD, LD.n)
            for oc in range(16):
                slot = oc % 2
                if oc >= 2:
                    p.wait("tensor", VS, VS.n)
                p.op("tensor", "transpose", ps_trf[slot][:],
                     y_all[:, oc * 512 + tt * 128: oc * 512 + tt * 128 + 128],
                     C.ident_f[:],
                     wait=[y_done, VS.at()] if (oc == 0 and tt == 0) else
                          ([VS.at()] if oc < 2 else []),
                     inc=(TSm, 1))
                p.op("vector", "tensor_add", xres[:, oc * 128:(oc + 1) * 128],
                     xres[:, oc * 128:(oc + 1) * 128], ps_trf[slot][:],
                     wait=[TSm.at()], inc=(VS, 1))
            p.op("sync", "dma_start", out=x_out[trow:trow + 128, :], in_=xres[:],
                 wait=[VS.at()], inc=(ST, 16))
    p.wait("sync", ST, ST.n)
    p.pop_scope()
    return p


def run_l1ffn(x, weight, bias):
    c = _consts()
    p = build_l1ffn()
    nc = p.finalize()
    b2d = np.ascontiguousarray(bias.reshape(16, 128).T)
    in_maps = []
    for b in range(NCORES):
        in_maps.append({
            "x_in": np.ascontiguousarray(x[b]),
            "weight": weight, "bias2d": b2d,
            "c_ident": c["ident"], "c_cbank": c["cbank"],
        })
    res = run_bass_kernel_spmd(nc, in_maps, core_ids=list(range(NCORES)))
    return np.stack([res.results[b]["x_out"] for b in range(NCORES)])



NEXP = 17
GU_WG = 4
DN_WG = 3

def build_l0ffn(phases=(1, 1, 1), nexp=NEXP, debug=False, fused=None):
    if fused is None:
        p = P("l0ffn")
        C = Common(p)
        x_in_ap = x_out_ap = None
    else:
        p, C, x_in_ap, x_out_ap = fused
    p.push_scope("m0_")
    x_in = x_in_ap if x_in_ap is not None else p.dram("x_in", (T, D), F32, kind="ExternalInput").ap()
    x_out = x_out_ap if x_out_ap is not None else p.dram("x_out", (T, D), F32, kind="ExternalOutput").ap()
    gate_w = p.dram("gate_w", (16, D), F32, kind="ExternalInput").ap()
    gup = p.dram("gup", (NEXP, 256, 3072), I32, kind="ExternalInput").ap()
    gus = p.dram("gus", (NEXP, 16, 3072), F32, kind="ExternalInput").ap()
    dnp = p.dram("dnp", (NEXP, 192, 2048), I32, kind="ExternalInput").ap()
    dns = p.dram("dns", (NEXP, 12, 2048), F32, kind="ExternalInput").ap()
    sel4_in = p.dram("sel4", (4, 128), F32, kind="ExternalInput").ap()
    sel16_in = p.dram("sel16", (16, 16 * 128), F32, kind="ExternalInput").ap()
    shb_in = p.dram("shbank", (128, 8), I32, kind="ExternalInput").ap()
    xlnT_hbm = p.dram("xlnT_scratch", (8, D, 512), BF16, kind="Internal").ap()
    dbk = "ExternalOutput" if debug else "Internal"
    wgu_d = p.dram("wgu_dense", (NEXP, 12, D, 256), BF16, kind=dbk).ap()
    wdn_d = p.dram("wdn_dense", (NEXP, 1536, D), BF16, kind=dbk).ap()

    combT = p.sbuf("combT", (16, T), BF16)
    gwT = p.sbuf("gwT", (128, 16 * 16), BF16)
    sel4 = p.sbuf("sel4_sb", (4, 128), BF16)
    sel16 = p.sbuf("sel16_sb", (16, 16 * 128), BF16)
    shb = p.sbuf("shb_sb", (128, 8), I32)

    ps_tr = [p.psum(f"ps_tr{i}", (128, 128), BF16) for i in range(2)]

    LD, LG, ST, TSm, VS, SS = C.LD, C.LG, C.ST, C.TS, C.VS, C.SS
    C.load_consts()

    prep = ExitStack()
    gw_f = p.sbuf_scoped(prep, "gw_f", (16, D), F32)
    gw_b = p.sbuf_scoped(prep, "gw_b", (16, D), BF16)
    p.op("sync", "dma_start", out=gw_f[:], in_=gate_w, inc=(LD, 16))
    p.op("gpsimd", "dma_start", out=sel4[:], in_=sel4_in, inc=(LG, 16))
    p.op("gpsimd", "dma_start", out=sel16[:], in_=sel16_in, inc=(LG, 16))
    p.op("gpsimd", "dma_start", out=shb[:], in_=shb_in, inc=(LG, 16))
    p.wait("vector", LD, LD.n)
    p.op("vector", "tensor_copy", gw_b[:], gw_f[:], wait=[LG.at()], inc=(VS, 1))
    for h in range(16):
        slot = h % 2
        if h >= 2:
            p.wait("tensor", SS, SS.n)
        p.op("tensor", "transpose", ps_tr[slot][:, 0:16],
             gw_b[0:16, h * 128:(h + 1) * 128], C.ident[0:16, 0:16],
             wait=[VS.at()] if h == 0 else [], inc=(TSm, 1))
        p.op("scalar", "activation", gwT[:, h * 16:(h + 1) * 16],
             ps_tr[slot][:, 0:16], AF.Copy, scale=1.0, wait=[TSm.at()], inc=(SS, 1))
    prep.close()
    consts_ld = [LD.at(), LG.at(), VS.at(), SS.at()]

    do_a, do_c, do_d = phases
    pa = ExitStack()
    ps_p = p.psum_scoped(pa, "ps_p", (128, 16), F32)
    ps_ct = p.psum_scoped(pa, "ps_ct", (16, 128), BF16)
    ps_a = p.psum_scoped(pa, "ps_sea", (128, 512), F32)
    ps_b = p.psum_scoped(pa, "ps_seb", (128, 512), F32)
    xr = [p.sbuf_scoped(pa, f"xr{i}", (128, D), F32) for i in range(2)]
    xsq = p.sbuf_scoped(pa, "xsq", (128, D), F32)
    stats = p.sbuf_scoped(pa, "stats", (128, 8), F32)
    xln = [p.sbuf_scoped(pa, f"xln{i}", (128, D), BF16) for i in range(2)]
    stage = p.sbuf_scoped(pa, "stage", (128, 16 * 512), BF16)
    pr = p.sbuf_scoped(pa, "pr", (128, 16), F32)
    pz = p.sbuf_scoped(pa, "pz", (128, 16), F32)
    msk = p.sbuf_scoped(pa, "msk", (128, 16), U8)
    mkf = p.sbuf_scoped(pa, "mkf", (128, 16), F32)
    num = p.sbuf_scoped(pa, "num", (128, 16), F32)
    rstats = p.sbuf_scoped(pa, "rstats", (128, 8), F32)
    cmb = p.sbuf_scoped(pa, "cmb", (128, 16), BF16)

    pc = pa
    pk = p.sbuf_scoped(pc, "pk", (128, 3072), I32)
    mag = [p.sbuf_scoped(pc, f"mag{i}", (128, 3072), I32) for i in range(2)]
    sgn = [p.sbuf_scoped(pc, f"sgn{i}", (128, 3072), I32) for i in range(2)]
    t_cm = [p.sbuf_scoped(pc, f"t_cm{i}", (128, 3072), BF16) for i in range(2)]
    t_r1 = [p.sbuf_scoped(pc, f"t_r1{i}", (128, 3072), BF16) for i in range(2)]
    t_r2 = [p.sbuf_scoped(pc, f"t_r2{i}", (128, 3072), BF16) for i in range(2)]
    t_sg = [p.sbuf_scoped(pc, f"t_sg{i}", (128, 3072), BF16) for i in range(2)]
    se_f = p.sbuf_scoped(pc, "se_f", (4, 3072), F32)
    se_b = p.sbuf_scoped(pc, "se_b", (4, 3072), BF16)
    se = p.sbuf_scoped(pc, "se", (128, 3072), BF16)
    dq = [p.sbuf_scoped(pc, f"dq{i}", (128, 3072), BF16) for i in range(2)]

    first_dq = [True]
    dqst = {"wr": [0, 0], "pk_free": 0, "ex": [0, 0]}

    def dequant_block(packed2d, scales2d, ncols, wg, dense_out_fn, nparts=64):
        p.op("sync", "dma_start", out=pk[0:64, 0:ncols],
             in_=packed2d[wg * 64:(wg + 1) * 64, :], wait=[(VS, dqst["pk_free"])],
             inc=(LD, 16))
        p.op("sync", "dma_start", out=pk[64:128, 0:ncols],
             in_=packed2d[wg * 64:(wg + 1) * 64, :], inc=(LD, 16))
        p.op("sync", "dma_start", out=se_f[:, 0:ncols],
             in_=scales2d[wg * 4:(wg + 1) * 4, :], inc=(LD, 16))
        p.wait("vector", LD, LD.n)
        p.op("vector", "tensor_copy", se_b[:, 0:ncols], se_f[:, 0:ncols], inc=(VS, 1))
        nse = (ncols + 511) // 512
        for i in range(nse):
            c0, c1 = i * 512, min(ncols, (i + 1) * 512)
            pb = ps_a if i % 2 == 0 else ps_b
            if i >= 2:
                p.wait("tensor", SS, SS.n)
            p.op("tensor", "matmul", pb[:, 0:c1 - c0], sel4[:], se_b[:, c0:c1],
                 start=True, stop=True, wait=[VS.at()] if i == 0 else [],
                 inc=(TSm, 1))
            p.op("scalar", "activation", se[:, c0:c1], pb[:, 0:c1 - c0], AF.Copy,
                 scale=1.0, wait=[TSm.at()], inc=(SS, 1))
        se_done = SS.at()
        cols = slice(0, ncols)

        def extract(j, par):
            pi = j // 2
            p.op("vector", "tensor_scalar", mag[par][:, cols], pk[:, cols],
                 shb[:, pi:pi + 1], 7, ALU.logical_shift_right, ALU.bitwise_and)
            p.op("vector", "tensor_scalar", sgn[par][:, cols], pk[:, cols],
                 shb[:, 4 + pi:5 + pi], 1, ALU.logical_shift_right, ALU.bitwise_and,
                 inc=(VS, 1))
            dqst["ex"][par] = VS.n
            p.op("scalar", "activation", t_sg[par][:, cols], sgn[par][:, cols],
                 AF.Copy, scale=-2.0, bias=1.0, wait=[VS.at()])
            if j == 6:
                dqst["pk_free"] = VS.n

        extract(0, 0)
        for jp in range(4):
            par = jp % 2
            buf = dq[par]
            ex_v = (VS, dqst["ex"][par])
            p.op("scalar", "activation", t_cm[par][:, cols], mag[par][:, cols], AF.Copy,
                 scale=0.5, wait=[ex_v])
            p.op("scalar", "activation", t_r1[par][:, cols], mag[par][:, cols], AF.Relu,
                 scale=0.5, bias=C.cbank_ap(CB_NEG2, 128))
            p.op("scalar", "activation", t_r2[par][:, cols], mag[par][:, cols], AF.Relu,
                 scale=1.0, bias=C.cbank_ap(CB_NEG6, 128), inc=(SS, 1))
            acts_v = SS.at()
            if jp < 3:
                extract(2 * (jp + 1), 1 - par)
            p.op("vector", "tensor_add", t_cm[par][:, cols], t_cm[par][:, cols],
                 t_r1[par][:, cols], wait=[acts_v])
            p.op("vector", "tensor_add", t_cm[par][:, cols], t_cm[par][:, cols],
                 t_r2[par][:, cols])
            p.op("vector", "tensor_mul", t_cm[par][:, cols], t_cm[par][:, cols],
                 t_sg[par][:, cols])
            p.op("vector", "tensor_mul", buf[:, cols], t_cm[par][:, cols], se[:, cols],
                 wait=[se_done, (LG, dqst["wr"][par])], inc=(VS, 1))
            dsts = dense_out_fn(jp)
            p.op("gpsimd", "dma_start", out=dsts[0], in_=buf[0:64, cols],
                 wait=[VS.at()], inc=(LG, 16))
            p.op("gpsimd", "dma_start", out=dsts[1], in_=buf[64:128, cols],
                 inc=(LG, 16))
            dqst["wr"][par] = LG.n

    def emit_dequant(e):
        for wg in range(GU_WG):
            def gu_out(jp, e=e, wg=wg):
                hc = wg * 4 + jp
                h0 = wgu_d[e, :, hc * 128:hc * 128 + 64, :].rearrange("cb p i -> p cb i")
                h1 = wgu_d[e, :, hc * 128 + 64:(hc + 1) * 128, :].rearrange("cb p i -> p cb i")
                return (h0, h1)
            dequant_block(gup[e], gus[e], 3072, wg, gu_out)
        for wg in range(DN_WG):
            dequant_block(dnp[e], dns[e], 2048, wg,
                          lambda jp, e=e, wg=wg: _dn_dest(wdn_d, e, wg, jp))

    _dq_pending = list(range(nexp if do_c else 0))

    for g in range(8 if do_a else 0):
        stage_done = emit_ln_stage(p, C, x_in, stage, xr, xsq, stats, xln, ps_tr, g,
                                   consts_ld if g == 0 else None)
        for tt in range(4):
            trow = g * 512 + tt * 128
            p.wait("tensor", VS, VS.n)
            for h in range(16):
                p.op("tensor", "matmul", ps_p[:],
                     stage[:, h * 512 + tt * 128: h * 512 + tt * 128 + 128],
                     gwT[:, h * 16:(h + 1) * 16],
                     start=(h == 0), stop=(h == 15),
                     wait=[stage_done] if h == 0 else [],
                     inc=(TSm, 1) if h == 15 else None)
            p.op("vector", "reduce_max", rstats[:, 0:1], ps_p[:], axis=AX.X,
                 negate=True, wait=[TSm.at()], inc=(VS, 1))
            p.op("vector", "drain")
            p.op("scalar", "activation", pr[:], ps_p[:], AF.Exp,
                 bias=rstats[:, 0:1], scale=1.0, wait=[VS.at()], inc=(SS, 1))
            p.op("vector", "memset", num[:], 0.0, wait=[SS.at()])
            p.op("vector", "memset", rstats[:, 1:2], 0.0)
            p.op("vector", "drain")
            for k in range(4):
                src = pr if k == 0 else pz
                p.op("vector", "reduce_max", rstats[:, 2:3], src[:], axis=AX.X)
                p.op("vector", "drain")
                p.op("vector", "tensor_scalar", msk[:], src[:], rstats[:, 2:3], None,
                     ALU.is_ge)
                p.op("vector", "drain")
                p.op("vector", "tensor_copy", mkf[:], msk[:])
                p.op("vector", "drain")
                p.op("vector", "scalar_tensor_tensor", num[:], mkf[:], rstats[:, 2:3],
                     num[:], ALU.mult, ALU.add)
                p.op("vector", "tensor_add", rstats[:, 1:2], rstats[:, 1:2], rstats[:, 2:3])
                p.op("vector", "drain")
                if k < 3:
                    p.op("vector", "memset", mkf[:], 0.0)
                    p.op("vector", "drain")
                    p.op("vector", "select", pz[:], msk[:], mkf[:], src[:])
                    p.op("vector", "drain")
            p.op("vector", "reciprocal", rstats[:, 3:4], rstats[:, 1:2])
            p.op("vector", "drain")
            p.op("vector", "tensor_scalar", cmb[:], num[:], rstats[:, 3:4], None,
                 ALU.mult, inc=(VS, 1))
            p.op("vector", "drain")
            p.op("tensor", "transpose", ps_ct[:], cmb[:], C.ident[:],
                 wait=[VS.at(), SS.at()], inc=(TSm, 1))
            p.op("scalar", "activation", combT[:, trow:trow + 128], ps_ct[0:16, :],
                 AF.Copy, scale=1.0, wait=[TSm.at()], inc=(SS, 1))
        p.op("sync", "dma_start", out=xlnT_hbm[g].rearrange("(h p) t -> p h t", p=128),
             in_=stage[:].rearrange("p (h t) -> p h t", h=16),
             wait=[SS.at()], inc=(LD, 16))
        for _ in range(2):
            if _dq_pending:
                emit_dequant(_dq_pending.pop(0))
    xln_all = LD.at()
    comb_all = SS.at()

    for e in _dq_pending:
        emit_dequant(e)
    dq_all = LG.at()
    pa.close()

    pd = ExitStack()
    ps_ga = [p.psum_scoped(pd, f"ps_ga{i}", (128, 512), F32) for i in range(2)]
    ps_gb = [p.psum_scoped(pd, f"ps_gb{i}", (128, 512), F32) for i in range(2)]
    ps_dnb = [p.psum_scoped(pd, f"ps_dnn{i}", (128, 512), F32) for i in range(2)]
    xp = p.sbuf_scoped(pd, "xp", (128, 16 * 1024), BF16)
    out_acc = p.sbuf_scoped(pd, "out_acc", (128, 16 * 1024), BF16)
    act = p.sbuf_scoped(pd, "act", (128, 12 * 1024), BF16)
    wdsb = p.sbuf_scoped(pd, "wdsb", (128, 12 * 2048), BF16)
    gblk = [p.sbuf_scoped(pd, f"gblk{i}", (128, 16 * 256), BF16) for i in range(2)]
    ublk = [p.sbuf_scoped(pd, f"ublk{i}", (128, 16 * 256), BF16) for i in range(2)]
    cB = p.sbuf_scoped(pd, "cB", (128, 1024), BF16)
    sg_sb = [p.sbuf_scoped(pd, f"sg_sb{i}", (128, 512), BF16) for i in range(2)]
    m1_sb = [p.sbuf_scoped(pd, f"m1_sb{i}", (128, 512), BF16) for i in range(2)]
    xres = p.sbuf_scoped(pd, "xres", (128, D), F32)

    first_d = [True]
    for tg in range(4 if do_d else 0):
        if first_d[0]:
            p.wait("sync", LG, LG.n)
            p.wait("vector", LG, LG.n)
            first_d[0] = False
        p.wait("sync", TSm, TSm.n)
        for hc in range(16):
            wg, jp = hc // 4, hc % 4
            for gb in range(2):
                g = tg * 2 + gb
                blk = xlnT_hbm[g][512 * wg: 512 * wg + 512, :]
                r = blk.rearrange("(pp j) t -> j pp t", j=8)
                w0 = [xln_all] if (tg == 0 and hc == 0 and gb == 0) else []
                cslice = slice(hc * 1024 + gb * 512, hc * 1024 + (gb + 1) * 512)
                p.op("sync", "dma_start", out=xp[0:64, cslice], in_=r[2 * jp],
                     wait=w0, inc=(LD, 16))
                p.op("sync", "dma_start", out=xp[64:128, cslice], in_=r[2 * jp + 1],
                     inc=(LD, 16))
        xp_ld = LD.at()
        p.op("vector", "memset", out_acc[:], 0.0, wait=[TSm.at()], inc=(VS, 1))
        g_evac = {}
        d_evac = {}
        blk_mm = {}
        for e in range(nexp):
            if e == 16:
                p.op("vector", "memset", cB[:], 1.0, wait=[TSm.at()], inc=(VS, 1))
                cb_done = VS.at()
            else:
                for i in range(2):
                    p.op("tensor", "matmul", ps_dnb[0][:, 0:512],
                         sel16[:, e * 128:(e + 1) * 128],
                         combT[:, tg * 1024 + i * 512: tg * 1024 + (i + 1) * 512],
                         start=True, stop=True,
                         wait=[comb_all, VS.at()] if i == 0 else [SS.at()], inc=(TSm, 1))
                    p.op("scalar", "activation", cB[:, i * 512:(i + 1) * 512],
                         ps_dnb[0][:, 0:512], AF.Copy, scale=1.0, wait=[TSm.at()], inc=(SS, 1))
                cb_done = SS.at()
            for bp in range(6):
                lb = bp % 2
                gk = (e, bp)
                wblk_wait = [dq_all] if tg == 0 else []
                prev = blk_mm.get((e, bp - 2)) if bp >= 2 else (
                    blk_mm.get((e - 1, 4)) if e > 0 and bp == 0 else (
                    blk_mm.get((e - 1, 5)) if e > 0 and bp == 1 else None))
                if prev is None:
                    prev = TSm.at()
                p.op("gpsimd", "dma_start", out=gblk[lb][:],
                     in_=wgu_d[e, bp].rearrange("(hc p) i -> p hc i", p=128),
                     wait=wblk_wait + [prev], inc=(LG, 16))
                p.op("gpsimd", "dma_start", out=ublk[lb][:],
                     in_=wgu_d[e, 6 + bp].rearrange("(hc p) i -> p hc i", p=128),
                     inc=(LG, 16))
                wld = LG.at()
                for ci_in in range(2):
                    ci = bp * 2 + ci_in
                    for th in range(2):
                        q = (bp * 2 + ci_in) * 2 + th
                        qp = q % 2
                        w0 = [wld, xp_ld, cb_done]
                        pg = g_evac.get((e, q - 2))
                        if pg is not None:
                            w0.append((pg[0], pg[1]))
                        for hc in range(16):
                            p.op("tensor", "matmul", ps_ga[qp][:],
                                 gblk[lb][:, hc * 256 + ci_in * 128: hc * 256 + ci_in * 128 + 128],
                                 xp[:, hc * 1024 + th * 512: hc * 1024 + (th + 1) * 512],
                                 start=(hc == 0), stop=(hc == 15),
                                 wait=w0 if hc == 0 else [],
                                 inc=(TSm, 1) if hc == 15 else None)
                            p.op("tensor", "matmul", ps_gb[qp][:],
                                 ublk[lb][:, hc * 256 + ci_in * 128: hc * 256 + ci_in * 128 + 128],
                                 xp[:, hc * 1024 + th * 512: hc * 1024 + (th + 1) * 512],
                                 start=(hc == 0), stop=(hc == 15),
                                 inc=(TSm, 1) if hc == 15 else None)
                        p.op("scalar", "activation", sg_sb[qp][:], ps_ga[qp][:], AF.Silu,
                             wait=[TSm.at()], inc=(SS, 1))
                        p.op("vector", "tensor_mul", m1_sb[qp][:], sg_sb[qp][:], ps_gb[qp][:],
                             wait=[SS.at()])
                        p.op("vector", "tensor_mul",
                             act[:, ci * 1024 + th * 512: ci * 1024 + (th + 1) * 512],
                             m1_sb[qp][:], cB[:, th * 512:(th + 1) * 512], inc=(VS, 1))
                        g_evac[(e, q)] = (VS, VS.n)
                blk_mm[(e, bp)] = TSm.at()
            act_done = VS.at()
            for ic in range(12):
                p.op("gpsimd", "dma_start", out=wdsb[:, ic * 2048:(ic + 1) * 2048],
                     in_=wdn_d[e, ic * 128:(ic + 1) * 128, :],
                     wait=[TSm.at()] if ic == 0 else [], inc=(LG, 16))
            wd_ld = LG.at()
            for th in range(2):
                for oc in range(16):
                    dqi = th * 16 + oc
                    dp = dqi % 2
                    w0 = [act_done, wd_ld]
                    pv = d_evac.get((e, dqi - 2))
                    if pv is not None:
                        w0.append((pv[0], pv[1]))
                    elif dqi < 2:
                        w0.append((VS, VS.n))
                    for ic in range(12):
                        p.op("tensor", "matmul", ps_dnb[dp][:],
                             wdsb[:, ic * 2048 + oc * 128: ic * 2048 + oc * 128 + 128],
                             act[:, ic * 1024 + th * 512: ic * 1024 + (th + 1) * 512],
                             start=(ic == 0), stop=(ic == 11),
                             wait=w0 if ic == 0 else [],
                             inc=(TSm, 1) if ic == 11 else None)
                    sl = oc * 1024 + th * 512
                    p.op("vector", "tensor_add", out_acc[:, sl:sl + 512],
                         out_acc[:, sl:sl + 512], ps_dnb[dp][:], wait=[TSm.at()], inc=(VS, 1))
                    d_evac[(e, dqi)] = (VS, VS.n)
        acc_done = VS.at()
        for tt in range(8):
            trow = tg * 1024 + tt * 128
            p.op("sync", "dma_start", out=xres[:], in_=x_in[trow:trow + 128, :],
                 wait=[ST.at()], inc=(LD, 16))
            p.wait("vector", LD, LD.n)
            for oc in range(16):
                slot = oc % 2
                if oc >= 2:
                    p.wait("tensor", VS, VS.n)
                p.op("tensor", "transpose", ps_tr[slot][:],
                     out_acc[:, oc * 1024 + tt * 128: oc * 1024 + tt * 128 + 128],
                     C.ident[:],
                     wait=[acc_done, VS.at()] if oc < 2 else [],
                     inc=(TSm, 1))
                p.op("vector", "tensor_add", xres[:, oc * 128:(oc + 1) * 128],
                     xres[:, oc * 128:(oc + 1) * 128], ps_tr[slot][:],
                     wait=[TSm.at()], inc=(VS, 1))
            p.op("sync", "dma_start", out=x_out[trow:trow + 128, :], in_=xres[:],
                 wait=[VS.at()], inc=(ST, 16))
    pd.close()
    p.wait("sync", ST, ST.n)
    p.pop_scope()
    return p


def _dn_dest(wdn_d, e, wg, jp):
    v = wdn_d[e, 512 * wg: 512 * wg + 512, :]
    r = v.rearrange("(pp j) c -> j pp c", j=8)
    return (r[2 * jp], r[2 * jp + 1])


def _shbank():
    shbank = np.zeros((128, 8), np.int32)
    for pi in range(4):
        shbank[0:64, pi] = 8 * pi
        shbank[64:128, pi] = 8 * pi + 4
    shbank[:, 4:8] = shbank[:, 0:4] + 3
    return shbank


def run_l0ffn(x, gate_w, gup, gus, dnp, dns, sgup, sgus, sdnp, sdns):
    c = _consts()
    p = build_l0ffn()
    nc = p.finalize()
    gup_a = np.concatenate([gup, sgup[None]], 0)
    gus_a = np.concatenate([gus, sgus[None]], 0)
    dnp_a = np.concatenate([dnp, sdnp[None]], 0)
    dns_a = np.concatenate([dns, sdns[None]], 0)
    pp = np.arange(128)
    sel4 = np.stack([(((pp % 64) >> 4) == k) for k in range(4)]).astype(np.float32)
    sel16 = np.zeros((16, 16 * 128), np.float32)
    for e in range(16):
        sel16[e, e * 128:(e + 1) * 128] = 1.0
    in_maps = []
    for b in range(NCORES):
        in_maps.append({
            "x_in": np.ascontiguousarray(x[b]), "gate_w": gate_w,
            "gup": gup_a, "gus": gus_a, "dnp": dnp_a, "dns": dns_a,
            "sel4": sel4, "sel16": sel16, "shbank": _shbank(),
            "c_ident": c["ident"], "c_cbank": c["cbank"],
        })
    res = run_bass_kernel_spmd(nc, in_maps, core_ids=list(range(NCORES)))
    return np.stack([res.results[b]["x_out"] for b in range(NCORES)])



DIN, DST, NHM, HDM, CHK = 4096, 128, 64, 64, 256
CONVD = DIN + 2 * DST

def build_l1attn(debug=False, fused=None):
    if fused is None:
        p = P("l1attn")
        C = Common(p)
        x_in_ap = x_out_ap = None
    else:
        p, C, x_in_ap, x_out_ap = fused
    p.push_scope("a1_")
    x_in = x_in_ap if x_in_ap is not None else p.dram("x_in", (T, D), F32, kind="ExternalInput").ap()
    x_out = x_out_ap if x_out_ap is not None else p.dram("x_out", (T, D), F32, kind="ExternalOutput").ap()
    w_inp = p.dram("in_proj_w", (8512, D), F32, kind="ExternalInput").ap()
    w_outp = p.dram("out_proj_w", (D, DIN), F32, kind="ExternalInput").ap()
    conv_w_in = p.dram("conv_w2d", (128, 34 * 4), F32, kind="ExternalInput").ap()
    conv_b_in = p.dram("conv_b2d", (128, 34), F32, kind="ExternalInput").ap()
    dt_bias_in = p.dram("dt_bias2d", (64, 1), F32, kind="ExternalInput").ap()
    a_log_in = p.dram("a_log2d", (64, 1), F32, kind="ExternalInput").ap()
    d_in = p.dram("d2d", (64, 1), F32, kind="ExternalInput").ap()
    nw_in = p.dram("norm_w2d", (128, 32), F32, kind="ExternalInput").ap()
    sel64_in = p.dram("sel64", (64, 64 * 128), F32, kind="ExternalInput").ap()
    tri_in = p.dram("tri2", (128, 2 * 256), F32, kind="ExternalInput").ap()

    xlnT_hbm = p.dram("xlnT_scratch", (8, D, 512), BF16, kind="Internal").ap()
    w_inT = p.dram("w_inT_t", (67, D, 128), BF16, kind="Internal").ap()
    w_outT = p.dram("w_outT_t", (16, DIN, 128), BF16, kind="Internal").ap()
    dbk = "ExternalOutput" if debug else "Internal"
    if debug:
        dbg2 = p.dram("dbg2", (8, 128, 512), F32, kind="ExternalOutput").ap()
    zxb = p.dram("zxbcdtT", (8512, T), BF16, kind=dbk).ap()
    dtf32_hbm = p.dram("dt_f32", (64, T), F32, kind="Internal").ap()
    convT = p.dram("convT_s", (CONVD, T), BF16, kind=dbk).ap()
    y_hbm = p.dram("y_scr", (16, DIN, 256), BF16, kind=dbk).ap()

    ps_tr = [p.psum(f"ps_tr{i}", (128, 128), BF16) for i in range(2)]
    ps_trf0 = p.psum("ps_trf0", (128, 128), F32)
    ps_trf = [ps_trf0, ps_trf0]
    ps_mm = p.psum("ps_mm", (128, 512), F32)

    LD, LG, ST, TSm, VS, SS = C.LD, C.LG, C.ST, C.TS, C.VS, C.SS
    C.load_consts()
    sel64 = p.sbuf("sel64_sb", (64, 64 * 128), F32)
    tri2 = p.sbuf("tri2_sb", (128, 2 * 256), BF16)
    ones_rf = p.sbuf("ones_rf", (1, 128), F32)
    adt = p.sbuf("adt_fm", (64, T), F32)
    dt_tm = p.sbuf("dt_tm_all", (128, 32 * 64), F32)
    db_t = p.sbuf("db_t", (128, 64), F32)
    state = p.sbuf("state_sb", (128, NHM * 64), F32)
    p.op("gpsimd", "dma_start", out=sel64[:], in_=sel64_in, inc=(LG, 16))
    p.op("gpsimd", "dma_start", out=tri2[:], in_=tri_in, inc=(LG, 16))
    p.op("vector", "memset", ones_rf[:], 1.0, inc=(VS, 1))
    consts_ld = [LD.at(), LG.at(), VS.at()]

    pw = ExitStack()
    wrow = p.sbuf_scoped(pw, "wrow", (128, DIN), BF16)
    wst = p.sbuf_scoped(pw, "wst", (128, DIN), BF16)
    for roc in range(67):
        nr = 128 if roc < 66 else 64
        p.op("gpsimd", "dma_start", out=wrow[0:nr, 0:D],
             in_=w_inp[roc * 128: roc * 128 + nr, :],
             wait=[LD.at()] if roc > 0 else consts_ld, inc=(LG, 16))
        p.wait("tensor", LG, LG.n)
        for hc in range(16):
            slot = hc % 2
            if hc >= 2:
                p.wait("tensor", SS, SS.n)
            p.op("tensor", "transpose", ps_tr[slot][:, 0:nr],
                 wrow[0:nr, hc * 128:(hc + 1) * 128], C.ident[0:nr, 0:nr],
                 inc=(TSm, 1))
            p.op("scalar", "activation", wst[:, hc * 128:hc * 128 + nr],
                 ps_tr[slot][:, 0:nr], AF.Copy, scale=1.0, wait=[TSm.at()], inc=(SS, 1))
        p.op("sync", "dma_start",
             out=w_inT[roc, :, 0:nr].rearrange("(hc p) o -> p hc o", p=128),
             in_=wst[:, 0:2048].rearrange("p (hc o) -> p hc o", hc=16)[:, :, 0:nr],
             wait=[SS.at()], inc=(LD, 16))
    for oc in range(16):
        p.op("gpsimd", "dma_start", out=wrow[:, 0:DIN],
             in_=w_outp[oc * 128:(oc + 1) * 128, :], wait=[LD.at()], inc=(LG, 16))
        p.wait("tensor", LG, LG.n)
        for dc in range(32):
            slot = dc % 2
            if dc >= 2:
                p.wait("tensor", SS, SS.n)
            p.op("tensor", "transpose", ps_tr[slot][:],
                 wrow[:, dc * 128:(dc + 1) * 128], C.ident[:], inc=(TSm, 1))
            p.op("scalar", "activation", wst[:, dc * 128:(dc + 1) * 128],
                 ps_tr[slot][:], AF.Copy, scale=1.0, wait=[TSm.at()], inc=(SS, 1))
        p.op("sync", "dma_start",
             out=w_outT[oc].rearrange("(dc p) o -> p dc o", p=128),
             in_=wst[:, 0:DIN].rearrange("p (dc o) -> p dc o", dc=32),
             wait=[SS.at()], inc=(LD, 16))
    pw.close()
    wT_done_ld = LD.at()

    pa = ExitStack()
    xr = [p.sbuf_scoped(pa, f"xr{i}", (128, D), F32) for i in range(2)]
    xsq = p.sbuf_scoped(pa, "xsq", (128, D), F32)
    stats = p.sbuf_scoped(pa, "stats", (128, 8), F32)
    xln = [p.sbuf_scoped(pa, f"xln{i}", (128, D), BF16) for i in range(2)]
    stage = p.sbuf_scoped(pa, "stage", (128, 16 * 512), BF16)
    wblk = [p.sbuf_scoped(pa, f"wblk{i}", (128, 16 * 128), BF16) for i in range(2)]
    evb = [p.sbuf_scoped(pa, f"evb{i}", (128, 512), BF16) for i in range(2)]
    evbf = p.sbuf_scoped(pa, "evbf", (64, 512), F32)
    ps_mm2 = p.psum_scoped(pa, "ps_mm2", (128, 512), F32)

    zxb_done = {}
    for g in range(8):
        stage_done = emit_ln_stage(p, C, x_in, stage, xr, xsq, stats, xln, ps_tr, g,
                                   consts_ld if g == 0 else None)
        mm_v = {}
        ev_v = {}
        st_v = {}
        for roc in range(67):
            nr = 128 if roc < 66 else 64
            lb = roc % 2
            wl0 = [wT_done_ld] if (g == 0 and roc < 2) else []
            if roc >= 2:
                wl0.append((TSm, mm_v[roc - 2]))
            elif g > 0:
                wl0.append((TSm, TSm.n))
            p.op("gpsimd", "dma_start", out=wblk[lb][:],
                 in_=w_inT[roc].rearrange("(hc p) o -> p hc o", p=128),
                 wait=wl0, inc=(LG, 16))
            wld = LG.at()
            pmb = ps_mm if roc % 2 == 0 else ps_mm2
            w0 = [stage_done, wld]
            if roc >= 2:
                w0.append((SS, ev_v[roc - 2]))
            for hc in range(16):
                p.op("tensor", "matmul", pmb[0:nr, 0:512],
                     wblk[lb][:, hc * 128: hc * 128 + nr],
                     stage[:, hc * 512:(hc + 1) * 512],
                     start=(hc == 0), stop=(hc == 15),
                     wait=w0 if hc == 0 else [],
                     inc=(TSm, 1) if hc == 15 else None)
            mm_v[roc] = TSm.n
            if roc < 66:
                wv = [TSm.at()]
                if roc >= 2:
                    wv.append((LD, st_v[roc - 2]))
                elif g > 0:
                    wv.append((LD, LD.n))
                p.op("scalar", "activation", evb[lb][0:nr, :], pmb[0:nr, 0:512], AF.Copy,
                     scale=1.0, wait=wv, inc=(SS, 1))
                ev_v[roc] = SS.n
                p.op("sync", "dma_start", out=zxb[roc * 128: roc * 128 + nr,
                                                  g * 512:(g + 1) * 512],
                     in_=evb[lb][0:nr, :], wait=[SS.at()], inc=(LD, 16))
                st_v[roc] = LD.n
                if g == 7:
                    zxb_done[roc] = LD.n
            else:
                p.op("scalar", "activation", evbf[0:nr, :], pmb[0:nr, 0:512], AF.Copy,
                     scale=1.0, wait=[TSm.at()], inc=(SS, 1))
                ev_v[roc] = SS.n
                p.op("sync", "dma_start", out=dtf32_hbm[0:nr, g * 512:(g + 1) * 512],
                     in_=evbf[0:nr, :], wait=[SS.at()], inc=(LD, 16))
    pa.close()
    inproj_ld = LD.at()

    pc = ExitStack()
    xc = p.sbuf_scoped(pc, "xc", (128, T + 4), BF16)
    cw = p.sbuf_scoped(pc, "cw_sb", (128, 34 * 4), F32)
    cbv = p.sbuf_scoped(pc, "cbv_sb", (128, 34), F32)
    cacc = p.sbuf_scoped(pc, "cacc", (128, T), F32)
    cout = p.sbuf_scoped(pc, "cout", (128, T), BF16)
    p.op("sync", "dma_start", out=cw[:], in_=conv_w_in, inc=(LD, 16))
    p.op("sync", "dma_start", out=cbv[:], in_=conv_b_in, inc=(LD, 16))
    for cc in range(34):
        p.op("vector", "memset", xc[:, 0:3], 0.0, wait=[SS.at()] if cc > 0 else [],
             inc=(VS, 1))
        p.op("sync", "dma_start", out=xc[:, 3:3 + T],
             in_=zxb[DIN + cc * 128: DIN + (cc + 1) * 128, :],
             wait=[(LD, zxb_done[32 + cc]), VS.at()], inc=(LD, 16))
        p.wait("vector", LD, LD.n)
        p.op("vector", "tensor_scalar", cacc[:], xc[:, 0:T], cw[:, 4 * cc:4 * cc + 1],
             None, ALU.mult)
        for k in range(1, 4):
            p.op("vector", "scalar_tensor_tensor", cacc[:], xc[:, k:k + T],
                 cw[:, 4 * cc + k:4 * cc + k + 1], cacc[:], ALU.mult, ALU.add)
        p.op("vector", "drain")
        p.op("vector", "tensor_copy", xc[:, 0:1], xc[:, 0:1], inc=(VS, 1))
        p.op("scalar", "activation", cout[:], cacc[:], AF.Silu,
             bias=cbv[:, cc:cc + 1], scale=1.0, wait=[VS.at()], inc=(SS, 1))
        p.op("sync", "dma_start", out=convT[cc * 128:(cc + 1) * 128, :], in_=cout[:],
             wait=[SS.at()], inc=(LD, 16))
    pc.close()
    conv_ld = LD.at()

    pef = ExitStack()
    ps_row = p.psum_scoped(pef, "ps_row", (128, 256), F32)
    ps_row2 = ps_mm[0:128, 0:256]
    ps_y = p.psum_scoped(pef, "ps_y", (64, 256), F32)
    ps_y2 = p.psum_scoped(pef, "ps_y2", (64, 256), F32)
    ps_s2 = p.psum_scoped(pef, "ps_s2", (128, 64), F32)
    pe = ExitStack()
    dtw = p.sbuf_scoped(pe, "dtw", (64, T), BF16)
    dtraw = p.sbuf_scoped(pe, "dtraw", (64, T), F32)
    dtf = p.sbuf_scoped(pe, "dtf", (64, T), F32)
    dtt1 = p.sbuf_scoped(pe, "dtt1", (64, T), F32)
    dtt2 = p.sbuf_scoped(pe, "dtt2", (64, T), F32)
    dtb = p.sbuf_scoped(pe, "dtb", (64, 1), F32)
    alog = p.sbuf_scoped(pe, "alog", (64, 4), F32)
    dcol = p.sbuf_scoped(pe, "dcol", (64, 1), F32)
    lr1 = p.sbuf_scoped(pe, "lr1", (1, 64), F32)
    p.op("sync", "dma_start", out=dtraw[:], in_=dtf32_hbm,
         wait=[inproj_ld], inc=(LD, 16))
    p.op("sync", "dma_start", out=dtb[:], in_=dt_bias_in, inc=(LD, 16))
    p.op("sync", "dma_start", out=alog[:, 0:1], in_=a_log_in, inc=(LD, 16))
    p.op("sync", "dma_start", out=dcol[:], in_=d_in, inc=(LD, 16))
    p.wait("vector", LD, LD.n)
    p.op("vector", "tensor_scalar", dtf[:], dtraw[:], dtb[:], None, ALU.add, inc=(VS, 1))
    p.op("scalar", "activation", dtt1[:], dtf[:], AF.Abs, wait=[VS.at()])
    p.op("scalar", "activation", dtt1[:], dtt1[:], AF.Exp, scale=-1.0)
    p.op("scalar", "activation", dtt1[:], dtt1[:], AF.Ln,
         bias=C.cbank_ap(CB_ONE, 64), scale=1.0, inc=(SS, 1))
    p.op("vector", "tensor_scalar", dtt2[:], dtf[:], 0.0, None, ALU.max,
         wait=[SS.at()])
    p.op("vector", "tensor_add", dtf[:], dtt2[:], dtt1[:], inc=(VS, 1))
    p.op("scalar", "activation", alog[:, 1:2], alog[:, 0:1], AF.Exp, wait=[VS.at()],
         inc=(SS, 1))
    p.op("vector", "tensor_scalar", alog[:, 2:3], alog[:, 1:2], -1.0, None, ALU.mult,
         wait=[SS.at()], inc=(VS, 1))
    p.op("vector", "drain")
    p.op("vector", "tensor_scalar", adt[:], dtf[:], alog[:, 2:3], None, ALU.mult,
         inc=(VS, 1))
    p.op("vector", "tensor_copy", dtw[:], dtf[:], inc=(VS, 1))
    for lc in range(32):
        slot = lc % 2
        if lc >= 2:
            p.wait("tensor", SS, SS.n)
        p.op("tensor", "transpose", ps_tr[slot][:, 0:64],
             dtw[:, lc * 128:(lc + 1) * 128], C.ident[0:64, 0:64],
             wait=[VS.at()] if lc == 0 else [], inc=(TSm, 1))
        p.op("scalar", "activation", dt_tm[:, lc * 64:(lc + 1) * 64],
             ps_tr[slot][:, 0:64], AF.Copy, scale=1.0, wait=[TSm.at()], inc=(SS, 1))
    p.op("tensor", "transpose", ps_trf[0][0:1, 0:64], dcol[:], C.ident_f[0:64, 0:64],
         wait=[VS.at()], inc=(TSm, 1))
    p.op("vector", "tensor_copy", lr1[:], ps_trf[0][0:1, 0:64], wait=[TSm.at()],
         inc=(VS, 1))
    p.op("tensor", "matmul", ps_s2[:], ones_rf[:], lr1[:], start=True, stop=True,
         wait=[VS.at()], inc=(TSm, 1))
    p.op("scalar", "activation", db_t[:], ps_s2[:], AF.Copy, scale=1.0,
         wait=[TSm.at()], inc=(SS, 1))
    pe.close()
    dt_done = (SS.at(), VS.at())

    pf = ExitStack()
    xs_sb = p.sbuf_scoped(pf, "xs_sb", (128, 32 * 256), BF16)
    bct = p.sbuf_scoped(pf, "bct", (128, 2 * 256), BF16)
    b_tm = p.sbuf_scoped(pf, "b_tm", (128, 2 * 128), BF16)
    x_tm = p.sbuf_scoped(pf, "x_tm", (128, 2 * 4096), BF16)
    xdt = p.sbuf_scoped(pf, "xdt", (128, 2 * 4096), BF16)
    acs = p.sbuf_scoped(pf, "acs", (64, 256), F32)
    acst = p.sbuf_scoped(pf, "acst", (128, 2 * 64), F32)
    lastb = p.sbuf_scoped(pf, "lastb", (128, 64), F32)
    cdb = p.sbuf_scoped(pf, "cdb", (128, 64), F32)
    dec = p.sbuf_scoped(pf, "dec", (128, 2 * 64), F32)
    cbm = p.sbuf_scoped(pf, "cbm", (128, 2 * 256), BF16)
    dx = [p.sbuf_scoped(pf, f"dx{i}", (128, 2 * 256), F32) for i in range(2)]
    exm = [p.sbuf_scoped(pf, f"exm{i}", (128, 2 * 256), BF16) for i in range(2)]
    p_sb = [p.sbuf_scoped(pf, f"p_sb{i}", (128, 2 * 256), BF16) for i in range(2)]
    exa = [p.sbuf_scoped(pf, f"exa{i}", (128, 256), BF16) for i in range(2)]
    cet = [p.sbuf_scoped(pf, f"cet{i}", (128, 256), BF16) for i in range(2)]
    bd = [p.sbuf_scoped(pf, f"bd{i}", (128, 2 * 128), BF16) for i in range(2)]
    yt_sb = p.sbuf_scoped(pf, "yt_sb", (128, 32 * 256), BF16)
    state_bf = p.sbuf_scoped(pf, "state_bf", (128, NHM * 64), BF16)
    lr2 = p.sbuf_scoped(pf, "lr2", (1, 64), F32)

    p.op("vector", "memset", state[:], 0.0, wait=[dt_done[0]], inc=(VS, 1))

    for c in range(16):
        t0 = c * 256
        p.op("sync", "dma_start", out=xs_sb[:],
             in_=convT[0:DIN, t0:t0 + 256].rearrange("(hc p) t -> p hc t", p=128),
             wait=[conv_ld, TSm.at(), VS.at()], inc=(LD, 16))
        p.op("sync", "dma_start", out=bct[:, 0:256], in_=convT[DIN:DIN + 128, t0:t0 + 256],
             inc=(LD, 16))
        p.op("sync", "dma_start", out=bct[:, 256:512],
             in_=convT[DIN + 128:DIN + 256, t0:t0 + 256], inc=(LD, 16))
        ld_c = LD.at()
        p.op("vector", "memset", dx[0][0:64, 0:256], 1.0, wait=[ld_c])
        p.op("vector", "drain")
        p.op("vector", "tensor_tensor_scan", acs[:], dx[0][0:64, 0:256], adt[:, t0:t0 + 256],
             0.0, ALU.mult, ALU.add, inc=(VS, 1))
        p.op("vector", "drain")
        for lc in range(2):
            p.op("tensor", "transpose", ps_trf[lc][:, 0:64],
                 acs[:, lc * 128:(lc + 1) * 128], C.ident_f[0:64, 0:64],
                 wait=[VS.at(), SS.at()], inc=(TSm, 1))
            p.op("scalar", "activation", acst[:, lc * 64:(lc + 1) * 64],
                 ps_trf[lc][:, 0:64], AF.Copy, scale=1.0, wait=[TSm.at()], inc=(SS, 1))
        p.op("tensor", "transpose", ps_trf[0][0:1, 0:64], acs[:, 255:256],
             C.ident_f[0:64, 0:64], wait=[SS.at()], inc=(TSm, 1))
        p.op("vector", "tensor_copy", lr2[:], ps_trf[0][0:1, 0:64], wait=[TSm.at()],
             inc=(VS, 1))
        p.op("vector", "drain")
        p.op("tensor", "matmul", ps_s2[:], ones_rf[:], lr2[:], start=True, stop=True,
             wait=[VS.at()], inc=(TSm, 1))
        p.op("scalar", "activation", lastb[:], ps_s2[:], AF.Copy, scale=1.0,
             wait=[TSm.at()], inc=(SS, 1))
        p.op("scalar", "activation", cdb[:], lastb[:], AF.Exp, inc=(SS, 1))
        for lc in range(2):
            p.op("vector", "tensor_sub", dx[1][:, 0:64], lastb[:],
                 acst[:, lc * 64:(lc + 1) * 64], wait=[SS.at()])
            p.op("vector", "drain")
            p.op("vector", "tensor_copy", dx[1][:, 64:128], dx[1][:, 0:64], inc=(VS, 1))
            p.op("scalar", "activation", dec[:, lc * 64:(lc + 1) * 64], dx[1][:, 0:64],
                 AF.Exp, wait=[VS.at()], inc=(SS, 1))
        for lc in range(2):
            slot = lc
            p.op("tensor", "transpose", ps_tr[slot][:],
                 bct[:, lc * 128:(lc + 1) * 128], C.ident[:],
                 wait=[SS.at(), VS.at()], inc=(TSm, 1))
            p.op("scalar", "activation", b_tm[:, lc * 128:(lc + 1) * 128],
                 ps_tr[slot][:], AF.Copy, scale=1.0, wait=[TSm.at()], inc=(SS, 1))
        for hc in range(32):
            for lc in range(2):
                slot = (hc * 2 + lc) % 2
                if hc * 2 + lc >= 2:
                    p.wait("tensor", SS, SS.n)
                p.op("tensor", "transpose", ps_tr[slot][:],
                     xs_sb[:, hc * 256 + lc * 128: hc * 256 + (lc + 1) * 128],
                     C.ident[:], inc=(TSm, 1))
                p.op("scalar", "activation",
                     x_tm[:, lc * 4096 + hc * 128: lc * 4096 + (hc + 1) * 128],
                     ps_tr[slot][:], AF.Copy, scale=1.0, wait=[TSm.at()], inc=(SS, 1))
        xtm_done = SS.at()
        for lc in range(2):
            for h in range(NHM):
                p.op("vector", "tensor_scalar",
                     xdt[:, lc * 4096 + h * 64: lc * 4096 + (h + 1) * 64],
                     x_tm[:, lc * 4096 + h * 64: lc * 4096 + (h + 1) * 64],
                     dt_tm[:, (c * 2 + lc) * 64 + h: (c * 2 + lc) * 64 + h + 1],
                     None, ALU.mult,
                     wait=[xtm_done] if (lc == 0 and h == 0) else [])
        p.op("vector", "tensor_copy", dx[0][0:1, 0:1], dx[0][0:1, 0:1], inc=(VS, 1))
        xdt_done = VS.at()
        for sc in range(2):
            p.op("tensor", "matmul", ps_row[:], bct[:, sc * 128:(sc + 1) * 128],
                 bct[:, 256:512], start=True, stop=True,
                 wait=[xdt_done] if sc == 0 else [VS.at()], inc=(TSm, 1))
            p.op("vector", "tensor_mul", cbm[:, sc * 256:(sc + 1) * 256], ps_row[:],
                 tri2[:, sc * 256:(sc + 1) * 256], wait=[TSm.at()], inc=(VS, 1))
        p.op("vector", "tensor_copy", state_bf[:], state[:], wait=[VS.at()], inc=(VS, 1))
        dxv = {}; exav = {}; pv = {}; cetv = {}; ytv = {}; rowv = {}; stv = {}
        yqv = {}; smv = {}
        for h in range(NHM):
            par = h % 2
            rb = ps_row[:, 0:256] if par == 0 else ps_row2
            w0 = []
            if h >= 2:
                w0 += [(VS, dxv[h - 2]), (SS, exav[h - 2])]
            else:
                w0 += [(VS, VS.at()[1]), (SS, SS.at()[1])]
            p.op("tensor", "matmul", rb, sel64[:, h * 128:(h + 1) * 128],
                 acs[:], start=True, stop=True, wait=w0, inc=(TSm, 1))
            rowv[h] = TSm.n
            for sc in range(2):
                p.op("vector", "tensor_scalar", dx[par][:, sc * 256:(sc + 1) * 256],
                     rb,
                     acst[:, sc * 64 + h: sc * 64 + h + 1], 0.0,
                     ALU.subtract, ALU.min,
                     wait=[(TSm, rowv[h])] if sc == 0 else [], inc=(VS, 1))
                p.op("scalar", "activation", exm[par][:, sc * 256:(sc + 1) * 256],
                     dx[par][:, sc * 256:(sc + 1) * 256], AF.Exp,
                     wait=[VS.at()], inc=(SS, 1))
                wp = [SS.at()]
                if sc == 0 and h >= 2:
                    wp.append((TSm, yqv[h - 2]))
                p.op("vector", "tensor_mul", p_sb[par][:, sc * 256:(sc + 1) * 256],
                     exm[par][:, sc * 256:(sc + 1) * 256],
                     cbm[:, sc * 256:(sc + 1) * 256], wait=wp, inc=(VS, 1))
            dxv[h] = VS.n
            pv[h] = VS.n
            p.op("scalar", "activation", exa[par][:], rb, AF.Exp,
                 wait=[VS.at()], inc=(SS, 1))
            exav[h] = SS.n
            wc = [SS.at()]
            if h >= 2:
                wc.append((TSm, yqv[h - 2]))
            p.op("vector", "tensor_mul", cet[par][:], bct[:, 256:512], exa[par][:],
                 wait=wc, inc=(VS, 1))
            cetv[h] = VS.n
            yb = ps_y if par == 0 else ps_y2
            wy = [(VS, pv[h])]
            if h >= 2:
                wy.append((VS, ytv[h - 2]))
            for sc in range(2):
                p.op("tensor", "matmul", yb[:],
                     xdt[:, sc * 4096 + h * 64: sc * 4096 + (h + 1) * 64],
                     p_sb[par][:, sc * 256:(sc + 1) * 256],
                     start=(sc == 0), stop=False, wait=wy if sc == 0 else [])
            p.op("tensor", "matmul", yb[:], state_bf[:, h * 64:(h + 1) * 64], cet[par][:],
                 start=False, stop=True, wait=[(VS, cetv[h])], inc=(TSm, 1))
            yq = TSm.n
            yqv[h] = yq
            hc, off = h // 2, (h % 2) * 64
            p.op("vector", "scalar_tensor_tensor",
                 yt_sb[off:off + 64, hc * 256:(hc + 1) * 256],
                 xs_sb[off:off + 64, hc * 256:(hc + 1) * 256],
                 db_t[off:off + 64, h:h + 1], yb[:], ALU.mult, ALU.add,
                 wait=[(TSm, yq)], inc=(VS, 1))
            ytv[h] = VS.n
            for sc in range(2):
                wb = []
                if sc == 0 and h >= 2:
                    wb.append((TSm, smv[h - 2]))
                p.op("vector", "tensor_scalar", bd[par][:, sc * 128:(sc + 1) * 128],
                     b_tm[:, sc * 128:(sc + 1) * 128],
                     dec[:, sc * 64 + h: sc * 64 + h + 1], None, ALU.mult, wait=wb,
                     inc=(VS, 1))
            bdv = VS.n
            ws = [(VS, bdv)]
            if h >= 1:
                ws.append((VS, stv[h - 1]))
            for sc in range(2):
                p.op("tensor", "matmul", ps_s2[:], bd[par][:, sc * 128:(sc + 1) * 128],
                     xdt[:, sc * 4096 + h * 64: sc * 4096 + (h + 1) * 64],
                     start=(sc == 0), stop=(sc == 1),
                     wait=ws if sc == 0 else [], inc=(TSm, 1) if sc == 1 else None)
                if sc == 1:
                    smv[h] = TSm.n
            p.op("vector", "scalar_tensor_tensor", state[:, h * 64:(h + 1) * 64],
                 state[:, h * 64:(h + 1) * 64], cdb[:, h:h + 1], ps_s2[:],
                 ALU.mult, ALU.add, wait=[TSm.at()], inc=(VS, 1))
            stv[h] = VS.n
        p.op("sync", "dma_start",
             out=y_hbm[c].rearrange("(hc p) t -> p hc t", p=128),
             in_=yt_sb[:].rearrange("p (hc t) -> p hc t", hc=32),
             wait=[VS.at()], inc=(LD, 16))
    pf.close()
    pef.close()
    y_ld = LD.at()

    pg = ExitStack()
    y_sb = p.sbuf_scoped(pg, "y_sb", (128, 32 * 512), BF16)
    z_sb = p.sbuf_scoped(pg, "z_sb", (128, 32 * 512), BF16)
    sqb = [p.sbuf_scoped(pg, f"sqb{i}", (128, 512), BF16) for i in range(2)]
    ssq_f = p.sbuf_scoped(pg, "ssq_f", (1, 512), F32)
    rst = p.sbuf_scoped(pg, "rst", (1, 512), BF16)
    rb = p.sbuf_scoped(pg, "rb", (128, 512), BF16)
    nw = p.sbuf_scoped(pg, "nw32", (128, 32), F32)
    wo = [p.sbuf_scoped(pg, f"wo{i}", (128, 32 * 128), BF16) for i in range(2)]
    y2 = p.sbuf_scoped(pg, "y2", (128, 16 * 512), BF16)
    xres = p.sbuf_scoped(pg, "xres", (128, D), F32)
    ps_dn2 = p.psum_scoped(pg, "ps_dn2", (1, 512), F32)
    ps_r2 = p.psum_scoped(pg, "ps_r2", (128, 512), F32)
    ps_mm2g = p.psum_scoped(pg, "ps_mm2g", (128, 512), F32)

    p.op("sync", "dma_start", out=nw[:], in_=nw_in, inc=(LD, 16))
    for g in range(8):
        for half in range(2):
            cc = 2 * g + half
            p.op("sync", "dma_start",
                 out=y_sb[:, 0:32 * 512].rearrange("p (hc t) -> p hc t", hc=32)[:, :, half * 256:(half + 1) * 256],
                 in_=y_hbm[cc].rearrange("(hc p) t -> p hc t", p=128),
                 wait=[y_ld, TSm.at()] if half == 0 else [], inc=(LD, 16))
        for i in range(4):
            p.op("sync", "dma_start",
                 out=z_sb[:].rearrange("p (hc t) -> p hc t", hc=32)[:, i * 8:(i + 1) * 8, :],
                 in_=zxb[i * 1024:(i + 1) * 1024, g * 512:(g + 1) * 512]
                     .rearrange("(hc p) t -> p hc t", p=128),
                 inc=(LD, 16))
        gld = LD.at()
        p.wait("scalar", LD, LD.n)
        p.op("scalar", "activation", z_sb[:], z_sb[:], AF.Silu, inc=(SS, 1))
        p.op("vector", "tensor_mul", y_sb[:], y_sb[:], z_sb[:], wait=[SS.at()],
             inc=(VS, 1))
        y32_done = VS.at()
        sq_mm = {}
        for hc in range(32):
            wq = [y32_done] if hc == 0 else []
            if hc >= 2:
                wq.append((TSm, sq_mm[hc - 2]))
            p.op("scalar", "activation", sqb[hc % 2][:], y_sb[:, hc * 512:(hc + 1) * 512],
                 AF.Square, wait=wq, inc=(SS, 1))
            p.op("tensor", "matmul", ps_dn2[:], C.ones_col[:], sqb[hc % 2][:],
                 start=(hc == 0), stop=(hc == 31), wait=[SS.at()],
                 inc=(TSm, 1))
            sq_mm[hc] = TSm.n
        p.op("scalar", "activation", ssq_f[:], ps_dn2[:], AF.Ln, scale=1.0 / DIN,
             bias=C.cbank_ap(CB_EPS_LN, 1), wait=[TSm.at()])
        p.op("scalar", "activation", rst[:], ssq_f[:], AF.Exp, scale=-0.5, inc=(SS, 1))
        p.op("tensor", "matmul", ps_r2[:], C.ones_row[:], rst[:], start=True, stop=True,
             wait=[SS.at()], inc=(TSm, 1))
        p.op("scalar", "activation", rb[:], ps_r2[:], AF.Copy, scale=1.0,
             wait=[TSm.at()], inc=(SS, 1))
        rb_done = SS.at()
        for hc in range(32):
            p.op("vector", "scalar_tensor_tensor", y_sb[:, hc * 512:(hc + 1) * 512],
                 y_sb[:, hc * 512:(hc + 1) * 512], nw[:, hc:hc + 1], rb[:],
                 ALU.mult, ALU.mult, wait=[rb_done] if hc == 0 else [], inc=(VS, 1))
        nrm_done = VS.at()
        op_mm = {}
        op_ev = {}
        for oc in range(16):
            lb = oc % 2
            wl = [wT_done_ld]
            if oc >= 2:
                wl.append((TSm, op_mm[oc - 2]))
            else:
                wl += [(TSm, TSm.n)] + ([(VS, VS.n), (SS, SS.n)] if g == 0 else [])
            p.op("gpsimd", "dma_start", out=wo[lb][:],
                 in_=w_outT[oc].rearrange("(dc p) o -> p dc o", p=128),
                 wait=wl, inc=(LG, 16))
            wld = LG.at()
            pb = ps_mm if oc % 2 == 0 else ps_mm2g
            w0 = [nrm_done, wld]
            if oc >= 2:
                w0.append((SS, op_ev[oc - 2]))
            else:
                w0.append((SS, SS.n))
            for dc in range(32):
                p.op("tensor", "matmul", pb[:, 0:512],
                     wo[lb][:, dc * 128:(dc + 1) * 128],
                     y_sb[:, dc * 512:(dc + 1) * 512],
                     start=(dc == 0), stop=(dc == 31),
                     wait=w0 if dc == 0 else [],
                     inc=(TSm, 1) if dc == 31 else None)
            op_mm[oc] = TSm.n
            p.op("scalar", "activation", y2[:, oc * 512:(oc + 1) * 512],
                 pb[:, 0:512], AF.Copy, scale=1.0, wait=[TSm.at()], inc=(SS, 1))
            op_ev[oc] = SS.n
        y2_done = SS.at()
        for tt in range(4):
            trow = g * 512 + tt * 128
            p.op("sync", "dma_start", out=xres[:], in_=x_in[trow:trow + 128, :],
                 wait=[ST.at()], inc=(LD, 16))
            p.wait("vector", LD, LD.n)
            for oc in range(16):
                slot = oc % 2
                if oc >= 2:
                    p.wait("tensor", VS, VS.n)
                p.op("tensor", "transpose", ps_tr[slot][:],
                     y2[:, oc * 512 + tt * 128: oc * 512 + tt * 128 + 128],
                     C.ident[:],
                     wait=[y2_done, VS.at()] if oc < 2 else [], inc=(TSm, 1))
                p.op("vector", "tensor_add", xres[:, oc * 128:(oc + 1) * 128],
                     xres[:, oc * 128:(oc + 1) * 128], ps_tr[slot][:],
                     wait=[TSm.at()], inc=(VS, 1))
            p.op("sync", "dma_start", out=x_out[trow:trow + 128, :], in_=xres[:],
                 wait=[VS.at()], inc=(ST, 16))
    pg.close()
    p.wait("sync", ST, ST.n)
    p.pop_scope()
    return p


def run_l1attn(x, in_proj_w, conv_w, conv_b, dt_bias, A_log, Dp, norm_w, out_proj_w, debug=False):
    c = _consts()
    p = build_l1attn(debug=debug)
    nc = p.finalize()
    cw2d = np.zeros((128, 34 * 4), np.float32)
    cb2d = np.zeros((128, 34), np.float32)
    for cc in range(34):
        cw2d[:, 4 * cc:4 * (cc + 1)] = conv_w[cc * 128:(cc + 1) * 128, :]
        cb2d[:, cc] = conv_b[cc * 128:(cc + 1) * 128]
    sel64 = np.zeros((64, 64 * 128), np.float32)
    for h in range(64):
        sel64[h, h * 128:(h + 1) * 128] = 1.0
    sp = np.arange(128)[:, None]
    lv = np.arange(256)[None, :]
    tri2 = np.concatenate([(sp <= lv).astype(np.float32),
                           (128 + sp <= lv).astype(np.float32)], axis=1)
    in_maps = []
    for b in range(NCORES):
        in_maps.append({
            "x_in": np.ascontiguousarray(x[b]),
            "in_proj_w": in_proj_w, "out_proj_w": out_proj_w,
            "conv_w2d": cw2d, "conv_b2d": cb2d,
            "dt_bias2d": dt_bias.reshape(64, 1), "a_log2d": A_log.reshape(64, 1),
            "d2d": Dp.reshape(64, 1),
            "norm_w2d": np.ascontiguousarray(norm_w.reshape(32, 128).T),
            "sel64": sel64, "tri2": tri2,
            "c_ident": c["ident"], "c_cbank": c["cbank"],
        })
    res = run_bass_kernel_spmd(nc, in_maps, core_ids=list(range(NCORES)))
    out = np.stack([res.results[b]["x_out"] for b in range(NCORES)])
    if debug:
        return out, res.results
    return out



_CONSTS = None

def _consts():
    global _CONSTS
    if _CONSTS is None:
        _CONSTS = host_consts()
    return _CONSTS


def deinterleave_wattn(w_attn):
    w = np.array(w_attn, np.float32, copy=True)
    perm = np.concatenate([np.arange(0, 128, 2), np.arange(1, 128, 2)])
    for base in range(0, 2048 + 512, 128):
        w[:, base:base + 128] = w[:, base + perm]
    return w


def run_l0attn(x, w_attn, w_proj, norm_w, debug=False):
    c = _consts()
    w_attn = deinterleave_wattn(w_attn)
    p = build_l0attn(debug=debug)
    nc = p.finalize()
    in_maps = []
    nw2d = np.ascontiguousarray(norm_w.reshape(16, 128).T)
    for b in range(NCORES):
        in_maps.append({
            "x_in": np.ascontiguousarray(x[b]),
            "w_attn": w_attn, "w_proj": w_proj, "norm_w": nw2d,
            "rope_c": c["rope_c"], "rope_s": c["rope_s"],
            "masks_diag": c["masks_diag"],
            "c_ident": c["ident"], "c_cbank": c["cbank"],
        })
    res = run_bass_kernel_spmd(nc, in_maps, core_ids=list(range(NCORES)))
    out = np.stack([res.results[b]["x_out"] for b in range(NCORES)])
    if debug:
        return out, res.results
    return out


def build_fused():
    p = P("mk")
    C = Common(p)
    x_in = p.dram("x_full_in", (T, D), F32, kind="ExternalInput").ap()
    x_out = p.dram("x_full_out", (T, D), F32, kind="ExternalOutput").ap()
    xA = p.dram("x_chain_a", (T, D), F32, kind="Internal").ap()
    xB = p.dram("x_chain_b", (T, D), F32, kind="Internal").ap()
    build_l0attn(fused=(p, C, x_in, xA))
    p.barrier(C)
    build_l0ffn(fused=(p, C, xA, xB))
    p.barrier(C)
    build_l1attn(fused=(p, C, xB, xA))
    p.barrier(C)
    build_l1ffn(fused=(p, C, xA, x_out))
    return p


_FUSED_NC = None

def _fused_nc():
    global _FUSED_NC
    if _FUSED_NC is None:
        p = build_fused()
        _FUSED_NC = p.finalize()
    return _FUSED_NC


def kernel(**inputs):
    f32 = lambda k: np.asarray(inputs[k], np.float32)
    c = _consts()
    nc = _fused_nc()
    w_attn = deinterleave_wattn(f32("L0_attn_w_attn"))
    gup_a = np.concatenate([np.asarray(inputs["L0_ffn_gate_up_packed"]),
                            np.asarray(inputs["L0_ffn_shared_gate_up_packed"])[None]], 0)
    gus_a = np.concatenate([f32("L0_ffn_gate_up_scales"),
                            f32("L0_ffn_shared_gate_up_scales")[None]], 0)
    dnp_a = np.concatenate([np.asarray(inputs["L0_ffn_down_packed"]),
                            np.asarray(inputs["L0_ffn_shared_down_packed"])[None]], 0)
    dns_a = np.concatenate([f32("L0_ffn_down_scales"),
                            f32("L0_ffn_shared_down_scales")[None]], 0)
    pp = np.arange(128)
    sel4 = np.stack([(((pp % 64) >> 4) == k) for k in range(4)]).astype(np.float32)
    sel16 = np.zeros((16, 16 * 128), np.float32)
    for e in range(16):
        sel16[e, e * 128:(e + 1) * 128] = 1.0
    sel64 = np.zeros((64, 64 * 128), np.float32)
    for h in range(64):
        sel64[h, h * 128:(h + 1) * 128] = 1.0
    sp = np.arange(128)[:, None]
    lv = np.arange(256)[None, :]
    tri2 = np.concatenate([(sp <= lv).astype(np.float32),
                           (128 + sp <= lv).astype(np.float32)], axis=1)
    conv_w = f32("L1_attn_conv_w")
    conv_b = f32("L1_attn_conv_b")
    cw2d = np.zeros((128, 34 * 4), np.float32)
    cb2d = np.zeros((128, 34), np.float32)
    for cc in range(34):
        cw2d[:, 4 * cc:4 * (cc + 1)] = conv_w[cc * 128:(cc + 1) * 128, :]
        cb2d[:, cc] = conv_b[cc * 128:(cc + 1) * 128]
    x = f32("L0_attn_x")
    base = {
        "w_attn": w_attn, "w_proj": f32("L0_attn_w_proj"),
        "norm_w": np.ascontiguousarray(f32("L0_attn_norm_w").reshape(16, 128).T),
        "rope_c": c["rope_c"], "rope_s": c["rope_s"], "masks_diag": c["masks_diag"],
        "c_ident": c["ident"], "c_cbank": c["cbank"],
        "gate_w": f32("L0_ffn_gate_w"),
        "gup": gup_a, "gus": gus_a, "dnp": dnp_a, "dns": dns_a,
        "sel4": sel4, "sel16": sel16, "shbank": _shbank(),
        "in_proj_w": f32("L1_attn_in_proj_w"), "out_proj_w": f32("L1_attn_out_proj_w"),
        "conv_w2d": cw2d, "conv_b2d": cb2d,
        "dt_bias2d": f32("L1_attn_dt_bias").reshape(64, 1),
        "a_log2d": f32("L1_attn_A_log").reshape(64, 1),
        "d2d": f32("L1_attn_D").reshape(64, 1),
        "norm_w2d": np.ascontiguousarray(f32("L1_attn_norm_w").reshape(32, 128).T),
        "sel64": sel64, "tri2": tri2,
        "weight": f32("L1_ffn_weight"),
        "bias2d": np.ascontiguousarray(f32("L1_ffn_bias").reshape(16, 128).T),
    }
    in_maps = []
    for b in range(NCORES):
        m = dict(base)
        m["x_full_in"] = np.ascontiguousarray(x[b])
        in_maps.append(m)
    res = run_bass_kernel_spmd(nc, in_maps, core_ids=list(range(NCORES)))
    return np.stack([res.results[b]["x_full_out"] for b in range(NCORES)]).astype(np.float32)


# version 18 (final)
# speedup vs baseline: 1.1710x; 1.0172x over previous
import numpy as np
import ml_dtypes
from contextlib import ExitStack
import concourse.bass as bass
import concourse.mybir as mybir
from concourse.bass_utils import run_bass_kernel_spmd

F32 = mybir.dt.float32
BF16 = mybir.dt.bfloat16
I32 = mybir.dt.int32
U8 = mybir.dt.uint8
AF = mybir.ActivationFunctionType
ALU = mybir.AluOpType
AX = mybir.AxisListType

T = 4096
D = 2048
NCORES = 8

ENGINES = ("sync", "gpsimd", "vector", "scalar", "tensor")


class Sem:
    __slots__ = ("h", "n", "name")

    def __init__(self, h, name):
        self.h = h
        self.n = 0
        self.name = name

    def at(self):
        return (self, self.n)


class P:

    def __init__(self, name):
        self.nc = bass.Bass("TRN2", target_bir_lowering=False, debug=False)
        self.stack = ExitStack()
        self.ops = {e: [] for e in ENGINES}
        self.n_inst = 0
        self.scope = None
        self.prefix = ""

    def push_scope(self, prefix):
        self.scope = ExitStack()
        self.prefix = prefix

    def pop_scope(self):
        self.scope.close()
        self.scope = None
        self.prefix = ""

    def barrier(self, C):
        sems = [C.LD, C.LG, C.ST, C.TS, C.VS, C.SS]
        for eng in ENGINES:
            for s in sems:
                if s.n:
                    self.wait(eng, s, s.n)

    def sem(self, name):
        h = self.stack.enter_context(self.nc.semaphore(name))
        return Sem(h, name)

    def sbuf(self, name, shape, dtype):
        name = self.prefix + name
        if self.scope is not None:
            return self.scope.enter_context(self.nc.sbuf_tensor(name, list(shape), dtype))
        return self.nc.alloc_sbuf_tensor(name, list(shape), dtype)

    def sbuf_scoped(self, st, name, shape, dtype):
        return st.enter_context(self.nc.sbuf_tensor(self.prefix + name, list(shape), dtype))

    def psum(self, name, shape, dtype=F32):
        name = self.prefix + name
        if self.scope is not None:
            return self.scope.enter_context(self.nc.psum_tensor(name, list(shape), dtype))
        return self.nc.alloc_psum_tensor(name, list(shape), dtype)

    def psum_scoped(self, st, name, shape, dtype=F32):
        return st.enter_context(self.nc.psum_tensor(self.prefix + name, list(shape), dtype))

    def dram(self, name, shape, dtype, kind="Internal"):
        if kind == "Internal":
            name = self.prefix + name
        return self.nc.dram_tensor(name, list(shape), dtype, kind=kind)

    def op(self, eng, meth, *args, wait=(), inc=None, lowp=False, **kw):
        self.n_inst += 1
        if inc is not None:
            inc[0].n += inc[1]
        wait = tuple((s, v) for (s, v) in wait)

        if meth in ("transpose", "reciprocal"):
            lowp = True

        def f(e, meth=meth, args=args, kw=kw, wait=wait, inc=inc, lowp=lowp):
            for s, v in wait:
                e.wait_ge(s.h, v)
            try:
                if lowp:
                    with self.nc.allow_low_precision(reason="intentional bf16 path"):
                        ins = getattr(e, meth)(*args, **kw)
                else:
                    ins = getattr(e, meth)(*args, **kw)
            except Exception as ex:
                raise RuntimeError(f"op {meth} failed: {ex}") from ex
            if inc is not None:
                ins.then_inc(inc[0].h, inc[1])

        self.ops[eng].append(f)

    def wait(self, eng, sem, val):
        self.n_inst += 1

        def f(e, sem=sem, val=val):
            e.wait_ge(sem.h, val)

        self.ops[eng].append(f)

    def raw(self, eng, fn):
        self.ops[eng].append(fn)

    def finalize(self):
        with self.stack:
            with self.nc.Block() as block:
                for eng in ENGINES:
                    fns = self.ops[eng]
                    if not fns:
                        continue
                    getattr(block, eng)(lambda e, fns=fns: [f(e) for f in fns])
        return self.nc



def host_consts():
    c = {}
    c["ident"] = np.eye(128, dtype=np.float32)
    hd = 128
    inv = 1.0 / (10000.0 ** (np.arange(0, hd, 2, dtype=np.float64) / hd))
    ang = np.arange(T, dtype=np.float64)[:, None] * inv[None, :]
    c["rope_c"] = np.cos(ang).T.astype(np.float32).copy()
    c["rope_s"] = np.sin(ang).T.astype(np.float32).copy()
    kp = np.arange(128)[:, None]
    qf = np.arange(512)[None, :]
    m = np.concatenate([(qf >= kp + 128 * r).astype(np.float32) for r in range(4)], axis=1)
    c["masks_diag"] = m
    cvals = [-2.0, -6.0, 1e-5, 1e-6, 0.0, 1.0, 0.0, 0.0]
    c["cbank"] = np.tile(np.array(cvals, np.float32)[None, :], (128, 1))
    return c

CB_NEG2, CB_NEG6, CB_EPS_LN, CB_EPS_RMS, CB_ZERO, CB_ONE = 0, 1, 2, 3, 4, 5



class Common:

    def __init__(self, p: P):
        self.p = p
        nc = p.nc
        self.d_ident = p.dram("c_ident", (128, 128), F32, kind="ExternalInput").ap()
        self.d_cbank = p.dram("c_cbank", (128, 8), F32, kind="ExternalInput").ap()
        self.ident = p.sbuf("ident_bf", (128, 128), BF16)
        self.ident_f = p.sbuf("ident_f", (128, 128), F32)
        self.cb = p.sbuf("cbank", (128, 8), F32)
        self.ones_col = p.sbuf("ones_col", (128, 1), BF16)
        self.ones_row = p.sbuf("ones_row", (1, 128), BF16)
        self.LD = p.sem("LD")
        self.LG = p.sem("LG")
        self.ST = p.sem("ST")
        self.TS = p.sem("TSm")
        self.VS = p.sem("VSm")
        self.SS = p.sem("SSm")

    def load_consts(self):
        if getattr(self, "_consts_loaded", False):
            return
        self._consts_loaded = True
        p = self.p
        p.op("gpsimd", "dma_start", out=self.ident[:], in_=self.d_ident, inc=(self.LG, 16))
        p.op("sync", "dma_start", out=self.ident_f[:], in_=self.d_ident, inc=(self.LD, 16))
        p.op("sync", "dma_start", out=self.cb[:], in_=self.d_cbank, inc=(self.LD, 16))
        p.op("vector", "memset", self.ones_col[:], 1.0)
        p.op("vector", "memset", self.ones_row[:], 1.0, inc=(self.VS, 1))

    def cbank_ap(self, idx, nparts=1):
        return self.cb[0:nparts, idx:idx + 1]

    def ln_tile(self, xr, xsq, xln_out, stats, eps_idx=CB_EPS_LN):
        p, cb = self.p, self.cb
        VS, SS = self.VS, self.SS
        for i in range(4):
            p.op("vector", "bn_stats", xsq[:, 8 + i * 6: 8 + (i + 1) * 6],
                 xr[:, i * 512:(i + 1) * 512])
        p.op("vector", "drain")
        p.op("vector", "bn_aggr", stats[:, 2:4], xsq[:, 8:32], inc=(VS, 1))
        p.op("vector", "drain")
        p.op("scalar", "activation", stats[:, 5:6], stats[:, 3:4], AF.Ln,
             bias=self.cbank_ap(eps_idx, 128), scale=1.0, wait=[VS.at()])
        p.op("scalar", "activation", stats[:, 6:7], stats[:, 5:6], AF.Exp,
             scale=-0.5, inc=(SS, 1))
        p.op("vector", "tensor_scalar", xln_out[:], xr[:], stats[:, 2:3], stats[:, 6:7],
             ALU.subtract, ALU.mult, wait=[SS.at()], inc=(VS, 1))



def build_l0attn(debug=False, fused=None):
    if fused is None:
        p = P("l0attn")
        C = Common(p)
        x_in_ap = x_out_ap = None
    else:
        p, C, x_in_ap, x_out_ap = fused
    p.push_scope("a0_")
    x_in = x_in_ap if x_in_ap is not None else p.dram("x_in", (T, D), F32, kind="ExternalInput").ap()
    x_out = x_out_ap if x_out_ap is not None else p.dram("x_out", (T, D), F32, kind="ExternalOutput").ap()
    w_attn = p.dram("w_attn", (D, 3072), F32, kind="ExternalInput").ap()
    w_proj = p.dram("w_proj", (D, D), F32, kind="ExternalInput").ap()
    nw_in = p.dram("norm_w", (128, 16), F32, kind="ExternalInput").ap()
    rope_c_in = p.dram("rope_c", (64, T), F32, kind="ExternalInput").ap()
    rope_s_in = p.dram("rope_s", (64, T), F32, kind="ExternalInput").ap()
    masks_in = p.dram("masks_diag", (128, 2048), F32, kind="ExternalInput").ap()
    dbg_kind = "ExternalOutput" if debug else "Internal"
    if debug:
        xr_dump = p.dram("xr_dump", (4, 128, D), F32, kind="ExternalOutput").ap()
        st_dump = p.dram("st_dump", (4, 128, 8), F32, kind="ExternalOutput").ap()
    qT_hbm = p.dram("qT_scratch", (8, D, 512), BF16, kind=dbg_kind).ap()
    xlnT_hbm = p.dram("xlnT_scratch", (8, D, 512), BF16, kind=dbg_kind).ap()

    stage = p.sbuf("stage", (128, 16 * 512), BF16)
    wbuf = p.sbuf("wbuf", (128, 16 * 1024), BF16)
    kT = p.sbuf("kT", (128, 4 * T), BF16)
    v_tm = p.sbuf("v_tm", (128, 4 * T), BF16)
    masks = p.sbuf("masks_sb", (128, 2048), BF16)
    nw = p.sbuf("nw_sb", (128, 16), F32)

    ps_s = [p.psum(f"ps_s{i}", (128, 512), F32) for i in range(2)]
    ps_av = p.psum("ps_av", (128, 512), F32)
    ps_dn = p.psum("ps_dn", (1, 512), F32)
    ps_r = p.psum("ps_r", (128, 512), F32)
    ps_proj = p.psum("ps_proj", (128, 512), F32)
    ps_tr = [p.psum(f"ps_tr{i}", (128, 128), BF16) for i in range(2)]

    LD, LG, ST, TSm, VS, SS = C.LD, C.LG, C.ST, C.TS, C.VS, C.SS
    C.load_consts()
    p.op("sync", "dma_start", out=nw[:], in_=nw_in, inc=(LD, 16))
    p.op("gpsimd", "dma_start", out=masks[:], in_=masks_in, inc=(LG, 16))
    consts_ld = [LD.at(), LG.at(), VS.at()]

    def qk_cols(base):
        return wbuf[:, base:base + 128]

    ab = ExitStack()
    xr = [p.sbuf_scoped(ab, f"xr{i}", (128, D), F32) for i in range(2)]
    xsq = p.sbuf_scoped(ab, "xsq", (128, D), F32)
    stats = p.sbuf_scoped(ab, "stats", (128, 8), F32)
    xln = [p.sbuf_scoped(ab, f"xln{i}", (128, D), BF16) for i in range(2)]
    qT = p.sbuf_scoped(ab, "qT", (128, 8 * 512), BF16)
    rope_c = p.sbuf_scoped(ab, "rope_c_sb", (64, T), BF16)
    rope_s = p.sbuf_scoped(ab, "rope_s_sb", (64, T), BF16)
    rtmp = [p.sbuf_scoped(ab, f"rtmp{i}", (64, 4 * 512), F32) for i in range(2)]
    vt_stage = p.sbuf_scoped(ab, "vt_stage", (128, 512), BF16)

    p.op("gpsimd", "dma_start", out=rope_c[:], in_=rope_c_in, inc=(LG, 16))
    p.op("gpsimd", "dma_start", out=rope_s[:], in_=rope_s_in, inc=(LG, 16))
    rope_ld = LG.at()

    for h in range(16):
        p.op("gpsimd", "dma_start", out=wbuf[:, h * 1024:(h + 1) * 1024],
             in_=w_attn[h * 128:(h + 1) * 128, 2048:3072], inc=(LG, 16))
    wkv_ld = LG.at()

    xln_stored = []
    pa_vguard = [0]
    for g in range(8):
        for tt in range(4):
            trow = g * 512 + tt * 128
            buf = tt % 2
            p.op("sync", "dma_start", out=xr[buf][:], in_=x_in[trow:trow + 128, :],
                 wait=[VS.at()] if (g > 0 or tt > 0) else consts_ld, inc=(LD, 16))
            p.wait("vector", LD, LD.n)
            if debug and g == 0:
                p.op("sync", "dma_start", out=xr_dump[tt], in_=xr[buf][:],
                     wait=[LD.at()], inc=(ST, 16))
            C.ln_tile(xr[buf], xsq, xln[buf], stats)
            if debug and g == 0:
                p.op("sync", "dma_start", out=st_dump[tt], in_=stats[:],
                     wait=[VS.at()], inc=(ST, 16))
            ln_v = VS.at()
            for c in range(16):
                slot = c % 2
                if c >= 2:
                    p.wait("tensor", SS, SS.n)
                p.op("tensor", "transpose", ps_tr[slot][:],
                     xln[buf][:, c * 128:(c + 1) * 128], C.ident[:],
                     wait=[ln_v] if c == 0 else [], inc=(TSm, 1))
                p.op("scalar", "activation",
                     stage[:, c * 512 + tt * 128: c * 512 + tt * 128 + 128],
                     ps_tr[slot][:], AF.Copy, scale=1.0,
                     wait=[TSm.at()], inc=(SS, 1))
            p.wait("vector", SS, SS.n)
            p.op("vector", "memset", stats[:, 7:8], 0.0, inc=(VS, 1))
        stage_done = SS.at()

        p.op("sync", "dma_start", out=xlnT_hbm[g].rearrange("(h p) t -> p h t", p=128),
             in_=stage[:].rearrange("p (h t) -> p h t", h=16),
             wait=[stage_done], inc=(LD, 16))
        xln_stored.append(LD.at())

        rope_v = {}
        for kvh in range(4):
            sl = kvh % 2
            if kvh >= 2:
                p.wait("tensor", VS, rope_v[kvh - 2])
            else:
                p.wait("tensor", VS, pa_vguard[0])
            for h in range(16):
                p.op("tensor", "matmul", ps_s[sl][:], qk_cols(h * 1024 + kvh * 128),
                     stage[:, h * 512:(h + 1) * 512],
                     start=(h == 0), stop=(h == 15),
                     wait=[stage_done, wkv_ld, rope_ld] if h == 0 else [],
                     inc=(TSm, 1) if h == 15 else None)
            dst = kT[:, kvh * T + g * 512: kvh * T + g * 512 + 512]
            _rope(p, C, ps_s[sl], rope_c[:, g * 512:g * 512 + 512],
                  rope_s[:, g * 512:g * 512 + 512], rtmp[kvh % 2], dst, TSm.at())
            rope_v[kvh] = VS.n
        for kvh in range(4):
            sl = kvh % 2
            p.wait("tensor", SS, SS.n)
            p.wait("tensor", VS, VS.n)
            for h in range(16):
                p.op("tensor", "matmul", ps_s[sl][:],
                     wbuf[:, h * 1024 + 512 + kvh * 128: h * 1024 + 512 + kvh * 128 + 128],
                     stage[:, h * 512:(h + 1) * 512],
                     start=(h == 0), stop=(h == 15), inc=(TSm, 1) if h == 15 else None)
            p.op("vector", "tensor_copy", vt_stage[:], ps_s[sl][:], wait=[TSm.at()], inc=(VS, 1))
            for ts in range(4):
                slot = ts % 2
                if ts >= 2:
                    p.wait("tensor", SS, SS.n)
                p.op("tensor", "transpose", ps_tr[slot][:],
                     vt_stage[:, ts * 128:ts * 128 + 128], C.ident[:],
                     wait=[VS.at()] if ts == 0 else [], inc=(TSm, 1))
                p.op("scalar", "activation",
                     v_tm[:, kvh * T + (g * 4 + ts) * 128: kvh * T + (g * 4 + ts) * 128 + 128],
                     ps_tr[slot][:], AF.Copy, scale=1.0,
                     wait=[TSm.at()], inc=(SS, 1))
            if kvh == 3:
                pa_vguard[0] = VS.n

    pb_vguard = [pa_vguard[0]]
    for half in range(2):
        for h in range(16):
            p.op("gpsimd", "dma_start", out=wbuf[:, h * 1024:(h + 1) * 1024],
                 in_=w_attn[h * 128:(h + 1) * 128, half * 1024:(half + 1) * 1024],
                 wait=[TSm.at()], inc=(LG, 16))
        wq_ld = LG.at()
        for g in range(8):
            p.op("sync", "dma_start", out=stage[:].rearrange("p (h t) -> p h t", h=16),
                 in_=xlnT_hbm[g].rearrange("(h p) t -> p h t", p=128),
                 wait=[xln_stored[g], TSm.at()], inc=(LD, 16))
            st_ld = LD.at()
            ropeq_v = {}
            for ql in range(8):
                qh = half * 8 + ql
                sl = ql % 2
                if ql >= 2:
                    p.wait("tensor", VS, ropeq_v[ql - 2])
                else:
                    p.wait("tensor", VS, pb_vguard[0])
                for h in range(16):
                    p.op("tensor", "matmul", ps_s[sl][:], qk_cols(h * 1024 + ql * 128),
                         stage[:, h * 512:(h + 1) * 512],
                         start=(h == 0), stop=(h == 15),
                         wait=[st_ld, wq_ld] if h == 0 else [],
                         inc=(TSm, 1) if h == 15 else None)
                dst = qT[:, ql * 512:(ql + 1) * 512]
                _rope(p, C, ps_s[sl], rope_c[:, g * 512:g * 512 + 512],
                      rope_s[:, g * 512:g * 512 + 512], rtmp[ql % 2], dst, TSm.at())
                ropeq_v[ql] = VS.n
            pb_vguard[0] = VS.n
            p.op("sync", "dma_start",
                 out=qT_hbm[g][half * 1024:(half + 1) * 1024, :].rearrange("(h p) t -> p h t", p=128),
                 in_=qT[:].rearrange("p (h t) -> p h t", h=8),
                 wait=[VS.at()], inc=(LD, 16))
    qT_stored = LD.at()

    ab.close()

    cs = ExitStack()
    qT_sb = stage
    pexp = [p.sbuf_scoped(cs, f"pexp{i}", (128, 512), BF16) for i in range(2)]
    rD = p.sbuf_scoped(cs, "rD", (1, 512), BF16)
    rB = p.sbuf_scoped(cs, "rB", (128, 512), BF16)
    yT = p.sbuf_scoped(cs, "yT", (128, 16 * 512), BF16)
    y2 = p.sbuf_scoped(cs, "y2", (128, 16 * 512), BF16)
    sqb = [p.sbuf_scoped(cs, f"sqb{i}", (128, 512), BF16) for i in range(2)]
    ssq_f = p.sbuf_scoped(cs, "ssq_f", (1, 512), F32)
    rstd_r = p.sbuf_scoped(cs, "rstd_r", (1, 512), BF16)
    nrm = p.sbuf_scoped(cs, "nrm", (128, 128), BF16)
    xres = p.sbuf_scoped(cs, "xres", (128, 2048), F32)

    for g in range(8):
        p.op("sync", "dma_start", out=qT_sb[:].rearrange("p (h t) -> p h t", h=16),
             in_=qT_hbm[g].rearrange("(h p) t -> p h t", p=128),
             wait=[qT_stored, TSm.at()], inc=(LD, 16))
        qld = LD.at()
        n_kt = 4 * g + 4
        for hh in range(16):
            kv = hh // 4
            qsl = qT_sb[:, hh * 512:(hh + 1) * 512]
            exp_v = {}
            prd_v = {}
            def emit_scores(kt):
                sl = kt % 2
                w0 = []
                if kt == 0 and hh == 0:
                    w0.append(qld)
                if kt >= 2:
                    w0.append((SS, exp_v[kt - 2]))
                else:
                    w0.append((SS, SS.n))
                p.op("tensor", "matmul", ps_s[sl][:],
                     kT[:, kv * T + kt * 128: kv * T + kt * 128 + 128], qsl,
                     start=True, stop=True, wait=w0, inc=(TSm, 1))
            emit_scores(0)
            for kt in range(n_kt):
                sl = kt % 2
                p.op("scalar", "activation", pexp[sl][:], ps_s[sl][:], AF.Exp,
                     scale=float(1.0 / np.sqrt(128.0)), wait=[TSm.at()], inc=(SS, 1))
                exp_v[kt] = SS.n
                if kt >= 4 * g:
                    r = kt - 4 * g
                    p.op("vector", "tensor_mul", pexp[sl][:], pexp[sl][:],
                         masks[:, r * 512:(r + 1) * 512], wait=[SS.at()], inc=(VS, 1))
                    prd_v[kt] = (VS, VS.n)
                else:
                    prd_v[kt] = (SS, SS.n)
                if kt + 1 < n_kt:
                    emit_scores(kt + 1)
                p.wait("tensor", prd_v[kt][0], prd_v[kt][1])
                p.op("tensor", "matmul", ps_dn[:], C.ones_col[:], pexp[sl][:],
                     start=(kt == 0), stop=(kt == n_kt - 1))
                p.op("tensor", "matmul", ps_av[:],
                     v_tm[:, kv * T + kt * 128: kv * T + kt * 128 + 128], pexp[sl][:],
                     start=(kt == 0), stop=(kt == n_kt - 1),
                     inc=(TSm, 1) if kt == n_kt - 1 else None)
            p.op("vector", "reciprocal", rD[:], ps_dn[:], wait=[TSm.at()], inc=(VS, 1))
            p.op("tensor", "matmul", ps_r[:], C.ones_row[:], rD[:], start=True, stop=True,
                 wait=[VS.at()], inc=(TSm, 1))
            p.op("scalar", "activation", rB[:], ps_r[:], AF.Copy, scale=1.0,
                 wait=[TSm.at()], inc=(SS, 1))
            p.op("vector", "tensor_mul", yT[:, hh * 512:(hh + 1) * 512], ps_av[:], rB[:],
                 wait=[SS.at()], inc=(VS, 1))
            p.wait("tensor", VS, VS.n)
        yT_done = VS.at()

        evac_p = {}
        for half in range(2):
            for h in range(16):
                p.op("gpsimd", "dma_start", out=wbuf[:, h * 1024:(h + 1) * 1024],
                     in_=w_proj[h * 128:(h + 1) * 128, half * 1024:(half + 1) * 1024],
                     wait=[yT_done, TSm.at()] if h == 0 else [], inc=(LG, 16))
            wp_ld = LG.at()
            for ol in range(8):
                oc = half * 8 + ol
                pb = ps_proj if oc % 2 == 0 else ps_av
                w0 = [yT_done, wp_ld] if ol == 0 else []
                if oc >= 2:
                    w0.append((SS, evac_p[oc - 2]))
                for h in range(16):
                    p.op("tensor", "matmul", pb[:],
                         wbuf[:, h * 1024 + ol * 128: h * 1024 + ol * 128 + 128],
                         yT[:, h * 512:(h + 1) * 512],
                         start=(h == 0), stop=(h == 15),
                         wait=w0 if h == 0 else [],
                         inc=(TSm, 1) if h == 15 else None)
                p.op("scalar", "activation", y2[:, oc * 512:(oc + 1) * 512], pb[:],
                     AF.Copy, scale=1.0, wait=[TSm.at()], inc=(SS, 1))
                p.op("scalar", "activation", sqb[oc % 2][:], pb[:], AF.Square, inc=(SS, 1))
                evac_p[oc] = SS.n
                p.op("tensor", "matmul", ps_dn[:], C.ones_col[:], sqb[oc % 2][:],
                     start=(oc == 0), stop=(oc == 15), wait=[SS.at()],
                     inc=(TSm, 1) if oc == 15 else None)
        p.op("scalar", "activation", ssq_f[:], ps_dn[:], AF.Ln, scale=1.0 / D,
             bias=C.cbank_ap(CB_EPS_RMS, 1), wait=[TSm.at()])
        p.op("scalar", "activation", rstd_r[:], ssq_f[:], AF.Exp, scale=-0.5, inc=(SS, 1))
        p.op("tensor", "matmul", ps_r[:], C.ones_row[:], rstd_r[:], start=True, stop=True,
             wait=[SS.at()], inc=(TSm, 1))
        p.op("scalar", "activation", rB[:], ps_r[:], AF.Copy, scale=1.0,
             wait=[TSm.at()], inc=(SS, 1))
        rb_done = SS.at()

        for tt in range(4):
            trow = g * 512 + tt * 128
            p.op("sync", "dma_start", out=xres[:], in_=x_in[trow:trow + 128, :],
                 wait=[ST.at()], inc=(LD, 16))
            p.wait("vector", LD, LD.n)
            for oc in range(16):
                slot = oc % 2
                p.op("vector", "scalar_tensor_tensor", nrm[:, 0:128],
                     y2[:, oc * 512 + tt * 128: oc * 512 + tt * 128 + 128],
                     nw[:, oc:oc + 1], rB[:, tt * 128:tt * 128 + 128],
                     ALU.mult, ALU.mult,
                     wait=[rb_done, TSm.at()] if (oc == 0 and tt == 0) else [TSm.at()],
                     inc=(VS, 1))
                if oc >= 2:
                    p.wait("tensor", VS, VS.n)
                p.op("tensor", "transpose", ps_tr[slot][:],
                     nrm[:, 0:128], C.ident[:], wait=[VS.at()], inc=(TSm, 1))
                p.op("vector", "tensor_add", xres[:, oc * 128:(oc + 1) * 128],
                     xres[:, oc * 128:(oc + 1) * 128],
                     ps_tr[slot][:],
                     wait=[TSm.at()], inc=(VS, 1))
            p.op("sync", "dma_start", out=x_out[trow:trow + 128, :], in_=xres[:],
                 wait=[VS.at()], inc=(ST, 16))
    cs.close()
    p.wait("sync", ST, ST.n)
    p.pop_scope()
    return p


def _rope(p, C, ps, ctile, stile, rtmp, dst, mm_done):
    VS = C.VS
    t0, t1 = rtmp[:, 0:512], rtmp[:, 512:1024]
    t2, t3 = rtmp[:, 1024:1536], rtmp[:, 1536:2048]
    p.op("vector", "tensor_mul", t0[0:64, :], ps[0:64, :], ctile[0:64, :], wait=[mm_done])
    p.op("vector", "tensor_mul", t1[0:64, :], ps[64:128, :], stile[0:64, :])
    p.op("vector", "tensor_sub", dst[0:64, :], t0[0:64, :], t1[0:64, :])
    p.op("vector", "tensor_mul", t2[0:64, :], ps[0:64, :], stile[0:64, :])
    p.op("vector", "tensor_mul", t3[0:64, :], ps[64:128, :], ctile[0:64, :])
    p.op("vector", "tensor_add", dst[64:128, :], t2[0:64, :], t3[0:64, :], inc=(VS, 1))



def emit_ln_stage(p, C, x_in, stage, xr, xsq, stats, xln, ps_tr, g, consts_ld=None):
    LD, VS, SS, TSm = C.LD, C.VS, C.SS, C.TS
    for tt in range(4):
        trow = g * 512 + tt * 128
        buf = tt % 2
        first = (g == 0 and tt == 0 and consts_ld is not None)
        p.op("sync", "dma_start", out=xr[buf][:], in_=x_in[trow:trow + 128, :],
             wait=consts_ld if first else [VS.at()], inc=(LD, 16))
        p.wait("vector", LD, LD.n)
        C.ln_tile(xr[buf], xsq, xln[buf], stats)
        ln_v = VS.at()
        for c in range(16):
            slot = c % 2
            if c >= 2:
                p.wait("tensor", SS, SS.n)
            p.op("tensor", "transpose", ps_tr[slot][:],
                 xln[buf][:, c * 128:(c + 1) * 128], C.ident[:],
                 wait=[ln_v] if c == 0 else [], inc=(TSm, 1))
            p.op("scalar", "activation",
                 stage[:, c * 512 + tt * 128: c * 512 + tt * 128 + 128],
                 ps_tr[slot][:], AF.Copy, scale=1.0,
                 wait=[TSm.at(), LD.at()] if (tt == 0 and c == 0) else [TSm.at()],
                 inc=(SS, 1))
        p.wait("vector", SS, SS.n)
        p.op("vector", "memset", stats[:, 7:8], 0.0, inc=(VS, 1))
    return SS.at()



def build_l1ffn(fused=None):
    if fused is None:
        p = P("l1ffn")
        C = Common(p)
        x_in_ap = x_out_ap = None
    else:
        p, C, x_in_ap, x_out_ap = fused
    p.push_scope("f1_")
    x_in = x_in_ap if x_in_ap is not None else p.dram("x_in", (T, D), F32, kind="ExternalInput").ap()
    x_out = x_out_ap if x_out_ap is not None else p.dram("x_out", (T, D), F32, kind="ExternalOutput").ap()
    w_in = p.dram("weight", (D, D), F32, kind="ExternalInput").ap()
    b_in = p.dram("bias2d", (128, 16), F32, kind="ExternalInput").ap()

    stage = p.sbuf("stage", (128, 16 * 512), BF16)
    bwT = p.sbuf("bwT", (128, 16 * 2048), BF16)
    bias = p.sbuf("bias_sb", (128, 16), F32)
    xr = [p.sbuf(f"xr{i}", (128, D), F32) for i in range(2)]
    xsq = p.sbuf("xsq", (128, D), F32)
    stats = p.sbuf("stats", (128, 8), F32)
    xln = [p.sbuf(f"xln{i}", (128, D), BF16) for i in range(2)]
    wrow = p.sbuf("wrow", (128, D), F32)
    sgn = p.sbuf("sgn", (128, D), BF16)
    bwrow = p.sbuf("bwrow", (128, D), BF16)
    alpha = p.sbuf("alpha", (128, 2), F32)
    y_all = p.sbuf("y_all", (128, 16 * 512), F32)
    yrow = p.sbuf("yrow", (128, 128), F32)
    xres = p.sbuf("xres", (128, D), F32)

    ps_mm = p.psum("ps_mm", (128, 512), F32)
    ps_tr = [p.psum(f"ps_tr{i}", (128, 128), BF16) for i in range(2)]
    ps_trf = [p.psum(f"ps_trf{i}", (128, 128), F32) for i in range(2)]

    LD, LG, ST, TSm, VS, SS = C.LD, C.LG, C.ST, C.TS, C.VS, C.SS
    C.load_consts()
    p.op("sync", "dma_start", out=bias[:], in_=b_in, inc=(LD, 16))
    consts_ld = [LD.at(), LG.at(), VS.at()]

    for oc in range(16):
        p.op("gpsimd", "dma_start", out=wrow[:], in_=w_in[oc * 128:(oc + 1) * 128, :],
             wait=[SS.at(), VS.at()] if oc > 0 else [], inc=(LG, 16))
        p.wait("vector", LG, LG.n)
        p.op("vector", "reduce_sum", alpha[:, 0:1], wrow[:], axis=AX.X,
             apply_absolute_value=True)
        p.op("vector", "drain")
        p.op("vector", "tensor_scalar", alpha[:, 1:2], alpha[:, 0:1], 1.0 / D, None,
             ALU.mult, inc=(VS, 1))
        p.op("vector", "drain")
        p.op("scalar", "sign", sgn[:], wrow[:], wait=[VS.at()], inc=(SS, 1))
        p.op("vector", "tensor_scalar", bwrow[:], sgn[:], alpha[:, 1:2], None,
             ALU.mult, wait=[SS.at()], inc=(VS, 1))
        for ic in range(16):
            slot = ic % 2
            if ic >= 2:
                p.wait("tensor", SS, SS.n)
            p.op("tensor", "transpose", ps_tr[slot][:],
                 bwrow[:, ic * 128:(ic + 1) * 128], C.ident[:],
                 wait=[VS.at()] if ic == 0 else [], inc=(TSm, 1))
            p.op("scalar", "activation", bwT[:, ic * 2048 + oc * 128: ic * 2048 + oc * 128 + 128],
                 ps_tr[slot][:], AF.Copy, scale=1.0, wait=[TSm.at()], inc=(SS, 1))
    bw_done = SS.at()

    for g in range(8):
        stage_done = emit_ln_stage(p, C, x_in, stage, xr, xsq, stats, xln, ps_tr, g,
                                   consts_ld if g == 0 else None)
        for oc in range(16):
            for ic in range(16):
                p.op("tensor", "matmul", ps_mm[:],
                     bwT[:, ic * 2048 + oc * 128: ic * 2048 + oc * 128 + 128],
                     stage[:, ic * 512:(ic + 1) * 512],
                     start=(ic == 0), stop=(ic == 15),
                     wait=[stage_done, bw_done] if (oc == 0 and ic == 0) else
                          ([VS.at()] if ic == 0 else []),
                     inc=(TSm, 1) if ic == 15 else None)
            p.op("vector", "tensor_scalar", y_all[:, oc * 512:(oc + 1) * 512],
                 ps_mm[:], bias[:, oc:oc + 1], None, ALU.add,
                 wait=[TSm.at()], inc=(VS, 1))
        y_done = VS.at()
        for tt in range(4):
            trow = g * 512 + tt * 128
            p.op("sync", "dma_start", out=xres[:], in_=x_in[trow:trow + 128, :],
                 wait=[ST.at()], inc=(LD, 16))
            p.wait("vector", LD, LD.n)
            for oc in range(16):
                slot = oc % 2
                if oc >= 2:
                    p.wait("tensor", VS, VS.n)
                p.op("tensor", "transpose", ps_trf[slot][:],
                     y_all[:, oc * 512 + tt * 128: oc * 512 + tt * 128 + 128],
                     C.ident_f[:],
                     wait=[y_done, VS.at()] if (oc == 0 and tt == 0) else
                          ([VS.at()] if oc < 2 else []),
                     inc=(TSm, 1))
                p.op("vector", "tensor_add", xres[:, oc * 128:(oc + 1) * 128],
                     xres[:, oc * 128:(oc + 1) * 128], ps_trf[slot][:],
                     wait=[TSm.at()], inc=(VS, 1))
            p.op("sync", "dma_start", out=x_out[trow:trow + 128, :], in_=xres[:],
                 wait=[VS.at()], inc=(ST, 16))
    p.wait("sync", ST, ST.n)
    p.pop_scope()
    return p


def run_l1ffn(x, weight, bias):
    c = _consts()
    p = build_l1ffn()
    nc = p.finalize()
    b2d = np.ascontiguousarray(bias.reshape(16, 128).T)
    in_maps = []
    for b in range(NCORES):
        in_maps.append({
            "x_in": np.ascontiguousarray(x[b]),
            "weight": weight, "bias2d": b2d,
            "c_ident": c["ident"], "c_cbank": c["cbank"],
        })
    res = run_bass_kernel_spmd(nc, in_maps, core_ids=list(range(NCORES)))
    return np.stack([res.results[b]["x_out"] for b in range(NCORES)])



NEXP = 17
GU_WG = 4
DN_WG = 3

def build_l0ffn(phases=(1, 1, 1), nexp=NEXP, debug=False, fused=None):
    if fused is None:
        p = P("l0ffn")
        C = Common(p)
        x_in_ap = x_out_ap = None
    else:
        p, C, x_in_ap, x_out_ap = fused
    p.push_scope("m0_")
    x_in = x_in_ap if x_in_ap is not None else p.dram("x_in", (T, D), F32, kind="ExternalInput").ap()
    x_out = x_out_ap if x_out_ap is not None else p.dram("x_out", (T, D), F32, kind="ExternalOutput").ap()
    gate_w = p.dram("gate_w", (16, D), F32, kind="ExternalInput").ap()
    gup = p.dram("gup", (NEXP, 256, 3072), I32, kind="ExternalInput").ap()
    gus = p.dram("gus", (NEXP, 16, 3072), F32, kind="ExternalInput").ap()
    dnp = p.dram("dnp", (NEXP, 192, 2048), I32, kind="ExternalInput").ap()
    dns = p.dram("dns", (NEXP, 12, 2048), F32, kind="ExternalInput").ap()
    sel4_in = p.dram("sel4", (4, 128), F32, kind="ExternalInput").ap()
    sel16_in = p.dram("sel16", (16, 16 * 128), F32, kind="ExternalInput").ap()
    shb_in = p.dram("shbank", (128, 8), I32, kind="ExternalInput").ap()
    xlnT_hbm = p.dram("xlnT_scratch", (8, D, 512), BF16, kind="Internal").ap()
    dbk = "ExternalOutput" if debug else "Internal"
    wgu_d = p.dram("wgu_dense", (NEXP, 12, D, 256), BF16, kind=dbk).ap()
    wdn_d = p.dram("wdn_dense", (NEXP, 1536, D), BF16, kind=dbk).ap()

    combT = p.sbuf("combT", (16, T), BF16)
    gwT = p.sbuf("gwT", (128, 16 * 16), BF16)
    sel4 = p.sbuf("sel4_sb", (4, 128), BF16)
    sel16 = p.sbuf("sel16_sb", (16, 16 * 128), BF16)
    shb = p.sbuf("shb_sb", (128, 8), I32)

    ps_tr = [p.psum(f"ps_tr{i}", (128, 128), BF16) for i in range(2)]

    LD, LG, ST, TSm, VS, SS = C.LD, C.LG, C.ST, C.TS, C.VS, C.SS
    C.load_consts()

    prep = ExitStack()
    gw_f = p.sbuf_scoped(prep, "gw_f", (16, D), F32)
    gw_b = p.sbuf_scoped(prep, "gw_b", (16, D), BF16)
    p.op("sync", "dma_start", out=gw_f[:], in_=gate_w, inc=(LD, 16))
    p.op("gpsimd", "dma_start", out=sel4[:], in_=sel4_in, inc=(LG, 16))
    p.op("gpsimd", "dma_start", out=sel16[:], in_=sel16_in, inc=(LG, 16))
    p.op("gpsimd", "dma_start", out=shb[:], in_=shb_in, inc=(LG, 16))
    p.wait("vector", LD, LD.n)
    p.op("vector", "tensor_copy", gw_b[:], gw_f[:], wait=[LG.at()], inc=(VS, 1))
    for h in range(16):
        slot = h % 2
        if h >= 2:
            p.wait("tensor", SS, SS.n)
        p.op("tensor", "transpose", ps_tr[slot][:, 0:16],
             gw_b[0:16, h * 128:(h + 1) * 128], C.ident[0:16, 0:16],
             wait=[VS.at()] if h == 0 else [], inc=(TSm, 1))
        p.op("scalar", "activation", gwT[:, h * 16:(h + 1) * 16],
             ps_tr[slot][:, 0:16], AF.Copy, scale=1.0, wait=[TSm.at()], inc=(SS, 1))
    prep.close()
    consts_ld = [LD.at(), LG.at(), VS.at(), SS.at()]

    do_a, do_c, do_d = phases
    pa = ExitStack()
    ps_p = p.psum_scoped(pa, "ps_p", (128, 16), F32)
    ps_ct = p.psum_scoped(pa, "ps_ct", (16, 128), BF16)
    ps_a = p.psum_scoped(pa, "ps_sea", (128, 512), F32)
    ps_b = p.psum_scoped(pa, "ps_seb", (128, 512), F32)
    xr = [p.sbuf_scoped(pa, f"xr{i}", (128, D), F32) for i in range(2)]
    xsq = p.sbuf_scoped(pa, "xsq", (128, D), F32)
    stats = p.sbuf_scoped(pa, "stats", (128, 8), F32)
    xln = [p.sbuf_scoped(pa, f"xln{i}", (128, D), BF16) for i in range(2)]
    stage = p.sbuf_scoped(pa, "stage", (128, 16 * 512), BF16)
    pr = p.sbuf_scoped(pa, "pr", (128, 16), F32)
    pz = p.sbuf_scoped(pa, "pz", (128, 16), F32)
    msk = p.sbuf_scoped(pa, "msk", (128, 16), U8)
    mkf = p.sbuf_scoped(pa, "mkf", (128, 16), F32)
    num = p.sbuf_scoped(pa, "num", (128, 16), F32)
    rstats = p.sbuf_scoped(pa, "rstats", (128, 8), F32)
    cmb = p.sbuf_scoped(pa, "cmb", (128, 16), BF16)

    pc = pa
    pk = p.sbuf_scoped(pc, "pk", (128, 3072), I32)
    mag = [p.sbuf_scoped(pc, f"mag{i}", (128, 3072), I32) for i in range(2)]
    sgn = [p.sbuf_scoped(pc, f"sgn{i}", (128, 3072), I32) for i in range(2)]
    t_cm = [p.sbuf_scoped(pc, f"t_cm{i}", (128, 3072), BF16) for i in range(2)]
    t_r1 = [p.sbuf_scoped(pc, f"t_r1{i}", (128, 3072), BF16) for i in range(2)]
    t_r2 = [p.sbuf_scoped(pc, f"t_r2{i}", (128, 3072), BF16) for i in range(2)]
    t_sg = [p.sbuf_scoped(pc, f"t_sg{i}", (128, 3072), BF16) for i in range(2)]
    se_f = p.sbuf_scoped(pc, "se_f", (4, 3072), F32)
    se_b = p.sbuf_scoped(pc, "se_b", (4, 3072), BF16)
    se = p.sbuf_scoped(pc, "se", (128, 3072), BF16)
    dq = [p.sbuf_scoped(pc, f"dq{i}", (128, 3072), BF16) for i in range(2)]

    first_dq = [True]
    dqst = {"wr": [0, 0], "pk_free": 0, "ex": [0, 0], "se_mm": 0}

    def dequant_block(packed2d, scales2d, ncols, wg, dense_out_fn, nparts=64):
        p.op("sync", "dma_start", out=pk[0:64, 0:ncols],
             in_=packed2d[wg * 64:(wg + 1) * 64, :], wait=[(VS, dqst["pk_free"])],
             inc=(LD, 16))
        p.op("sync", "dma_start", out=pk[64:128, 0:ncols],
             in_=packed2d[wg * 64:(wg + 1) * 64, :], inc=(LD, 16))
        p.op("gpsimd", "dma_start", out=se_b[:, 0:ncols],
             in_=scales2d[wg * 4:(wg + 1) * 4, :],
             wait=[(TSm, dqst["se_mm"])], inc=(LG, 16))
        seb_ld = LG.n
        p.wait("vector", LD, LD.n)
        nse = (ncols + 511) // 512
        for i in range(nse):
            c0, c1 = i * 512, min(ncols, (i + 1) * 512)
            pb = ps_a if i % 2 == 0 else ps_b
            if i >= 2:
                p.wait("tensor", SS, SS.n)
            p.op("tensor", "matmul", pb[:, 0:c1 - c0], sel4[:], se_b[:, c0:c1],
                 start=True, stop=True,
                 wait=[VS.at(), (LG, seb_ld)] if i == 0 else [],
                 inc=(TSm, 1))
            p.op("scalar", "activation", se[:, c0:c1], pb[:, 0:c1 - c0], AF.Copy,
                 scale=1.0, wait=[TSm.at()], inc=(SS, 1))
        dqst["se_mm"] = TSm.n
        se_done = SS.at()
        cols = slice(0, ncols)

        def extract(j, par):
            pi = j // 2
            p.op("vector", "tensor_scalar", mag[par][:, cols], pk[:, cols],
                 shb[:, pi:pi + 1], 7, ALU.logical_shift_right, ALU.bitwise_and)
            p.op("vector", "tensor_scalar", sgn[par][:, cols], pk[:, cols],
                 shb[:, 4 + pi:5 + pi], 1, ALU.logical_shift_right, ALU.bitwise_and,
                 inc=(VS, 1))
            dqst["ex"][par] = VS.n
            p.op("scalar", "activation", t_sg[par][:, cols], sgn[par][:, cols],
                 AF.Copy, scale=-2.0, bias=1.0, wait=[VS.at()])
            if j == 6:
                dqst["pk_free"] = VS.n

        extract(0, 0)
        for jp in range(4):
            par = jp % 2
            buf = dq[par]
            ex_v = (VS, dqst["ex"][par])
            p.op("scalar", "activation", t_cm[par][:, cols], mag[par][:, cols], AF.Copy,
                 scale=0.5, wait=[ex_v])
            p.op("scalar", "activation", t_r1[par][:, cols], mag[par][:, cols], AF.Relu,
                 scale=0.5, bias=C.cbank_ap(CB_NEG2, 128))
            p.op("scalar", "activation", t_r2[par][:, cols], mag[par][:, cols], AF.Relu,
                 scale=1.0, bias=C.cbank_ap(CB_NEG6, 128), inc=(SS, 1))
            acts_v = SS.at()
            if jp < 3:
                extract(2 * (jp + 1), 1 - par)
            p.op("vector", "tensor_add", t_cm[par][:, cols], t_cm[par][:, cols],
                 t_r1[par][:, cols], wait=[acts_v])
            p.op("vector", "tensor_add", t_cm[par][:, cols], t_cm[par][:, cols],
                 t_r2[par][:, cols])
            p.op("vector", "tensor_mul", t_cm[par][:, cols], t_cm[par][:, cols],
                 t_sg[par][:, cols])
            p.op("vector", "tensor_mul", buf[:, cols], t_cm[par][:, cols], se[:, cols],
                 wait=[se_done, (LG, dqst["wr"][par])], inc=(VS, 1))
            dsts = dense_out_fn(jp)
            p.op("gpsimd", "dma_start", out=dsts[0], in_=buf[0:64, cols],
                 wait=[VS.at()], inc=(LG, 16))
            p.op("gpsimd", "dma_start", out=dsts[1], in_=buf[64:128, cols],
                 inc=(LG, 16))
            dqst["wr"][par] = LG.n

    def emit_dequant(e):
        for wg in range(GU_WG):
            def gu_out(jp, e=e, wg=wg):
                hc = wg * 4 + jp
                h0 = wgu_d[e, :, hc * 128:hc * 128 + 64, :].rearrange("cb p i -> p cb i")
                h1 = wgu_d[e, :, hc * 128 + 64:(hc + 1) * 128, :].rearrange("cb p i -> p cb i")
                return (h0, h1)
            dequant_block(gup[e], gus[e], 3072, wg, gu_out)
        for wg in range(DN_WG):
            dequant_block(dnp[e], dns[e], 2048, wg,
                          lambda jp, e=e, wg=wg: _dn_dest(wdn_d, e, wg, jp))

    _dq_pending = list(range(nexp if do_c else 0))

    for g in range(8 if do_a else 0):
        stage_done = emit_ln_stage(p, C, x_in, stage, xr, xsq, stats, xln, ps_tr, g,
                                   consts_ld if g == 0 else None)
        for tt in range(4):
            trow = g * 512 + tt * 128
            p.wait("tensor", VS, VS.n)
            for h in range(16):
                p.op("tensor", "matmul", ps_p[:],
                     stage[:, h * 512 + tt * 128: h * 512 + tt * 128 + 128],
                     gwT[:, h * 16:(h + 1) * 16],
                     start=(h == 0), stop=(h == 15),
                     wait=[stage_done] if h == 0 else [],
                     inc=(TSm, 1) if h == 15 else None)
            p.op("vector", "reduce_max", rstats[:, 0:1], ps_p[:], axis=AX.X,
                 negate=True, wait=[TSm.at()], inc=(VS, 1))
            p.op("vector", "drain")
            p.op("scalar", "activation", pr[:], ps_p[:], AF.Exp,
                 bias=rstats[:, 0:1], scale=1.0, wait=[VS.at()], inc=(SS, 1))
            p.op("vector", "memset", num[:], 0.0, wait=[SS.at()])
            p.op("vector", "memset", rstats[:, 1:2], 0.0)
            p.op("vector", "drain")
            for k in range(4):
                src = pr if k == 0 else pz
                p.op("vector", "reduce_max", rstats[:, 2:3], src[:], axis=AX.X)
                p.op("vector", "drain")
                p.op("vector", "tensor_scalar", msk[:], src[:], rstats[:, 2:3], None,
                     ALU.is_ge)
                p.op("vector", "drain")
                p.op("vector", "tensor_copy", mkf[:], msk[:])
                p.op("vector", "drain")
                p.op("vector", "scalar_tensor_tensor", num[:], mkf[:], rstats[:, 2:3],
                     num[:], ALU.mult, ALU.add)
                p.op("vector", "tensor_add", rstats[:, 1:2], rstats[:, 1:2], rstats[:, 2:3])
                p.op("vector", "drain")
                if k < 3:
                    p.op("vector", "memset", mkf[:], 0.0)
                    p.op("vector", "drain")
                    p.op("vector", "select", pz[:], msk[:], mkf[:], src[:])
                    p.op("vector", "drain")
            p.op("vector", "reciprocal", rstats[:, 3:4], rstats[:, 1:2])
            p.op("vector", "drain")
            p.op("vector", "tensor_scalar", cmb[:], num[:], rstats[:, 3:4], None,
                 ALU.mult, inc=(VS, 1))
            p.op("vector", "drain")
            p.op("tensor", "transpose", ps_ct[:], cmb[:], C.ident[:],
                 wait=[VS.at(), SS.at()], inc=(TSm, 1))
            p.op("scalar", "activation", combT[:, trow:trow + 128], ps_ct[0:16, :],
                 AF.Copy, scale=1.0, wait=[TSm.at()], inc=(SS, 1))
        p.op("sync", "dma_start", out=xlnT_hbm[g].rearrange("(h p) t -> p h t", p=128),
             in_=stage[:].rearrange("p (h t) -> p h t", h=16),
             wait=[SS.at()], inc=(LD, 16))
        for _ in range(2):
            if _dq_pending:
                emit_dequant(_dq_pending.pop(0))
    xln_all = LD.at()
    comb_all = SS.at()

    for e in _dq_pending:
        emit_dequant(e)
    dq_all = LG.at()
    pa.close()

    pd = ExitStack()
    ps_ga = [p.psum_scoped(pd, f"ps_ga{i}", (128, 512), F32) for i in range(2)]
    ps_gb = [p.psum_scoped(pd, f"ps_gb{i}", (128, 512), F32) for i in range(2)]
    ps_dnb = [p.psum_scoped(pd, f"ps_dnn{i}", (128, 512), F32) for i in range(2)]
    xp = p.sbuf_scoped(pd, "xp", (128, 16 * 1024), BF16)
    out_acc = p.sbuf_scoped(pd, "out_acc", (128, 16 * 1024), BF16)
    act = p.sbuf_scoped(pd, "act", (128, 12 * 1024), BF16)
    wdsb = p.sbuf_scoped(pd, "wdsb", (128, 12 * 2048), BF16)
    gblk = [p.sbuf_scoped(pd, f"gblk{i}", (128, 16 * 256), BF16) for i in range(2)]
    ublk = [p.sbuf_scoped(pd, f"ublk{i}", (128, 16 * 256), BF16) for i in range(2)]
    cB = p.sbuf_scoped(pd, "cB", (128, 1024), BF16)
    sg_sb = [p.sbuf_scoped(pd, f"sg_sb{i}", (128, 512), BF16) for i in range(2)]
    m1_sb = [p.sbuf_scoped(pd, f"m1_sb{i}", (128, 512), BF16) for i in range(2)]
    xres = p.sbuf_scoped(pd, "xres", (128, D), F32)

    first_d = [True]
    for tg in range(4 if do_d else 0):
        if first_d[0]:
            p.wait("sync", LG, LG.n)
            p.wait("vector", LG, LG.n)
            first_d[0] = False
        p.wait("sync", TSm, TSm.n)
        for hc in range(16):
            wg, jp = hc // 4, hc % 4
            for gb in range(2):
                g = tg * 2 + gb
                blk = xlnT_hbm[g][512 * wg: 512 * wg + 512, :]
                r = blk.rearrange("(pp j) t -> j pp t", j=8)
                w0 = [xln_all] if (tg == 0 and hc == 0 and gb == 0) else []
                cslice = slice(hc * 1024 + gb * 512, hc * 1024 + (gb + 1) * 512)
                p.op("sync", "dma_start", out=xp[0:64, cslice], in_=r[2 * jp],
                     wait=w0, inc=(LD, 16))
                p.op("sync", "dma_start", out=xp[64:128, cslice], in_=r[2 * jp + 1],
                     inc=(LD, 16))
        xp_ld = LD.at()
        p.op("vector", "memset", out_acc[:], 0.0, wait=[TSm.at()], inc=(VS, 1))
        g_evac = {}
        d_evac = {}
        blk_mm = {}
        for e in range(nexp):
            if e == 16:
                p.op("vector", "memset", cB[:], 1.0, wait=[TSm.at()], inc=(VS, 1))
                cb_done = VS.at()
            else:
                for i in range(2):
                    p.op("tensor", "matmul", ps_dnb[0][:, 0:512],
                         sel16[:, e * 128:(e + 1) * 128],
                         combT[:, tg * 1024 + i * 512: tg * 1024 + (i + 1) * 512],
                         start=True, stop=True,
                         wait=[comb_all, VS.at()] if i == 0 else [SS.at()], inc=(TSm, 1))
                    p.op("scalar", "activation", cB[:, i * 512:(i + 1) * 512],
                         ps_dnb[0][:, 0:512], AF.Copy, scale=1.0, wait=[TSm.at()], inc=(SS, 1))
                cb_done = SS.at()
            for bp in range(6):
                lb = bp % 2
                gk = (e, bp)
                wblk_wait = [dq_all] if tg == 0 else []
                prev = blk_mm.get((e, bp - 2)) if bp >= 2 else (
                    blk_mm.get((e - 1, 4)) if e > 0 and bp == 0 else (
                    blk_mm.get((e - 1, 5)) if e > 0 and bp == 1 else None))
                if prev is None:
                    prev = TSm.at()
                p.op("gpsimd", "dma_start", out=gblk[lb][:],
                     in_=wgu_d[e, bp].rearrange("(hc p) i -> p hc i", p=128),
                     wait=wblk_wait + [prev], inc=(LG, 16))
                p.op("gpsimd", "dma_start", out=ublk[lb][:],
                     in_=wgu_d[e, 6 + bp].rearrange("(hc p) i -> p hc i", p=128),
                     inc=(LG, 16))
                wld = LG.at()
                for ci_in in range(2):
                    ci = bp * 2 + ci_in
                    for th in range(2):
                        q = (bp * 2 + ci_in) * 2 + th
                        qp = q % 2
                        w0 = [wld, xp_ld, cb_done]
                        pg = g_evac.get((e, q - 2))
                        if pg is not None:
                            w0.append((pg[0], pg[1]))
                        for hc in range(16):
                            p.op("tensor", "matmul", ps_ga[qp][:],
                                 gblk[lb][:, hc * 256 + ci_in * 128: hc * 256 + ci_in * 128 + 128],
                                 xp[:, hc * 1024 + th * 512: hc * 1024 + (th + 1) * 512],
                                 start=(hc == 0), stop=(hc == 15),
                                 wait=w0 if hc == 0 else [],
                                 inc=(TSm, 1) if hc == 15 else None)
                            p.op("tensor", "matmul", ps_gb[qp][:],
                                 ublk[lb][:, hc * 256 + ci_in * 128: hc * 256 + ci_in * 128 + 128],
                                 xp[:, hc * 1024 + th * 512: hc * 1024 + (th + 1) * 512],
                                 start=(hc == 0), stop=(hc == 15),
                                 inc=(TSm, 1) if hc == 15 else None)
                        p.op("scalar", "activation", sg_sb[qp][:], ps_ga[qp][:], AF.Silu,
                             wait=[TSm.at()], inc=(SS, 1))
                        p.op("vector", "tensor_mul", m1_sb[qp][:], sg_sb[qp][:], ps_gb[qp][:],
                             wait=[SS.at()])
                        p.op("vector", "tensor_mul",
                             act[:, ci * 1024 + th * 512: ci * 1024 + (th + 1) * 512],
                             m1_sb[qp][:], cB[:, th * 512:(th + 1) * 512], inc=(VS, 1))
                        g_evac[(e, q)] = (VS, VS.n)
                blk_mm[(e, bp)] = TSm.at()
            act_done = VS.at()
            for ic in range(12):
                p.op("gpsimd", "dma_start", out=wdsb[:, ic * 2048:(ic + 1) * 2048],
                     in_=wdn_d[e, ic * 128:(ic + 1) * 128, :],
                     wait=[TSm.at()] if ic == 0 else [], inc=(LG, 16))
            wd_ld = LG.at()
            for th in range(2):
                for oc in range(16):
                    dqi = th * 16 + oc
                    dp = dqi % 2
                    w0 = [act_done, wd_ld]
                    pv = d_evac.get((e, dqi - 2))
                    if pv is not None:
                        w0.append((pv[0], pv[1]))
                    elif dqi < 2:
                        w0.append((VS, VS.n))
                    for ic in range(12):
                        p.op("tensor", "matmul", ps_dnb[dp][:],
                             wdsb[:, ic * 2048 + oc * 128: ic * 2048 + oc * 128 + 128],
                             act[:, ic * 1024 + th * 512: ic * 1024 + (th + 1) * 512],
                             start=(ic == 0), stop=(ic == 11),
                             wait=w0 if ic == 0 else [],
                             inc=(TSm, 1) if ic == 11 else None)
                    sl = oc * 1024 + th * 512
                    p.op("vector", "tensor_add", out_acc[:, sl:sl + 512],
                         out_acc[:, sl:sl + 512], ps_dnb[dp][:], wait=[TSm.at()], inc=(VS, 1))
                    d_evac[(e, dqi)] = (VS, VS.n)
        acc_done = VS.at()
        for tt in range(8):
            trow = tg * 1024 + tt * 128
            p.op("sync", "dma_start", out=xres[:], in_=x_in[trow:trow + 128, :],
                 wait=[ST.at()], inc=(LD, 16))
            p.wait("vector", LD, LD.n)
            for oc in range(16):
                slot = oc % 2
                if oc >= 2:
                    p.wait("tensor", VS, VS.n)
                p.op("tensor", "transpose", ps_tr[slot][:],
                     out_acc[:, oc * 1024 + tt * 128: oc * 1024 + tt * 128 + 128],
                     C.ident[:],
                     wait=[acc_done, VS.at()] if oc < 2 else [],
                     inc=(TSm, 1))
                p.op("vector", "tensor_add", xres[:, oc * 128:(oc + 1) * 128],
                     xres[:, oc * 128:(oc + 1) * 128], ps_tr[slot][:],
                     wait=[TSm.at()], inc=(VS, 1))
            p.op("sync", "dma_start", out=x_out[trow:trow + 128, :], in_=xres[:],
                 wait=[VS.at()], inc=(ST, 16))
    pd.close()
    p.wait("sync", ST, ST.n)
    p.pop_scope()
    return p


def _dn_dest(wdn_d, e, wg, jp):
    v = wdn_d[e, 512 * wg: 512 * wg + 512, :]
    r = v.rearrange("(pp j) c -> j pp c", j=8)
    return (r[2 * jp], r[2 * jp + 1])


def _shbank():
    shbank = np.zeros((128, 8), np.int32)
    for pi in range(4):
        shbank[0:64, pi] = 8 * pi
        shbank[64:128, pi] = 8 * pi + 4
    shbank[:, 4:8] = shbank[:, 0:4] + 3
    return shbank


def run_l0ffn(x, gate_w, gup, gus, dnp, dns, sgup, sgus, sdnp, sdns):
    c = _consts()
    p = build_l0ffn()
    nc = p.finalize()
    gup_a = np.concatenate([gup, sgup[None]], 0)
    gus_a = np.concatenate([gus, sgus[None]], 0)
    dnp_a = np.concatenate([dnp, sdnp[None]], 0)
    dns_a = np.concatenate([dns, sdns[None]], 0)
    pp = np.arange(128)
    sel4 = np.stack([(((pp % 64) >> 4) == k) for k in range(4)]).astype(np.float32)
    sel16 = np.zeros((16, 16 * 128), np.float32)
    for e in range(16):
        sel16[e, e * 128:(e + 1) * 128] = 1.0
    in_maps = []
    for b in range(NCORES):
        in_maps.append({
            "x_in": np.ascontiguousarray(x[b]), "gate_w": gate_w,
            "gup": gup_a, "gus": gus_a, "dnp": dnp_a, "dns": dns_a,
            "sel4": sel4, "sel16": sel16, "shbank": _shbank(),
            "c_ident": c["ident"], "c_cbank": c["cbank"],
        })
    res = run_bass_kernel_spmd(nc, in_maps, core_ids=list(range(NCORES)))
    return np.stack([res.results[b]["x_out"] for b in range(NCORES)])



DIN, DST, NHM, HDM, CHK = 4096, 128, 64, 64, 256
CONVD = DIN + 2 * DST

def build_l1attn(debug=False, fused=None):
    if fused is None:
        p = P("l1attn")
        C = Common(p)
        x_in_ap = x_out_ap = None
    else:
        p, C, x_in_ap, x_out_ap = fused
    p.push_scope("a1_")
    x_in = x_in_ap if x_in_ap is not None else p.dram("x_in", (T, D), F32, kind="ExternalInput").ap()
    x_out = x_out_ap if x_out_ap is not None else p.dram("x_out", (T, D), F32, kind="ExternalOutput").ap()
    w_inp = p.dram("in_proj_w", (8512, D), F32, kind="ExternalInput").ap()
    w_outp = p.dram("out_proj_w", (D, DIN), F32, kind="ExternalInput").ap()
    conv_w_in = p.dram("conv_w2d", (128, 34 * 4), F32, kind="ExternalInput").ap()
    conv_b_in = p.dram("conv_b2d", (128, 34), F32, kind="ExternalInput").ap()
    dt_bias_in = p.dram("dt_bias2d", (64, 1), F32, kind="ExternalInput").ap()
    a_log_in = p.dram("a_log2d", (64, 1), F32, kind="ExternalInput").ap()
    d_in = p.dram("d2d", (64, 1), F32, kind="ExternalInput").ap()
    nw_in = p.dram("norm_w2d", (128, 32), F32, kind="ExternalInput").ap()
    sel64_in = p.dram("sel64", (64, 64 * 128), F32, kind="ExternalInput").ap()
    tri_in = p.dram("tri2", (128, 2 * 256), F32, kind="ExternalInput").ap()

    xlnT_hbm = p.dram("xlnT_scratch", (8, D, 512), BF16, kind="Internal").ap()
    w_inT = p.dram("w_inT_t", (67, D, 128), BF16, kind="Internal").ap()
    w_outT = p.dram("w_outT_t", (16, DIN, 128), BF16, kind="Internal").ap()
    dbk = "ExternalOutput" if debug else "Internal"
    if debug:
        dbg2 = p.dram("dbg2", (8, 128, 512), F32, kind="ExternalOutput").ap()
    zxb = p.dram("zxbcdtT", (8512, T), BF16, kind=dbk).ap()
    dtf32_hbm = p.dram("dt_f32", (64, T), F32, kind="Internal").ap()
    convT = p.dram("convT_s", (CONVD, T), BF16, kind=dbk).ap()
    y_hbm = p.dram("y_scr", (16, DIN, 256), BF16, kind=dbk).ap()

    ps_tr = [p.psum(f"ps_tr{i}", (128, 128), BF16) for i in range(2)]
    ps_trf0 = p.psum("ps_trf0", (128, 128), F32)
    ps_trf = [ps_trf0, ps_trf0]
    ps_mm = p.psum("ps_mm", (128, 512), F32)

    LD, LG, ST, TSm, VS, SS = C.LD, C.LG, C.ST, C.TS, C.VS, C.SS
    C.load_consts()
    sel64 = p.sbuf("sel64_sb", (64, 64 * 128), F32)
    tri2 = p.sbuf("tri2_sb", (128, 2 * 256), BF16)
    ones_rf = p.sbuf("ones_rf", (1, 128), F32)
    adt = p.sbuf("adt_fm", (64, T), F32)
    dt_tm = p.sbuf("dt_tm_all", (128, 32 * 64), F32)
    db_t = p.sbuf("db_t", (128, 64), F32)
    state = p.sbuf("state_sb", (128, NHM * 64), F32)
    p.op("gpsimd", "dma_start", out=sel64[:], in_=sel64_in, inc=(LG, 16))
    p.op("gpsimd", "dma_start", out=tri2[:], in_=tri_in, inc=(LG, 16))
    p.op("vector", "memset", ones_rf[:], 1.0, inc=(VS, 1))
    consts_ld = [LD.at(), LG.at(), VS.at()]

    pw = ExitStack()
    wrow = p.sbuf_scoped(pw, "wrow", (128, DIN), BF16)
    wst = p.sbuf_scoped(pw, "wst", (128, DIN), BF16)
    for roc in range(67):
        nr = 128 if roc < 66 else 64
        p.op("gpsimd", "dma_start", out=wrow[0:nr, 0:D],
             in_=w_inp[roc * 128: roc * 128 + nr, :],
             wait=[LD.at()] if roc > 0 else consts_ld, inc=(LG, 16))
        p.wait("tensor", LG, LG.n)
        for hc in range(16):
            slot = hc % 2
            if hc >= 2:
                p.wait("tensor", SS, SS.n)
            p.op("tensor", "transpose", ps_tr[slot][:, 0:nr],
                 wrow[0:nr, hc * 128:(hc + 1) * 128], C.ident[0:nr, 0:nr],
                 inc=(TSm, 1))
            p.op("scalar", "activation", wst[:, hc * 128:hc * 128 + nr],
                 ps_tr[slot][:, 0:nr], AF.Copy, scale=1.0, wait=[TSm.at()], inc=(SS, 1))
        p.op("sync", "dma_start",
             out=w_inT[roc, :, 0:nr].rearrange("(hc p) o -> p hc o", p=128),
             in_=wst[:, 0:2048].rearrange("p (hc o) -> p hc o", hc=16)[:, :, 0:nr],
             wait=[SS.at()], inc=(LD, 16))
    for oc in range(16):
        p.op("gpsimd", "dma_start", out=wrow[:, 0:DIN],
             in_=w_outp[oc * 128:(oc + 1) * 128, :], wait=[LD.at()], inc=(LG, 16))
        p.wait("tensor", LG, LG.n)
        for dc in range(32):
            slot = dc % 2
            if dc >= 2:
                p.wait("tensor", SS, SS.n)
            p.op("tensor", "transpose", ps_tr[slot][:],
                 wrow[:, dc * 128:(dc + 1) * 128], C.ident[:], inc=(TSm, 1))
            p.op("scalar", "activation", wst[:, dc * 128:(dc + 1) * 128],
                 ps_tr[slot][:], AF.Copy, scale=1.0, wait=[TSm.at()], inc=(SS, 1))
        p.op("sync", "dma_start",
             out=w_outT[oc].rearrange("(dc p) o -> p dc o", p=128),
             in_=wst[:, 0:DIN].rearrange("p (dc o) -> p dc o", dc=32),
             wait=[SS.at()], inc=(LD, 16))
    pw.close()
    wT_done_ld = LD.at()

    pa = ExitStack()
    xr = [p.sbuf_scoped(pa, f"xr{i}", (128, D), F32) for i in range(2)]
    xsq = p.sbuf_scoped(pa, "xsq", (128, D), F32)
    stats = p.sbuf_scoped(pa, "stats", (128, 8), F32)
    xln = [p.sbuf_scoped(pa, f"xln{i}", (128, D), BF16) for i in range(2)]
    stage = p.sbuf_scoped(pa, "stage", (128, 16 * 512), BF16)
    wblk = [p.sbuf_scoped(pa, f"wblk{i}", (128, 16 * 128), BF16) for i in range(2)]
    evb = [p.sbuf_scoped(pa, f"evb{i}", (128, 512), BF16) for i in range(2)]
    evbf = p.sbuf_scoped(pa, "evbf", (64, 512), F32)
    ps_mm2 = p.psum_scoped(pa, "ps_mm2", (128, 512), F32)

    zxb_done = {}
    for g in range(8):
        stage_done = emit_ln_stage(p, C, x_in, stage, xr, xsq, stats, xln, ps_tr, g,
                                   consts_ld if g == 0 else None)
        mm_v = {}
        ev_v = {}
        st_v = {}
        for roc in range(67):
            nr = 128 if roc < 66 else 64
            lb = roc % 2
            wl0 = [wT_done_ld] if (g == 0 and roc < 2) else []
            if roc >= 2:
                wl0.append((TSm, mm_v[roc - 2]))
            elif g > 0:
                wl0.append((TSm, TSm.n))
            p.op("gpsimd", "dma_start", out=wblk[lb][:],
                 in_=w_inT[roc].rearrange("(hc p) o -> p hc o", p=128),
                 wait=wl0, inc=(LG, 16))
            wld = LG.at()
            pmb = ps_mm if roc % 2 == 0 else ps_mm2
            w0 = [stage_done, wld]
            if roc >= 2:
                w0.append((SS, ev_v[roc - 2]))
            for hc in range(16):
                p.op("tensor", "matmul", pmb[0:nr, 0:512],
                     wblk[lb][:, hc * 128: hc * 128 + nr],
                     stage[:, hc * 512:(hc + 1) * 512],
                     start=(hc == 0), stop=(hc == 15),
                     wait=w0 if hc == 0 else [],
                     inc=(TSm, 1) if hc == 15 else None)
            mm_v[roc] = TSm.n
            if roc < 66:
                wv = [TSm.at()]
                if roc >= 2:
                    wv.append((LD, st_v[roc - 2]))
                elif g > 0:
                    wv.append((LD, LD.n))
                p.op("scalar", "activation", evb[lb][0:nr, :], pmb[0:nr, 0:512], AF.Copy,
                     scale=1.0, wait=wv, inc=(SS, 1))
                ev_v[roc] = SS.n
                p.op("sync", "dma_start", out=zxb[roc * 128: roc * 128 + nr,
                                                  g * 512:(g + 1) * 512],
                     in_=evb[lb][0:nr, :], wait=[SS.at()], inc=(LD, 16))
                st_v[roc] = LD.n
                if g == 7:
                    zxb_done[roc] = LD.n
            else:
                p.op("scalar", "activation", evbf[0:nr, :], pmb[0:nr, 0:512], AF.Copy,
                     scale=1.0, wait=[TSm.at()], inc=(SS, 1))
                ev_v[roc] = SS.n
                p.op("sync", "dma_start", out=dtf32_hbm[0:nr, g * 512:(g + 1) * 512],
                     in_=evbf[0:nr, :], wait=[SS.at()], inc=(LD, 16))
    pa.close()
    inproj_ld = LD.at()

    pc = ExitStack()
    xc = p.sbuf_scoped(pc, "xc", (128, T + 4), BF16)
    cw = p.sbuf_scoped(pc, "cw_sb", (128, 34 * 4), F32)
    cbv = p.sbuf_scoped(pc, "cbv_sb", (128, 34), F32)
    cacc = p.sbuf_scoped(pc, "cacc", (128, T), F32)
    cout = p.sbuf_scoped(pc, "cout", (128, T), BF16)
    p.op("sync", "dma_start", out=cw[:], in_=conv_w_in, inc=(LD, 16))
    p.op("sync", "dma_start", out=cbv[:], in_=conv_b_in, inc=(LD, 16))
    for cc in range(34):
        p.op("vector", "memset", xc[:, 0:3], 0.0, wait=[SS.at()] if cc > 0 else [],
             inc=(VS, 1))
        p.op("sync", "dma_start", out=xc[:, 3:3 + T],
             in_=zxb[DIN + cc * 128: DIN + (cc + 1) * 128, :],
             wait=[(LD, zxb_done[32 + cc]), VS.at()], inc=(LD, 16))
        p.wait("vector", LD, LD.n)
        p.op("vector", "tensor_scalar", cacc[:], xc[:, 0:T], cw[:, 4 * cc:4 * cc + 1],
             None, ALU.mult)
        for k in range(1, 4):
            p.op("vector", "scalar_tensor_tensor", cacc[:], xc[:, k:k + T],
                 cw[:, 4 * cc + k:4 * cc + k + 1], cacc[:], ALU.mult, ALU.add)
        p.op("vector", "drain")
        p.op("vector", "tensor_copy", xc[:, 0:1], xc[:, 0:1], inc=(VS, 1))
        p.op("scalar", "activation", cout[:], cacc[:], AF.Silu,
             bias=cbv[:, cc:cc + 1], scale=1.0, wait=[VS.at()], inc=(SS, 1))
        p.op("sync", "dma_start", out=convT[cc * 128:(cc + 1) * 128, :], in_=cout[:],
             wait=[SS.at()], inc=(LD, 16))
    pc.close()
    conv_ld = LD.at()

    pef = ExitStack()
    ps_row = p.psum_scoped(pef, "ps_row", (128, 256), F32)
    ps_row2 = ps_mm[0:128, 0:256]
    ps_y = p.psum_scoped(pef, "ps_y", (64, 256), F32)
    ps_y2 = p.psum_scoped(pef, "ps_y2", (64, 256), F32)
    ps_s2 = p.psum_scoped(pef, "ps_s2", (128, 64), F32)
    pe = ExitStack()
    dtw = p.sbuf_scoped(pe, "dtw", (64, T), BF16)
    dtraw = p.sbuf_scoped(pe, "dtraw", (64, T), F32)
    dtf = p.sbuf_scoped(pe, "dtf", (64, T), F32)
    dtt1 = p.sbuf_scoped(pe, "dtt1", (64, T), F32)
    dtt2 = p.sbuf_scoped(pe, "dtt2", (64, T), F32)
    dtb = p.sbuf_scoped(pe, "dtb", (64, 1), F32)
    alog = p.sbuf_scoped(pe, "alog", (64, 4), F32)
    dcol = p.sbuf_scoped(pe, "dcol", (64, 1), F32)
    lr1 = p.sbuf_scoped(pe, "lr1", (1, 64), F32)
    p.op("sync", "dma_start", out=dtraw[:], in_=dtf32_hbm,
         wait=[inproj_ld], inc=(LD, 16))
    p.op("sync", "dma_start", out=dtb[:], in_=dt_bias_in, inc=(LD, 16))
    p.op("sync", "dma_start", out=alog[:, 0:1], in_=a_log_in, inc=(LD, 16))
    p.op("sync", "dma_start", out=dcol[:], in_=d_in, inc=(LD, 16))
    p.wait("vector", LD, LD.n)
    p.op("vector", "tensor_scalar", dtf[:], dtraw[:], dtb[:], None, ALU.add, inc=(VS, 1))
    p.op("scalar", "activation", dtt1[:], dtf[:], AF.Abs, wait=[VS.at()])
    p.op("scalar", "activation", dtt1[:], dtt1[:], AF.Exp, scale=-1.0)
    p.op("scalar", "activation", dtt1[:], dtt1[:], AF.Ln,
         bias=C.cbank_ap(CB_ONE, 64), scale=1.0, inc=(SS, 1))
    p.op("vector", "tensor_scalar", dtt2[:], dtf[:], 0.0, None, ALU.max,
         wait=[SS.at()])
    p.op("vector", "tensor_add", dtf[:], dtt2[:], dtt1[:], inc=(VS, 1))
    p.op("scalar", "activation", alog[:, 1:2], alog[:, 0:1], AF.Exp, wait=[VS.at()],
         inc=(SS, 1))
    p.op("vector", "tensor_scalar", alog[:, 2:3], alog[:, 1:2], -1.0, None, ALU.mult,
         wait=[SS.at()], inc=(VS, 1))
    p.op("vector", "drain")
    p.op("vector", "tensor_scalar", adt[:], dtf[:], alog[:, 2:3], None, ALU.mult,
         inc=(VS, 1))
    p.op("vector", "tensor_copy", dtw[:], dtf[:], inc=(VS, 1))
    for lc in range(32):
        slot = lc % 2
        if lc >= 2:
            p.wait("tensor", SS, SS.n)
        p.op("tensor", "transpose", ps_tr[slot][:, 0:64],
             dtw[:, lc * 128:(lc + 1) * 128], C.ident[0:64, 0:64],
             wait=[VS.at()] if lc == 0 else [], inc=(TSm, 1))
        p.op("scalar", "activation", dt_tm[:, lc * 64:(lc + 1) * 64],
             ps_tr[slot][:, 0:64], AF.Copy, scale=1.0, wait=[TSm.at()], inc=(SS, 1))
    p.op("tensor", "transpose", ps_trf[0][0:1, 0:64], dcol[:], C.ident_f[0:64, 0:64],
         wait=[VS.at()], inc=(TSm, 1))
    p.op("vector", "tensor_copy", lr1[:], ps_trf[0][0:1, 0:64], wait=[TSm.at()],
         inc=(VS, 1))
    p.op("tensor", "matmul", ps_s2[:], ones_rf[:], lr1[:], start=True, stop=True,
         wait=[VS.at()], inc=(TSm, 1))
    p.op("scalar", "activation", db_t[:], ps_s2[:], AF.Copy, scale=1.0,
         wait=[TSm.at()], inc=(SS, 1))
    pe.close()
    dt_done = (SS.at(), VS.at())

    pf = ExitStack()
    xs_sb = p.sbuf_scoped(pf, "xs_sb", (128, 32 * 256), BF16)
    bct = p.sbuf_scoped(pf, "bct", (128, 2 * 256), BF16)
    b_tm = p.sbuf_scoped(pf, "b_tm", (128, 2 * 128), BF16)
    x_tm = p.sbuf_scoped(pf, "x_tm", (128, 2 * 4096), BF16)
    xdt = p.sbuf_scoped(pf, "xdt", (128, 2 * 4096), BF16)
    acs = p.sbuf_scoped(pf, "acs", (64, 256), F32)
    acst = p.sbuf_scoped(pf, "acst", (128, 2 * 64), F32)
    lastb = p.sbuf_scoped(pf, "lastb", (128, 64), F32)
    cdb = p.sbuf_scoped(pf, "cdb", (128, 64), F32)
    dec = p.sbuf_scoped(pf, "dec", (128, 2 * 64), F32)
    cbm = p.sbuf_scoped(pf, "cbm", (128, 2 * 256), BF16)
    dx = [p.sbuf_scoped(pf, f"dx{i}", (128, 2 * 256), F32) for i in range(2)]
    exm = [p.sbuf_scoped(pf, f"exm{i}", (128, 2 * 256), BF16) for i in range(2)]
    p_sb = [p.sbuf_scoped(pf, f"p_sb{i}", (128, 2 * 256), BF16) for i in range(2)]
    exa = [p.sbuf_scoped(pf, f"exa{i}", (128, 256), BF16) for i in range(2)]
    cet = [p.sbuf_scoped(pf, f"cet{i}", (128, 256), BF16) for i in range(2)]
    bd = [p.sbuf_scoped(pf, f"bd{i}", (128, 2 * 128), BF16) for i in range(2)]
    yt_sb = p.sbuf_scoped(pf, "yt_sb", (128, 32 * 256), BF16)
    state_bf = p.sbuf_scoped(pf, "state_bf", (128, NHM * 64), BF16)
    lr2 = p.sbuf_scoped(pf, "lr2", (1, 64), F32)

    p.op("vector", "memset", state[:], 0.0, wait=[dt_done[0]], inc=(VS, 1))

    for c in range(16):
        t0 = c * 256
        p.op("sync", "dma_start", out=xs_sb[:],
             in_=convT[0:DIN, t0:t0 + 256].rearrange("(hc p) t -> p hc t", p=128),
             wait=[conv_ld, TSm.at(), VS.at()], inc=(LD, 16))
        p.op("sync", "dma_start", out=bct[:, 0:256], in_=convT[DIN:DIN + 128, t0:t0 + 256],
             inc=(LD, 16))
        p.op("sync", "dma_start", out=bct[:, 256:512],
             in_=convT[DIN + 128:DIN + 256, t0:t0 + 256], inc=(LD, 16))
        ld_c = LD.at()
        p.op("vector", "memset", dx[0][0:64, 0:256], 1.0, wait=[ld_c])
        p.op("vector", "drain")
        p.op("vector", "tensor_tensor_scan", acs[:], dx[0][0:64, 0:256], adt[:, t0:t0 + 256],
             0.0, ALU.mult, ALU.add, inc=(VS, 1))
        p.op("vector", "drain")
        for lc in range(2):
            p.op("tensor", "transpose", ps_trf[lc][:, 0:64],
                 acs[:, lc * 128:(lc + 1) * 128], C.ident_f[0:64, 0:64],
                 wait=[VS.at(), SS.at()], inc=(TSm, 1))
            p.op("scalar", "activation", acst[:, lc * 64:(lc + 1) * 64],
                 ps_trf[lc][:, 0:64], AF.Copy, scale=1.0, wait=[TSm.at()], inc=(SS, 1))
        p.op("tensor", "transpose", ps_trf[0][0:1, 0:64], acs[:, 255:256],
             C.ident_f[0:64, 0:64], wait=[SS.at()], inc=(TSm, 1))
        p.op("vector", "tensor_copy", lr2[:], ps_trf[0][0:1, 0:64], wait=[TSm.at()],
             inc=(VS, 1))
        p.op("vector", "drain")
        p.op("tensor", "matmul", ps_s2[:], ones_rf[:], lr2[:], start=True, stop=True,
             wait=[VS.at()], inc=(TSm, 1))
        p.op("scalar", "activation", lastb[:], ps_s2[:], AF.Copy, scale=1.0,
             wait=[TSm.at()], inc=(SS, 1))
        p.op("scalar", "activation", cdb[:], lastb[:], AF.Exp, inc=(SS, 1))
        for lc in range(2):
            p.op("vector", "tensor_sub", dx[1][:, 0:64], lastb[:],
                 acst[:, lc * 64:(lc + 1) * 64], wait=[SS.at()])
            p.op("vector", "drain")
            p.op("vector", "tensor_copy", dx[1][:, 64:128], dx[1][:, 0:64], inc=(VS, 1))
            p.op("scalar", "activation", dec[:, lc * 64:(lc + 1) * 64], dx[1][:, 0:64],
                 AF.Exp, wait=[VS.at()], inc=(SS, 1))
        for lc in range(2):
            slot = lc
            p.op("tensor", "transpose", ps_tr[slot][:],
                 bct[:, lc * 128:(lc + 1) * 128], C.ident[:],
                 wait=[SS.at(), VS.at()], inc=(TSm, 1))
            p.op("scalar", "activation", b_tm[:, lc * 128:(lc + 1) * 128],
                 ps_tr[slot][:], AF.Copy, scale=1.0, wait=[TSm.at()], inc=(SS, 1))
        for hc in range(32):
            for lc in range(2):
                slot = (hc * 2 + lc) % 2
                if hc * 2 + lc >= 2:
                    p.wait("tensor", SS, SS.n)
                p.op("tensor", "transpose", ps_tr[slot][:],
                     xs_sb[:, hc * 256 + lc * 128: hc * 256 + (lc + 1) * 128],
                     C.ident[:], inc=(TSm, 1))
                p.op("scalar", "activation",
                     x_tm[:, lc * 4096 + hc * 128: lc * 4096 + (hc + 1) * 128],
                     ps_tr[slot][:], AF.Copy, scale=1.0, wait=[TSm.at()], inc=(SS, 1))
        xtm_done = SS.at()
        for lc in range(2):
            for h in range(NHM):
                p.op("vector", "tensor_scalar",
                     xdt[:, lc * 4096 + h * 64: lc * 4096 + (h + 1) * 64],
                     x_tm[:, lc * 4096 + h * 64: lc * 4096 + (h + 1) * 64],
                     dt_tm[:, (c * 2 + lc) * 64 + h: (c * 2 + lc) * 64 + h + 1],
                     None, ALU.mult,
                     wait=[xtm_done] if (lc == 0 and h == 0) else [])
        p.op("vector", "tensor_copy", dx[0][0:1, 0:1], dx[0][0:1, 0:1], inc=(VS, 1))
        xdt_done = VS.at()
        for sc in range(2):
            p.op("tensor", "matmul", ps_row[:], bct[:, sc * 128:(sc + 1) * 128],
                 bct[:, 256:512], start=True, stop=True,
                 wait=[xdt_done] if sc == 0 else [VS.at()], inc=(TSm, 1))
            p.op("vector", "tensor_mul", cbm[:, sc * 256:(sc + 1) * 256], ps_row[:],
                 tri2[:, sc * 256:(sc + 1) * 256], wait=[TSm.at()], inc=(VS, 1))
        p.op("vector", "tensor_copy", state_bf[:], state[:], wait=[VS.at()], inc=(VS, 1))
        dxv = {}; exav = {}; pv = {}; cetv = {}; ytv = {}; rowv = {}; stv = {}
        yqv = {}; smv = {}
        for h in range(NHM):
            par = h % 2
            rb = ps_row[:, 0:256] if par == 0 else ps_row2
            w0 = []
            if h >= 2:
                w0 += [(VS, dxv[h - 2]), (SS, exav[h - 2])]
            else:
                w0 += [(VS, VS.at()[1]), (SS, SS.at()[1])]
            p.op("tensor", "matmul", rb, sel64[:, h * 128:(h + 1) * 128],
                 acs[:], start=True, stop=True, wait=w0, inc=(TSm, 1))
            rowv[h] = TSm.n
            for sc in range(2):
                p.op("vector", "tensor_scalar", dx[par][:, sc * 256:(sc + 1) * 256],
                     rb,
                     acst[:, sc * 64 + h: sc * 64 + h + 1], 0.0,
                     ALU.subtract, ALU.min,
                     wait=[(TSm, rowv[h])] if sc == 0 else [], inc=(VS, 1))
                p.op("scalar", "activation", exm[par][:, sc * 256:(sc + 1) * 256],
                     dx[par][:, sc * 256:(sc + 1) * 256], AF.Exp,
                     wait=[VS.at()], inc=(SS, 1))
                wp = [SS.at()]
                if sc == 0 and h >= 2:
                    wp.append((TSm, yqv[h - 2]))
                p.op("vector", "tensor_mul", p_sb[par][:, sc * 256:(sc + 1) * 256],
                     exm[par][:, sc * 256:(sc + 1) * 256],
                     cbm[:, sc * 256:(sc + 1) * 256], wait=wp, inc=(VS, 1))
            dxv[h] = VS.n
            pv[h] = VS.n
            p.op("scalar", "activation", exa[par][:], rb, AF.Exp,
                 wait=[VS.at()], inc=(SS, 1))
            exav[h] = SS.n
            wc = [SS.at()]
            if h >= 2:
                wc.append((TSm, yqv[h - 2]))
            p.op("vector", "tensor_mul", cet[par][:], bct[:, 256:512], exa[par][:],
                 wait=wc, inc=(VS, 1))
            cetv[h] = VS.n
            yb = ps_y if par == 0 else ps_y2
            wy = [(VS, pv[h])]
            if h >= 2:
                wy.append((VS, ytv[h - 2]))
            for sc in range(2):
                p.op("tensor", "matmul", yb[:],
                     xdt[:, sc * 4096 + h * 64: sc * 4096 + (h + 1) * 64],
                     p_sb[par][:, sc * 256:(sc + 1) * 256],
                     start=(sc == 0), stop=False, wait=wy if sc == 0 else [])
            p.op("tensor", "matmul", yb[:], state_bf[:, h * 64:(h + 1) * 64], cet[par][:],
                 start=False, stop=True, wait=[(VS, cetv[h])], inc=(TSm, 1))
            yq = TSm.n
            yqv[h] = yq
            hc, off = h // 2, (h % 2) * 64
            p.op("vector", "scalar_tensor_tensor",
                 yt_sb[off:off + 64, hc * 256:(hc + 1) * 256],
                 xs_sb[off:off + 64, hc * 256:(hc + 1) * 256],
                 db_t[off:off + 64, h:h + 1], yb[:], ALU.mult, ALU.add,
                 wait=[(TSm, yq)], inc=(VS, 1))
            ytv[h] = VS.n
            for sc in range(2):
                wb = []
                if sc == 0 and h >= 2:
                    wb.append((TSm, smv[h - 2]))
                p.op("vector", "tensor_scalar", bd[par][:, sc * 128:(sc + 1) * 128],
                     b_tm[:, sc * 128:(sc + 1) * 128],
                     dec[:, sc * 64 + h: sc * 64 + h + 1], None, ALU.mult, wait=wb,
                     inc=(VS, 1))
            bdv = VS.n
            ws = [(VS, bdv)]
            if h >= 1:
                ws.append((VS, stv[h - 1]))
            for sc in range(2):
                p.op("tensor", "matmul", ps_s2[:], bd[par][:, sc * 128:(sc + 1) * 128],
                     xdt[:, sc * 4096 + h * 64: sc * 4096 + (h + 1) * 64],
                     start=(sc == 0), stop=(sc == 1),
                     wait=ws if sc == 0 else [], inc=(TSm, 1) if sc == 1 else None)
                if sc == 1:
                    smv[h] = TSm.n
            p.op("vector", "scalar_tensor_tensor", state[:, h * 64:(h + 1) * 64],
                 state[:, h * 64:(h + 1) * 64], cdb[:, h:h + 1], ps_s2[:],
                 ALU.mult, ALU.add, wait=[TSm.at()], inc=(VS, 1))
            stv[h] = VS.n
        p.op("sync", "dma_start",
             out=y_hbm[c].rearrange("(hc p) t -> p hc t", p=128),
             in_=yt_sb[:].rearrange("p (hc t) -> p hc t", hc=32),
             wait=[VS.at()], inc=(LD, 16))
    pf.close()
    pef.close()
    y_ld = LD.at()

    pg = ExitStack()
    y_sb = p.sbuf_scoped(pg, "y_sb", (128, 32 * 512), BF16)
    z_sb = p.sbuf_scoped(pg, "z_sb", (128, 32 * 512), BF16)
    sqb = [p.sbuf_scoped(pg, f"sqb{i}", (128, 512), BF16) for i in range(2)]
    ssq_f = p.sbuf_scoped(pg, "ssq_f", (1, 512), F32)
    rst = p.sbuf_scoped(pg, "rst", (1, 512), BF16)
    rb = p.sbuf_scoped(pg, "rb", (128, 512), BF16)
    nw = p.sbuf_scoped(pg, "nw32", (128, 32), F32)
    wo = [p.sbuf_scoped(pg, f"wo{i}", (128, 32 * 128), BF16) for i in range(2)]
    y2 = p.sbuf_scoped(pg, "y2", (128, 16 * 512), BF16)
    xres = p.sbuf_scoped(pg, "xres", (128, D), F32)
    ps_dn2 = p.psum_scoped(pg, "ps_dn2", (1, 512), F32)
    ps_r2 = p.psum_scoped(pg, "ps_r2", (128, 512), F32)
    ps_mm2g = p.psum_scoped(pg, "ps_mm2g", (128, 512), F32)

    p.op("sync", "dma_start", out=nw[:], in_=nw_in, inc=(LD, 16))
    for g in range(8):
        for half in range(2):
            cc = 2 * g + half
            p.op("sync", "dma_start",
                 out=y_sb[:, 0:32 * 512].rearrange("p (hc t) -> p hc t", hc=32)[:, :, half * 256:(half + 1) * 256],
                 in_=y_hbm[cc].rearrange("(hc p) t -> p hc t", p=128),
                 wait=[y_ld, TSm.at()] if half == 0 else [], inc=(LD, 16))
        for i in range(4):
            p.op("sync", "dma_start",
                 out=z_sb[:].rearrange("p (hc t) -> p hc t", hc=32)[:, i * 8:(i + 1) * 8, :],
                 in_=zxb[i * 1024:(i + 1) * 1024, g * 512:(g + 1) * 512]
                     .rearrange("(hc p) t -> p hc t", p=128),
                 inc=(LD, 16))
        gld = LD.at()
        p.wait("scalar", LD, LD.n)
        p.op("scalar", "activation", z_sb[:], z_sb[:], AF.Silu, inc=(SS, 1))
        p.op("vector", "tensor_mul", y_sb[:], y_sb[:], z_sb[:], wait=[SS.at()],
             inc=(VS, 1))
        y32_done = VS.at()
        sq_mm = {}
        for hc in range(32):
            wq = [y32_done] if hc == 0 else []
            if hc >= 2:
                wq.append((TSm, sq_mm[hc - 2]))
            p.op("scalar", "activation", sqb[hc % 2][:], y_sb[:, hc * 512:(hc + 1) * 512],
                 AF.Square, wait=wq, inc=(SS, 1))
            p.op("tensor", "matmul", ps_dn2[:], C.ones_col[:], sqb[hc % 2][:],
                 start=(hc == 0), stop=(hc == 31), wait=[SS.at()],
                 inc=(TSm, 1))
            sq_mm[hc] = TSm.n
        p.op("scalar", "activation", ssq_f[:], ps_dn2[:], AF.Ln, scale=1.0 / DIN,
             bias=C.cbank_ap(CB_EPS_LN, 1), wait=[TSm.at()])
        p.op("scalar", "activation", rst[:], ssq_f[:], AF.Exp, scale=-0.5, inc=(SS, 1))
        p.op("tensor", "matmul", ps_r2[:], C.ones_row[:], rst[:], start=True, stop=True,
             wait=[SS.at()], inc=(TSm, 1))
        p.op("scalar", "activation", rb[:], ps_r2[:], AF.Copy, scale=1.0,
             wait=[TSm.at()], inc=(SS, 1))
        rb_done = SS.at()
        for hc in range(32):
            p.op("vector", "scalar_tensor_tensor", y_sb[:, hc * 512:(hc + 1) * 512],
                 y_sb[:, hc * 512:(hc + 1) * 512], nw[:, hc:hc + 1], rb[:],
                 ALU.mult, ALU.mult, wait=[rb_done] if hc == 0 else [], inc=(VS, 1))
        nrm_done = VS.at()
        op_mm = {}
        op_ev = {}
        for oc in range(16):
            lb = oc % 2
            wl = [wT_done_ld]
            if oc >= 2:
                wl.append((TSm, op_mm[oc - 2]))
            else:
                wl += [(TSm, TSm.n)] + ([(VS, VS.n), (SS, SS.n)] if g == 0 else [])
            p.op("gpsimd", "dma_start", out=wo[lb][:],
                 in_=w_outT[oc].rearrange("(dc p) o -> p dc o", p=128),
                 wait=wl, inc=(LG, 16))
            wld = LG.at()
            pb = ps_mm if oc % 2 == 0 else ps_mm2g
            w0 = [nrm_done, wld]
            if oc >= 2:
                w0.append((SS, op_ev[oc - 2]))
            else:
                w0.append((SS, SS.n))
            for dc in range(32):
                p.op("tensor", "matmul", pb[:, 0:512],
                     wo[lb][:, dc * 128:(dc + 1) * 128],
                     y_sb[:, dc * 512:(dc + 1) * 512],
                     start=(dc == 0), stop=(dc == 31),
                     wait=w0 if dc == 0 else [],
                     inc=(TSm, 1) if dc == 31 else None)
            op_mm[oc] = TSm.n
            p.op("scalar", "activation", y2[:, oc * 512:(oc + 1) * 512],
                 pb[:, 0:512], AF.Copy, scale=1.0, wait=[TSm.at()], inc=(SS, 1))
            op_ev[oc] = SS.n
        y2_done = SS.at()
        for tt in range(4):
            trow = g * 512 + tt * 128
            p.op("sync", "dma_start", out=xres[:], in_=x_in[trow:trow + 128, :],
                 wait=[ST.at()], inc=(LD, 16))
            p.wait("vector", LD, LD.n)
            for oc in range(16):
                slot = oc % 2
                if oc >= 2:
                    p.wait("tensor", VS, VS.n)
                p.op("tensor", "transpose", ps_tr[slot][:],
                     y2[:, oc * 512 + tt * 128: oc * 512 + tt * 128 + 128],
                     C.ident[:],
                     wait=[y2_done, VS.at()] if oc < 2 else [], inc=(TSm, 1))
                p.op("vector", "tensor_add", xres[:, oc * 128:(oc + 1) * 128],
                     xres[:, oc * 128:(oc + 1) * 128], ps_tr[slot][:],
                     wait=[TSm.at()], inc=(VS, 1))
            p.op("sync", "dma_start", out=x_out[trow:trow + 128, :], in_=xres[:],
                 wait=[VS.at()], inc=(ST, 16))
    pg.close()
    p.wait("sync", ST, ST.n)
    p.pop_scope()
    return p


def run_l1attn(x, in_proj_w, conv_w, conv_b, dt_bias, A_log, Dp, norm_w, out_proj_w, debug=False):
    c = _consts()
    p = build_l1attn(debug=debug)
    nc = p.finalize()
    cw2d = np.zeros((128, 34 * 4), np.float32)
    cb2d = np.zeros((128, 34), np.float32)
    for cc in range(34):
        cw2d[:, 4 * cc:4 * (cc + 1)] = conv_w[cc * 128:(cc + 1) * 128, :]
        cb2d[:, cc] = conv_b[cc * 128:(cc + 1) * 128]
    sel64 = np.zeros((64, 64 * 128), np.float32)
    for h in range(64):
        sel64[h, h * 128:(h + 1) * 128] = 1.0
    sp = np.arange(128)[:, None]
    lv = np.arange(256)[None, :]
    tri2 = np.concatenate([(sp <= lv).astype(np.float32),
                           (128 + sp <= lv).astype(np.float32)], axis=1)
    in_maps = []
    for b in range(NCORES):
        in_maps.append({
            "x_in": np.ascontiguousarray(x[b]),
            "in_proj_w": in_proj_w, "out_proj_w": out_proj_w,
            "conv_w2d": cw2d, "conv_b2d": cb2d,
            "dt_bias2d": dt_bias.reshape(64, 1), "a_log2d": A_log.reshape(64, 1),
            "d2d": Dp.reshape(64, 1),
            "norm_w2d": np.ascontiguousarray(norm_w.reshape(32, 128).T),
            "sel64": sel64, "tri2": tri2,
            "c_ident": c["ident"], "c_cbank": c["cbank"],
        })
    res = run_bass_kernel_spmd(nc, in_maps, core_ids=list(range(NCORES)))
    out = np.stack([res.results[b]["x_out"] for b in range(NCORES)])
    if debug:
        return out, res.results
    return out



_CONSTS = None

def _consts():
    global _CONSTS
    if _CONSTS is None:
        _CONSTS = host_consts()
    return _CONSTS


def deinterleave_wattn(w_attn):
    w = np.array(w_attn, np.float32, copy=True)
    perm = np.concatenate([np.arange(0, 128, 2), np.arange(1, 128, 2)])
    for base in range(0, 2048 + 512, 128):
        w[:, base:base + 128] = w[:, base + perm]
    return w


def run_l0attn(x, w_attn, w_proj, norm_w, debug=False):
    c = _consts()
    w_attn = deinterleave_wattn(w_attn)
    p = build_l0attn(debug=debug)
    nc = p.finalize()
    in_maps = []
    nw2d = np.ascontiguousarray(norm_w.reshape(16, 128).T)
    for b in range(NCORES):
        in_maps.append({
            "x_in": np.ascontiguousarray(x[b]),
            "w_attn": w_attn, "w_proj": w_proj, "norm_w": nw2d,
            "rope_c": c["rope_c"], "rope_s": c["rope_s"],
            "masks_diag": c["masks_diag"],
            "c_ident": c["ident"], "c_cbank": c["cbank"],
        })
    res = run_bass_kernel_spmd(nc, in_maps, core_ids=list(range(NCORES)))
    out = np.stack([res.results[b]["x_out"] for b in range(NCORES)])
    if debug:
        return out, res.results
    return out


def build_fused():
    p = P("mk")
    C = Common(p)
    x_in = p.dram("x_full_in", (T, D), F32, kind="ExternalInput").ap()
    x_out = p.dram("x_full_out", (T, D), F32, kind="ExternalOutput").ap()
    xA = p.dram("x_chain_a", (T, D), F32, kind="Internal").ap()
    xB = p.dram("x_chain_b", (T, D), F32, kind="Internal").ap()
    build_l0attn(fused=(p, C, x_in, xA))
    p.barrier(C)
    build_l0ffn(fused=(p, C, xA, xB))
    p.barrier(C)
    build_l1attn(fused=(p, C, xB, xA))
    p.barrier(C)
    build_l1ffn(fused=(p, C, xA, x_out))
    return p


_FUSED_NC = None

def _fused_nc():
    global _FUSED_NC
    if _FUSED_NC is None:
        p = build_fused()
        _FUSED_NC = p.finalize()
    return _FUSED_NC


def kernel(**inputs):
    f32 = lambda k: np.asarray(inputs[k], np.float32)
    c = _consts()
    nc = _fused_nc()
    w_attn = deinterleave_wattn(f32("L0_attn_w_attn"))
    gup_a = np.concatenate([np.asarray(inputs["L0_ffn_gate_up_packed"]),
                            np.asarray(inputs["L0_ffn_shared_gate_up_packed"])[None]], 0)
    gus_a = np.concatenate([f32("L0_ffn_gate_up_scales"),
                            f32("L0_ffn_shared_gate_up_scales")[None]], 0)
    dnp_a = np.concatenate([np.asarray(inputs["L0_ffn_down_packed"]),
                            np.asarray(inputs["L0_ffn_shared_down_packed"])[None]], 0)
    dns_a = np.concatenate([f32("L0_ffn_down_scales"),
                            f32("L0_ffn_shared_down_scales")[None]], 0)
    pp = np.arange(128)
    sel4 = np.stack([(((pp % 64) >> 4) == k) for k in range(4)]).astype(np.float32)
    sel16 = np.zeros((16, 16 * 128), np.float32)
    for e in range(16):
        sel16[e, e * 128:(e + 1) * 128] = 1.0
    sel64 = np.zeros((64, 64 * 128), np.float32)
    for h in range(64):
        sel64[h, h * 128:(h + 1) * 128] = 1.0
    sp = np.arange(128)[:, None]
    lv = np.arange(256)[None, :]
    tri2 = np.concatenate([(sp <= lv).astype(np.float32),
                           (128 + sp <= lv).astype(np.float32)], axis=1)
    conv_w = f32("L1_attn_conv_w")
    conv_b = f32("L1_attn_conv_b")
    cw2d = np.zeros((128, 34 * 4), np.float32)
    cb2d = np.zeros((128, 34), np.float32)
    for cc in range(34):
        cw2d[:, 4 * cc:4 * (cc + 1)] = conv_w[cc * 128:(cc + 1) * 128, :]
        cb2d[:, cc] = conv_b[cc * 128:(cc + 1) * 128]
    x = f32("L0_attn_x")
    base = {
        "w_attn": w_attn, "w_proj": f32("L0_attn_w_proj"),
        "norm_w": np.ascontiguousarray(f32("L0_attn_norm_w").reshape(16, 128).T),
        "rope_c": c["rope_c"], "rope_s": c["rope_s"], "masks_diag": c["masks_diag"],
        "c_ident": c["ident"], "c_cbank": c["cbank"],
        "gate_w": f32("L0_ffn_gate_w"),
        "gup": gup_a, "gus": gus_a, "dnp": dnp_a, "dns": dns_a,
        "sel4": sel4, "sel16": sel16, "shbank": _shbank(),
        "in_proj_w": f32("L1_attn_in_proj_w"), "out_proj_w": f32("L1_attn_out_proj_w"),
        "conv_w2d": cw2d, "conv_b2d": cb2d,
        "dt_bias2d": f32("L1_attn_dt_bias").reshape(64, 1),
        "a_log2d": f32("L1_attn_A_log").reshape(64, 1),
        "d2d": f32("L1_attn_D").reshape(64, 1),
        "norm_w2d": np.ascontiguousarray(f32("L1_attn_norm_w").reshape(32, 128).T),
        "sel64": sel64, "tri2": tri2,
        "weight": f32("L1_ffn_weight"),
        "bias2d": np.ascontiguousarray(f32("L1_ffn_bias").reshape(16, 128).T),
    }
    in_maps = []
    for b in range(NCORES):
        m = dict(base)
        m["x_full_in"] = np.ascontiguousarray(x[b])
        in_maps.append(m)
    res = run_bass_kernel_spmd(nc, in_maps, core_ids=list(range(NCORES)))
    return np.stack([res.results[b]["x_full_out"] for b in range(NCORES)]).astype(np.float32)
